# version 06
# baseline (speedup 1.0000x reference)
import numpy as np
from contextlib import ExitStack
import concourse.bass as bass
import concourse.mybir as mybir
from concourse.bass_utils import run_bass_kernel_spmd

F32 = mybir.dt.float32
BF16 = mybir.dt.bfloat16
AF = mybir.ActivationFunctionType
ALU = mybir.AluOpType

NC8 = 8
DM = 1024
T = 4096


def ts(i, n=128):
    return slice(i * n, (i + 1) * n)


class Sem:
    def __init__(self, es, nc, name):
        self.h = es.enter_context(nc.semaphore(name))
        self.name = name
        self.v = 0
        self.emitted = 0

    def reserve(self, n=1, step=1):
        out = [self.v + step * (i + 1) for i in range(n)]
        self.v = out[-1]
        return out

    def tag(self, instr, step=1):
        self.emitted += step
        instr.then_inc(self.h, step)


def check_sems(c):
    bad = [(s.name, s.v, s.emitted) for s in c.sems.values() if s.v != s.emitted]
    assert not bad, "sem reserve/tag mismatch (name, planned, emitted): %s" % bad


class Ctx:
    GUID = [0]

    def __init__(self, nc, es, block):
        self.nc, self.es, self.block = nc, es, block
        self.sems = {}
        self._uid = 0

    def sem(self, name):
        if name not in self.sems:
            self._uid += 1
            self.sems[name] = Sem(self.es, self.nc, f"{name}{self._uid}")
        return self.sems[name]

    def sb(self, name, shape, dtype):
        self._uid += 1
        return self.es.enter_context(self.nc.sbuf_tensor(f"{name}_{self._uid}", shape, dtype))

    def scope(self):
        from contextlib import contextmanager

        @contextmanager
        def _scope():
            with ExitStack() as es2:
                c2 = Ctx(self.nc, es2, self.block)
                c2.sems = self.sems
                Ctx.GUID[0] += 1
                c2._uid = Ctx.GUID[0] * 100000
                yield c2

        return _scope()

    def ps(self, name, shape, dtype=F32):
        self._uid += 1
        return self.es.enter_context(self.nc.psum_tensor(f"{name}_{self._uid}", shape, dtype))



def emit_binarize(c, w_dram, M, K, wT_sb, bw_scratch_dram, tag):
    block = c.block
    s_ld = c.sem("s_ld"); s_sc = c.sem("s_sc"); s_gp = c.sem("s_gp")
    s_act = c.sem("s_act"); s_dve = c.sem("s_dve")
    nh = M // 128
    wrow = c.sb(f"{tag}_wrow", [128, K], F32)
    sg = c.sb(f"{tag}_sg", [128, K], BF16)
    bwt = c.sb(f"{tag}_bw", [128, K], BF16)
    asum = c.sb(f"{tag}_asum", [128, 1], F32)
    dve0, gp0, act0 = s_dve.v, s_gp.v, s_act.v

    for i in range(nh):
        tok_ld = s_ld.reserve(1, 16)[0]

        @block.sync
        def _(sync, i=i, tok_ld=tok_ld):
            if i == 0:
                sync.wait_ge(s_dve.h, dve0)
                sync.wait_ge(s_gp.h, gp0)
                sync.wait_ge(s_act.h, act0)
            else:
                sync.wait_ge(s_gp.h, s_gp.v)
            s_ld.tag(sync.dma_start(out=wrow[:, :], in_=w_dram[ts(i), :]), 16)

        tok_sign = s_act.reserve(1)[0]

        @block.scalar
        def _(scalar, tok_ld=tok_ld):
            scalar.wait_ge(s_ld.h, tok_ld)
            s_act.tag(scalar.sign(sg[:, :], wrow[:, :]))

        tok_red, tok_bw = s_dve.reserve(2)

        @block.vector
        def _(vector, tok_ld=tok_ld, tok_sign=tok_sign, tok_red=tok_red):
            vector.wait_ge(s_ld.h, tok_ld)
            s_dve.tag(vector.tensor_reduce(asum[:, 0:1], wrow[:, :], mybir.AxisListType.X,
                                           ALU.add, apply_absolute_value=True))
            vector.wait_ge(s_dve.h, tok_red)
            vector.wait_ge(s_act.h, tok_sign)
            s_dve.tag(vector.tensor_scalar(bwt[:, :], sg[:, :], asum[:, 0:1], 1.0 / K,
                                           ALU.mult, ALU.mult))

        tok_wst = s_gp.reserve(1, 16)[0]

        @block.gpsimd
        def _(gpsimd, i=i, tok_bw=tok_bw):
            gpsimd.wait_ge(s_dve.h, tok_bw)
            s_gp.tag(gpsimd.dma_start(out=bw_scratch_dram[ts(i), :], in_=bwt[:, :]), 16)

    tok_tr = s_sc.reserve(K // 128, 16)

    @block.scalar
    def _(scalar):
        scalar.wait_ge(s_gp.h, s_gp.v)
        for kc in range(K // 128):
            s_sc.tag(scalar.dma_start_transpose(wT_sb[:, kc, :], bw_scratch_dram[:, ts(kc)]), 16)

    return tok_tr[-1]



class LNRes:
    pass


def emit_ln_chunk(c, x_t, xln_bf, t_bf, nG, Tc, ones_col, ones_row, xsq, ps_sum, ps_ssq,
                  ps_rep2, stat, tok_x, prev, eps_t=None, out_f32=False,
                  extra_dve_wait=None, ssq_start=True, sq_war=None):
    block = c.block
    s_mm = c.sem("s_mm"); s_act = c.sem("s_act"); s_dve = c.sem("s_dve"); s_ld = c.sem("s_ld")
    r = LNRes()
    mean_t = stat[0:1, 0:Tc]
    rstd_t = stat[0:1, Tc:2 * Tc]
    mean2_t = stat[0:1, 2 * Tc:3 * Tc]
    var_t = stat[0:1, 3 * Tc:4 * Tc]
    sd_t = stat[0:1, 2 * Tc:3 * Tc]

    tok_sq = s_act.reserve(nG)
    tok_mm = s_mm.reserve(2)
    r.tok_sum, r.tok_ssq = tok_mm
    r.tok_mean, r.tok_mean2 = s_act.reserve(2)
    r.tok_var = s_dve.reserve(1)[0]
    r.tok_sd = s_act.reserve(1)[0]
    r.tok_rstd = s_dve.reserve(1)[0]
    r.tok_rep = s_mm.reserve(1)[0]
    _, r.tok_xln = s_dve.reserve(2)

    @block.scalar
    def _(scalar):
        scalar.wait_ge(s_ld.h, tok_x)
        if prev is not None:
            scalar.wait_ge(s_mm.h, prev.tok_ssq)
        if sq_war is not None:
            scalar.wait_ge(sq_war[0].h, sq_war[1])
        for g in range(nG):
            s_act.tag(scalar.square(xsq[:, g, :], x_t[:, g, :]))

    @block.tensor
    def _(tensor):
        tensor.wait_ge(s_ld.h, tok_x)
        if prev is not None:
            tensor.wait_ge(s_act.h, prev.tok_mean)
            tensor.wait_ge(s_dve.h, prev.tok_var)
        for g in range(nG):
            ins = tensor.matmul(ps_sum[0:1, :], ones_col[:, 0:1], x_t[:, g, :],
                                start=(g == 0), stop=(g == nG - 1))
        s_mm.tag(ins)
        for g in range(nG):
            tensor.wait_ge(s_act.h, tok_sq[g])
            ins = tensor.matmul(ps_ssq[0:1, :], ones_col[:, 0:1], xsq[:, g, :],
                                start=(g == 0 and ssq_start), stop=(g == nG - 1))
        s_mm.tag(ins)

    @block.scalar
    def _(scalar):
        scalar.wait_ge(s_mm.h, r.tok_ssq)
        s_act.tag(scalar.mul(mean_t, ps_sum[0:1, :], 1.0 / DM))
        scalar.wait_ge(s_act.h, r.tok_mean)
        s_act.tag(scalar.square(mean2_t, mean_t))

    @block.vector
    def _(vector):
        vector.wait_ge(s_act.h, r.tok_mean2)
        s_dve.tag(vector.scalar_tensor_tensor(var_t, ps_ssq[0:1, :], 1.0 / DM, mean2_t,
                                              ALU.mult, ALU.subtract))

    @block.scalar
    def _(scalar):
        scalar.wait_ge(s_dve.h, r.tok_var)
        s_act.tag(scalar.activation(sd_t, var_t, AF.Sqrt, bias=eps_t[0:1, 0:1]))

    @block.vector
    def _(vector):
        vector.wait_ge(s_act.h, r.tok_sd)
        s_dve.tag(vector.reciprocal(rstd_t, sd_t))

    @block.tensor
    def _(tensor):
        tensor.wait_ge(s_dve.h, r.tok_rstd)
        if prev is not None:
            tensor.wait_ge(s_dve.h, prev.tok_xln)
        ins = tensor.matmul(ps_rep2[:, :], ones_row[0:1, :], stat[0:1, 0:2 * Tc],
                            start=True, stop=True)
        s_mm.tag(ins)

    @block.vector
    def _(vector):
        vector.wait_ge(s_mm.h, r.tok_rep)
        if extra_dve_wait is not None:
            vector.wait_ge(s_dve.h, extra_dve_wait)
        mrep_b = ps_rep2[:, 0:Tc].unsqueeze(1).broadcast_to([128, nG, Tc])
        rrep_b = ps_rep2[:, Tc:2 * Tc].unsqueeze(1).broadcast_to([128, nG, Tc])
        s_dve.tag(vector.scalar_tensor_tensor(t_bf[:, :, :], x_t[:, :, :], 1.0, mrep_b,
                                              ALU.mult, ALU.subtract))
        vector.wait_ge(s_dve.h, r.tok_xln - 1)
        s_dve.tag(vector.tensor_tensor(xln_bf[:, :, :], t_bf[:, :, :], rrep_b, ALU.mult))
    return r



def emit_l0ffn(c, x_d, y_d, w1_d, w2_d, b1t_d, b2t_d, bw1_scr, bw2_scr):
    block = c.block
    s_ld = c.sem("s_ld"); s_sc = c.sem("s_sc"); s_act = c.sem("s_act")
    s_dve = c.sem("s_dve"); s_mm = c.sem("s_mm"); s_st = c.sem("s_st")

    Tc = 256
    NCH = T // Tc
    nG = DM // 128

    bw1t = c.sb("bw1t", [128, 8, 4096], BF16)
    bw2t = c.sb("bw2t", [128, 32, 1024], BF16)
    ones_col = c.sb("ones_col", [128, 1], F32)
    ones_row = c.sb("ones_row", [1, 128], F32)
    eps_t = c.sb("eps_t", [1, 1], F32)
    b1t = c.sb("b1t", [128, 32], F32)
    b2t = c.sb("b2t", [128, 8], F32)
    x_t = c.sb("x_t", [128, nG, Tc], F32)
    y_t = c.sb("y_t", [128, nG, Tc], F32)
    xsq = y_t
    t_bf = c.sb("t_bf", [128, nG, Tc], BF16)
    xln_bf = c.sb("xln_bf", [128, nG, Tc], BF16)
    h_bf = c.sb("h_bf", [128, 32, Tc], BF16)
    ffn_t = [c.sb(f"ffn{i}", [128, Tc], F32) for i in range(2)]
    stat = c.sb("stat", [1, 4 * Tc], F32)
    with c.scope() as c1:
        emit_binarize(c1, w1_d, 4096, 1024, bw1t, bw1_scr, "w1")
    with c.scope() as c2:
        tok_w = emit_binarize(c2, w2_d, 1024, 4096, bw2t, bw2_scr, "w2")
    ps_sum = c.ps("ps_sum", [1, Tc])
    ps_ssq = c.ps("ps_ssq", [1, Tc])
    ps_rep2 = c.ps("ps_rep2", [128, 2 * Tc])
    ps_h = [c.ps(f"ps_h{i}", [128, Tc]) for i in range(2)]
    ps_o = [c.ps(f"ps_o{i}", [128, Tc]) for i in range(2)]

    tok_misc = s_ld.reserve(2, 16)
    tok_ones = s_dve.reserve(1)[0]

    @block.vector
    def _(vector):
        vector.memset(ones_col[:, :], 1.0)
        vector.memset(eps_t[:, :], 1e-5)
        s_dve.tag(vector.memset(ones_row[:, :], 1.0))

    @block.sync
    def _(sync):
        s_ld.tag(sync.dma_start(out=b1t[:, :], in_=b1t_d), 16)
        s_ld.tag(sync.dma_start(out=b2t[:, :], in_=b2t_d), 16)

    prev_ln = None
    prev_ffn = None
    for ci in range(NCH):
        col = slice(ci * Tc, (ci + 1) * Tc)
        tok_x = s_ld.reserve(1, 16)[0]

        @block.sync
        def _(sync):
            if prev_ffn is not None:
                sync.wait_ge(s_mm.h, prev_ffn["tok_om"][-1])
                sync.wait_ge(s_act.h, prev_ffn["tok_oadd"][-1])
                sync.wait_ge(s_dve.h, prev_ffn["tok_res"][-1])
            else:
                sync.wait_ge(s_sc.h, tok_w)
                sync.wait_ge(s_dve.h, tok_ones)
            s_ld.tag(sync.dma_start(out=x_t[:, :, :],
                                    in_=x_d[:, col].rearrange("(g p) t -> p g t", p=128)), 16)

        ln = emit_ln_chunk(c, x_t, xln_bf, t_bf, nG, Tc, ones_col, ones_row, xsq,
                           ps_sum, ps_ssq, ps_rep2, stat, tok_x, prev_ln, eps_t,
                           sq_war=(s_st, prev_ffn["tok_st"]) if prev_ffn else None)
        prev_ln = ln

        ffn = {}
        ffn["tok_hm"] = s_mm.reserve(32)
        ffn["tok_hevac"] = s_act.reserve(32)
        ffn["tok_om"] = s_mm.reserve(8)
        ffn["tok_oadd"] = s_act.reserve(8)
        ffn["tok_res"] = s_dve.reserve(8)

        @block.tensor
        def _(tensor):
            tensor.wait_ge(s_dve.h, ln.tok_xln)
            for m in range(32):
                par = m % 2
                if m >= 2:
                    tensor.wait_ge(s_act.h, ffn["tok_hevac"][m - 2])
                elif prev_ffn is not None:
                    tensor.wait_ge(s_act.h, prev_ffn["tok_hevac"][30 + m])
                for k in range(nG):
                    ins = tensor.matmul(ps_h[par][:, :], bw1t[:, k, ts(m)], xln_bf[:, k, :],
                                        start=(k == 0), stop=(k == nG - 1))
                s_mm.tag(ins)

        @block.scalar
        def _(scalar):
            for m in range(32):
                par = m % 2
                scalar.wait_ge(s_mm.h, ffn["tok_hm"][m])
                s_act.tag(scalar.activation(h_bf[:, m, :], ps_h[par][:, :],
                                            AF.Gelu, bias=b1t[:, m:m + 1]))

        @block.tensor
        def _(tensor):
            tensor.wait_ge(s_act.h, ffn["tok_hevac"][31])
            for m2 in range(8):
                par = m2 % 2
                if m2 >= 2:
                    tensor.wait_ge(s_act.h, ffn["tok_oadd"][m2 - 2])
                elif prev_ffn is not None:
                    tensor.wait_ge(s_act.h, prev_ffn["tok_oadd"][6 + m2])
                for k2 in range(32):
                    ins = tensor.matmul(ps_o[par][:, :], bw2t[:, k2, ts(m2)], h_bf[:, k2, :],
                                        start=(k2 == 0), stop=(k2 == 31))
                s_mm.tag(ins)

        @block.scalar
        def _(scalar):
            for m2 in range(8):
                par = m2 % 2
                scalar.wait_ge(s_mm.h, ffn["tok_om"][m2])
                s_act.tag(scalar.add(ffn_t[par][:, :], ps_o[par][:, :], b2t[:, m2:m2 + 1]))

        @block.vector
        def _(vector):
            if prev_ffn is not None:
                vector.wait_ge(s_st.h, prev_ffn["tok_st"])
            for m2 in range(8):
                par = m2 % 2
                vector.wait_ge(s_act.h, ffn["tok_oadd"][m2])
                s_dve.tag(vector.scalar_tensor_tensor(
                    y_t[:, m2, :], ffn_t[par][:, :], 0.5, x_t[:, m2, :],
                    ALU.mult, ALU.add))

        ffn["tok_st"] = s_st.reserve(1, 16)[0]

        @block.sync
        def _(sync):
            sync.wait_ge(s_dve.h, ffn["tok_res"][-1])
            s_st.tag(sync.dma_start(out=y_d[:, col].rearrange("(g p) t -> p g t", p=128),
                                    in_=y_t[:, :, :]), 16)
        prev_ffn = ffn

    @block.sync
    def _(sync):
        sync.wait_ge(s_st.h, s_st.v)


def build_l0ffn():
    nc = bass.Bass("TRN2", target_bir_lowering=False)
    x_d = nc.dram_tensor("x", [DM, T], F32, kind="ExternalInput").ap()
    w1_d = nc.dram_tensor("w1", [4096, 1024], F32, kind="ExternalInput").ap()
    w2_d = nc.dram_tensor("w2", [1024, 4096], F32, kind="ExternalInput").ap()
    b1t_d = nc.dram_tensor("b1t", [128, 32], F32, kind="ExternalInput").ap()
    b2t_d = nc.dram_tensor("b2t", [128, 8], F32, kind="ExternalInput").ap()
    y_d = nc.dram_tensor("y", [DM, T], F32, kind="ExternalOutput").ap()
    bw1_scr = nc.dram_tensor("bw1_scr", [4096, 1024], BF16, kind="Internal").ap()
    bw2_scr = nc.dram_tensor("bw2_scr", [1024, 4096], BF16, kind="Internal").ap()
    with ExitStack() as es:
        block = es.enter_context(nc.Block())
        c = Ctx(nc, es, block)
        emit_l0ffn(c, x_d, y_d, w1_d, w2_d, b1t_d, b2t_d, bw1_scr, bw2_scr)
        check_sems(c)
    return nc




def emit_rms_chunk(c, x_t, xln_bf, t_f32, nG, Tc, ones_col, ones_row, xsq, ps_ssq,
                   ps_rep2, stat, tok_x, prev, eps_t, gw_t, sq_war=None, out_war=None):
    block = c.block
    s_mm = c.sem("s_mm"); s_act = c.sem("s_act"); s_dve = c.sem("s_dve"); s_ld = c.sem("s_ld")
    r = LNRes()
    ms_t = stat[0:1, 0:Tc]
    sd_t = stat[0:1, Tc:2 * Tc]
    rstd_t = stat[0:1, 2 * Tc:3 * Tc]

    tok_sq = s_act.reserve(nG)
    r.tok_ssq = s_mm.reserve(1)[0]
    r.tok_mean = s_act.reserve(1)[0]
    r.tok_sd = s_act.reserve(1)[0]
    r.tok_rstd = s_dve.reserve(1)[0]
    r.tok_rep = s_mm.reserve(1)[0]
    toks_xln = s_dve.reserve(1 + nG)
    r.tok_xln = toks_xln[-1]
    r.tok_var = r.tok_rstd

    @block.scalar
    def _(scalar):
        scalar.wait_ge(s_ld.h, tok_x)
        if prev is not None:
            scalar.wait_ge(s_mm.h, prev.tok_ssq)
        if sq_war is not None:
            scalar.wait_ge(sq_war[0].h, sq_war[1])
        for g in range(nG):
            s_act.tag(scalar.square(xsq[:, g, :], x_t[:, g, :]))

    @block.tensor
    def _(tensor):
        if prev is not None:
            tensor.wait_ge(s_act.h, prev.tok_mean)
        for g in range(nG):
            tensor.wait_ge(s_act.h, tok_sq[g])
            ins = tensor.matmul(ps_ssq[0:1, :], ones_col[:, 0:1], xsq[:, g, :],
                                start=(g == 0), stop=(g == nG - 1))
        s_mm.tag(ins)

    @block.scalar
    def _(scalar):
        scalar.wait_ge(s_mm.h, r.tok_ssq)
        s_act.tag(scalar.mul(ms_t, ps_ssq[0:1, :], 1.0 / DM))
        scalar.wait_ge(s_act.h, r.tok_mean)
        s_act.tag(scalar.activation(sd_t, ms_t, AF.Sqrt, bias=eps_t[0:1, 0:1]))

    @block.vector
    def _(vector):
        vector.wait_ge(s_act.h, r.tok_sd)
        s_dve.tag(vector.reciprocal(rstd_t, sd_t))

    @block.tensor
    def _(tensor):
        tensor.wait_ge(s_dve.h, r.tok_rstd)
        if prev is not None:
            tensor.wait_ge(s_dve.h, prev.tok_xln)
        ins = tensor.matmul(ps_rep2[:, 0:Tc], ones_row[0:1, :], rstd_t, start=True, stop=True)
        s_mm.tag(ins)

    @block.vector
    def _(vector):
        vector.wait_ge(s_mm.h, r.tok_rep)
        if out_war is not None:
            vector.wait_ge(out_war[0].h, out_war[1])
        rrep_b = ps_rep2[:, 0:Tc].unsqueeze(1).broadcast_to([128, nG, Tc])
        s_dve.tag(vector.tensor_tensor(t_f32[:, :, :], x_t[:, :, :], rrep_b, ALU.mult))
        vector.wait_ge(s_dve.h, toks_xln[0])
        for g in range(nG):
            s_dve.tag(vector.tensor_scalar(xln_bf[:, g, :], t_f32[:, g, :],
                                           gw_t[:, g:g + 1], 1.0, ALU.mult, ALU.mult))
    return r


def emit_mamba(c, x_d, y_d, P, scr, norm, tag, ydbg=None):
    block = c.block
    s_ld = c.sem("s_ld"); s_gp = c.sem("s_gp")
    s_act = c.sem("s_act"); s_dve = c.sem("s_dve"); s_mm = c.sem("s_mm")
    s_st = c.sem("s_st")
    Tc = 256
    NCH = T // Tc
    nG = 8
    nE = 16

    convw = c.sb(f"{tag}_convw", [128, 16, 4], F32)
    convb = c.sb(f"{tag}_convb", [128, 16], F32)
    bdt = c.sb(f"{tag}_bdt", [128, 16], F32)
    negA = c.sb(f"{tag}_negA", [128, 16, 16], F32)
    aexp = c.sb(f"{tag}_aexp", [128, 16, 16], F32)
    d_t = c.sb(f"{tag}_dt", [128, 16], F32)
    ident = c.sb(f"{tag}_ident", [128, 128], BF16)
    gw_t = c.sb(f"{tag}_gw", [128, 8], F32)
    ones_col = c.sb(f"{tag}_ones_col", [128, 1], F32)
    ones_row = c.sb(f"{tag}_ones_row", [1, 128], F32)
    eps_t = c.sb(f"{tag}_eps", [1, 1], F32)
    hstate = c.sb(f"{tag}_hstate", [128, 16, 16], F32)

    ent0 = dict(ld=s_ld.v, dve=s_dve.v, act=s_act.v, mm=s_mm.v, st=s_st.v, gp=s_gp.v)
    tok_wld = s_gp.reserve(8, 16)[-1]

    @block.gpsimd
    def _(gpsimd):
        gpsimd.wait_ge(s_gp.h, ent0["gp"])
        gpsimd.wait_ge(s_ld.h, ent0["ld"])
        gpsimd.wait_ge(s_dve.h, ent0["dve"])
        gpsimd.wait_ge(s_act.h, ent0["act"])
        gpsimd.wait_ge(s_mm.h, ent0["mm"])
        gpsimd.wait_ge(s_st.h, ent0["st"])
        s_gp.tag(gpsimd.dma_start(out=ident[:, :], in_=P["ident"]), 16)
        s_gp.tag(gpsimd.dma_start(out=convw[:, :, :], in_=P["convw_t"]), 16)
        s_gp.tag(gpsimd.dma_start(out=convb[:, :], in_=P["convb_t"]), 16)
        s_gp.tag(gpsimd.dma_start(out=bdt[:, :], in_=P["bdt_t"]), 16)
        s_gp.tag(gpsimd.dma_start(out=negA[:, :, :], in_=P["alog_t"]), 16)
        s_gp.tag(gpsimd.dma_start(out=d_t[:, :], in_=P["d_t"]), 16)
        s_gp.tag(gpsimd.dma_start(out=gw_t[:, :], in_=P["gw_t"]), 16)
        s_gp.tag(gpsimd.dma_start(out=hstate[:, :, :], in_=P["zero_state"]), 16)

    tok_aexp = s_act.reserve(1)[0]
    tok_anegv = s_dve.reserve(1)[0]
    tok_cst = s_dve.reserve(1)[0]

    @block.scalar
    def _(scalar):
        scalar.wait_ge(s_gp.h, tok_wld)
        s_act.tag(scalar.activation(aexp[:, :, :], negA[:, :, :], AF.Exp))

    @block.vector
    def _(vector):
        vector.wait_ge(s_act.h, tok_aexp)
        s_dve.tag(vector.tensor_scalar(negA[:, :, :], aexp[:, :, :], -1.0, 1.0, ALU.mult, ALU.mult))
        vector.memset(ones_col[:, :], 1.0)
        vector.memset(ones_row[:, :], 1.0)
        s_dve.tag(vector.memset(eps_t[:, :], 1e-5 if norm == "ln" else 1e-6))

    with c.scope() as ca:
        winT = ca.sb("winT", [128, 8, 4096], BF16)
        wxpT = ca.sb("wxpT", [128, 16, 96], F32)
        wdtT = ca.sb("wdtT", [64, 2048], F32)
        ent = dict(ld=s_ld.v, dve=s_dve.v, act=s_act.v, mm=s_mm.v, st=s_st.v)
        tok_wldA = s_gp.reserve(3, 16)[-1]

        @block.gpsimd
        def _(gpsimd):
            gpsimd.wait_ge(s_gp.h, tok_wld)
            gpsimd.wait_ge(s_ld.h, ent["ld"])
            gpsimd.wait_ge(s_st.h, ent["st"])
            gpsimd.wait_ge(s_dve.h, ent["dve"])
            gpsimd.wait_ge(s_act.h, ent["act"])
            gpsimd.wait_ge(s_mm.h, ent["mm"])
            s_gp.tag(gpsimd.dma_start(out=winT[:, :, :], in_=P["winT"].rearrange("(g p) m -> p g m", p=128)), 16)
            s_gp.tag(gpsimd.dma_start(out=wxpT[:, :, :], in_=P["wxpT"].rearrange("(g p) m -> p g m", p=128)), 16)
            s_gp.tag(gpsimd.dma_start(out=wdtT[:, :], in_=P["wdtT"]), 16)

        x_t = ca.sb("x_t", [128, nG, Tc], F32)
        arena = ca.sb("arena", [128, 16, Tc], F32)
        xsq = arena[:, 0:nG, :]
        t_f = arena[:, nG:16, :]
        dt_t = arena[:, 0:16, :]
        xln_bf = ca.sb("xln_bf", [128, nG, Tc], BF16)
        xin_ext = ca.sb("xin_ext", [128, 16, Tc + 3], F32)
        sz_t = ca.sb("sz_t", [128, 16, Tc], BF16)
        cvb = ca.sb("cvb", [128, 16, Tc], F32)
        xc_t = ca.sb("xc_t", [128, 16, Tc], F32)
        xdbl = ca.sb("xdbl", [96, Tc], F32)
        dts = [ca.sb(f"dts{i}", [128, Tc], F32) for i in range(3)]
        stat = ca.sb("stat", [1, 4 * Tc], F32)
        ps_sum = ca.ps("ps_sum", [1, Tc])
        ps_ssq = ca.ps("ps_ssq", [1, Tc])
        ps_rep2 = ca.ps("ps_rep2", [128, 2 * Tc])
        ps_ab = [ca.ps(f"ps_ab{i}", [128, Tc]) for i in range(2)]
        ps_xdbl = ca.ps("ps_xdbl", [96, Tc])

        prev_ln = None
        prev = None
        for ci in range(NCH):
            col = slice(ci * Tc, (ci + 1) * Tc)
            tok_x = s_ld.reserve(1, 16)[0]

            @block.sync
            def _(sync, ci=ci, col=col, tok_x=tok_x, prev=prev):
                if prev is None:
                    sync.wait_ge(s_gp.h, tok_wldA)
                    sync.wait_ge(s_dve.h, tok_cst)
                else:
                    sync.wait_ge(s_mm.h, prev["tok_xz"][-1])
                    sync.wait_ge(s_dve.h, prev["tok_dtdve"][-1])
                    sync.wait_ge(s_act.h, prev["tok_sq_proxy"])
                s_ld.tag(sync.dma_start(out=x_t[:, :, :],
                                        in_=x_d[:, col].rearrange("(g p) t -> p g t", p=128)), 16)

            war = (s_st, prev["tok_stall"]) if prev is not None else None
            if norm == "ln":
                ln = emit_ln_chunk(c, x_t, xln_bf, t_f, nG, Tc, ones_col, ones_row, xsq,
                                   ps_sum, ps_ssq, ps_rep2, stat, tok_x, prev_ln, eps_t,
                                   sq_war=war)
            else:
                ln = emit_rms_chunk(c, x_t, xln_bf, t_f, nG, Tc, ones_col, ones_row, xsq,
                                    ps_ssq, ps_rep2, stat, tok_x, prev_ln, eps_t, gw_t,
                                    sq_war=war)
            prev_ln = ln

            cur = {}
            cur["tok_sq_proxy"] = ln.tok_mean
            cur["tok_xz"] = s_mm.reserve(32)
            cur["tok_evac"] = s_act.reserve(32)
            tok_halo = s_dve.reserve(1)[0]

            @block.vector
            def _(vector, prev=prev):
                if prev is not None:
                    vector.wait_ge(s_dve.h, prev["tok_conv"][-1])
                    s_dve.tag(vector.tensor_copy(xin_ext[:, :, 0:3], xin_ext[:, :, Tc:Tc + 3]))
                else:
                    s_dve.tag(vector.memset(xin_ext[:, :, 0:3], 0.0))

            @block.tensor
            def _(tensor, ln=ln, cur=cur, prev=prev):
                tensor.wait_ge(s_dve.h, ln.tok_xln)
                if prev is None:
                    tensor.wait_ge(s_gp.h, tok_wldA)
                for m in range(32):
                    par = m % 2
                    if m >= 2:
                        tensor.wait_ge(s_act.h, cur["tok_evac"][m - 2])
                    elif prev is not None:
                        tensor.wait_ge(s_act.h, prev["tok_dtevac"][14 + m])
                    for k in range(nG):
                        ins = tensor.matmul(ps_ab[par][:, :], winT[:, k, ts(m)], xln_bf[:, k, :],
                                            start=(k == 0), stop=(k == nG - 1))
                    s_mm.tag(ins)

            @block.scalar
            def _(scalar, cur=cur, tok_halo=tok_halo, prev=prev):
                for m in range(32):
                    par = m % 2
                    scalar.wait_ge(s_mm.h, cur["tok_xz"][m])
                    if m == 0:
                        scalar.wait_ge(s_dve.h, tok_halo)
                        if prev is not None:
                            scalar.wait_ge(s_st.h, prev["tok_stall"])
                    if m < 16:
                        s_act.tag(scalar.copy(xin_ext[:, m, 3:3 + Tc], ps_ab[par][:, :]))
                    else:
                        s_act.tag(scalar.activation(sz_t[:, m - 16, :], ps_ab[par][:, :], AF.Silu))

            cur["tok_conv"] = s_dve.reserve(8)

            @block.vector
            def _(vector, cur=cur):
                tk = cur["tok_conv"]
                tmp = arena[:, 0:16, :]
                vector.wait_ge(s_act.h, cur["tok_evac"][15])
                s_dve.tag(vector.tensor_tensor(
                    cvb[:, :, :], xin_ext[:, :, 0:Tc],
                    convw[:, :, 0:1].broadcast_to([128, 16, Tc]), ALU.mult))
                s_dve.tag(vector.tensor_tensor(
                    tmp, xin_ext[:, :, 1:1 + Tc],
                    convw[:, :, 1:2].broadcast_to([128, 16, Tc]), ALU.mult))
                vector.wait_ge(s_dve.h, tk[1])
                s_dve.tag(vector.tensor_tensor(cvb[:, :, :], cvb[:, :, :], tmp, ALU.add))
                s_dve.tag(vector.tensor_tensor(
                    tmp, xin_ext[:, :, 2:2 + Tc],
                    convw[:, :, 2:3].broadcast_to([128, 16, Tc]), ALU.mult))
                vector.wait_ge(s_dve.h, tk[3])
                s_dve.tag(vector.tensor_tensor(cvb[:, :, :], cvb[:, :, :], tmp, ALU.add))
                s_dve.tag(vector.tensor_tensor(
                    tmp, xin_ext[:, :, 3:3 + Tc],
                    convw[:, :, 3:4].broadcast_to([128, 16, Tc]), ALU.mult))
                vector.wait_ge(s_dve.h, tk[5])
                s_dve.tag(vector.tensor_tensor(cvb[:, :, :], cvb[:, :, :], tmp, ALU.add))
                vector.wait_ge(s_dve.h, tk[6])
                s_dve.tag(vector.tensor_tensor(
                    cvb[:, :, :], cvb[:, :, :],
                    convb[:, :].unsqueeze(2).broadcast_to([128, 16, Tc]), ALU.add))

            tok_cvsilu = s_act.reserve(1)[0]

            @block.scalar
            def _(scalar, cur=cur, prev=prev):
                scalar.wait_ge(s_dve.h, cur["tok_conv"][-1])
                if prev is not None:
                    scalar.wait_ge(s_st.h, prev["tok_stall"])
                s_act.tag(scalar.activation(xc_t[:, :, :], cvb[:, :, :], AF.Silu))

            cur["tok_xdbl_mm"] = s_mm.reserve(1)[0]
            tok_xdbl_cp = s_dve.reserve(1)[0]
            cur["tok_xdbl_cp"] = tok_xdbl_cp

            @block.tensor
            def _(tensor, cur=cur, prev=prev):
                tensor.wait_ge(s_act.h, tok_cvsilu)
                if prev is not None:
                    tensor.wait_ge(s_dve.h, prev["tok_xdbl_cp"])
                for k in range(nE):
                    ins = tensor.matmul(ps_xdbl[:, :], wxpT[:, k, :], xc_t[:, k, :],
                                        start=(k == 0), stop=(k == nE - 1))
                s_mm.tag(ins)

            @block.vector
            def _(vector, cur=cur, prev=prev):
                vector.wait_ge(s_mm.h, cur["tok_xdbl_mm"])
                if prev is not None:
                    vector.wait_ge(s_gp.h, prev["tok_stb"])
                s_dve.tag(vector.tensor_copy(xdbl[:, :], ps_xdbl[:, :]))

            cur["tok_dtm"] = s_mm.reserve(16)
            cur["tok_dtevac"] = s_act.reserve(16)
            cur["tok_dtact"] = s_act.reserve(3)
            cur["tok_dtdve"] = s_dve.reserve(3)

            @block.tensor
            def _(tensor, cur=cur):
                tensor.wait_ge(s_dve.h, cur["tok_xdbl_cp"])
                for m in range(16):
                    par = m % 2
                    if m >= 2:
                        tensor.wait_ge(s_act.h, cur["tok_dtevac"][m - 2])
                    else:
                        tensor.wait_ge(s_act.h, cur["tok_evac"][30 + m])
                    ins = tensor.matmul(ps_ab[par][:, :], wdtT[:, ts(m)], xdbl[0:64, :],
                                        start=True, stop=True)
                    s_mm.tag(ins)

            @block.scalar
            def _(scalar, cur=cur, prev=prev):
                for m in range(16):
                    par = m % 2
                    scalar.wait_ge(s_mm.h, cur["tok_dtm"][m])
                    if m == 0 and prev is not None:
                        scalar.wait_ge(s_st.h, prev["tok_stall"])
                    s_act.tag(scalar.copy(dt_t[:, m, :], ps_ab[par][:, :]))

            @block.vector
            def _(vector, cur=cur):
                vector.wait_ge(s_act.h, cur["tok_dtevac"][-1])
                s_dve.tag(vector.tensor_tensor(
                    dt_t[:, :, :], dt_t[:, :, :],
                    bdt[:, :].unsqueeze(2).broadcast_to([128, 16, Tc]), ALU.add))

            @block.scalar
            def _(scalar, cur=cur):
                scalar.wait_ge(s_dve.h, cur["tok_dtdve"][0])
                s_act.tag(scalar.activation(cvb[:, :, :], dt_t[:, :, :], AF.Abs))
                scalar.wait_ge(s_act.h, cur["tok_dtact"][0])
                s_act.tag(scalar.activation(cvb[:, :, :], cvb[:, :, :], AF.Exp, scale=-1.0))
                scalar.wait_ge(s_act.h, cur["tok_dtact"][1])
                s_act.tag(scalar.activation(cvb[:, :, :], cvb[:, :, :], AF.Ln, bias=1.0))

            @block.vector
            def _(vector, cur=cur):
                rbuf = xin_ext[:, :, 0:Tc]
                vector.wait_ge(s_dve.h, cur["tok_dtdve"][0])
                s_dve.tag(vector.tensor_scalar_max(rbuf, dt_t[:, :, :], 0.0))
                vector.wait_ge(s_dve.h, cur["tok_dtdve"][1])
                vector.wait_ge(s_act.h, cur["tok_dtact"][2])
                s_dve.tag(vector.tensor_tensor(dt_t[:, :, :], rbuf, cvb[:, :, :], ALU.add))

            tok_sta = s_st.reserve(3, 16)
            cur["tok_stall"] = tok_sta[-1]

            @block.sync
            def _(sync, cur=cur, col=col):
                sync.wait_ge(s_act.h, cur["tok_evac"][31])
                s_st.tag(sync.dma_start(out=scr["sz"][:, col].rearrange("(g p) t -> p g t", p=128),
                                        in_=sz_t[:, :, :]), 16)
                sync.wait_ge(s_mm.h, cur["tok_xdbl_mm"])
                s_st.tag(sync.dma_start(out=scr["xc"][:, col].rearrange("(g p) t -> p g t", p=128),
                                        in_=xc_t[:, :, :]), 16)
                sync.wait_ge(s_dve.h, cur["tok_dtdve"][-1])
                s_st.tag(sync.dma_start(out=scr["dt"][:, col].rearrange("(g p) t -> p g t", p=128),
                                        in_=dt_t[:, :, :]), 16)

            tok_stb = s_gp.reserve(1, 16)[0]
            cur["tok_stb"] = tok_stb

            @block.gpsimd
            def _(gpsimd, cur=cur, col=col, prev=prev):
                if prev is not None:
                    gpsimd.wait_ge(s_gp.h, prev["tok_stb"])
                else:
                    gpsimd.wait_ge(s_gp.h, tok_wldA)
                gpsimd.wait_ge(s_dve.h, cur["tok_xdbl_cp"])
                s_gp.tag(gpsimd.dma_start(out=scr["bc"][:, col], in_=xdbl[64:96, :]), 16)

            prev = cur
        tok_a_end = dict(mm=s_mm.v, act=s_act.v, dve=s_dve.v, ld=s_ld.v, gp=s_gp.v,
                         st=s_st.v)

    with c.scope() as cb:
        woutT = cb.sb("woutT", [128, 16, 1024], BF16)
        tok_wldB = s_gp.reserve(1, 16)[0]

        @block.gpsimd
        def _(gpsimd):
            gpsimd.wait_ge(s_gp.h, tok_a_end["gp"])
            gpsimd.wait_ge(s_ld.h, tok_a_end["ld"])
            gpsimd.wait_ge(s_dve.h, tok_a_end["dve"])
            gpsimd.wait_ge(s_act.h, tok_a_end["act"])
            gpsimd.wait_ge(s_mm.h, tok_a_end["mm"])
            gpsimd.wait_ge(s_st.h, tok_a_end["st"])
            s_gp.tag(gpsimd.dma_start(out=woutT[:, :, :], in_=P["woutT"].rearrange("(g p) m -> p g m", p=128)), 16)

        dtc = cb.sb("dtc", [128, 16, Tc], F32)
        y_t = cb.sb("y_t", [128, 8, Tc], F32)
        xo_t = cb.sb("xo_t", [128, 8, Tc], F32)
        xcc = [cb.sb(f"xcc{i}", [128, 16, Tc], F32) for i in range(2)]
        szc = cb.sb("szc", [128, 16, Tc], BF16)
        dtx = cb.sb("dtx", [128, 16, Tc], BF16)
        brep = [cb.sb(f"brep{i}", [128, 16, Tc], BF16) for i in range(2)]
        crep1 = cb.sb("crep1", [128, 16, Tc], BF16)
        crep = [crep1, crep1]
        x_t = cb.sb("x_t", [128, nG, Tc], F32)
        PW = Tc + 1
        dA_t = [cb.sb(f"dA_t{i}", [128, 8, PW], BF16) for i in range(2)]
        dBx_t = [cb.sb(f"dBx_t{i}", [128, 8, PW], BF16) for i in range(2)]
        h_t = [cb.sb(f"h_t{i}", [128, 8, PW], F32) for i in range(2)]
        yg_bf = cb.sb("yg_bf", [128, 16, Tc], BF16)
        prod_bf = [cb.sb(f"prod_bf{i}", [128, 8, Tc], BF16) for i in range(2)]
        ps_y8 = cb.ps("ps_y8", [128, 8, Tc])
        ps_y = [ps_y8[:, i, :] for i in range(8)]
        ps_o = [cb.ps(f"ps_o{i}", [128, Tc]) for i in range(2)]

        tok_colz = c.sem("s_dve").reserve(1)[0]

        @block.vector
        def _(vector):
            vector.wait_ge(s_gp.h, s_gp.v)
            vector.memset(dA_t[0][:, :, 0:1], 0.0)
            s_dve.tag(vector.memset(dA_t[1][:, :, 0:1], 0.0))

        prevb = None
        prevb2 = None
        for ci in range(NCH):
            col = slice(ci * Tc, (ci + 1) * Tc)
            pc = ci % 2
            tok_in = s_ld.reserve(4, 16)[-1]
            tok_g3 = s_ld.reserve(2, 16)[-1]

            @block.sync
            def _(sync, ci=ci, col=col, pc=pc, prevb=prevb, prevb2=prevb2):
                if prevb is None:
                    sync.wait_ge(s_gp.h, tok_a_end["gp"])
                    sync.wait_ge(s_dve.h, tok_a_end["dve"])
                    sync.wait_ge(s_act.h, tok_a_end["act"])
                    sync.wait_ge(s_mm.h, tok_a_end["mm"])
                    sync.wait_ge(s_st.h, tok_a_end["st"])
                else:
                    sync.wait_ge(s_act.h, prevb["tok_exp_last"])
                    sync.wait_ge(s_gpc.h, prevb["tok_dtx"])
                if prevb2 is not None:
                    sync.wait_ge(s_dve.h, prevb2["tok_gate"])
                if prevb is not None:
                    sync.wait_ge(s_mm.h, prevb["tok_ymm_last"])
                s_ld.tag(sync.dma_start(out=dtc[:, :, :], in_=scr["dt"][:, col].rearrange("(g p) t -> p g t", p=128)), 16)
                s_ld.tag(sync.dma_start(out=xcc[pc][:, :, :], in_=scr["xc"][:, col].rearrange("(g p) t -> p g t", p=128)), 16)
                s_ld.tag(sync.dma_start(out=brep[pc][:, :, :], in_=scr["bc"][0:16, col].unsqueeze(0).broadcast_to([128, 16, Tc])), 16)
                s_ld.tag(sync.dma_start(out=crep[pc][:, :, :], in_=scr["bc"][16:32, col].unsqueeze(0).broadcast_to([128, 16, Tc])), 16)
                if prevb is not None:
                    sync.wait_ge(s_dve.h, prevb["tok_gate"])
                    sync.wait_ge(s_dve.h, prevb["tok_res"][-1])
                s_ld.tag(sync.dma_start(out=szc[:, :, :], in_=scr["sz"][:, col].rearrange("(g p) t -> p g t", p=128)), 16)
                s_ld.tag(sync.dma_start(out=x_t[:, :, :], in_=x_d[:, col].rearrange("(g p) t -> p g t", p=128)), 16)

            curb = {}
            s_gpc = c.sem("s_gpc")
            tok_dtx = s_gpc.reserve(1)[0]
            curb["tok_dtx"] = tok_dtx

            @block.gpsimd
            def _(gpsimd, tok_in=tok_in, pc=pc):
                gpsimd.wait_ge(s_ld.h, tok_in)
                s_gpc.tag(gpsimd.tensor_tensor(dtx[:, :, :], dtc[:, :, :], xcc[pc][:, :, :], ALU.mult))

            ev_last = None
            prev_scan = prevb["tok_scan_last"] if prevb is not None else None
            prev_scan2 = prevb["tok_scan_last2"] if prevb is not None else None
            last_ymm = prevb["tok_ymm_last"] if prevb is not None else None
            last_ymm2 = prevb["tok_ymm_last2"] if prevb is not None else None
            pend_prods = prevb["pend_prods"] if prevb is not None else []
            prod_toks = prevb["prod_toks"] if prevb is not None else {}
            pend_ymms = prevb["pend_ymms"] if prevb is not None else []

            def emit_ymm(py):
                @block.tensor
                def _(tensor, py=py):
                    tp = prod_toks[py["s"]]
                    tensor.wait_ge(tp[0].h, tp[1])
                    if py["n"] == 0:
                        if py["half"] == 1:
                            tensor.wait_ge(s_dve.h, py["half_ev0"])
                        elif py["prevb"] is not None:
                            tensor.wait_ge(s_dve.h, py["prevb"]["tok_ev_last"])
                    for b in range(4):
                        ins = tensor.matmul(ps_y8[:, 2 * b:2 * b + 2, :], ident[:, :],
                                            prod_bf[py["pn"]][:, 2 * b:2 * b + 2, :],
                                            start=(py["n"] == 0), stop=(py["n"] == nE - 1))
                    s_mm.tag(ins)

            for half in range(2):
                g0 = half * 8
                half_ev = None
                for n in range(nE):
                    s = ci * 2 * nE + half * nE + n
                    pn = s % 2
                    t_dbx = s_gpc.reserve(1)[0]

                    @block.gpsimd
                    def _(gpsimd, g0=g0, n=n, pn=pn, pc=pc, prev_scan2=prev_scan2):
                        if prev_scan2 is not None:
                            gpsimd.wait_ge(s_dve.h, prev_scan2)
                        s_gpc.tag(gpsimd.tensor_tensor(
                            dBx_t[pn][:, :, 1:Tc + 1], dtx[:, g0:g0 + 8, :],
                            brep[pc][:, n:n + 1, :].broadcast_to([128, 8, Tc]), ALU.mult))

                    while pend_prods and s - pend_prods[0]["s"] >= 2:
                        pp = pend_prods.pop(0)
                        pp["tok"] = (s_gpc, s_gpc.reserve(1)[0])
                        prod_toks[pp["s"]] = pp["tok"]

                        @block.gpsimd
                        def _(gpsimd, pp=pp, pc=pc):
                            gpsimd.wait_ge(s_dve.h, pp["t_scan"])
                            if pp["ymm2"] is not None:
                                gpsimd.wait_ge(s_mm.h, pp["ymm2"])
                            s_gpc.tag(gpsimd.tensor_tensor(
                                prod_bf[pp["pn"]][:, :, :], h_t[pp["pn"]][:, :, 1:Tc + 1],
                                crep[pc][:, pp["n"]:pp["n"] + 1, :].broadcast_to([128, 8, Tc]),
                                ALU.mult))
                    t_exp = s_act.reserve(1)[0]
                    t_colc = s_dve.reserve(1)[0]
                    t_scan = s_dve.reserve(1)
                    t_stc = s_dve.reserve(1)[0]
                    if pn == 0:
                        t_prod = (s_dve, s_dve.reserve(1)[0])
                        prod_toks[s] = t_prod
                    else:
                        t_prod = None

                    if pn == 1:
                        pend_prods.append(dict(s=s, n=n, pn=pn, t_scan=t_scan[-1],
                                               ymm2=last_ymm2))

                    @block.scalar
                    def _(scalar, g0=g0, n=n, pn=pn, half=half, tok_in=tok_in,
                          prev_scan2=prev_scan2):
                        if half == 0 and n == 0:
                            scalar.wait_ge(s_ld.h, tok_in)
                        if prev_scan2 is not None:
                            scalar.wait_ge(s_dve.h, prev_scan2)
                        for j in range(8):
                            ins = scalar.activation(dA_t[pn][:, j, 1:Tc + 1], dtc[:, g0 + j, :],
                                                    AF.Exp, scale=negA[:, g0 + j, n:n + 1])
                        s_act.tag(ins)
                    curb["tok_exp_last"] = t_exp

                    @block.vector
                    def _(vector, g0=g0, n=n, pn=pn, s=s, pc=pc, t_exp=t_exp, t_scan=t_scan,
                          t_dbx=t_dbx, t_colc=t_colc, last_ymm=last_ymm, half=half,
                          prevb=prevb):
                        vector.wait_ge(s_act.h, t_exp)
                        vector.wait_ge(s_gpc.h, t_dbx)
                        pp2 = prod_toks.get(s - 2)
                        if pp2 is not None and pp2[0] is s_gpc:
                            vector.wait_ge(s_gpc.h, pp2[1])
                        s_dve.tag(vector.tensor_copy(dBx_t[pn][:, :, 0:1],
                                                     hstate[:, g0:g0 + 8, n:n + 1]))
                        vector.wait_ge(s_dve.h, t_colc)
                        s_dve.tag(vector.tensor_tensor_scan(
                            h_t[pn].rearrange("p j t -> p (j t)"),
                            dA_t[pn].rearrange("p j t -> p (j t)"),
                            dBx_t[pn].rearrange("p j t -> p (j t)"),
                            0.0, ALU.mult, ALU.add))
                        vector.wait_ge(s_dve.h, t_scan[-1])
                        s_dve.tag(vector.tensor_copy(hstate[:, g0:g0 + 8, n:n + 1],
                                                     h_t[pn][:, :, Tc:Tc + 1]))
                        if pn == 0:
                            if last_ymm is not None:
                                vector.wait_ge(s_mm.h, last_ymm)
                            s_dve.tag(vector.tensor_tensor(
                                prod_bf[pn][:, :, :], h_t[pn][:, :, 1:Tc + 1],
                                crep[pc][:, n:n + 1, :].broadcast_to([128, 8, Tc]), ALU.mult))
                    prev_scan2 = prev_scan
                    prev_scan = t_scan[-1]

                    while pend_ymms and s - pend_ymms[0]["s"] >= 2:
                        emit_ymm(pend_ymms.pop(0))
                    t_ymm = s_mm.reserve(1)[0]
                    pend_ymms.append(dict(n=n, pn=pn, s=s, half=half, half_ev0=ev_last,
                                          prevb=prevb, tok=t_ymm))
                    last_ymm2 = last_ymm
                    last_ymm = t_ymm
                    curb["tok_ymm_last"] = t_ymm

                while pend_prods:
                    pp = pend_prods.pop(0)
                    pp["tok"] = (s_gpc, s_gpc.reserve(1)[0])
                    prod_toks[pp["s"]] = pp["tok"]

                    @block.gpsimd
                    def _(gpsimd, pp=pp, pc=pc):
                        gpsimd.wait_ge(s_dve.h, pp["t_scan"])
                        if pp["ymm2"] is not None:
                            gpsimd.wait_ge(s_mm.h, pp["ymm2"])
                        s_gpc.tag(gpsimd.tensor_tensor(
                            prod_bf[pp["pn"]][:, :, :], h_t[pp["pn"]][:, :, 1:Tc + 1],
                            crep[pc][:, pp["n"]:pp["n"] + 1, :].broadcast_to([128, 8, Tc]),
                            ALU.mult))
                while pend_ymms:
                    emit_ymm(pend_ymms.pop(0))
                t_ev = s_dve.reserve(9)
                ev_last = t_ev[-1]

                @block.vector
                def _(vector, g0=g0, half=half, t_ev=t_ev, tymm=curb["tok_ymm_last"], ci=ci, pc=pc, tok_g3=tok_g3, prevb=prevb):
                    vector.wait_ge(s_mm.h, tymm)
                    vector.wait_ge(s_ld.h, tok_g3)
                    if half == 0 and prevb is not None:
                        vector.wait_ge(s_mm.h, prevb["tok_om"][-1])
                        vector.wait_ge(s_dve.h, prevb["tok_res"][-1])
                    for j in range(8):
                        s_dve.tag(vector.scalar_tensor_tensor(
                            y_t[:, j, :], xcc[pc][:, g0 + j, :], d_t[:, g0 + j:g0 + j + 1],
                            ps_y[j][:, :], ALU.mult, ALU.add))
                    vector.wait_ge(s_dve.h, t_ev[7])
                    s_dve.tag(vector.tensor_tensor(yg_bf[:, g0:g0 + 8, :], y_t[:, :, :],
                                                   szc[:, g0:g0 + 8, :], ALU.mult))
            curb["tok_ev_last"] = ev_last
            curb["tok_scan_last"] = prev_scan
            curb["tok_scan_last2"] = prev_scan2
            curb["tok_ymm_last2"] = last_ymm2
            curb["pend_prods"] = pend_prods
            curb["prod_toks"] = prod_toks
            curb["pend_ymms"] = pend_ymms
            tok_gate = ev_last
            curb["tok_gate"] = tok_gate


            curb["tok_om"] = s_mm.reserve(8)
            curb["tok_res"] = s_dve.reserve(8)
            tok_st = s_st.reserve(1, 16)[0]
            curb["tok_st"] = tok_st

            @block.tensor
            def _(tensor, curb=curb, tok_gate=tok_gate, prevb=prevb):
                tensor.wait_ge(s_dve.h, tok_gate)
                if prevb is None:
                    tensor.wait_ge(s_gp.h, tok_wldB)
                for m in range(8):
                    par = m % 2
                    if m >= 2:
                        tensor.wait_ge(s_dve.h, curb["tok_res"][m - 2])
                    elif prevb is not None:
                        tensor.wait_ge(s_dve.h, prevb["tok_res"][6 + m])
                    for k in range(nE):
                        ins = tensor.matmul(ps_o[par][:, :], woutT[:, k, ts(m)], yg_bf[:, k, :],
                                            start=(k == 0), stop=(k == nE - 1))
                    s_mm.tag(ins)

            @block.vector
            def _(vector, curb=curb, tok_g3=tok_g3, prevb=prevb):
                vector.wait_ge(s_ld.h, tok_g3)
                if prevb is not None:
                    vector.wait_ge(s_st.h, prevb["tok_st"])
                for m in range(8):
                    par = m % 2
                    vector.wait_ge(s_mm.h, curb["tok_om"][m])
                    s_dve.tag(vector.scalar_tensor_tensor(
                        xo_t[:, m, :], ps_o[par][:, :], 1.0, x_t[:, m, :], ALU.mult, ALU.add))

            if ydbg is not None:
                s_gp.reserve(1, 16)

                @block.gpsimd
                def _(gpsimd, col=col, tok_gate=tok_gate):
                    gpsimd.wait_ge(s_dve.h, tok_gate)
                    s_gp.tag(gpsimd.dma_start(out=ydbg[:, col].rearrange("(g p) t -> p g t", p=128),
                                              in_=yg_bf[:, :, :]), 16)

            @block.sync
            def _(sync, curb=curb, col=col):
                sync.wait_ge(s_dve.h, curb["tok_res"][-1])
                s_st.tag(sync.dma_start(out=y_d[:, col].rearrange("(g p) t -> p g t", p=128),
                                        in_=xo_t[:, :, :]), 16)
            prevb2 = prevb
            prevb = curb

    @block.sync
    def _(sync):
        sync.wait_ge(s_st.h, s_st.v)


def mamba_host_params(W_in, conv_w, conv_b, W_xp, W_dt, b_dt, A_log, Dv, W_out, gw=None):
    d = {}
    d["winT"] = np.ascontiguousarray(np.asarray(W_in, np.float32).T)
    d["wxpT"] = np.ascontiguousarray(np.asarray(W_xp, np.float32).T)
    d["wdtT"] = np.ascontiguousarray(np.asarray(W_dt, np.float32).T)
    d["woutT"] = np.ascontiguousarray(np.asarray(W_out, np.float32).T)
    cw = np.asarray(conv_w, np.float32).reshape(2048, 4)
    d["convw_t"] = np.ascontiguousarray(cw.reshape(16, 128, 4).transpose(1, 0, 2))
    d["convb_t"] = _tile_vec(conv_b, 16)
    d["bdt_t"] = _tile_vec(b_dt, 16)
    d["alog_t"] = np.ascontiguousarray(np.asarray(A_log, np.float32).reshape(16, 128, 16).transpose(1, 0, 2))
    d["d_t"] = _tile_vec(Dv, 16)
    d["ident"] = np.eye(128, dtype=np.float32)
    d["zero_state"] = np.zeros((128, 16, 16), dtype=np.float32)
    d["gw_t"] = _tile_vec(gw if gw is not None else np.ones(1024, np.float32), 8)
    return d


def build_mamba(norm, debug=False):
    nc = bass.Bass("TRN2", target_bir_lowering=False)
    x_d = nc.dram_tensor("x", [DM, T], F32, kind="ExternalInput").ap()
    y_d = nc.dram_tensor("y", [DM, T], F32, kind="ExternalOutput").ap()
    names = ["winT", "wxpT", "wdtT", "woutT", "convw_t", "convb_t", "bdt_t",
             "alog_t", "d_t", "ident", "gw_t", "zero_state"]
    shapes = dict(winT=[1024, 4096], wxpT=[2048, 96], wdtT=[64, 2048], woutT=[2048, 1024],
                  convw_t=[128, 16, 4], convb_t=[128, 16], bdt_t=[128, 16],
                  alog_t=[128, 16, 16], d_t=[128, 16], ident=[128, 128], gw_t=[128, 8],
                  zero_state=[128, 16, 16])
    P = {nm: nc.dram_tensor(nm, shapes[nm], F32, kind="ExternalInput").ap() for nm in names}
    kd = "ExternalOutput" if debug else "Internal"
    ydbg = nc.dram_tensor("ydbg", [2048, T], BF16, kind=kd).ap() if debug else None
    scr = dict(
        xc=nc.dram_tensor("scr_xc", [2048, T], F32, kind=kd).ap(),
        sz=nc.dram_tensor("scr_sz", [2048, T], BF16, kind=kd).ap(),
        dt=nc.dram_tensor("scr_dt", [2048, T], F32, kind=kd).ap(),
        bc=nc.dram_tensor("scr_bc", [32, T], BF16, kind=kd).ap(),
    )
    with ExitStack() as es:
        block = es.enter_context(nc.Block())
        c = Ctx(nc, es, block)
        emit_mamba(c, x_d, y_d, P, scr, norm, "mb", ydbg=ydbg)
        check_sems(c)
    return nc


def l0attn_inmaps(inp, xs):
    P = mamba_host_params(inp["L0_attn_W_in"], inp["L0_attn_conv_w"], inp["L0_attn_conv_b"],
                          inp["L0_attn_W_xp"], inp["L0_attn_W_dt"], inp["L0_attn_b_dt"],
                          inp["L0_attn_A_log"], inp["L0_attn_D"], inp["L0_attn_W_out"])
    return [dict(x=xs[cid], **P) for cid in range(NC8)]



def emit_l3ffn(c, x_d, y_d, P, tag):
    block = c.block
    s_ld = c.sem("s_ld"); s_gp = c.sem("s_gp")
    s_act = c.sem("s_act"); s_dve = c.sem("s_dve"); s_mm = c.sem("s_mm")
    s_st = c.sem("s_st")
    Tc = 256
    NCH = T // Tc
    nG = 8

    wrT = c.sb(f"{tag}_wrT", [128, 8, 1024], BF16)
    wkT = c.sb(f"{tag}_wkT", [128, 8, 1024], BF16)
    wvT = c.sb(f"{tag}_wvT", [128, 8, 1024], BF16)
    mur = c.sb(f"{tag}_mur", [128, 8], F32)
    muk = c.sb(f"{tag}_muk", [128, 8], F32)
    ones_col = c.sb(f"{tag}_onc", [128, 1], F32)
    ones_row = c.sb(f"{tag}_onr", [1, 128], F32)
    eps_t = c.sb(f"{tag}_eps", [1, 1], F32)

    tok_wld = s_gp.reserve(5, 16)[-1]

    @block.gpsimd
    def _(gpsimd):
        s_gp.tag(gpsimd.dma_start(out=wrT[:, :, :], in_=P["wrT"].rearrange("(g p) m -> p g m", p=128)), 16)
        s_gp.tag(gpsimd.dma_start(out=wkT[:, :, :], in_=P["wkT"].rearrange("(g p) m -> p g m", p=128)), 16)
        s_gp.tag(gpsimd.dma_start(out=wvT[:, :, :], in_=P["wvT"].rearrange("(g p) m -> p g m", p=128)), 16)
        s_gp.tag(gpsimd.dma_start(out=mur[:, :], in_=P["mur_t"]), 16)
        s_gp.tag(gpsimd.dma_start(out=muk[:, :], in_=P["muk_t"]), 16)

    tok_cst = s_dve.reserve(1)[0]

    @block.vector
    def _(vector):
        vector.memset(ones_col[:, :], 1.0)
        vector.memset(ones_row[:, :], 1.0)
        s_dve.tag(vector.memset(eps_t[:, :], 1e-5))

    x_t = c.sb(f"{tag}_x", [128, nG, Tc], F32)
    xsq = c.sb(f"{tag}_xsq", [128, nG, Tc], F32)
    t_f = c.sb(f"{tag}_tf", [128, nG, Tc], F32)
    xln_ext = c.sb(f"{tag}_xle", [128, nG, Tc + 1], F32)
    xr_bf = c.sb(f"{tag}_xr", [128, nG, Tc], BF16)
    xk_bf = c.sb(f"{tag}_xk", [128, nG, Tc], BF16)
    r_t = c.sb(f"{tag}_r", [128, nG, Tc], F32)
    ka_bf = c.sb(f"{tag}_ka", [128, nG, Tc], BF16)
    vv_t = c.sb(f"{tag}_vv", [128, Tc], F32)
    xo_t = c.sb(f"{tag}_xo", [128, nG, Tc], F32)
    stat = c.sb(f"{tag}_st", [1, 4 * Tc], F32)
    ps_sum = c.ps(f"{tag}_psum", [1, Tc])
    ps_ssq = c.ps(f"{tag}_pssq", [1, Tc])
    ps_rep2 = c.ps(f"{tag}_prep2", [128, 2 * Tc])
    ps_ab = [c.ps(f"{tag}_pab{i}", [128, Tc]) for i in range(2)]

    xln = xln_ext[:, :, 1:Tc + 1]
    xlnp = xln_ext[:, :, 0:Tc]

    prev_ln = None
    prev = None
    for ci in range(NCH):
        col = slice(ci * Tc, (ci + 1) * Tc)
        tok_x = s_ld.reserve(1, 16)[0]

        @block.sync
        def _(sync, col=col, prev=prev):
            if prev is None:
                sync.wait_ge(s_gp.h, tok_wld)
                sync.wait_ge(s_dve.h, tok_cst)
            else:
                sync.wait_ge(s_mm.h, prev["tok_km"][-1])
                sync.wait_ge(s_dve.h, prev["tok_res"][-1])
                sync.wait_ge(s_act.h, prev["tok_ract"][-1])
            s_ld.tag(sync.dma_start(out=x_t[:, :, :],
                                    in_=x_d[:, col].rearrange("(g p) t -> p g t", p=128)), 16)

        tok_halo = s_dve.reserve(1)[0]

        @block.vector
        def _(vector, prev=prev, prev_ln=prev_ln):
            if prev is None:
                s_dve.tag(vector.memset(xln_ext[:, :, 0:1], 0.0))
            else:
                vector.wait_ge(s_dve.h, prev["tok_mix"][-1])
                s_dve.tag(vector.tensor_copy(xln_ext[:, :, 0:1], xln_ext[:, :, Tc:Tc + 1]))

        ln = emit_ln_chunk(c, x_t, xln, t_f, nG, Tc, ones_col, ones_row, xsq,
                           ps_sum, ps_ssq, ps_rep2, stat, tok_x, prev_ln, eps_t,
                           out_f32=True, extra_dve_wait=tok_halo)
        prev_ln = ln

        cur = {}
        cur["tok_mix"] = s_dve.reserve(1 + 16)

        @block.vector
        def _(vector, ln=ln, cur=cur):
            vector.wait_ge(s_dve.h, ln.tok_xln)
            tk = cur["tok_mix"]
            s_dve.tag(vector.tensor_tensor(t_f[:, :, :], xln, xlnp, ALU.subtract))
            vector.wait_ge(s_dve.h, tk[0])
            for g in range(nG):
                s_dve.tag(vector.scalar_tensor_tensor(xr_bf[:, g, :], t_f[:, g, :],
                                                      mur[:, g:g + 1], xlnp[:, g, :],
                                                      ALU.mult, ALU.add))
                s_dve.tag(vector.scalar_tensor_tensor(xk_bf[:, g, :], t_f[:, g, :],
                                                      muk[:, g:g + 1], xlnp[:, g, :],
                                                      ALU.mult, ALU.add))

        cur["tok_rm"] = s_mm.reserve(8)
        cur["tok_ract"] = s_act.reserve(8)
        cur["tok_km"] = s_mm.reserve(8)
        cur["tok_kact"] = s_act.reserve(16)

        @block.tensor
        def _(tensor, cur=cur, prev=prev):
            tensor.wait_ge(s_dve.h, cur["tok_mix"][-1])
            for m in range(nG):
                par = m % 2
                if m >= 2:
                    tensor.wait_ge(s_act.h, cur["tok_ract"][m - 2])
                elif prev is not None:
                    tensor.wait_ge(s_dve.h, prev["tok_res"][6 + m])
                for k in range(nG):
                    ins = tensor.matmul(ps_ab[par][:, :], wrT[:, k, ts(m)], xr_bf[:, k, :],
                                        start=(k == 0), stop=(k == nG - 1))
                s_mm.tag(ins)

        @block.scalar
        def _(scalar, cur=cur):
            for m in range(nG):
                par = m % 2
                scalar.wait_ge(s_mm.h, cur["tok_rm"][m])
                s_act.tag(scalar.activation(r_t[:, m, :], ps_ab[par][:, :], AF.Sigmoid))

        @block.tensor
        def _(tensor, cur=cur):
            for m in range(nG):
                par = m % 2
                if m >= 2:
                    tensor.wait_ge(s_act.h, cur["tok_kact"][2 * (m - 2)])
                else:
                    tensor.wait_ge(s_act.h, cur["tok_ract"][6 + m])
                for k in range(nG):
                    ins = tensor.matmul(ps_ab[par][:, :], wkT[:, k, ts(m)], xk_bf[:, k, :],
                                        start=(k == 0), stop=(k == nG - 1))
                s_mm.tag(ins)

        @block.scalar
        def _(scalar, cur=cur):
            for m in range(nG):
                par = m % 2
                a = cur["tok_kact"][2 * m:2 * m + 2]
                scalar.wait_ge(s_mm.h, cur["tok_km"][m])
                s_act.tag(scalar.activation(vv_t[:, :], ps_ab[par][:, :], AF.Relu))
                scalar.wait_ge(s_act.h, a[0])
                s_act.tag(scalar.square(ka_bf[:, m, :], vv_t[:, :]))

        cur["tok_vm"] = s_mm.reserve(8)
        _vr = s_dve.reserve(16)
        cur["tok_vdve"] = _vr[0::2]
        cur["tok_res"] = _vr[1::2]

        @block.tensor
        def _(tensor, cur=cur, prev=prev):
            tensor.wait_ge(s_act.h, cur["tok_kact"][-1])
            for m in range(nG):
                par = m % 2
                if m >= 2:
                    tensor.wait_ge(s_dve.h, cur["tok_vdve"][m - 2])
                for k in range(nG):
                    ins = tensor.matmul(ps_ab[par][:, :], wvT[:, k, ts(m)], ka_bf[:, k, :],
                                        start=(k == 0), stop=(k == nG - 1))
                s_mm.tag(ins)

        @block.vector
        def _(vector, cur=cur, tok_x=tok_x, prev=prev):
            vector.wait_ge(s_ld.h, tok_x)
            if prev is not None:
                vector.wait_ge(s_st.h, prev["tok_st"])
            for m in range(nG):
                par = m % 2
                vector.wait_ge(s_mm.h, cur["tok_vm"][m])
                s_dve.tag(vector.tensor_tensor(t_f[:, m, :], r_t[:, m, :], ps_ab[par][:, :], ALU.mult))
                vector.wait_ge(s_dve.h, cur["tok_vdve"][m])
                s_dve.tag(vector.scalar_tensor_tensor(xo_t[:, m, :], t_f[:, m, :], 0.5,
                                                      x_t[:, m, :], ALU.mult, ALU.add))

        tok_st = s_st.reserve(1, 16)[0]
        cur["tok_st"] = tok_st

        @block.sync
        def _(sync, cur=cur, col=col):
            sync.wait_ge(s_dve.h, cur["tok_res"][-1])
            s_st.tag(sync.dma_start(out=y_d[:, col].rearrange("(g p) t -> p g t", p=128),
                                    in_=xo_t[:, :, :]), 16)
        prev = cur

    @block.sync
    def _(sync):
        sync.wait_ge(s_st.h, s_st.v)


def build_l3ffn():
    nc = bass.Bass("TRN2", target_bir_lowering=False)
    x_d = nc.dram_tensor("x", [DM, T], F32, kind="ExternalInput").ap()
    y_d = nc.dram_tensor("y", [DM, T], F32, kind="ExternalOutput").ap()
    names = dict(wrT=[1024, 1024], wkT=[1024, 1024], wvT=[1024, 1024],
                 mur_t=[128, 8], muk_t=[128, 8])
    P = {nm: nc.dram_tensor(nm, sh, F32, kind="ExternalInput").ap() for nm, sh in names.items()}
    with ExitStack() as es:
        block = es.enter_context(nc.Block())
        c = Ctx(nc, es, block)
        emit_l3ffn(c, x_d, y_d, P, "l3f")
        check_sems(c)
    return nc


def l3ffn_inmaps(inp, xs):
    P = dict(wrT=np.ascontiguousarray(inp["L3_ffn_W_r"].T),
             wkT=np.ascontiguousarray(inp["L3_ffn_W_k"].T),
             wvT=np.ascontiguousarray(inp["L3_ffn_W_v"].T),
             mur_t=_tile_vec(inp["L3_ffn_mu_r"], 8),
             muk_t=_tile_vec(inp["L3_ffn_mu_k"], 8))
    return [dict(x=xs[cid], **P) for cid in range(NC8)]



def emit_l2ffn(c, x_d, y_d, P, tag, bisect=0):
    block = c.block
    s_ld = c.sem("s_ld"); s_gp = c.sem("s_gp")
    s_act = c.sem("s_act"); s_dve = c.sem("s_dve"); s_mm = c.sem("s_mm")
    s_st = c.sem("s_st")
    Tc = 256
    NCH = T // Tc
    nG = 8
    BIG = 60000.0

    w1T = c.sb(f"{tag}_w1T", [128, 8, 512], BF16)
    w2T = c.sb(f"{tag}_w2T", [128, 4, 1024], BF16)
    wgT = c.sb(f"{tag}_wgT", [128, 8, 8], F32)
    bg_row = c.sb(f"{tag}_bg", [1, 8], F32)
    sel = c.sb(f"{tag}_sel", [8, 4, 128], F32)
    identf = c.sb(f"{tag}_idf", [128, 128], F32)
    ones_col = c.sb(f"{tag}_onc", [128, 1], F32)
    ones_row = c.sb(f"{tag}_onr", [1, 128], F32)
    eps_t = c.sb(f"{tag}_eps", [1, 1], F32)

    tok_wld = s_gp.reserve(6, 16)[-1]

    @block.gpsimd
    def _(gpsimd):
        s_gp.tag(gpsimd.dma_start(out=w1T[:, :, :], in_=P["w1eT"].rearrange("(g p) m -> p g m", p=128)), 16)
        s_gp.tag(gpsimd.dma_start(out=w2T[:, :, :], in_=P["w2eT"].rearrange("(g p) m -> p g m", p=128)), 16)
        s_gp.tag(gpsimd.dma_start(out=wgT[:, :, :], in_=P["wgT"].rearrange("(g p) m -> p g m", p=128)), 16)
        s_gp.tag(gpsimd.dma_start(out=bg_row[:, :], in_=P["bg_row"]), 16)
        s_gp.tag(gpsimd.dma_start(out=sel[:, :, :], in_=P["sel"]), 16)
        s_gp.tag(gpsimd.dma_start(out=identf[:, :], in_=P["identf"]), 16)

    tok_cst = s_dve.reserve(1)[0]

    @block.vector
    def _(vector):
        vector.memset(ones_col[:, :], 1.0)
        vector.memset(ones_row[:, :], 1.0)
        s_dve.tag(vector.memset(eps_t[:, :], 1e-5))

    x_t = c.sb(f"{tag}_x", [128, nG, Tc], F32)
    xsq = c.sb(f"{tag}_xsq", [128, nG, Tc], F32)
    t_f = c.sb(f"{tag}_tf", [128, nG, Tc], F32)
    xln_f = c.sb(f"{tag}_xlf", [128, nG, Tc], F32)
    xln_bf = c.sb(f"{tag}_xlb", [128, nG, Tc], BF16)
    h_bf = c.sb(f"{tag}_h", [128, 4, Tc], BF16)
    ht_bf = c.sb(f"{tag}_ht", [128, 4, Tc], BF16)
    lgt = [c.sb(f"{tag}_lg{i}", [128, 8], F32) for i in range(2)]
    rsc = c.sb(f"{tag}_rsc", [128, 24], F32)
    rs1 = c.sb(f"{tag}_rs1", [128, 6], F32)
    m_tok = [c.sb(f"{tag}_mtk{i}", [128, 8], F32) for i in range(2)]
    m_fm = c.sb(f"{tag}_mfm", [8, 2, 128], F32)
    xo_t = c.sb(f"{tag}_xo", [128, nG, Tc], F32)
    stat = c.sb(f"{tag}_st", [1, 4 * Tc], F32)
    ps_sum = c.ps(f"{tag}_psum", [1, Tc])
    ps_ssq = c.ps(f"{tag}_pssq", [1, Tc])
    ps_rep2 = c.ps(f"{tag}_prep2", [128, 2 * Tc])
    ps_big = c.ps(f"{tag}_pbig", [128, Tc])
    ps_mr = c.ps(f"{tag}_pmrj", [128, Tc])
    m_rep = c.sb(f"{tag}_mrep", [128, 4, Tc], F32)
    ps_ab = [c.ps(f"{tag}_pab{i}", [128, Tc]) for i in range(2)]

    prev_ln = None
    prev = None
    for ci in range(NCH):
        col = slice(ci * Tc, (ci + 1) * Tc)
        tok_x = s_ld.reserve(1, 16)[0]

        @block.sync
        def _(sync, col=col, prev=prev):
            if prev is None:
                sync.wait_ge(s_gp.h, tok_wld)
                sync.wait_ge(s_dve.h, tok_cst)
            else:
                sync.wait_ge(s_mm.h, prev["tok_dm"][-1])
                sync.wait_ge(s_dve.h, prev["tok_res"][-1])
                sync.wait_ge(s_act.h, prev["tok_hact"][-1])
            s_ld.tag(sync.dma_start(out=x_t[:, :, :],
                                    in_=x_d[:, col].rearrange("(g p) t -> p g t", p=128)), 16)

        ln = emit_ln_chunk(c, x_t, xln_f, t_f, nG, Tc, ones_col, ones_row, xsq,
                           ps_sum, ps_ssq, ps_rep2, stat, tok_x, prev_ln, eps_t)
        prev_ln = ln
        cur = {}
        tok_cast = s_act.reserve(1)[0]

        @block.scalar
        def _(scalar, ln=ln):
            scalar.wait_ge(s_dve.h, ln.tok_xln)
            s_act.tag(scalar.copy(xln_bf[:, :, :], xln_f[:, :, :]))

        if bisect != 1:
            cur["tok_lgm"] = [None, None]
            cur["tok_lgev"] = [None, None]
            cur["tok_rt"] = [None] * 18
            cur["tok_rtact"] = [None, None]
            cur["tok_mtr"] = [None, None]
            cur["tok_mev"] = [None, None]

        for sub in range(2):
            if bisect == 1:
                break
            cur["tok_lgm"][sub] = s_mm.reserve(1)[0]
            cur["tok_lgev"][sub] = s_dve.reserve(1)[0]
            if bisect in (0, 3, 4):
                cur["tok_rt"][9 * sub:9 * sub + 9] = s_dve.reserve(9)
                cur["tok_rtact"][sub] = s_act.reserve(1)[0]
            if bisect in (0, 4):
                cur["tok_mtr"][sub] = s_mm.reserve(1)[0]
                cur["tok_mev"][sub] = s_dve.reserve(1)[0]
            tsl = slice(sub * 128, (sub + 1) * 128)

            @block.tensor
            def _(tensor, sub=sub, tsl=tsl, ln=ln, cur=cur, prev=prev):
                tensor.wait_ge(s_dve.h, ln.tok_xln)
                if sub == 1:
                    if bisect in (0, 4):
                        tensor.wait_ge(s_dve.h, cur["tok_mev"][0])
                    else:
                        tensor.wait_ge(s_dve.h, cur["tok_lgev"][0])
                elif prev is not None:
                    if bisect in (0, 4):
                        tensor.wait_ge(s_dve.h, prev["tok_mev"][1])
                    tensor.wait_ge(s_dve.h, prev["tok_lgev"][1])
                for k in range(nG):
                    tensor.matmul(ps_big[:, 0:8], xln_f[:, k, tsl], wgT[:, k, :],
                                  start=(k == 0), stop=False)
                ins = tensor.matmul(ps_big[:, 0:8], ones_row[0:1, :], bg_row[0:1, :],
                                    start=False, stop=True)
                s_mm.tag(ins)

            @block.vector
            def _(vector, sub=sub, cur=cur):
                vector.wait_ge(s_mm.h, cur["tok_lgm"][sub])
                s_dve.tag(vector.tensor_copy(lgt[sub][:, :], ps_big[:, 0:8]))

            if bisect == 2:
                continue

            @block.vector
            def _(vector, sub=sub, cur=cur):
                tk = cur["tok_rt"][9 * sub:9 * sub + 9]
                vector.wait_ge(s_dve.h, cur["tok_lgev"][sub])
                s_dve.tag(vector.tensor_reduce(rs1[:, 0:1], lgt[sub][:, :],
                                               mybir.AxisListType.X, ALU.max))
                vector.wait_ge(s_dve.h, tk[0])
                s_dve.tag(vector.tensor_scalar(rsc[:, 0:8], lgt[sub][:, :],
                                               rs1[:, 0:1], 1.0, ALU.is_ge, ALU.mult))
                vector.wait_ge(s_dve.h, tk[1])
                s_dve.tag(vector.scalar_tensor_tensor(rsc[:, 8:16], rsc[:, 0:8],
                                                      -BIG, lgt[sub][:, :], ALU.mult, ALU.add))
                vector.wait_ge(s_dve.h, tk[2])
                s_dve.tag(vector.tensor_reduce(rs1[:, 1:2], rsc[:, 8:16],
                                               mybir.AxisListType.X, ALU.max))
                vector.wait_ge(s_dve.h, tk[3])
                s_dve.tag(vector.tensor_scalar(rsc[:, 16:24], rsc[:, 8:16],
                                               rs1[:, 1:2], 1.0, ALU.is_ge, ALU.mult))
                vector.wait_ge(s_dve.h, tk[4])
                s_dve.tag(vector.tensor_tensor(rs1[:, 2:3], rs1[:, 0:1],
                                               rs1[:, 1:2], ALU.subtract))

            @block.scalar
            def _(scalar, sub=sub, cur=cur):
                scalar.wait_ge(s_dve.h, cur["tok_rt"][9 * sub + 5])
                s_act.tag(scalar.activation(rs1[:, 3:4], rs1[:, 2:3], AF.Sigmoid))

            @block.vector
            def _(vector, sub=sub, cur=cur):
                tk = cur["tok_rt"][9 * sub:9 * sub + 9]
                vector.wait_ge(s_act.h, cur["tok_rtact"][sub])
                s_dve.tag(vector.tensor_scalar(rs1[:, 4:5], rs1[:, 3:4],
                                               -1.0, 1.0, ALU.mult, ALU.add))
                vector.wait_ge(s_dve.h, tk[6])
                s_dve.tag(vector.tensor_scalar(m_tok[sub][:, :], rsc[:, 0:8],
                                               rs1[:, 3:4], 1.0, ALU.mult, ALU.mult))
                vector.wait_ge(s_dve.h, tk[7])
                s_dve.tag(vector.scalar_tensor_tensor(m_tok[sub][:, :], rsc[:, 16:24],
                                                      rs1[:, 4:5], m_tok[sub][:, :],
                                                      ALU.mult, ALU.add))

            if bisect == 3:
                continue

            @block.tensor
            def _(tensor, sub=sub, cur=cur):
                tensor.wait_ge(s_dve.h, cur["tok_rt"][9 * sub + 8])
                s_mm.tag(tensor.transpose(ps_big[0:8, 128:256], m_tok[sub][:, :], identf[:, :]))

            @block.vector
            def _(vector, sub=sub, cur=cur):
                vector.wait_ge(s_mm.h, cur["tok_mtr"][sub])
                s_dve.tag(vector.tensor_copy(m_fm[:, sub, :], ps_big[0:8, 128:256]))


        if bisect == 0:
            cur["tok_mrj"] = s_mm.reserve(4)
            cur["tok_mrd"] = s_dve.reserve(4)

            @block.tensor
            def _(tensor, cur=cur, prev=prev):
                tensor.wait_ge(s_dve.h, cur["tok_mev"][1])
                for j in range(4):
                    if j > 0:
                        tensor.wait_ge(s_dve.h, cur["tok_mrd"][j - 1])
                    elif prev is not None:
                        tensor.wait_ge(s_dve.h, prev["tok_mrd"][3])
                    ins = tensor.matmul(ps_mr[:, :], sel[:, j, :],
                                        m_fm[:, :, :].rearrange("e s t -> e (s t)"),
                                        start=True, stop=True)
                    s_mm.tag(ins)

            @block.vector
            def _(vector, cur=cur):
                for j in range(4):
                    vector.wait_ge(s_mm.h, cur["tok_mrj"][j])
                    s_dve.tag(vector.tensor_copy(m_rep[:, j, :], ps_mr[:, :]))

        cur["tok_hm"] = s_mm.reserve(4)
        cur["tok_hact"] = s_act.reserve(4)
        cur["tok_htd"] = s_dve.reserve(4)

        @block.tensor
        def _(tensor, cur=cur, prev=prev):
            tensor.wait_ge(s_act.h, cur["tok_cast"] if False else tok_cast)
            for m in range(4):
                par = m % 2
                if m >= 2:
                    tensor.wait_ge(s_act.h, cur["tok_hact"][m - 2])
                elif prev is not None:
                    tensor.wait_ge(s_dve.h, prev["tok_res"][6 + m])
                for k in range(nG):
                    ins = tensor.matmul(ps_ab[par][:, :], w1T[:, k, ts(m)], xln_bf[:, k, :],
                                        start=(k == 0), stop=(k == nG - 1))
                s_mm.tag(ins)

        @block.scalar
        def _(scalar, cur=cur):
            for m in range(4):
                par = m % 2
                scalar.wait_ge(s_mm.h, cur["tok_hm"][m])
                s_act.tag(scalar.activation(h_bf[:, m, :], ps_ab[par][:, :], AF.Gelu))

        @block.vector
        def _(vector, cur=cur):
            for m in range(4):
                vector.wait_ge(s_act.h, cur["tok_hact"][m])
                if bisect != 0:
                    s_dve.tag(vector.tensor_copy(ht_bf[:, m, :], h_bf[:, m, :]))
                    continue
                if m == 0:
                    vector.wait_ge(s_dve.h, cur["tok_mrd"][-1])
                s_dve.tag(vector.tensor_tensor(ht_bf[:, m, :], h_bf[:, m, :],
                                               m_rep[:, m, :], ALU.mult))

        cur["tok_dm"] = s_mm.reserve(8)
        cur["tok_res"] = s_dve.reserve(8)

        @block.tensor
        def _(tensor, cur=cur, prev=prev):
            tensor.wait_ge(s_dve.h, cur["tok_htd"][-1])
            for m in range(nG):
                par = m % 2
                if m >= 2:
                    tensor.wait_ge(s_dve.h, cur["tok_res"][m - 2])
                elif prev is not None:
                    tensor.wait_ge(s_dve.h, prev["tok_res"][6 + m])
                for k in range(4):
                    ins = tensor.matmul(ps_ab[par][:, :], w2T[:, k, ts(m)], ht_bf[:, k, :],
                                        start=(k == 0), stop=(k == 3))
                s_mm.tag(ins)

        @block.vector
        def _(vector, cur=cur, tok_x=tok_x, prev=prev):
            vector.wait_ge(s_ld.h, tok_x)
            if prev is not None:
                vector.wait_ge(s_st.h, prev["tok_st"])
            for m in range(nG):
                par = m % 2
                vector.wait_ge(s_mm.h, cur["tok_dm"][m])
                s_dve.tag(vector.scalar_tensor_tensor(xo_t[:, m, :], ps_ab[par][:, :], 1.0,
                                                      x_t[:, m, :], ALU.mult, ALU.add))

        tok_st = s_st.reserve(1, 16)[0]
        cur["tok_st"] = tok_st

        @block.sync
        def _(sync, cur=cur, col=col):
            sync.wait_ge(s_dve.h, cur["tok_res"][-1])
            s_st.tag(sync.dma_start(out=y_d[:, col].rearrange("(g p) t -> p g t", p=128),
                                    in_=xo_t[:, :, :]), 16)
        prev = cur

    @block.sync
    def _(sync):
        sync.wait_ge(s_st.h, s_st.v)


def build_l2ffn(bisect=0):
    nc = bass.Bass("TRN2", target_bir_lowering=False)
    x_d = nc.dram_tensor("x", [DM, T], F32, kind="ExternalInput").ap()
    y_d = nc.dram_tensor("y", [DM, T], F32, kind="ExternalOutput").ap()
    names = dict(w1eT=[1024, 512], w2eT=[512, 1024], wgT=[1024, 8], bg_row=[1, 8],
                 sel=[8, 4, 128], identf=[128, 128])
    P = {nm: nc.dram_tensor(nm, sh, F32, kind="ExternalInput").ap() for nm, sh in names.items()}
    with ExitStack() as es:
        block = es.enter_context(nc.Block())
        c = Ctx(nc, es, block)
        emit_l2ffn(c, x_d, y_d, P, "l2f", bisect=bisect)
        check_sems(c)
    return nc


def l2ffn_inmaps(inp, xs):
    W1 = np.asarray(inp["L2_ffn_W1"], np.float32)
    W2 = np.asarray(inp["L2_ffn_W2"], np.float32)
    sel = np.zeros((8, 4, 128), np.float32)
    for j in range(4):
        for p in range(128):
            sel[(j * 128 + p) // 64, j, p] = 1.0
    P = dict(w1eT=np.ascontiguousarray(W1.transpose(2, 0, 1).reshape(1024, 512)),
             w2eT=np.ascontiguousarray(W2.transpose(0, 2, 1).reshape(8 * 64, 1024)),
             wgT=np.ascontiguousarray(np.asarray(inp["L2_ffn_Wg"], np.float32).T),
             bg_row=np.ascontiguousarray(np.asarray(inp["L2_ffn_bg"], np.float32).reshape(1, 8)),
             sel=sel, identf=np.eye(128, dtype=np.float32))
    return [dict(x=xs[cid], **P) for cid in range(NC8)]


def emit_final_rms(c, x_d, y_d, fw_t_d, tag, resid_d=None):
    block = c.block
    s_ld = c.sem("s_ld"); s_gp = c.sem("s_gp")
    s_act = c.sem("s_act"); s_dve = c.sem("s_dve"); s_mm = c.sem("s_mm")
    s_st = c.sem("s_st")
    Tc = 256
    NCH = T // Tc
    nG = 8
    with c.scope() as cf:
        gw_t = cf.sb(f"{tag}_gw", [128, 8], F32)
        ones_col = cf.sb(f"{tag}_onc", [128, 1], F32)
        ones_row = cf.sb(f"{tag}_onr", [1, 128], F32)
        eps_t = cf.sb(f"{tag}_eps", [1, 1], F32)
        x_t = cf.sb(f"{tag}_x", [128, nG, Tc], F32)
        xsq = cf.sb(f"{tag}_xsq", [128, nG, Tc], F32)
        t_f = cf.sb(f"{tag}_tf", [128, nG, Tc], F32)
        y_t = cf.sb(f"{tag}_y", [128, nG, Tc], F32)
        rs_t = cf.sb(f"{tag}_rs", [128, nG, Tc], F32) if resid_d is not None else None
        stat = cf.sb(f"{tag}_st", [1, 4 * Tc], F32)
        ps_ssq = cf.ps(f"{tag}_pssq", [1, Tc])
        ps_rep2 = cf.ps(f"{tag}_prep2", [128, 2 * Tc])
        ent = dict(ld=s_ld.v, dve=s_dve.v, act=s_act.v, mm=s_mm.v, st=s_st.v)
        tok_gw = s_gp.reserve(1, 16)[0]

        @block.gpsimd
        def _(gpsimd):
            gpsimd.wait_ge(s_ld.h, ent["ld"])
            gpsimd.wait_ge(s_st.h, ent["st"])
            gpsimd.wait_ge(s_dve.h, ent["dve"])
            gpsimd.wait_ge(s_act.h, ent["act"])
            gpsimd.wait_ge(s_mm.h, ent["mm"])
            s_gp.tag(gpsimd.dma_start(out=gw_t[:, :], in_=fw_t_d), 16)

        tok_cst = s_dve.reserve(1)[0]

        @block.vector
        def _(vector):
            vector.wait_ge(s_gp.h, tok_gw)
            vector.memset(ones_col[:, :], 1.0)
            vector.memset(ones_row[:, :], 1.0)
            s_dve.tag(vector.memset(eps_t[:, :], 1e-6))

        prev_ln = None
        prev_st = None
        for ci in range(NCH):
            col = slice(ci * Tc, (ci + 1) * Tc)
            tok_x = s_ld.reserve(1, 16)[0]

            if resid_d is not None:
                tok_x = s_ld.reserve(1, 16)[0]

            @block.sync
            def _(sync, col=col, prev_ln=prev_ln, tok_x=tok_x):
                if prev_ln is None:
                    sync.wait_ge(s_gp.h, tok_gw)
                    sync.wait_ge(s_dve.h, tok_cst)
                else:
                    sync.wait_ge(s_dve.h, prev_ln.tok_xln)
                    sync.wait_ge(s_act.h, prev_ln.tok_mean)
                s_ld.tag(sync.dma_start(out=x_t[:, :, :],
                                        in_=x_d[:, col].rearrange("(g p) t -> p g t", p=128)), 16)
                if resid_d is not None:
                    s_ld.tag(sync.dma_start(out=rs_t[:, :, :],
                                            in_=resid_d[:, col].rearrange("(g p) t -> p g t", p=128)), 16)

            ln = emit_rms_chunk(c, x_t, y_t, t_f, nG, Tc, ones_col, ones_row, xsq,
                                ps_ssq, ps_rep2, stat, tok_x, prev_ln, eps_t, gw_t,
                                out_war=(s_st, prev_st) if prev_st is not None else None)
            prev_ln = ln
            if resid_d is not None:
                tok_radd = s_dve.reserve(1)[0]

                @block.vector
                def _(vector, ln=ln, tok_x=tok_x):
                    vector.wait_ge(s_dve.h, ln.tok_xln)
                    vector.wait_ge(s_ld.h, tok_x)
                    s_dve.tag(vector.tensor_tensor(y_t[:, :, :], y_t[:, :, :],
                                                   rs_t[:, :, :], ALU.add))
                tok_fin = tok_radd
            else:
                tok_fin = None
            tok_st = s_st.reserve(1, 16)[0]
            prev_st = tok_st

            @block.sync
            def _(sync, col=col, ln=ln, tok_fin=tok_fin):
                sync.wait_ge(s_dve.h, ln.tok_xln if tok_fin is None else tok_fin)
                s_st.tag(sync.dma_start(out=y_d[:, col].rearrange("(g p) t -> p g t", p=128),
                                        in_=y_t[:, :, :]), 16)

    @block.sync
    def _(sync):
        sync.wait_ge(s_st.h, s_st.v)


def emit_ln_full(c, x_d, y_d, tag, out_bf16=False):
    block = c.block
    s_ld = c.sem("s_ld"); s_gp = c.sem("s_gp")
    s_act = c.sem("s_act"); s_dve = c.sem("s_dve"); s_mm = c.sem("s_mm")
    s_st = c.sem("s_st")
    Tc = 256
    NCH = T // Tc
    nG = 8
    with c.scope() as cf:
        ones_col = cf.sb(f"{tag}_onc", [128, 1], F32)
        ones_row = cf.sb(f"{tag}_onr", [1, 128], F32)
        eps_t = cf.sb(f"{tag}_eps", [1, 1], F32)
        x_t = cf.sb(f"{tag}_x", [128, nG, Tc], F32)
        xsq = cf.sb(f"{tag}_xsq", [128, nG, Tc], F32)
        t_f = cf.sb(f"{tag}_tf", [128, nG, Tc], F32)
        y_t = cf.sb(f"{tag}_y", [128, nG, Tc], BF16 if out_bf16 else F32)
        stat = cf.sb(f"{tag}_st", [1, 4 * Tc], F32)
        ps_sum = cf.ps(f"{tag}_psum", [1, Tc])
        ps_ssq = cf.ps(f"{tag}_pssq", [1, Tc])
        ps_rep2 = cf.ps(f"{tag}_prep2", [128, 2 * Tc])
        ent = dict(ld=s_ld.v, dve=s_dve.v, act=s_act.v, mm=s_mm.v, st=s_st.v, gp=s_gp.v)
        tok_cst = s_dve.reserve(1)[0]

        @block.vector
        def _(vector):
            vector.memset(ones_col[:, :], 1.0)
            vector.memset(ones_row[:, :], 1.0)
            s_dve.tag(vector.memset(eps_t[:, :], 1e-5))

        prev_ln = None
        prev_st = None
        for ci in range(NCH):
            col = slice(ci * Tc, (ci + 1) * Tc)
            tok_x = s_ld.reserve(1, 16)[0]

            @block.sync
            def _(sync, col=col, prev_ln=prev_ln):
                if prev_ln is None:
                    sync.wait_ge(s_dve.h, tok_cst)
                    sync.wait_ge(s_gp.h, ent["gp"])
                    sync.wait_ge(s_act.h, ent["act"])
                    sync.wait_ge(s_mm.h, ent["mm"])
                    sync.wait_ge(s_st.h, ent["st"])
                else:
                    sync.wait_ge(s_dve.h, prev_ln.tok_xln)
                    sync.wait_ge(s_act.h, prev_ln.tok_mean)
                s_ld.tag(sync.dma_start(out=x_t[:, :, :],
                                        in_=x_d[:, col].rearrange("(g p) t -> p g t", p=128)), 16)

            ln = emit_ln_chunk(c, x_t, y_t, t_f, nG, Tc, ones_col, ones_row, xsq,
                               ps_sum, ps_ssq, ps_rep2, stat, tok_x, prev_ln, eps_t,
                               sq_war=(s_st, prev_st) if prev_st is not None else None,
                               extra_dve_wait=None)
            prev_ln = ln
            tok_st = s_st.reserve(1, 16)[0]
            prev_st = tok_st

            @block.sync
            def _(sync, col=col, ln=ln):
                sync.wait_ge(s_dve.h, ln.tok_xln)
                s_st.tag(sync.dma_start(out=y_d[:, col].rearrange("(g p) t -> p g t", p=128),
                                        in_=y_t[:, :, :]), 16)

    @block.sync
    def _(sync):
        sync.wait_ge(s_st.h, s_st.v)


def build_l2attn(debug=False):
    nc = bass.Bass("TRN2", target_bir_lowering=False)
    x_d = nc.dram_tensor("x", [DM, T], F32, kind="ExternalInput").ap()
    y_d = nc.dram_tensor("y", [DM, T], F32, kind="ExternalOutput").ap()
    shapes = dict(winT=[1024, 4096], wxpT=[2048, 96], wdtT=[64, 2048], woutT=[2048, 1024],
                  convw_t=[128, 16, 4], convb_t=[128, 16], bdt_t=[128, 16],
                  alog_t=[128, 16, 16], d_t=[128, 16], ident=[128, 128], gw_t=[128, 8],
                  zero_state=[128, 16, 16])
    Ps = []
    for i in range(4):
        Ps.append({nm: nc.dram_tensor(f"L{i}_{nm}", shapes[nm], F32, kind="ExternalInput").ap()
                   for nm in shapes})
    fw_d = nc.dram_tensor("fw_t", [128, 8], F32, kind="ExternalInput").ap()
    xb = [nc.dram_tensor(f"xb{i}", [DM, T], F32,
                         kind="ExternalOutput" if debug else "Internal").ap() for i in range(4)]
    scr = dict(
        xc=nc.dram_tensor("scr_xc", [2048, T], F32, kind="Internal").ap(),
        sz=nc.dram_tensor("scr_sz", [2048, T], BF16, kind="Internal").ap(),
        dt=nc.dram_tensor("scr_dt", [2048, T], F32, kind="Internal").ap(),
        bc=nc.dram_tensor("scr_bc", [32, T], BF16, kind="Internal").ap(),
    )
    xln0 = nc.dram_tensor("xln0", [DM, T], F32, kind="Internal").ap()
    with ExitStack() as es:
        block = es.enter_context(nc.Block())
        c = Ctx(nc, es, block)
        emit_ln_full(c, x_d, xln0, "oln")
        cur = xln0
        for i in range(4):
            emit_mamba(c, cur, xb[i], Ps[i], scr, "rms", f"mb{i}")
            cur = xb[i]
        emit_final_rms(c, cur, y_d, fw_d, "fin", resid_d=x_d)
        check_sems(c)
    return nc


def l2attn_inmaps(inp, xs):
    base = {}
    for i in range(4):
        P = mamba_host_params(inp["L2_attn_in_w"][i], inp["L2_attn_conv_w"][i],
                              inp["L2_attn_conv_b"][i], inp["L2_attn_xproj_w"][i],
                              inp["L2_attn_dt_w"][i], inp["L2_attn_dt_b"][i],
                              inp["L2_attn_A_log"][i], inp["L2_attn_Dp"][i],
                              inp["L2_attn_out_w"][i], gw=inp["L2_attn_norm_w"][i])
        for nm, v in P.items():
            base[f"L{i}_{nm}"] = v
    base["fw_t"] = _tile_vec(inp["L2_attn_final_w"], 8)
    return [dict(x=xs[cid], **base) for cid in range(NC8)]



def emit_l1attn(c, x_d, y_d, P, scr, tag):
    block = c.block
    s_ld = c.sem("s_ld"); s_gp = c.sem("s_gp")
    s_act = c.sem("s_act"); s_dve = c.sem("s_dve"); s_mm = c.sem("s_mm")
    s_st = c.sem("s_st")
    nG = 8

    emit_ln_full(c, x_d, scr["xln"], f"{tag}_ln", out_bf16=True)

    with c.scope() as c1:
        Tc = 512
        qkvT = c1.sb(f"{tag}_qkvT", [128, 8, 3072], BF16)
        bqkv = c1.sb(f"{tag}_bq", [128, 24], F32)
        xln_t = c1.sb(f"{tag}_xln", [128, 8, Tc], BF16)
        ev = [c1.sb(f"{tag}_ev{i}", [128, Tc], BF16) for i in range(2)]
        ps_q = [c1.ps(f"{tag}_psq{i}", [128, Tc]) for i in range(2)]
        ent = dict(ld=s_ld.v, dve=s_dve.v, act=s_act.v, mm=s_mm.v, st=s_st.v, gp=s_gp.v)
        tok_w = s_gp.reserve(2, 16)[-1]

        @block.gpsimd
        def _(gpsimd):
            gpsimd.wait_ge(s_gp.h, ent["gp"])
            gpsimd.wait_ge(s_ld.h, ent["ld"])
            gpsimd.wait_ge(s_st.h, ent["st"])
            gpsimd.wait_ge(s_dve.h, ent["dve"])
            gpsimd.wait_ge(s_act.h, ent["act"])
            gpsimd.wait_ge(s_mm.h, ent["mm"])
            s_gp.tag(gpsimd.dma_start(out=qkvT[:, :, :], in_=P["qkvT"].rearrange("(g p) m -> p g m", p=128)), 16)
            s_gp.tag(gpsimd.dma_start(out=bqkv[:, :], in_=P["bqkv_t"]), 16)

        prev = None
        for ci in range(T // Tc):
            col = slice(ci * Tc, (ci + 1) * Tc)
            tok_x = s_ld.reserve(1, 16)[0]

            @block.sync
            def _(sync, col=col, prev=prev):
                if prev is None:
                    sync.wait_ge(s_gp.h, tok_w)
                else:
                    sync.wait_ge(s_mm.h, prev["mm"][-1])
                s_ld.tag(sync.dma_start(out=xln_t[:, :, :],
                                        in_=scr["xln"][:, col].rearrange("(g p) t -> p g t", p=128)), 16)

            cur = dict(mm=s_mm.reserve(24), act=s_act.reserve(24), st=s_st.reserve(24, 16))

            @block.tensor
            def _(tensor, cur=cur, tok_x=tok_x, prev=prev):
                tensor.wait_ge(s_ld.h, tok_x)
                for m in range(24):
                    par = m % 2
                    if m >= 2:
                        tensor.wait_ge(s_act.h, cur["act"][m - 2])
                    elif prev is not None:
                        tensor.wait_ge(s_act.h, prev["act"][22 + m])
                    for k in range(nG):
                        ins = tensor.matmul(ps_q[par][:, :], qkvT[:, k, ts(m)], xln_t[:, k, :],
                                            start=(k == 0), stop=(k == nG - 1))
                    s_mm.tag(ins)

            @block.scalar
            def _(scalar, cur=cur, prev=prev):
                for m in range(24):
                    par = m % 2
                    scalar.wait_ge(s_mm.h, cur["mm"][m])
                    if prev is not None and m < 2:
                        scalar.wait_ge(s_st.h, prev["st"][22 + m])
                    elif m >= 2:
                        scalar.wait_ge(s_st.h, cur["st"][m - 2] - 16)
                    s_act.tag(scalar.add(ev[par][:, :], ps_q[par][:, :], bqkv[:, m:m + 1]))

            @block.sync
            def _(sync, cur=cur, col=col):
                for m in range(24):
                    sync.wait_ge(s_act.h, cur["act"][m])
                    s_st.tag(sync.dma_start(out=scr["qkv"][ts(m), col], in_=ev[m % 2][:, :]), 16)
            prev = cur

    with c.scope() as c2:
        KH = 64
        nKC = T // 128
        nP = T // 512
        kq_f = [c2.sb(f"{tag}_kqf{i}", [64, T], BF16) for i in range(2)]
        kraw = [c2.sb(f"{tag}_kraw{i}", [64, T], BF16) for i in range(2)]
        vtok = [c2.sb(f"{tag}_vtok{i}", [128, nKC, 80], BF16) for i in range(2)]
        q_bf = c2.sb(f"{tag}_qbf", [64, 512], BF16)
        qraw = c2.sb(f"{tag}_qraw", [64, 512], BF16)
        cosT = c2.sb(f"{tag}_cos", [64, T], F32)
        sinT = c2.sb(f"{tag}_sin", [64, T], F32)
        pswap = c2.sb(f"{tag}_pswp", [64, 64], BF16)
        ra = c2.sb(f"{tag}_ra", [64, 512], F32)
        rb = c2.sb(f"{tag}_rb", [64, 512], F32)
        sqf = c2.sb(f"{tag}_sqf", [64, T], F32)
        nrm = c2.sb(f"{tag}_nrm", [1, T], F32)
        sc1 = c2.sb(f"{tag}_sc1", [1, 8], F32)
        nb_t = c2.sb(f"{tag}_nb", [128, 1], F32)
        pt_bf = [c2.sb(f"{tag}_pt{i}", [128, 512], BF16) for i in range(2)]
        y_f = c2.sb(f"{tag}_yf", [64, 512], F32)
        y_bf = c2.sb(f"{tag}_ybf", [64, 512], BF16)
        ones_col = c2.sb(f"{tag}_onc", [128, 1], F32)
        ones_row = c2.sb(f"{tag}_onr", [1, 128], F32)
        ps_s = [c2.ps(f"{tag}_pss{i}", [128, 512]) for i in range(2)]
        ps_sw = c2.ps(f"{tag}_pssw", [64, 512])
        ps_av = c2.ps(f"{tag}_psav", [72, 512])
        ps_rc = c2.ps(f"{tag}_psrc", [64, 512])
        ps_nr = c2.ps(f"{tag}_psnr", [1, 512])

        ent = dict(ld=s_ld.v, dve=s_dve.v, act=s_act.v, mm=s_mm.v, st=s_st.v, gp=s_gp.v)
        tok_tbl = s_gp.reserve(3, 16)[-1]

        @block.gpsimd
        def _(gpsimd):
            gpsimd.wait_ge(s_gp.h, ent["gp"])
            gpsimd.wait_ge(s_ld.h, ent["ld"])
            gpsimd.wait_ge(s_st.h, ent["st"])
            gpsimd.wait_ge(s_dve.h, ent["dve"])
            gpsimd.wait_ge(s_act.h, ent["act"])
            gpsimd.wait_ge(s_mm.h, ent["mm"])
            s_gp.tag(gpsimd.dma_start(out=cosT[:, :], in_=P["cosT"]), 16)
            s_gp.tag(gpsimd.dma_start(out=sinT[:, :], in_=P["sinT"]), 16)
            s_gp.tag(gpsimd.dma_start(out=pswap[:, :], in_=P["pswap"]), 16)

        tok_cst = s_dve.reserve(1)[0]

        @block.vector
        def _(vector):
            vector.wait_ge(s_gp.h, tok_tbl)
            vector.memset(ones_col[:, :], 1.0)
            vector.memset(vtok[0][:, :, :], 0.0)
            vector.memset(vtok[1][:, :, :], 0.0)
            s_dve.tag(vector.memset(ones_row[:, :], 1.0))

        def emit_rope(eng_tokens, dst, srcap, t0c, Tn):
            np_ = Tn // 512
            t_mm = s_mm.reserve(np_)
            t_dv = s_dve.reserve(3 * np_)

            @block.tensor
            def _(tensor):
                for p in range(np_):
                    for w in (eng_tokens if p == 0 else []):
                        tensor.wait_ge(w[0].h, w[1])
                    if p > 0:
                        tensor.wait_ge(s_dve.h, t_dv[3 * (p - 1) + 1])
                    s_mm.tag(tensor.matmul(ps_sw[:, :], pswap[:, :], srcap[:, ts(p, 512)],
                                           start=True, stop=True))

            @block.vector
            def _(vector):
                for p in range(np_):
                    for w in (eng_tokens if p == 0 else []):
                        vector.wait_ge(w[0].h, w[1])
                    csl = slice(t0c + p * 512, t0c + (p + 1) * 512)
                    s_dve.tag(vector.tensor_tensor(ra[:, :], srcap[:, ts(p, 512)],
                                                   cosT[:, csl], ALU.mult))
                    vector.wait_ge(s_mm.h, t_mm[p])
                    s_dve.tag(vector.tensor_tensor(rb[:, :], ps_sw[:, :],
                                                   sinT[:, csl], ALU.mult))
                    vector.wait_ge(s_dve.h, t_dv[3 * p + 1])
                    s_dve.tag(vector.tensor_tensor(dst[:, ts(p, 512)], ra[:, :],
                                                   rb[:, :], ALU.add))
            return t_dv[-1]

        prev_h = None
        prev2_h = None
        for h in range(16):
            ph = h % 2
            krow = slice(1024 + 64 * h, 1024 + 64 * h + 64)
            qrow = slice(64 * h, 64 * h + 64)
            vrow = slice(2048 + 64 * h, 2048 + 64 * h + 64)
            tok_kld = s_ld.reserve(1, 16)[0]
            tok_vld = s_ld.reserve(nKC, 16)[-1]

            @block.sync
            def _(sync, h=h, krow=krow, vrow=vrow, ph=ph, prev2_h=prev2_h):
                if prev2_h is None:
                    sync.wait_ge(s_dve.h, tok_cst)
                else:
                    sync.wait_ge(s_mm.h, prev2_h["avmm"])
                    sync.wait_ge(s_dve.h, prev2_h["last_dve"])
                s_ld.tag(sync.dma_start(out=kraw[ph][:, :], in_=scr["qkv"][krow, :]), 16)
                for tt in range(nKC):
                    s_ld.tag(sync.dma_start_transpose(vtok[ph][:, tt, 0:64],
                                                      scr["qkv"][vrow, ts(tt)]), 16)

            tok_krope = emit_rope([(s_ld, tok_kld)], kq_f[ph], kraw[ph][:, :], 0, T)

            tok_onev = s_dve.reserve(1)[0]

            @block.vector
            def _(vector, tok_vld=tok_vld):
                vector.wait_ge(s_ld.h, tok_vld)
                s_dve.tag(vector.memset(vtok[ph][:, :, 64:65], 1.0))

            t_sq = s_act.reserve(1)[0]
            t_nr = s_mm.reserve(nP)
            t_ne = s_dve.reserve(nP)
            t_km = s_dve.reserve(1)[0]
            t_ksq = s_act.reserve(1)[0]

            @block.scalar
            def _(scalar, tok_krope=tok_krope, ph=ph):
                scalar.wait_ge(s_dve.h, tok_krope)
                s_act.tag(scalar.square(sqf[:, :], kq_f[ph][:, :]))

            @block.tensor
            def _(tensor, t_sq=t_sq, t_ne=t_ne, prev_h=prev_h):
                tensor.wait_ge(s_act.h, t_sq)
                if prev_h is not None:
                    tensor.wait_ge(s_act.h, prev_h["ksq"])
                for piece in range(nP):
                    ins = tensor.matmul(ps_nr[0:1, :], ones_col[0:64, 0:1],
                                        sqf[:, ts(piece, 512)], start=True, stop=True)
                    s_mm.tag(ins)
                    tensor.wait_ge(s_dve.h, t_ne[piece])

            @block.vector
            def _(vector, t_nr=t_nr, t_ne=t_ne):
                for piece in range(nP):
                    vector.wait_ge(s_mm.h, t_nr[piece])
                    s_dve.tag(vector.tensor_copy(nrm[0:1, ts(piece, 512)], ps_nr[0:1, :]))

            @block.vector
            def _(vector, t_ne=t_ne, t_km=t_km):
                vector.wait_ge(s_dve.h, t_ne[-1])
                s_dve.tag(vector.tensor_reduce(sc1[0:1, 1:2], nrm[0:1, :],
                                               mybir.AxisListType.X, ALU.max))

            @block.scalar
            def _(scalar, t_km=t_km):
                scalar.wait_ge(s_dve.h, t_km)
                s_act.tag(scalar.activation(sc1[0:1, 3:4], sc1[0:1, 1:2], AF.Sqrt))

            cur_h = dict(ksq=t_ksq)
            last_dve = t_km
            avmm_last = None
            for qc in range(nP):
                qsl = slice(qc * 512, (qc + 1) * 512)
                tok_qld = s_ld.reserve(1, 16)[0]

                @block.sync
                def _(sync, h=h, qrow=qrow, qsl=qsl, last_dve=last_dve):
                    sync.wait_ge(s_dve.h, last_dve)
                    s_ld.tag(sync.dma_start(out=qraw[:, :], in_=scr["qkv"][qrow, qsl]), 16)

                tok_qrope = emit_rope([(s_ld, tok_qld)], q_bf, qraw[:, :], qc * 512, 512)

                tq = dict(sq=s_act.reserve(1)[0], nr=s_mm.reserve(1)[0], ne=s_dve.reserve(1)[0],
                          qm=s_dve.reserve(1)[0], qs=s_act.reserve(1)[0], mu=s_dve.reserve(1)[0],
                          ng=s_dve.reserve(1)[0], bmm2=s_mm.reserve(1)[0], bcp=s_dve.reserve(1)[0])

                @block.scalar
                def _(scalar, tok_qrope=tok_qrope, tq=tq):
                    scalar.wait_ge(s_dve.h, tok_qrope)
                    s_act.tag(scalar.square(sqf[:, 0:512], q_bf[:, :]))

                @block.tensor
                def _(tensor, tq=tq):
                    tensor.wait_ge(s_act.h, tq["sq"])
                    s_mm.tag(tensor.matmul(ps_nr[0:1, :], ones_col[0:64, 0:1], sqf[:, 0:512],
                                           start=True, stop=True))

                @block.vector
                def _(vector, tq=tq):
                    vector.wait_ge(s_mm.h, tq["nr"])
                    s_dve.tag(vector.tensor_copy(nrm[0:1, 0:512], ps_nr[0:1, :]))
                    vector.wait_ge(s_dve.h, tq["ne"])
                    s_dve.tag(vector.tensor_reduce(sc1[0:1, 0:1], nrm[0:1, 0:512],
                                                   mybir.AxisListType.X, ALU.max))

                @block.scalar
                def _(scalar, tq=tq):
                    scalar.wait_ge(s_dve.h, tq["qm"])
                    s_act.tag(scalar.activation(sc1[0:1, 2:3], sc1[0:1, 0:1], AF.Sqrt))

                @block.vector
                def _(vector, tq=tq, t_ksq=t_ksq):
                    vector.wait_ge(s_act.h, tq["qs"])
                    vector.wait_ge(s_act.h, t_ksq)
                    s_dve.tag(vector.tensor_tensor(sc1[0:1, 4:5], sc1[0:1, 2:3],
                                                   sc1[0:1, 3:4], ALU.mult))
                    vector.wait_ge(s_dve.h, tq["mu"])
                    s_dve.tag(vector.tensor_scalar(sc1[0:1, 5:6], sc1[0:1, 4:5],
                                                   -0.125, 1.0, ALU.mult, ALU.mult))

                @block.tensor
                def _(tensor, tq=tq):
                    tensor.wait_ge(s_dve.h, tq["ng"])
                    s_mm.tag(tensor.matmul(ps_s[0][:, 0:1], ones_row[0:1, :], sc1[0:1, 5:6],
                                           start=True, stop=True))

                @block.vector
                def _(vector, tq=tq):
                    vector.wait_ge(s_mm.h, tq["bmm2"])
                    s_dve.tag(vector.tensor_copy(nb_t[:, :], ps_s[0][:, 0:1]))

                _mmtoks = s_mm.reserve(2 * nKC)
                toks_sc = [_mmtoks[0]] + _mmtoks[1::2][:nKC - 1]
                toks_av = _mmtoks[2::2] + [_mmtoks[-1]]
                toks_ex = s_act.reserve(nKC)

                @block.tensor
                def _(tensor, tq=tq, toks_sc=toks_sc, toks_ex=toks_ex, toks_av=toks_av,
                      tok_qrope=tok_qrope, avmm_last=avmm_last):
                    tensor.wait_ge(s_dve.h, tq["bcp"])
                    for kc in range(nKC + 1):
                        if kc < nKC:
                            par = kc % 2
                            if kc >= 2:
                                tensor.wait_ge(s_act.h, toks_ex[kc - 2])
                            ins = tensor.matmul(ps_s[par][:, :], kq_f[ph][:, ts(kc)], q_bf[:, :],
                                                start=True, stop=True)
                            s_mm.tag(ins)
                        if kc >= 1:
                            j = kc - 1
                            tensor.wait_ge(s_act.h, toks_ex[j])
                            ins = tensor.matmul(ps_av[:, :], vtok[ph][:, j, 0:72], pt_bf[j % 2][:, :],
                                                start=(j == 0), stop=(j == nKC - 1))
                            s_mm.tag(ins)

                @block.scalar
                def _(scalar, toks_sc=toks_sc, toks_ex=toks_ex, toks_av=toks_av):
                    for kc in range(nKC):
                        par = kc % 2
                        scalar.wait_ge(s_mm.h, toks_sc[kc])
                        if kc >= 2:
                            scalar.wait_ge(s_mm.h, toks_av[kc - 2])
                        s_act.tag(scalar.activation(pt_bf[par][:, :], ps_s[par][:, :], AF.Exp,
                                                    bias=nb_t[:, 0:1], scale=0.125))

                tq["rcp"] = s_dve.reserve(1)[0]
                tq["yev"] = s_act.reserve(1)[0]
                tq["rbc"] = s_mm.reserve(1)[0]
                tq["ysc"] = s_dve.reserve(1)[0]
                tok_yst = s_st.reserve(1, 16)[0]

                @block.vector
                def _(vector, tq=tq, toks_av=toks_av):
                    vector.wait_ge(s_mm.h, toks_av[-1])
                    s_dve.tag(vector.reciprocal(nrm[0:1, 0:512], ps_av[64:65, :]))

                @block.scalar
                def _(scalar, tq=tq, toks_av=toks_av):
                    scalar.wait_ge(s_mm.h, toks_av[-1])
                    s_act.tag(scalar.copy(y_f[:, :], ps_av[0:64, :]))

                @block.tensor
                def _(tensor, tq=tq):
                    tensor.wait_ge(s_dve.h, tq["rcp"])
                    s_mm.tag(tensor.matmul(ps_rc[:, :], ones_row[0:1, 0:64], nrm[0:1, 0:512],
                                           start=True, stop=True))

                @block.vector
                def _(vector, tq=tq):
                    vector.wait_ge(s_mm.h, tq["rbc"])
                    vector.wait_ge(s_act.h, tq["yev"])
                    s_dve.tag(vector.tensor_tensor(y_bf[:, :], y_f[:, :], ps_rc[0:64, :], ALU.mult))

                @block.sync
                def _(sync, h=h, qsl=qsl, tq=tq):
                    sync.wait_ge(s_dve.h, tq["ysc"])
                    s_st.tag(sync.dma_start(out=scr["y"][ts(h, 64), qsl], in_=y_bf[:, :]), 16)
                last_dve = tq["ysc"]
                avmm_last = toks_av[-1]
            cur_h["avmm"] = avmm_last
            cur_h["last_dve"] = last_dve
            prev2_h = prev_h
            prev_h = cur_h

    with c.scope() as c3:
        Tc = 512
        woT = c3.sb(f"{tag}_woT", [128, 8, 1024], BF16)
        ob = c3.sb(f"{tag}_ob", [128, 8], F32)
        yin = c3.sb(f"{tag}_yin", [128, 8, Tc], BF16)
        x_t = c3.sb(f"{tag}_x3", [128, 8, Tc], F32)
        of = [c3.sb(f"{tag}_of{i}", [128, Tc], F32) for i in range(2)]
        xo_t = c3.sb(f"{tag}_xo3", [128, 8, Tc], F32)
        ps_o = [c3.ps(f"{tag}_pso{i}", [128, Tc]) for i in range(2)]
        ent = dict(ld=s_ld.v, dve=s_dve.v, act=s_act.v, mm=s_mm.v, st=s_st.v, gp=s_gp.v)
        tok_w = s_gp.reserve(2, 16)[-1]

        @block.gpsimd
        def _(gpsimd):
            gpsimd.wait_ge(s_gp.h, ent["gp"])
            gpsimd.wait_ge(s_ld.h, ent["ld"])
            gpsimd.wait_ge(s_st.h, ent["st"])
            gpsimd.wait_ge(s_dve.h, ent["dve"])
            gpsimd.wait_ge(s_act.h, ent["act"])
            gpsimd.wait_ge(s_mm.h, ent["mm"])
            s_gp.tag(gpsimd.dma_start(out=woT[:, :, :], in_=P["woT"].rearrange("(g p) m -> p g m", p=128)), 16)
            s_gp.tag(gpsimd.dma_start(out=ob[:, :], in_=P["ob_t"]), 16)

        prev = None
        for ci in range(T // Tc):
            col = slice(ci * Tc, (ci + 1) * Tc)
            tok_in = s_ld.reserve(2, 16)[-1]

            @block.sync
            def _(sync, col=col, prev=prev):
                if prev is None:
                    sync.wait_ge(s_gp.h, tok_w)
                else:
                    sync.wait_ge(s_dve.h, prev["res"][-1])
                    sync.wait_ge(s_st.h, prev["st"])
                s_ld.tag(sync.dma_start(out=yin[:, :, :],
                                        in_=scr["y"][:, col].rearrange("(g p) t -> p g t", p=128)), 16)
                s_ld.tag(sync.dma_start(out=x_t[:, :, :],
                                        in_=x_d[:, col].rearrange("(g p) t -> p g t", p=128)), 16)

            cur = dict(mm=s_mm.reserve(8), act=s_act.reserve(8), res=s_dve.reserve(8),
                       st=s_st.reserve(1, 16)[0])

            @block.tensor
            def _(tensor, cur=cur, tok_in=tok_in, prev=prev):
                tensor.wait_ge(s_ld.h, tok_in)
                for m in range(8):
                    par = m % 2
                    if m >= 2:
                        tensor.wait_ge(s_act.h, cur["act"][m - 2])
                    elif prev is not None:
                        tensor.wait_ge(s_act.h, prev["act"][6 + m])
                    for k in range(nG):
                        ins = tensor.matmul(ps_o[par][:, :], woT[:, k, ts(m)], yin[:, k, :],
                                            start=(k == 0), stop=(k == nG - 1))
                    s_mm.tag(ins)

            @block.scalar
            def _(scalar, cur=cur, prev=prev):
                for m in range(8):
                    par = m % 2
                    scalar.wait_ge(s_mm.h, cur["mm"][m])
                    if m >= 2:
                        scalar.wait_ge(s_dve.h, cur["res"][m - 2])
                    elif prev is not None:
                        scalar.wait_ge(s_dve.h, prev["res"][6 + m])
                    s_act.tag(scalar.add(of[par][:, :], ps_o[par][:, :], ob[:, m:m + 1]))

            @block.vector
            def _(vector, cur=cur, tok_in=tok_in):
                vector.wait_ge(s_ld.h, tok_in)
                for m in range(8):
                    par = m % 2
                    vector.wait_ge(s_act.h, cur["act"][m])
                    s_dve.tag(vector.scalar_tensor_tensor(xo_t[:, m, :], of[par][:, :], 1.0,
                                                          x_t[:, m, :], ALU.mult, ALU.add))

            @block.sync
            def _(sync, cur=cur, col=col):
                sync.wait_ge(s_dve.h, cur["res"][-1])
                s_st.tag(sync.dma_start(out=y_d[:, col].rearrange("(g p) t -> p g t", p=128),
                                        in_=xo_t[:, :, :]), 16)
            prev = cur

    @block.sync
    def _(sync):
        sync.wait_ge(s_st.h, s_st.v)


def build_l1attn(debug=False):
    nc = bass.Bass("TRN2", target_bir_lowering=False)
    x_d = nc.dram_tensor("x", [DM, T], F32, kind="ExternalInput").ap()
    y_d = nc.dram_tensor("y", [DM, T], F32, kind="ExternalOutput").ap()
    names = dict(qkvT=[1024, 3072], bqkv_t=[128, 24], woT=[1024, 1024], ob_t=[128, 8],
                 cosT=[64, T], sinT=[64, T], pswap=[64, 64])
    P = {nm: nc.dram_tensor(nm, sh, F32, kind="ExternalInput").ap() for nm, sh in names.items()}
    kd = "ExternalOutput" if debug else "Internal"
    scr = dict(
        xln=nc.dram_tensor("scr_xln", [DM, T], BF16, kind=kd).ap(),
        qkv=nc.dram_tensor("scr_qkv", [3072, T], BF16, kind=kd).ap(),
        y=nc.dram_tensor("scr_ya", [DM, T], BF16, kind=kd).ap(),
    )
    with ExitStack() as es:
        block = es.enter_context(nc.Block())
        c = Ctx(nc, es, block)
        emit_l1attn(c, x_d, y_d, P, scr, "l1a")
        check_sems(c)
    return nc


def l1attn_inmaps(inp, xs):
    qkv_w = np.asarray(inp["L1_attn_qkv_w"], np.float32)
    qkv_b = np.asarray(inp["L1_attn_qkv_b"], np.float32)
    perm = np.arange(3072)
    for h in range(16):
        base = 64 * h
        pq = np.concatenate([np.arange(base, base + 64, 2), np.arange(base + 1, base + 64, 2)])
        perm[base:base + 64] = pq
        perm[1024 + base:1024 + base + 64] = 1024 + pq
    qkv_wp = qkv_w[perm]
    qkv_bp = qkv_b[perm]
    inv = 1.0 / (10000.0 ** (np.arange(0, 64, 2, dtype=np.float32) / np.float32(64.0)))
    th = np.arange(T, dtype=np.float32)[:, None] * inv[None, :]
    cos1 = np.cos(th).T.astype(np.float32)
    sin1 = np.sin(th).T.astype(np.float32)
    cosT = np.ascontiguousarray(np.concatenate([cos1, cos1], 0))
    sinT = np.ascontiguousarray(np.concatenate([-sin1, sin1], 0))
    pswap = np.zeros((64, 64), np.float32)
    for i in range(32):
        pswap[32 + i, i] = 1.0
        pswap[i, 32 + i] = 1.0
    P = dict(qkvT=np.ascontiguousarray(qkv_wp.T), bqkv_t=_tile_vec(qkv_bp, 24),
             woT=np.ascontiguousarray(np.asarray(inp["L1_attn_out_w"], np.float32).T),
             ob_t=_tile_vec(inp["L1_attn_out_b"], 8), cosT=cosT, sinT=sinT, pswap=pswap)
    return [dict(x=xs[cid], **P) for cid in range(NC8)]




def emit_l1moe(c, x_d, y_d, P, scr, tag):
    block = c.block
    s_ld = c.sem("s_ld"); s_gp = c.sem("s_gp")
    s_act = c.sem("s_act"); s_dve = c.sem("s_dve"); s_mm = c.sem("s_mm")
    s_st = c.sem("s_st")
    nG = 8
    BIG = 60000.0

    emit_ln_full(c, x_d, scr["xlnf"], f"{tag}ln")

    with c.scope() as c1:
        Tc = 256
        NCH = T // Tc
        rw = c1.sb(f"{tag}_rw", [128, 8, 16], F32)
        identf = c1.sb(f"{tag}_idf", [128, 128], F32)
        x_t = c1.sb(f"{tag}_x1", [128, nG, Tc], F32)
        xbf_t = c1.sb(f"{tag}_xb1", [128, nG, Tc], BF16)
        lgt = c1.sb(f"{tag}_lgt", [128, 16], F32)
        prb = c1.sb(f"{tag}_prb", [128, 16], F32)
        rsc = c1.sb(f"{tag}_rsc", [128, 64], F32)
        rs1 = c1.sb(f"{tag}_rs1", [128, 6], F32)
        m_tok = c1.sb(f"{tag}_mtk", [128, 16], F32)
        m_fm = c1.sb(f"{tag}_mfm", [16, 2, 128], F32)
        ps_big = c1.ps(f"{tag}_pbig", [128, 256])
        ent = dict(ld=s_ld.v, dve=s_dve.v, act=s_act.v, mm=s_mm.v, st=s_st.v)
        tok_w = s_gp.reserve(2, 16)[-1]

        @block.gpsimd
        def _(gpsimd):
            for s, v in ((s_ld, ent["ld"]), (s_st, ent["st"]), (s_dve, ent["dve"]),
                         (s_act, ent["act"]), (s_mm, ent["mm"])):
                gpsimd.wait_ge(s.h, v)
            s_gp.tag(gpsimd.dma_start(out=rw[:, :, :], in_=P["rw"].rearrange("(g p) m -> p g m", p=128)), 16)
            s_gp.tag(gpsimd.dma_start(out=identf[:, :], in_=P["identf"]), 16)

        prev = None
        for ci in range(NCH):
            col = slice(ci * Tc, (ci + 1) * Tc)
            tok_x = s_ld.reserve(1, 16)[0]

            @block.sync
            def _(sync, col=col, prev=prev):
                if prev is None:
                    sync.wait_ge(s_gp.h, tok_w)
                else:
                    sync.wait_ge(s_act.h, prev["cast"])
                    sync.wait_ge(s_mm.h, prev["lgm"][1])
                s_ld.tag(sync.dma_start(out=x_t[:, :, :],
                                        in_=scr["xlnf"][:, col].rearrange("(g p) t -> p g t", p=128)), 16)

            cur = dict(lgm=[None, None], lgev=[None, None], rt=[None] * 24,
                       rtact=[None, None], mtr=[None, None], mev=[None, None])
            cur["cast"] = s_act.reserve(1)[0]

            @block.scalar
            def _(scalar, tok_x=tok_x, prev=prev):
                scalar.wait_ge(s_ld.h, tok_x)
                if prev is not None:
                    scalar.wait_ge(s_st.h, prev["xst"])
                s_act.tag(scalar.copy(xbf_t[:, :, :], x_t[:, :, :]))

            cur["xst"] = s_st.reserve(1, 16)[0]

            @block.sync
            def _(sync, cur=cur, col=col):
                sync.wait_ge(s_act.h, cur["cast"])
                s_st.tag(sync.dma_start(out=scr["xln"][:, col].rearrange("(g p) t -> p g t", p=128),
                                        in_=xbf_t[:, :, :]), 16)

            for sub in range(2):
                tsl = slice(sub * 128, (sub + 1) * 128)
                cur["lgm"][sub] = s_mm.reserve(1)[0]
                cur["lgev"][sub] = s_dve.reserve(1)[0]
                cur["rt"][12 * sub:12 * sub + 12] = s_dve.reserve(12)
                cur["rtact"][sub] = s_act.reserve(1)[0]
                cur["mtr"][sub] = s_mm.reserve(1)[0]
                cur["mev"][sub] = s_dve.reserve(1)[0]

                @block.tensor
                def _(tensor, sub=sub, tsl=tsl, cur=cur, prev=prev, tok_x=tok_x):
                    tensor.wait_ge(s_ld.h, tok_x)
                    if sub == 1:
                        tensor.wait_ge(s_dve.h, cur["mev"][0])
                    elif prev is not None:
                        tensor.wait_ge(s_dve.h, prev["mev"][1])
                    for k in range(nG):
                        ins = tensor.matmul(ps_big[:, 0:16], x_t[:, k, tsl], rw[:, k, :],
                                            start=(k == 0), stop=(k == nG - 1))
                    s_mm.tag(ins)

                @block.vector
                def _(vector, sub=sub, cur=cur):
                    vector.wait_ge(s_mm.h, cur["lgm"][sub])
                    s_dve.tag(vector.tensor_copy(lgt[:, :], ps_big[:, 0:16]))

                @block.vector
                def _(vector, sub=sub, cur=cur):
                    tk = cur["rt"][12 * sub:12 * sub + 12]
                    vector.wait_ge(s_dve.h, cur["lgev"][sub])
                    s_dve.tag(vector.tensor_reduce(rs1[:, 0:1], lgt[:, :], mybir.AxisListType.X, ALU.max))
                    vector.wait_ge(s_dve.h, tk[0])
                    s_dve.tag(vector.tensor_scalar(rs1[:, 1:2], rs1[:, 0:1], -1.0, 1.0, ALU.mult, ALU.mult))

                @block.scalar
                def _(scalar, sub=sub, cur=cur):
                    scalar.wait_ge(s_dve.h, cur["rt"][12 * sub + 1])
                    s_act.tag(scalar.activation(prb[:, :], lgt[:, :], AF.Exp, bias=rs1[:, 1:2]))

                @block.vector
                def _(vector, sub=sub, cur=cur):
                    tk = cur["rt"][12 * sub:12 * sub + 12]
                    vector.wait_ge(s_act.h, cur["rtact"][sub])
                    s_dve.tag(vector.tensor_reduce(rs1[:, 2:3], prb[:, :], mybir.AxisListType.X, ALU.add))
                    vector.wait_ge(s_dve.h, tk[2])
                    s_dve.tag(vector.reciprocal(rs1[:, 3:4], rs1[:, 2:3]))
                    vector.wait_ge(s_dve.h, tk[3])
                    s_dve.tag(vector.tensor_scalar(prb[:, :], prb[:, :], rs1[:, 3:4], 1.0, ALU.mult, ALU.mult))
                    vector.wait_ge(s_dve.h, tk[4])
                    s_dve.tag(vector.tensor_reduce(rs1[:, 4:5], prb[:, :], mybir.AxisListType.X, ALU.max))
                    vector.wait_ge(s_dve.h, tk[5])
                    s_dve.tag(vector.tensor_scalar(rsc[:, 0:16], prb[:, :], rs1[:, 4:5], 1.0, ALU.is_ge, ALU.mult))
                    vector.wait_ge(s_dve.h, tk[6])
                    s_dve.tag(vector.scalar_tensor_tensor(rsc[:, 16:32], rsc[:, 0:16], -BIG, prb[:, :], ALU.mult, ALU.add))
                    vector.wait_ge(s_dve.h, tk[7])
                    s_dve.tag(vector.tensor_reduce(rs1[:, 5:6], rsc[:, 16:32], mybir.AxisListType.X, ALU.max))
                    vector.wait_ge(s_dve.h, tk[8])
                    s_dve.tag(vector.tensor_scalar(rsc[:, 32:48], rsc[:, 16:32], rs1[:, 5:6], 1.0, ALU.is_ge, ALU.mult))
                    vector.wait_ge(s_dve.h, tk[9])
                    s_dve.tag(vector.tensor_tensor(rsc[:, 48:64], rsc[:, 0:16], rsc[:, 32:48], ALU.add))
                    vector.wait_ge(s_dve.h, tk[10])
                    s_dve.tag(vector.tensor_tensor(m_tok[:, :], prb[:, :], rsc[:, 48:64], ALU.mult))

                @block.tensor
                def _(tensor, sub=sub, cur=cur, prev=prev):
                    tensor.wait_ge(s_dve.h, cur["rt"][12 * sub + 11])
                    if sub == 0 and prev is not None:
                        tensor.wait_ge(s_st.h, prev["mst"])
                    s_mm.tag(tensor.transpose(ps_big[0:16, 128:256], m_tok[:, :], identf[:, :]))

                @block.vector
                def _(vector, sub=sub, cur=cur):
                    vector.wait_ge(s_mm.h, cur["mtr"][sub])
                    s_dve.tag(vector.tensor_copy(m_fm[:, sub, :], ps_big[0:16, 128:256]))

            cur["mst"] = s_st.reserve(1, 16)[0]

            @block.sync
            def _(sync, cur=cur, col=col):
                sync.wait_ge(s_dve.h, cur["mev"][1])
                s_st.tag(sync.dma_start(out=scr["m"][:, col],
                                        in_=m_fm[:, :, :].rearrange("e s t -> e (s t)")), 16)
            prev = cur

    with c.scope() as c2:
        Tc = 512
        NCH = T // Tc
        NE = 16
        gw = [c2.sb(f"{tag}_gw{i}", [128, 8, 1024], BF16) for i in range(2)]
        uw = [c2.sb(f"{tag}_uw{i}", [128, 8, 1024], BF16) for i in range(2)]
        dw = [c2.sb(f"{tag}_dw{i}", [128, 8, 1024], BF16) for i in range(2)]
        xln_t = [c2.sb(f"{tag}_xl2{i}", [128, nG, Tc], BF16) for i in range(2)]
        mrow = [c2.sb(f"{tag}_mr{i}", [1, Tc], F32) for i in range(2)]
        ones_row = c2.sb(f"{tag}_onr", [1, 128], F32)
        sg_bf = [c2.sb(f"{tag}_sg{i}", [128, Tc], BF16) for i in range(2)]
        h1 = [c2.sb(f"{tag}_h1{i}", [128, Tc], F32) for i in range(2)]
        h_bf = c2.sb(f"{tag}_h", [128, 8, Tc], BF16)
        o_t = c2.sb(f"{tag}_o", [128, 8, Tc], F32)
        ps_a = [c2.ps(f"{tag}_pa{i}", [128, Tc]) for i in range(2)]
        ps_b = [c2.ps(f"{tag}_pb{i}", [128, Tc]) for i in range(2)]
        ps_m = c2.ps(f"{tag}_pm", [128, Tc])
        ent = dict(ld=s_ld.v, dve=s_dve.v, act=s_act.v, mm=s_mm.v, st=s_st.v, gp=s_gp.v)
        tok_one = s_dve.reserve(1)[0]

        @block.vector
        def _(vector):
            for s, v in ((s_st, ent["st"]), (s_act, ent["act"]),
                         (s_mm, ent["mm"]), (s_gp, ent["gp"])):
                vector.wait_ge(s.h, v)
            s_dve.tag(vector.memset(ones_row[:, :], 1.0))

        tok_wl = [None] * NE
        down_last = [None] * NE
        acc_tok = [[None] * NCH for _ in range(NE)]
        prev_dve_last = tok_one
        prev_acc = None
        prev_dn_last = None
        gu_last = {}
        tok_xc = {}

        def emit_wload(e):
            tok_wl[e] = s_gp.reserve(3, 16)[-1]
            pe = e % 2

            @block.gpsimd
            def _(gpsimd, e=e, pe=pe):
                if e < 2:
                    for s, v in ((s_ld, ent["ld"]), (s_st, ent["st"]), (s_dve, ent["dve"]),
                                 (s_act, ent["act"]), (s_mm, ent["mm"])):
                        gpsimd.wait_ge(s.h, v)
                else:
                    gpsimd.wait_ge(s_mm.h, down_last[e - 2])
                s_gp.tag(gpsimd.dma_start(out=gw[pe][:, :, :],
                                          in_=P["gwa"][ts(e, 1024), :].rearrange("(g p) m -> p g m", p=128)), 16)
                s_gp.tag(gpsimd.dma_start(out=uw[pe][:, :, :],
                                          in_=P["uwa"][ts(e, 1024), :].rearrange("(g p) m -> p g m", p=128)), 16)
                s_gp.tag(gpsimd.dma_start(out=dw[pe][:, :, :],
                                          in_=P["dwa"][ts(e, 1024), :].rearrange("(g p) m -> p g m", p=128)), 16)

        emit_wload(0)
        emit_wload(1)

        gidx = 0
        for e in range(NE):
            pe = e % 2
            for ci in range(NCH):
                col = slice(ci * Tc, (ci + 1) * Tc)
                pxc = gidx % 2
                tok_xc[(e, ci)] = s_ld.reserve(2, 16)[-1]

                @block.sync
                def _(sync, e=e, ci=ci, col=col, pxc=pxc, gidx=gidx):
                    if gidx < 2:
                        for s, v in ((s_st, ent["st"]), (s_dve, ent["dve"]), (s_act, ent["act"]),
                                     (s_mm, ent["mm"]), (s_gp, ent["gp"])):
                            sync.wait_ge(s.h, v)
                    else:
                        g2 = (e * NCH + ci - 2)
                        sync.wait_ge(s_mm.h, gu_last[(g2 // NCH, g2 % NCH)])
                    s_ld.tag(sync.dma_start(out=xln_t[pxc][:, :, :],
                                            in_=scr["xln"][:, col].rearrange("(g p) t -> p g t", p=128)), 16)
                    s_ld.tag(sync.dma_start(out=mrow[pxc][:, :], in_=scr["m"][e:e + 1, col]), 16)

                t_mrm = s_mm.reserve(1)[0]
                t_gu = s_mm.reserve(16)
                t_sil = s_act.reserve(8)
                t_h = s_dve.reserve(16)
                t_dn = s_mm.reserve(8)
                t_oev = s_dve.reserve(8)
                gu_last[(e, ci)] = t_gu[-1]
                down_last[e] = t_dn[-1]

                @block.tensor
                def _(tensor, e=e, pe=pe, ci=ci, pxc=pxc, t_gu=t_gu, t_sil=t_sil,
                      t_h=t_h, t_dn=t_dn, pdl=prev_dve_last):
                    tensor.wait_ge(s_dve.h, pdl)
                    tensor.wait_ge(s_ld.h, tok_xc[(e, ci)])
                    ins = tensor.matmul(ps_m[:, :], ones_row[0:1, :], mrow[pxc][0:1, :],
                                        start=True, stop=True)
                    s_mm.tag(ins)
                    if ci == 0:
                        tensor.wait_ge(s_gp.h, tok_wl[e])
                    for m in range(8):
                        par = m % 2
                        if m >= 2:
                            tensor.wait_ge(s_act.h, t_sil[m - 2])
                        for k in range(nG):
                            ins = tensor.matmul(ps_a[par][:, :], gw[pe][:, k, ts(m)],
                                                xln_t[pxc][:, k, :], start=(k == 0), stop=(k == nG - 1))
                        s_mm.tag(ins)
                        if m >= 2:
                            tensor.wait_ge(s_dve.h, t_h[2 * (m - 2)])
                        for k in range(nG):
                            ins = tensor.matmul(ps_b[par][:, :], uw[pe][:, k, ts(m)],
                                                xln_t[pxc][:, k, :], start=(k == 0), stop=(k == nG - 1))
                        s_mm.tag(ins)
                    tensor.wait_ge(s_dve.h, t_h[15])
                    for mo in range(8):
                        par = mo % 2
                        if mo >= 2:
                            tensor.wait_ge(s_dve.h, t_oev[mo - 2])
                        for k in range(nG):
                            ins = tensor.matmul(ps_a[par][:, :], dw[pe][:, k, ts(mo)],
                                                h_bf[:, k, :], start=(k == 0), stop=(k == nG - 1))
                        s_mm.tag(ins)

                @block.scalar
                def _(scalar, t_gu=t_gu, t_sil=t_sil, t_h=t_h, pdl=prev_dve_last):
                    scalar.wait_ge(s_dve.h, pdl)
                    for m in range(8):
                        par = m % 2
                        scalar.wait_ge(s_mm.h, t_gu[2 * m])
                        if m >= 2:
                            scalar.wait_ge(s_dve.h, t_h[2 * (m - 2)])
                        s_act.tag(scalar.activation(sg_bf[par][:, :], ps_a[par][:, :], AF.Silu))

                @block.vector
                def _(vector, t_gu=t_gu, t_sil=t_sil, t_h=t_h, t_dn=t_dn, t_oev=t_oev,
                      t_mrm=t_mrm, pdn=prev_dn_last, pacc=prev_acc):
                    for m in range(8):
                        par = m % 2
                        vector.wait_ge(s_mm.h, t_gu[2 * m + 1])
                        vector.wait_ge(s_act.h, t_sil[m])
                        if m == 0 and pdn is not None:
                            vector.wait_ge(s_mm.h, pdn)
                        s_dve.tag(vector.tensor_tensor(h1[par][:, :], sg_bf[par][:, :],
                                                       ps_b[par][:, :], ALU.mult))
                        vector.wait_ge(s_dve.h, t_h[2 * m])
                        if m == 0:
                            vector.wait_ge(s_mm.h, t_mrm)
                        s_dve.tag(vector.tensor_tensor(h_bf[:, m, :], h1[par][:, :],
                                                       ps_m[:, :], ALU.mult))
                    for mo in range(8):
                        par = mo % 2
                        vector.wait_ge(s_mm.h, t_dn[mo])
                        if mo == 0 and pacc is not None:
                            vector.wait_ge(s_gp.h, pacc)
                        s_dve.tag(vector.tensor_copy(o_t[:, mo, :], ps_a[par][:, :]))

                t_acc = s_gp.reserve(1, 16)[0]
                acc_tok[e][ci] = t_acc

                @block.gpsimd
                def _(gpsimd, e=e, ci=ci, col=col, t_oev=t_oev):
                    gpsimd.wait_ge(s_dve.h, t_oev[-1])
                    if e > 0:
                        gpsimd.wait_ge(s_gp.h, acc_tok[e - 1][ci])
                        s_gp.tag(gpsimd.dma_start(out=scr["acc"][:, col].rearrange("(g p) t -> p g t", p=128),
                                                  in_=o_t[:, :, :], accum_op=ALU.add), 16)
                    else:
                        s_gp.tag(gpsimd.dma_start(out=scr["acc"][:, col].rearrange("(g p) t -> p g t", p=128),
                                                  in_=o_t[:, :, :]), 16)

                prev_dve_last = t_oev[-1]
                prev_acc = t_acc
                prev_dn_last = t_dn[-1]
                gidx += 1
            if e + 2 < NE:
                emit_wload(e + 2)

    with c.scope() as c3:
        Tc = 512
        NCH = T // Tc
        a_t = c3.sb(f"{tag}_a3", [128, nG, Tc], F32)
        x_t = c3.sb(f"{tag}_x3", [128, nG, Tc], F32)
        xo_t = c3.sb(f"{tag}_xo3", [128, nG, Tc], F32)
        ent = dict(ld=s_ld.v, dve=s_dve.v, act=s_act.v, mm=s_mm.v, st=s_st.v, gp=s_gp.v)
        prev_res = None
        prev_st = None
        for ci in range(NCH):
            col = slice(ci * Tc, (ci + 1) * Tc)
            tok_x = s_ld.reserve(2, 16)[-1]

            @block.sync
            def _(sync, col=col, prev_res=prev_res):
                if prev_res is None:
                    for s, v in ((s_st, ent["st"]), (s_dve, ent["dve"]), (s_act, ent["act"]),
                                 (s_mm, ent["mm"]), (s_gp, ent["gp"])):
                        sync.wait_ge(s.h, v)
                else:
                    sync.wait_ge(s_dve.h, prev_res)
                s_ld.tag(sync.dma_start(out=a_t[:, :, :],
                                        in_=scr["acc"][:, col].rearrange("(g p) t -> p g t", p=128)), 16)
                s_ld.tag(sync.dma_start(out=x_t[:, :, :],
                                        in_=x_d[:, col].rearrange("(g p) t -> p g t", p=128)), 16)

            tok_res = s_dve.reserve(1)[0]

            @block.vector
            def _(vector, tok_x=tok_x, prev_st=prev_st):
                vector.wait_ge(s_ld.h, tok_x)
                if prev_st is not None:
                    vector.wait_ge(s_st.h, prev_st)
                s_dve.tag(vector.scalar_tensor_tensor(xo_t[:, :, :], a_t[:, :, :], 0.5,
                                                      x_t[:, :, :], ALU.mult, ALU.add))

            tok_st = s_st.reserve(1, 16)[0]

            @block.sync
            def _(sync, col=col, tok_res=tok_res):
                sync.wait_ge(s_dve.h, tok_res)
                s_st.tag(sync.dma_start(out=y_d[:, col].rearrange("(g p) t -> p g t", p=128),
                                        in_=xo_t[:, :, :]), 16)
            prev_res = tok_res
            prev_st = tok_st

    @block.sync
    def _(sync):
        sync.wait_ge(s_st.h, s_st.v)


def build_l1moe():
    nc = bass.Bass("TRN2", target_bir_lowering=False)
    x_d = nc.dram_tensor("x", [DM, T], F32, kind="ExternalInput").ap()
    y_d = nc.dram_tensor("y", [DM, T], F32, kind="ExternalOutput").ap()
    names = dict(rw=[1024, 16], identf=[128, 128], gwa=[16 * 1024, 1024],
                 uwa=[16 * 1024, 1024], dwa=[16 * 1024, 1024])
    P = {nm: nc.dram_tensor(nm, sh, F32, kind="ExternalInput").ap() for nm, sh in names.items()}
    scr = dict(
        xlnf=nc.dram_tensor("scr_xlnf", [DM, T], F32, kind="Internal").ap(),
        xln=nc.dram_tensor("scr_xln", [DM, T], BF16, kind="Internal").ap(),
        m=nc.dram_tensor("scr_m", [16, T], F32, kind="Internal").ap(),
        acc=nc.dram_tensor("scr_acc", [DM, T], F32, kind="Internal").ap(),
    )
    with ExitStack() as es:
        block = es.enter_context(nc.Block())
        c = Ctx(nc, es, block)
        emit_l1moe(c, x_d, y_d, P, scr, "moe")
        check_sems(c)
    return nc


def l1moe_inmaps(inp, xs):
    P = dict(rw=np.ascontiguousarray(np.asarray(inp["L1_ffn_router_w"], np.float32)),
             identf=np.eye(128, dtype=np.float32),
             gwa=np.ascontiguousarray(np.asarray(inp["L1_ffn_gate_w"], np.float32).reshape(16384, 1024)),
             uwa=np.ascontiguousarray(np.asarray(inp["L1_ffn_up_w"], np.float32).reshape(16384, 1024)),
             dwa=np.ascontiguousarray(np.asarray(inp["L1_ffn_down_w"], np.float32).reshape(16384, 1024)))
    return [dict(x=xs[cid], **P) for cid in range(NC8)]




def _snap(c):
    return {k: s.v for k, s in c.sems.items()}


def _joinall(eng, c, sn):
    for k, v in sn.items():
        eng.wait_ge(c.sems[k].h, v)


def emit_l3rms(c, x_d, y_d, st_d, tag, y32_d=None):
    block = c.block
    s_ld = c.sem("s_ld"); s_gp = c.sem("s_gp")
    s_act = c.sem("s_act"); s_dve = c.sem("s_dve"); s_mm = c.sem("s_mm")
    s_st = c.sem("s_st")
    Tc = 256
    NCH = T // Tc
    nG = 8
    with c.scope() as cf:
        sc_t = cf.sb(f"{tag}_sc", [128, 8], F32)
        ones_col = cf.sb(f"{tag}_onc", [128, 1], F32)
        ones_row = cf.sb(f"{tag}_onr", [1, 128], F32)
        x_t = cf.sb(f"{tag}_x", [128, nG, Tc], F32)
        xsq = cf.sb(f"{tag}_xsq", [128, nG, Tc], F32)
        t_f = cf.sb(f"{tag}_tf", [128, nG, Tc], F32)
        y_t = cf.sb(f"{tag}_y", [128, nG, Tc], BF16)
        y32_t = cf.sb(f"{tag}_y32", [128, nG, Tc], F32) if y32_d is not None else None
        stat = cf.sb(f"{tag}_st", [1, 4 * Tc], F32)
        ps_ssq = cf.ps(f"{tag}_pssq", [1, Tc])
        ps_rep = cf.ps(f"{tag}_prep", [128, Tc])
        sn = _snap(c)
        tok_gw = s_gp.reserve(1, 16)[0]

        @block.gpsimd
        def _(gpsimd):
            _joinall(gpsimd, c, sn)
            s_gp.tag(gpsimd.dma_start(out=sc_t[:, :], in_=st_d), 16)

        tok_cst = s_dve.reserve(1)[0]

        @block.vector
        def _(vector):
            vector.wait_ge(s_gp.h, tok_gw)
            vector.memset(ones_col[:, :], 1.0)
            s_dve.tag(vector.memset(ones_row[:, :], 1.0))

        prev = None
        for ci in range(NCH):
            col = slice(ci * Tc, (ci + 1) * Tc)
            tok_x = s_ld.reserve(1, 16)[0]

            @block.sync
            def _(sync, col=col, prev=prev):
                if prev is None:
                    sync.wait_ge(s_dve.h, tok_cst)
                else:
                    sync.wait_ge(s_dve.h, prev["xln"][-1])
                    sync.wait_ge(s_act.h, prev["sq"][-1])
                s_ld.tag(sync.dma_start(out=x_t[:, :, :],
                                        in_=x_d[:, col].rearrange("(g p) t -> p g t", p=128)), 16)

            cur = {}
            cur["sq"] = s_act.reserve(nG)
            cur["ssq"] = s_mm.reserve(1)[0]
            cur["ms"] = s_act.reserve(2)
            cur["rc"] = s_dve.reserve(2)
            cur["rep"] = s_mm.reserve(1)[0]
            cur["xln"] = s_dve.reserve(1 + nG + (nG if y32_d is not None else 0))

            @block.scalar
            def _(scalar, cur=cur, prev=prev, tok_x=tok_x):
                scalar.wait_ge(s_ld.h, tok_x)
                if prev is not None:
                    scalar.wait_ge(s_mm.h, prev["ssq"])
                    scalar.wait_ge(s_st.h, prev["stt"])
                for g in range(nG):
                    s_act.tag(scalar.square(xsq[:, g, :], x_t[:, g, :]))

            @block.tensor
            def _(tensor, cur=cur, prev=prev):
                if prev is not None:
                    tensor.wait_ge(s_act.h, prev["ms"][0])
                for g in range(nG):
                    tensor.wait_ge(s_act.h, cur["sq"][g])
                    ins = tensor.matmul(ps_ssq[0:1, :], ones_col[:, 0:1], xsq[:, g, :],
                                        start=(g == 0), stop=(g == nG - 1))
                s_mm.tag(ins)

            @block.scalar
            def _(scalar, cur=cur):
                scalar.wait_ge(s_mm.h, cur["ssq"])
                s_act.tag(scalar.mul(stat[0:1, 0:Tc], ps_ssq[0:1, :], 1.0 / DM))
                scalar.wait_ge(s_act.h, cur["ms"][0])
                s_act.tag(scalar.activation(stat[0:1, Tc:2 * Tc], stat[0:1, 0:Tc], AF.Sqrt))

            @block.vector
            def _(vector, cur=cur):
                vector.wait_ge(s_act.h, cur["ms"][1])
                s_dve.tag(vector.tensor_scalar(stat[0:1, 2 * Tc:3 * Tc], stat[0:1, Tc:2 * Tc],
                                               1e-6, 1.0, ALU.add, ALU.mult))
                vector.wait_ge(s_dve.h, cur["rc"][0])
                s_dve.tag(vector.reciprocal(stat[0:1, 3 * Tc:4 * Tc], stat[0:1, 2 * Tc:3 * Tc]))

            @block.tensor
            def _(tensor, cur=cur, prev=prev):
                tensor.wait_ge(s_dve.h, cur["rc"][1])
                if prev is not None:
                    tensor.wait_ge(s_dve.h, prev["xln"][-1])
                ins = tensor.matmul(ps_rep[:, :], ones_row[0:1, :], stat[0:1, 3 * Tc:4 * Tc],
                                    start=True, stop=True)
                s_mm.tag(ins)

            @block.vector
            def _(vector, cur=cur, prev=prev):
                vector.wait_ge(s_mm.h, cur["rep"])
                if prev is not None:
                    vector.wait_ge(s_st.h, prev["stt"])
                rrep_b = ps_rep[:, :].unsqueeze(1).broadcast_to([128, nG, Tc])
                s_dve.tag(vector.tensor_tensor(t_f[:, :, :], x_t[:, :, :], rrep_b, ALU.mult))
                vector.wait_ge(s_dve.h, cur["xln"][0])
                for g in range(nG):
                    s_dve.tag(vector.tensor_scalar(y_t[:, g, :], t_f[:, g, :],
                                                   sc_t[:, g:g + 1], 1.0, ALU.mult, ALU.mult))
                if y32_d is not None:
                    for g in range(nG):
                        s_dve.tag(vector.tensor_scalar(y32_t[:, g, :], t_f[:, g, :],
                                                       sc_t[:, g:g + 1], 1.0, ALU.mult, ALU.mult))

            cur["stt"] = s_st.reserve(1 if y32_d is None else 2, 16)[-1]

            @block.sync
            def _(sync, col=col, cur=cur):
                sync.wait_ge(s_dve.h, cur["xln"][-1])
                s_st.tag(sync.dma_start(out=y_d[:, col].rearrange("(g p) t -> p g t", p=128),
                                        in_=y_t[:, :, :]), 16)
                if y32_d is not None:
                    s_st.tag(sync.dma_start(out=y32_d[:, col].rearrange("(g p) t -> p g t", p=128),
                                            in_=y32_t[:, :, :]), 16)
            prev = cur

    @block.sync
    def _(sync):
        sync.wait_ge(s_st.h, s_st.v)


def emit_l3attn(c, x_d, y_d, P, scr, tag, upto=10):
    def _bail():
        @c.block.sync
        def _(sync):
            sync.wait_ge(c.sem("s_st").h, c.sem("s_st").v)

    block = c.block
    s_ld = c.sem("s_ld"); s_gp = c.sem("s_gp")
    s_act = c.sem("s_act"); s_dve = c.sem("s_dve"); s_mm = c.sem("s_mm")
    s_st = c.sem("s_st")
    nG = 8

    g1x = c.sb(f"{tag}_g1x", [16, T], BF16)

    emit_ln_full(c, x_d, scr["z"], f"{tag}ln")
    emit_l3rms(c, scr["z"], scr["r1"], P["s1_t"], f"{tag}r1", y32_d=scr["r1f"])
    if upto <= 1:
        return _bail()

    with c.scope() as c2:
        Tc = 512
        NCH = T // Tc
        wogT = c2.sb(f"{tag}_wog", [128, 8, 1024], BF16)
        wg1T = c2.sb(f"{tag}_wg1", [128, 8, 16], BF16)
        bog_t = c2.sb(f"{tag}_bog", [128, 8], F32)
        bg1c = c2.sb(f"{tag}_bg1", [16, 1], F32)
        r1_t = [c2.sb(f"{tag}_r1{i}", [128, nG, Tc], BF16) for i in range(2)]
        og_bf = c2.sb(f"{tag}_ogb", [128, nG, Tc], BF16)
        ps_ab = [c2.ps(f"{tag}_pab{i}", [128, Tc]) for i in range(2)]
        ps_g1 = c2.ps(f"{tag}_pg1", [16, Tc])
        sn = _snap(c)
        tok_w = s_gp.reserve(4, 16)[-1]

        @block.gpsimd
        def _(gpsimd):
            _joinall(gpsimd, c, sn)
            s_gp.tag(gpsimd.dma_start(out=wogT[:, :, :],
                                      in_=P["wogT"].rearrange("(g p) m -> p g m", p=128)), 16)
            s_gp.tag(gpsimd.dma_start(out=wg1T[:, :, :],
                                      in_=P["wg1T"].rearrange("(g p) m -> p g m", p=128)), 16)
            s_gp.tag(gpsimd.dma_start(out=bog_t[:, :], in_=P["bog_t"]), 16)
            s_gp.tag(gpsimd.dma_start(out=bg1c[:, :], in_=P["bg1c"]), 16)


        prev = None
        for ci in range(NCH):
            col = slice(ci * Tc, (ci + 1) * Tc)
            pxc = ci % 2
            tok_x = s_ld.reserve(1, 16)[0]

            @block.sync
            def _(sync, col=col, pxc=pxc, prev=prev, ci=ci):
                if prev is None:
                    sync.wait_ge(s_gp.h, tok_w)
                elif ci >= 2:
                    sync.wait_ge(s_mm.h, prev["pv"])
                s_ld.tag(sync.dma_start(out=r1_t[pxc][:, :, :],
                                        in_=scr["r1"][:, col].rearrange("(g p) t -> p g t", p=128)), 16)

            cur = {}
            cur["g1m"] = s_mm.reserve(1)[0]
            cur["ogm"] = s_mm.reserve(8)
            cur["pv"] = cur["ogm"][-1]
            cur["g1e"] = s_act.reserve(1)[0]
            cur["oge"] = s_act.reserve(8)

            @block.tensor
            def _(tensor, cur=cur, prev=prev, pxc=pxc, tok_x=tok_x):
                tensor.wait_ge(s_ld.h, tok_x)
                if prev is not None:
                    tensor.wait_ge(s_act.h, prev["g1e"])
                for k in range(nG):
                    ins = tensor.matmul(ps_g1[:, :], wg1T[:, k, :], r1_t[pxc][:, k, :],
                                        start=(k == 0), stop=(k == nG - 1))
                s_mm.tag(ins)
                for m in range(nG):
                    par = m % 2
                    if m >= 2:
                        tensor.wait_ge(s_act.h, cur["oge"][m - 2])
                    elif prev is not None:
                        tensor.wait_ge(s_act.h, prev["oge"][6 + m])
                    for k in range(nG):
                        ins = tensor.matmul(ps_ab[par][:, :], wogT[:, k, ts(m)],
                                            r1_t[pxc][:, k, :], start=(k == 0), stop=(k == nG - 1))
                    s_mm.tag(ins)

            @block.scalar
            def _(scalar, cur=cur, prev=prev, col=col):
                scalar.wait_ge(s_mm.h, cur["g1m"])
                s_act.tag(scalar.activation(g1x[0:16, col], ps_g1[:, :], AF.Silu,
                                            bias=bg1c[:, 0:1]))
                for m in range(nG):
                    par = m % 2
                    scalar.wait_ge(s_mm.h, cur["ogm"][m])
                    if prev is not None and m == 0:
                        scalar.wait_ge(s_st.h, prev["stt"])
                    s_act.tag(scalar.activation(og_bf[:, m, :], ps_ab[par][:, :], AF.Silu,
                                                bias=bog_t[:, m:m + 1]))

            cur["stt"] = s_st.reserve(1, 16)[0]

            @block.sync
            def _(sync, cur=cur, col=col):
                sync.wait_ge(s_act.h, cur["oge"][-1])
                s_st.tag(sync.dma_start(out=scr["ogs"][:, col].rearrange("(g p) t -> p g t", p=128),
                                        in_=og_bf[:, :, :]), 16)
            prev = cur

    if upto <= 2:
        return _bail()
    with c.scope() as c3:
        NT = T // 128
        wvT = c3.sb(f"{tag}_wv", [128, 8, 1024], BF16)
        wg2b = c3.sb(f"{tag}_wg2", [16, 1024], BF16)
        bg2r = c3.sb(f"{tag}_bg2", [1, 1024], BF16)
        ones_bf = c3.sb(f"{tag}_onb", [1, 128], BF16)
        r1t = [c3.sb(f"{tag}_r1t{i}", [128, nG, 128], BF16) for i in range(2)]
        obuf16 = c3.sb(f"{tag}_ob16", [128, 2, 1024], BF16)
        ps_ab = [c3.ps(f"{tag}_pab3{i}", [128, 512]) for i in range(2)]
        sn = _snap(c)
        tok_w = s_gp.reserve(3, 16)[-1]
        tok_onb = None

        @block.gpsimd
        def _(gpsimd):
            _joinall(gpsimd, c, sn)
            s_gp.tag(gpsimd.dma_start(out=wvT[:, :, :],
                                      in_=P["wvT"].rearrange("(g p) m -> p g m", p=128)), 16)
            s_gp.tag(gpsimd.dma_start(out=wg2b[:, :], in_=P["wg2b"]), 16)
            s_gp.tag(gpsimd.dma_start(out=bg2r[:, :], in_=P["bg2r"]), 16)

        tok_onb = s_dve.reserve(1)[0]

        @block.vector
        def _(vector):
            _joinall(vector, c, sn)
            s_dve.tag(vector.memset(ones_bf[:, :], 1.0))

        prev = None
        for i in range(NT):
            pxc = i % 2
            tsl = slice(i * 128, (i + 1) * 128)
            tok_x = s_ld.reserve(1, 16)[-1]

            @block.sync
            def _(sync, tsl=tsl, pxc=pxc, prev=prev, i=i):
                if prev is None:
                    sync.wait_ge(s_gp.h, tok_w)
                elif i >= 2:
                    sync.wait_ge(s_mm.h, prev["mm2"][-1])
                s_ld.tag(sync.dma_start(out=r1t[pxc][:, :, :],
                                        in_=scr["r1"][:, tsl].rearrange("(g p) t -> p g t", p=128)), 16)

            cur = {}
            cur["mm"] = s_mm.reserve(4)
            cur["mm2"] = cur["mm"]
            cur["ev"] = s_dve.reserve(2)
            cur["aev"] = s_act.reserve(2)

            @block.tensor
            def _(tensor, cur=cur, prev=prev, pxc=pxc, tok_x=tok_x, tsl=tsl):
                tensor.wait_ge(s_ld.h, tok_x)
                mi = 0
                for wt, which in ((wvT, 2),):
                    src_t = r1t[pxc]
                    for hf in range(2):
                        par = mi % 2
                        if prev is not None:
                            tensor.wait_ge(s_act.h, prev["aev"][mi]) if prev.get("aev") else None
                        for k in range(nG):
                            ins = tensor.matmul(ps_ab[par][:, :], src_t[:, k, :],
                                                wt[:, k, ts(hf, 512)],
                                                start=(k == 0), stop=(k == nG - 1))
                        s_mm.tag(ins)
                        mi += 1
                tensor.wait_ge(s_dve.h, tok_onb)
                for hf in range(2):
                    par = mi % 2
                    tensor.wait_ge(s_dve.h, cur["ev"][mi - 2])
                    tensor.matmul(ps_ab[par][:, :], g1x[:, tsl], wg2b[:, ts(hf, 512)],
                                  start=True, stop=False)
                    ins = tensor.matmul(ps_ab[par][:, :], ones_bf[0:1, :],
                                        bg2r[0:1, ts(hf, 512)], start=False, stop=True)
                    s_mm.tag(ins)
                    mi += 1

            @block.vector
            def _(vector, cur=cur, prev=prev):
                for j in range(2):
                    par = j % 2
                    vector.wait_ge(s_mm.h, cur["mm"][j])
                    if prev is not None:
                        vector.wait_ge(s_st.h, prev["stt"][-1])
                    s_dve.tag(vector.tensor_copy(obuf16[:, 0, ts(j, 512)], ps_ab[par][:, :]))

            @block.scalar
            def _(scalar, cur=cur):
                for hf in range(2):
                    par = hf
                    scalar.wait_ge(s_mm.h, cur["mm"][2 + hf])
                    s_act.tag(scalar.activation(obuf16[:, 1, ts(hf, 512)], ps_ab[par][:, :],
                                                AF.Sigmoid, scale=1.0 / 16.0))

            cur["stt"] = s_st.reserve(2, 16)

            @block.sync
            def _(sync, cur=cur, i=i):
                sync.wait_ge(s_dve.h, cur["ev"][-1])
                sync.wait_ge(s_act.h, cur["aev"][-1])
                for pl in range(2):
                    s_st.tag(sync.dma_start(out=scr["gin16"][pl * T + i * 128:pl * T + (i + 1) * 128, :],
                                            in_=obuf16[:, pl, :]), 16)
            prev = cur

    s_cc = c.sem("s_cc")
    tok_cc = s_cc.reserve(1, 1)[0]

    @block.gpsimd
    def _(gpsimd):
        gpsimd.wait_ge(s_st.h, s_st.v)
        ins = gpsimd.collective_compute("AllGather", ALU.bypass,
                                        replica_groups=[list(range(8))],
                                        ins=[scr["gin16"]], outs=[scr["gout16"]])
        s_cc.tag(ins, 1)

    with c.scope() as ck:
        NT = T // 128
        wkT = ck.sb(f"{tag}_wkk", [128, 8, 1024], F32)
        r1ft = [ck.sb(f"{tag}_r1fk{i}", [128, 8, 128], F32) for i in range(2)]
        kobuf = ck.sb(f"{tag}_kob", [128, 1024], F32)
        ps_ab = [ck.ps(f"{tag}_pabk{i}", [128, 512]) for i in range(2)]
        sn = _snap(c)
        tok_w = s_gp.reserve(1, 16)[0]

        @block.gpsimd
        def _(gpsimd):
            _joinall(gpsimd, c, sn)
            s_gp.tag(gpsimd.dma_start(out=wkT[:, :, :],
                                      in_=P["wkT"].rearrange("(g p) m -> p g m", p=128)), 16)

        prev = None
        for i in range(NT):
            pxc = i % 2
            tsl = slice(i * 128, (i + 1) * 128)
            tok_x = s_ld.reserve(1, 16)[0]

            @block.sync
            def _(sync, tsl=tsl, pxc=pxc, prev=prev, i=i):
                if prev is None:
                    sync.wait_ge(s_gp.h, tok_w)
                elif i >= 2:
                    sync.wait_ge(s_mm.h, prev["mm"][-1])
                s_ld.tag(sync.dma_start(out=r1ft[pxc][:, :, :],
                                        in_=scr["r1f"][:, tsl].rearrange("(g p) t -> p g t", p=128)), 16)

            cur = {}
            cur["mm"] = s_mm.reserve(2)
            cur["ev"] = s_dve.reserve(2)

            @block.tensor
            def _(tensor, cur=cur, prev=prev, pxc=pxc, tok_x=tok_x):
                tensor.wait_ge(s_ld.h, tok_x)
                for hf in range(2):
                    par = hf
                    if prev is not None:
                        tensor.wait_ge(s_dve.h, prev["ev"][hf])
                    for k in range(nG):
                        ins = tensor.matmul(ps_ab[par][:, :], r1ft[pxc][:, k, :],
                                            wkT[:, k, ts(hf, 512)],
                                            start=(k == 0), stop=(k == nG - 1))
                    s_mm.tag(ins)

            @block.vector
            def _(vector, cur=cur, prev=prev):
                for hf in range(2):
                    vector.wait_ge(s_mm.h, cur["mm"][hf])
                    if prev is not None:
                        vector.wait_ge(s_st.h, prev["stt"])
                    s_dve.tag(vector.tensor_copy(kobuf[:, ts(hf, 512)], ps_ab[hf][:, :]))

            cur["stt"] = s_st.reserve(1, 16)[0]

            @block.sync
            def _(sync, cur=cur, i=i):
                sync.wait_ge(s_dve.h, cur["ev"][-1])
                s_st.tag(sync.dma_start(out=scr["gin32"][i * 128:(i + 1) * 128, :],
                                        in_=kobuf[:, :]), 16)
            prev = cur

    tok_cc1 = s_cc.reserve(1, 1)[0]

    @block.gpsimd
    def _(gpsimd):
        gpsimd.wait_ge(s_st.h, s_st.v)
        ins = gpsimd.collective_compute("AllGather", ALU.bypass,
                                        replica_groups=[list(range(8))],
                                        ins=[scr["gin32"]], outs=[scr["gout32"]])
        s_cc.tag(ins, 1)

    with c.scope() as cq:
        NT = T // 128
        wqT = cq.sb(f"{tag}_wqq", [128, 8, 1024], F32)
        r1ft = [cq.sb(f"{tag}_r1fq{i}", [128, 8, 128], F32) for i in range(2)]
        qobuf = cq.sb(f"{tag}_qob", [128, 1024], F32)
        ps_ab = [cq.ps(f"{tag}_pabq{i}", [128, 512]) for i in range(2)]
        sn = _snap(c)
        tok_w = s_gp.reserve(1, 16)[0]

        @block.gpsimd
        def _(gpsimd):
            _joinall(gpsimd, c, sn)
            s_gp.tag(gpsimd.dma_start(out=wqT[:, :, :],
                                      in_=P["wqT"].rearrange("(g p) m -> p g m", p=128)), 16)

        prev = None
        for i in range(NT):
            pxc = i % 2
            tsl = slice(i * 128, (i + 1) * 128)
            tok_x = s_ld.reserve(1, 16)[0]

            @block.sync
            def _(sync, tsl=tsl, pxc=pxc, prev=prev, i=i):
                if prev is None:
                    sync.wait_ge(s_gp.h, tok_w)
                elif i >= 2:
                    sync.wait_ge(s_mm.h, prev["mm"][-1])
                s_ld.tag(sync.dma_start(out=r1ft[pxc][:, :, :],
                                        in_=scr["r1f"][:, tsl].rearrange("(g p) t -> p g t", p=128)), 16)

            cur = {}
            cur["mm"] = s_mm.reserve(2)
            cur["ev"] = s_dve.reserve(2)

            @block.tensor
            def _(tensor, cur=cur, prev=prev, pxc=pxc, tok_x=tok_x):
                tensor.wait_ge(s_ld.h, tok_x)
                for hf in range(2):
                    par = hf
                    if prev is not None:
                        tensor.wait_ge(s_dve.h, prev["ev"][hf])
                    for k in range(nG):
                        ins = tensor.matmul(ps_ab[par][:, :], r1ft[pxc][:, k, :],
                                            wqT[:, k, ts(hf, 512)],
                                            start=(k == 0), stop=(k == nG - 1))
                    s_mm.tag(ins)

            @block.vector
            def _(vector, cur=cur, prev=prev):
                for hf in range(2):
                    vector.wait_ge(s_mm.h, cur["mm"][hf])
                    if prev is not None:
                        vector.wait_ge(s_st.h, prev["stt"])
                    s_dve.tag(vector.tensor_copy(qobuf[:, ts(hf, 512)], ps_ab[hf][:, :]))

            cur["stt"] = s_st.reserve(1, 16)[0]

            @block.sync
            def _(sync, cur=cur, i=i):
                sync.wait_ge(s_dve.h, cur["ev"][-1])
                s_st.tag(sync.dma_start(out=scr["qloc"][i * 128:(i + 1) * 128, :],
                                        in_=qobuf[:, :]), 16)
            prev = cur

    if upto <= 3:
        return _bail()
    if upto <= 4:
        return _bail()
    with c.scope() as c5:
        NT = T // 128
        tmask = c5.sb(f"{tag}_tm", [128, 8], F32)
        imask = c5.sb(f"{tag}_im", [128, 8], F32)
        eps128 = c5.sb(f"{tag}_eps", [128, 1], F32)
        q8 = [c5.sb(f"{tag}_q{i}", [128, 1024], F32) for i in range(2)]
        k8 = [c5.sb(f"{tag}_k{i}", [128, 8, 1024], F32) for i in range(2)]
        v8 = [c5.sb(f"{tag}_v{i}", [128, 8, 1024], BF16) for i in range(2)]
        a8 = [c5.sb(f"{tag}_a{i}", [128, 8, 1024], BF16) for i in range(2)]
        qdec = c5.sb(f"{tag}_qd", [128, 1024], F32)
        atld = c5.sb(f"{tag}_at", [128, 1024], F32)
        tmp = c5.sb(f"{tag}_tp", [128, 1024], F32)
        tmp2 = c5.sb(f"{tag}_t2", [128, 1024], F32)
        y_t = c5.sb(f"{tag}_y", [128, 1024], F32)
        d_t = c5.sb(f"{tag}_d", [128, 1024], F32)
        oln_bf = [c5.sb(f"{tag}_ol{i}", [128, 1024], BF16) for i in range(2)]
        stt_s = c5.sb(f"{tag}_ss", [128, 16, 4], F32)
        sd_s = c5.sb(f"{tag}_sd", [128, 16], F32)
        sn = _snap(c)
        tok_msk = s_gp.reserve(2, 16)[-1]

        @block.gpsimd
        def _(gpsimd):
            _joinall(gpsimd, c, sn)
            s_gp.tag(gpsimd.dma_start(out=tmask[:, :], in_=P["tmask"]), 16)
            s_gp.tag(gpsimd.dma_start(out=imask[:, :], in_=P["imask"]), 16)

        tok_eps = s_dve.reserve(1)[0]

        @block.vector
        def _(vector):
            _joinall(vector, c, sn)
            s_dve.tag(vector.memset(eps128[:, :], 1e-5))

        prev = None
        for i in range(NT):
            pxc = i % 2
            tok_ld = s_ld.reserve(25, 16)[-1]

            @block.sync
            def _(sync, i=i, pxc=pxc, prev=prev):
                if prev is None:
                    sync.wait_ge(s_gp.h, s_gp.v)
                    sync.wait_ge(s_cc.h, tok_cc)
                elif i >= 2:
                    sync.wait_ge(s_dve.h, prev["free"])
                s_ld.tag(sync.dma_start(out=q8[pxc][:, :],
                                        in_=scr["qloc"][i * 128:(i + 1) * 128, :]), 16)
                for tau in range(8):
                    b32 = tau * T
                    b16 = tau * 2 * T
                    s_ld.tag(sync.dma_start(out=k8[pxc][:, tau, :],
                                            in_=scr["gout32"][b32 + i * 128:b32 + (i + 1) * 128, :]), 16)
                    s_ld.tag(sync.dma_start(out=v8[pxc][:, tau, :],
                                            in_=scr["gout16"][b16 + i * 128:b16 + (i + 1) * 128, :]), 16)
                    s_ld.tag(sync.dma_start(out=a8[pxc][:, tau, :],
                                            in_=scr["gout16"][b16 + T + i * 128:b16 + T + (i + 1) * 128, :]), 16)

            NOPS = 1 + 7 * 7 + 5 + 8
            tks = s_dve.reserve(NOPS)
            tok_sqr = s_act.reserve(1)[0]
            cur = {"free": None}

            @block.vector
            def _(vector, i=i, pxc=pxc, prev=prev, tks=tks, tok_ld=tok_ld, cur=cur):
                ji = [0]

                def step(w_extra=()):
                    j = ji[0]
                    if j > 0:
                        vector.wait_ge(s_dve.h, tks[j - 1])
                    for sem, v in w_extra:
                        vector.wait_ge(sem.h, v)
                    ji[0] += 1
                    return tks[j]

                w0 = [(s_ld, tok_ld), (s_dve, tok_eps)]
                if prev is not None:
                    w0.append((s_st, prev["stt"]))
                step(w0)
                vector.tensor_copy(qdec[:, :], q8[pxc][:, :]).then_inc(s_dve.h, 1)
                s_dve.emitted += 1
                for tau in range(7, -1, -1):
                    step()
                    vector.tensor_tensor(tmp[:, :], qdec[:, :], k8[pxc][:, tau, :],
                                         ALU.mult).then_inc(s_dve.h, 1)
                    s_dve.emitted += 1
                    step()
                    vector.tensor_reduce(stt_s[:, :, 0:1], tmp[:, :].rearrange("p (h k) -> p h k", k=64),
                                         mybir.AxisListType.X, ALU.add).then_inc(s_dve.h, 1)
                    s_dve.emitted += 1
                    step()
                    vector.tensor_scalar(stt_s[:, :, 1:2], stt_s[:, :, 0:1], tmask[:, tau:tau + 1],
                                         1.0, ALU.mult, ALU.mult).then_inc(s_dve.h, 1)
                    s_dve.emitted += 1
                    step()
                    pbc = stt_s[:, :, 1:2].broadcast_to([128, 16, 64])
                    dst = y_t if tau == 7 else tmp2
                    vector.tensor_tensor(dst[:, :].rearrange("p (h k) -> p h k", k=64),
                                         v8[pxc][:, tau, :].rearrange("p (h k) -> p h k", k=64),
                                         pbc, ALU.mult).then_inc(s_dve.h, 1)
                    s_dve.emitted += 1
                    if tau != 7:
                        step()
                        vector.tensor_tensor(y_t[:, :], y_t[:, :], tmp2[:, :],
                                             ALU.add).then_inc(s_dve.h, 1)
                        s_dve.emitted += 1
                    else:
                        step()
                        vector.tensor_copy(tmp2[:, :], y_t[:, :]).then_inc(s_dve.h, 1)
                        s_dve.emitted += 1
                    if tau > 0:
                        step()
                        vector.tensor_scalar(atld[:, :], a8[pxc][:, tau, :], tmask[:, tau:tau + 1],
                                             imask[:, tau:tau + 1], ALU.mult, ALU.add).then_inc(s_dve.h, 1)
                        s_dve.emitted += 1
                        step()
                        vector.tensor_tensor(qdec[:, :], qdec[:, :], atld[:, :],
                                             ALU.mult).then_inc(s_dve.h, 1)
                        s_dve.emitted += 1
                step()
                vector.tensor_reduce(stt_s[:, :, 2:3], y_t[:, :].rearrange("p (h k) -> p h k", k=64),
                                     mybir.AxisListType.X, ALU.add).then_inc(s_dve.h, 1)
                s_dve.emitted += 1
                step()
                vector.tensor_scalar(stt_s[:, :, 2:3], stt_s[:, :, 2:3], 1.0 / 64.0, 1.0,
                                     ALU.mult, ALU.mult).then_inc(s_dve.h, 1)
                s_dve.emitted += 1
                step()
                mbc = stt_s[:, :, 2:3].broadcast_to([128, 16, 64])
                vector.tensor_tensor(d_t[:, :].rearrange("p (h k) -> p h k", k=64),
                                     y_t[:, :].rearrange("p (h k) -> p h k", k=64),
                                     mbc, ALU.subtract).then_inc(s_dve.h, 1)
                s_dve.emitted += 1
                step()
                vector.tensor_tensor(tmp[:, :], d_t[:, :], d_t[:, :], ALU.mult).then_inc(s_dve.h, 1)
                s_dve.emitted += 1
                step()
                vector.tensor_reduce(stt_s[:, :, 3:4], tmp[:, :].rearrange("p (h k) -> p h k", k=64),
                                     mybir.AxisListType.X, ALU.add).then_inc(s_dve.h, 1)
                s_dve.emitted += 1
                step()
                vector.tensor_scalar(stt_s[:, :, 3:4], stt_s[:, :, 3:4], 1.0 / 64.0, 1.0,
                                     ALU.mult, ALU.mult).then_inc(s_dve.h, 1)
                s_dve.emitted += 1
                step([(s_act, tok_sqr)])
                vector.reciprocal(stt_s[:, :, 2:3], stt_s[:, :, 3:4]).then_inc(s_dve.h, 1)
                s_dve.emitted += 1
                step()
                rbc = stt_s[:, :, 2:3].broadcast_to([128, 16, 64])
                vector.tensor_tensor(oln_bf[pxc][:, :].rearrange("p (h k) -> p h k", k=64),
                                     d_t[:, :].rearrange("p (h k) -> p h k", k=64),
                                     rbc, ALU.mult).then_inc(s_dve.h, 1)
                s_dve.emitted += 1
                cur["free"] = tks[ji[0] - 1]

            @block.scalar
            def _(scalar, tks=tks, tok_sqr=tok_sqr):
                scalar.wait_ge(s_dve.h, tks[NOPS - 3])
                s_act.tag(scalar.activation(stt_s[:, :, 3:4], stt_s[:, :, 3:4], AF.Sqrt,
                                            bias=eps128[:, 0:1]))

            cur["stt"] = s_st.reserve(1, 16)[0]

            @block.sync
            def _(sync, i=i, pxc=pxc, cur=cur, tks=tks):
                sync.wait_ge(s_dve.h, tks[-1])
                s_st.tag(sync.dma_start(out=scr["oln"][i * 128:(i + 1) * 128, :],
                                        in_=oln_bf[pxc][:, :]), 16)
            prev = cur

    if upto <= 5:
        return _bail()
    with c.scope() as c6:
        Tc = 512
        NCH = T // Tc
        woT = c6.sb(f"{tag}_wo", [128, 8, 1024], BF16)
        o_fm = c6.sb(f"{tag}_ofm", [128, nG, Tc], BF16)
        ogs_t = c6.sb(f"{tag}_ogt", [128, nG, Tc], BF16)
        z_t = c6.sb(f"{tag}_zt", [128, nG, Tc], F32)
        og_bf = c6.sb(f"{tag}_ogg", [128, nG, Tc], BF16)
        x1_t = c6.sb(f"{tag}_x1", [128, nG, Tc], F32)
        ps_ab = [c6.ps(f"{tag}_pab6{i}", [128, Tc]) for i in range(2)]
        sn = _snap(c)
        tok_w = s_gp.reserve(1, 16)[0]

        @block.gpsimd
        def _(gpsimd):
            _joinall(gpsimd, c, sn)
            s_gp.tag(gpsimd.dma_start(out=woT[:, :, :],
                                      in_=P["woT"].rearrange("(g p) m -> p g m", p=128)), 16)

        prev = None
        for ci in range(NCH):
            col = slice(ci * Tc, (ci + 1) * Tc)
            t0 = ci * Tc
            tok_tr = s_act.reserve(32, 16)

            @block.scalar
            def _(scalar, ci=ci, t0=t0, prev=prev):
                if prev is None:
                    scalar.wait_ge(s_gp.h, tok_w)
                else:
                    scalar.wait_ge(s_dve.h, prev["gg"])
                for g in range(nG):
                    for j in range(4):
                        s_act.tag(scalar.dma_start_transpose(
                            o_fm[:, g, ts(j)],
                            scr["oln"][t0 + j * 128:t0 + (j + 1) * 128, ts(g)]), 16)

            tok_x = s_ld.reserve(2, 16)[-1]

            @block.sync
            def _(sync, col=col, prev=prev):
                if prev is not None:
                    sync.wait_ge(s_dve.h, prev["gg"])
                    sync.wait_ge(s_dve.h, prev["res"][-1])
                s_ld.tag(sync.dma_start(out=ogs_t[:, :, :],
                                        in_=scr["ogs"][:, col].rearrange("(g p) t -> p g t", p=128)), 16)
                s_ld.tag(sync.dma_start(out=z_t[:, :, :],
                                        in_=scr["z"][:, col].rearrange("(g p) t -> p g t", p=128)), 16)

            cur = {}
            cur["gg"] = s_dve.reserve(1)[0]

            @block.vector
            def _(vector, cur=cur, prev=prev, tok_x=tok_x, tok_tr=tok_tr):
                vector.wait_ge(s_act.h, tok_tr[-1])
                vector.wait_ge(s_ld.h, tok_x)
                if prev is not None:
                    vector.wait_ge(s_mm.h, prev["mm"][-1])
                s_dve.tag(vector.tensor_tensor(og_bf[:, :, :], o_fm[:, :, :],
                                               ogs_t[:, :, :], ALU.mult))

            cur["mm"] = s_mm.reserve(8)
            cur["res"] = s_dve.reserve(8)

            @block.tensor
            def _(tensor, cur=cur, prev=prev):
                tensor.wait_ge(s_dve.h, cur["gg"])
                for m in range(nG):
                    par = m % 2
                    if m >= 2:
                        tensor.wait_ge(s_dve.h, cur["res"][m - 2])
                    elif prev is not None:
                        tensor.wait_ge(s_dve.h, prev["res"][6 + m])
                    for k in range(nG):
                        ins = tensor.matmul(ps_ab[par][:, :], woT[:, k, ts(m)],
                                            og_bf[:, k, :], start=(k == 0), stop=(k == nG - 1))
                    s_mm.tag(ins)

            @block.vector
            def _(vector, cur=cur, prev=prev):
                for m in range(nG):
                    par = m % 2
                    vector.wait_ge(s_mm.h, cur["mm"][m])
                    if m == 0 and prev is not None:
                        vector.wait_ge(s_st.h, prev["stt"])
                    s_dve.tag(vector.scalar_tensor_tensor(x1_t[:, m, :], ps_ab[par][:, :], 1.0,
                                                          z_t[:, m, :], ALU.mult, ALU.add))

            cur["stt"] = s_st.reserve(1, 16)[0]

            @block.sync
            def _(sync, col=col, cur=cur):
                sync.wait_ge(s_dve.h, cur["res"][-1])
                s_st.tag(sync.dma_start(out=scr["x1"][:, col].rearrange("(g p) t -> p g t", p=128),
                                        in_=x1_t[:, :, :]), 16)
            prev = cur

    if upto <= 6:
        return _bail()
    emit_l3rms(c, scr["x1"], scr["r2"], P["s2_t"], f"{tag}r2")
    if upto <= 7:
        return _bail()

    with c.scope() as c8:
        Tc = 512
        NCH = T // Tc
        wg_h = [c8.sb(f"{tag}_wgh{i}", [128, 8, 2048], BF16) for i in range(2)]
        wu_h = [c8.sb(f"{tag}_wuh{i}", [128, 8, 2048], BF16) for i in range(2)]
        r2_t = [c8.sb(f"{tag}_r28{i}", [128, nG, Tc], BF16) for i in range(2)]
        sg_bf = c8.sb(f"{tag}_sg8", [128, 16, Tc], BF16)
        hh_bf = c8.sb(f"{tag}_hh8", [128, 16, Tc], BF16)
        ps_ab = [c8.ps(f"{tag}_pab8{i}", [128, Tc]) for i in range(2)]
        sn = _snap(c)
        tok_w = [None, None]
        for hh in range(2):
            tok_w[hh] = s_gp.reserve(2, 16)[-1]

            @block.gpsimd
            def _(gpsimd, hh=hh):
                if hh == 0:
                    _joinall(gpsimd, c, sn)
                s_gp.tag(gpsimd.dma_start(
                    out=wg_h[hh][:, :, :],
                    in_=P["wgaT"][:, hh * 2048:(hh + 1) * 2048].rearrange("(g p) m -> p g m", p=128)), 16)
                s_gp.tag(gpsimd.dma_start(
                    out=wu_h[hh][:, :, :],
                    in_=P["wuaT"][:, hh * 2048:(hh + 1) * 2048].rearrange("(g p) m -> p g m", p=128)), 16)

        prev = None
        for hh in range(2):
            for ci in range(NCH):
                gidx = hh * NCH + ci
                pxc = gidx % 2
                col = slice(ci * Tc, (ci + 1) * Tc)
                tok_x = s_ld.reserve(1, 16)[0]

                @block.sync
                def _(sync, col=col, pxc=pxc, prev=prev, gidx=gidx, hh=hh):
                    if prev is None:
                        sync.wait_ge(s_gp.h, tok_w[0])
                    elif gidx >= 2:
                        sync.wait_ge(s_mm.h, prev["pv"])
                    s_ld.tag(sync.dma_start(out=r2_t[pxc][:, :, :],
                                            in_=scr["r2"][:, col].rearrange("(g p) t -> p g t", p=128)), 16)

                cur = {}
                cur["mm"] = s_mm.reserve(32)
                cur["pv"] = cur["mm"][-1]
                cur["sil"] = s_act.reserve(16)
                cur["ev"] = s_dve.reserve(16)

                @block.tensor
                def _(tensor, cur=cur, prev=prev, pxc=pxc, hh=hh, ci=ci, tok_x=tok_x):
                    tensor.wait_ge(s_ld.h, tok_x)
                    if ci == 0:
                        tensor.wait_ge(s_gp.h, tok_w[hh])
                    for m in range(16):
                        par = m % 2
                        if m >= 2:
                            tensor.wait_ge(s_dve.h, cur["ev"][m - 2])
                        elif prev is not None:
                            tensor.wait_ge(s_dve.h, prev["ev"][14 + m])
                        for k in range(nG):
                            ins = tensor.matmul(ps_ab[par][:, :], wg_h[hh][:, k, ts(m)],
                                                r2_t[pxc][:, k, :], start=(k == 0), stop=(k == nG - 1))
                        s_mm.tag(ins)
                        tensor.wait_ge(s_act.h, cur["sil"][m])
                        for k in range(nG):
                            ins = tensor.matmul(ps_ab[par][:, :], wu_h[hh][:, k, ts(m)],
                                                r2_t[pxc][:, k, :], start=(k == 0), stop=(k == nG - 1))
                        s_mm.tag(ins)

                @block.scalar
                def _(scalar, cur=cur, prev=prev):
                    for m in range(16):
                        par = m % 2
                        scalar.wait_ge(s_mm.h, cur["mm"][2 * m])
                        if m < 2 and prev is not None:
                            scalar.wait_ge(s_dve.h, prev["ev"][14 + m])
                        elif m >= 2:
                            scalar.wait_ge(s_dve.h, cur["ev"][m - 2])
                        s_act.tag(scalar.activation(sg_bf[:, m, :], ps_ab[par][:, :], AF.Silu))

                @block.vector
                def _(vector, cur=cur, prev=prev):
                    for m in range(16):
                        par = m % 2
                        vector.wait_ge(s_mm.h, cur["mm"][2 * m + 1])
                        if m == 0 and prev is not None:
                            vector.wait_ge(s_st.h, prev["stt"])
                        s_dve.tag(vector.tensor_tensor(hh_bf[:, m, :], sg_bf[:, m, :],
                                                       ps_ab[par][:, :], ALU.mult))

                cur["stt"] = s_st.reserve(1, 16)[0]

                @block.sync
                def _(sync, col=col, cur=cur, hh=hh):
                    sync.wait_ge(s_dve.h, cur["ev"][-1])
                    s_st.tag(sync.dma_start(
                        out=scr["hh"][hh * 2048:(hh + 1) * 2048, col].rearrange("(g p) t -> p g t", p=128),
                        in_=hh_bf[:, :, :]), 16)
                prev = cur

    with c.scope() as ca:
        Tc = 512
        NCH = T // Tc
        wda = ca.sb(f"{tag}_wda", [128, 32, 1024], BF16)
        hh_t = [ca.sb(f"{tag}_hht{i}", [128, 32, Tc], BF16) for i in range(2)]
        x_t = ca.sb(f"{tag}_xt", [128, nG, Tc], F32)
        x1_t = ca.sb(f"{tag}_x1c", [128, nG, Tc], F32)
        t1 = ca.sb(f"{tag}_t1", [128, Tc], F32)
        xo_t = ca.sb(f"{tag}_xoc", [128, nG, Tc], F32)
        ps_ab = [ca.ps(f"{tag}_pabA{i}", [128, Tc]) for i in range(2)]
        sn = _snap(c)
        tok_w = s_gp.reserve(1, 16)[0]

        @block.gpsimd
        def _(gpsimd):
            _joinall(gpsimd, c, sn)
            s_gp.tag(gpsimd.dma_start(out=wda[:, :, :],
                                      in_=P["wdaT"].rearrange("(g p) m -> p g m", p=128)), 16)

        prev = None
        for ci in range(NCH):
            col = slice(ci * Tc, (ci + 1) * Tc)
            pxc = ci % 2
            tok_x = s_ld.reserve(3, 16)[-1]

            @block.sync
            def _(sync, col=col, pxc=pxc, prev=prev, ci=ci):
                if prev is None:
                    sync.wait_ge(s_gp.h, tok_w)
                else:
                    if ci >= 2:
                        sync.wait_ge(s_mm.h, prev["pv"])
                    sync.wait_ge(s_dve.h, prev["res"][-1])
                s_ld.tag(sync.dma_start(out=hh_t[pxc][:, :, :],
                                        in_=scr["hh"][:, col].rearrange("(g p) t -> p g t", p=128)), 16)
                s_ld.tag(sync.dma_start(out=x_t[:, :, :],
                                        in_=x_d[:, col].rearrange("(g p) t -> p g t", p=128)), 16)
                s_ld.tag(sync.dma_start(out=x1_t[:, :, :],
                                        in_=scr["x1"][:, col].rearrange("(g p) t -> p g t", p=128)), 16)

            cur = {}
            cur["mm"] = s_mm.reserve(8)
            cur["pv"] = cur["mm"][-1]
            cur["res"] = s_dve.reserve(16)

            @block.tensor
            def _(tensor, cur=cur, prev=prev, pxc=pxc, tok_x=tok_x):
                tensor.wait_ge(s_ld.h, tok_x)
                for m in range(nG):
                    par = m % 2
                    if m >= 2:
                        tensor.wait_ge(s_dve.h, cur["res"][2 * (m - 2) + 1])
                    elif prev is not None:
                        tensor.wait_ge(s_dve.h, prev["res"][2 * (6 + m) + 1])
                    for k in range(32):
                        ins = tensor.matmul(ps_ab[par][:, :], wda[:, k, ts(m)],
                                            hh_t[pxc][:, k, :], start=(k == 0), stop=(k == 31))
                    s_mm.tag(ins)

            @block.vector
            def _(vector, cur=cur, prev=prev, tok_x=tok_x):
                for m in range(nG):
                    par = m % 2
                    vector.wait_ge(s_mm.h, cur["mm"][m])
                    if m == 0 and prev is not None:
                        vector.wait_ge(s_st.h, prev["stt"])
                    s_dve.tag(vector.scalar_tensor_tensor(t1[:, :], ps_ab[par][:, :], 1.0,
                                                          x1_t[:, m, :], ALU.mult, ALU.add))
                    vector.wait_ge(s_dve.h, cur["res"][2 * m])
                    s_dve.tag(vector.tensor_tensor(xo_t[:, m, :], t1[:, :], x_t[:, m, :],
                                                   ALU.add))

            cur["stt"] = s_st.reserve(1, 16)[0]

            @block.sync
            def _(sync, col=col, cur=cur):
                sync.wait_ge(s_dve.h, cur["res"][-1])
                s_st.tag(sync.dma_start(out=y_d[:, col].rearrange("(g p) t -> p g t", p=128),
                                        in_=xo_t[:, :, :]), 16)
            prev = cur

    @block.sync
    def _(sync):
        sync.wait_ge(s_st.h, s_st.v)


def build_l3attn(core_id=0, upto=10):
    nc = bass.Bass("TRN2", target_bir_lowering=False)
    x_d = nc.dram_tensor("x", [DM, T], F32, kind="ExternalInput").ap()
    y_d = nc.dram_tensor("y", [DM, T], F32, kind="ExternalOutput").ap()
    names = dict(s1_t=[128, 8], s2_t=[128, 8], wqT=[1024, 1024], wkT=[1024, 1024],
                 wvT=[1024, 1024], wogT=[1024, 1024], woT=[1024, 1024],
                 wg1T=[1024, 16], bg1c=[16, 1], bog_t=[128, 8], wg2b=[16, 1024],
                 bg2r=[1, 1024],
                 wgaT=[1024, 4096], wuaT=[1024, 4096], wdaT=[4096, 1024],
                 tmask=[128, 8], imask=[128, 8])
    P = {nm: nc.dram_tensor(nm, sh, F32, kind="ExternalInput").ap() for nm, sh in names.items()}
    scr = dict(
        z=nc.dram_tensor("scr_z", [DM, T], F32, kind="Internal").ap(),
        r1=nc.dram_tensor("scr_r1", [DM, T], BF16, kind="Internal").ap(),
        r1f=nc.dram_tensor("scr_r1f", [DM, T], F32, kind="Internal").ap(),
        ogs=nc.dram_tensor("scr_ogs", [DM, T], BF16, kind="Internal").ap(),
        qloc=nc.dram_tensor("scr_qloc", [T, DM], F32, kind="Internal").ap(),
        gin32=nc.dram_tensor("scr_gin32", [T, DM], F32, kind="Internal").ap(),
        gout32=nc.dram_tensor("scr_gout32", [8 * T, DM], F32, kind="Internal").ap(),
        gin16=nc.dram_tensor("scr_gin16", [2 * T, DM], BF16, kind="Internal").ap(),
        gout16=nc.dram_tensor("scr_gout16", [8 * 2 * T, DM], BF16, kind="Internal").ap(),
        oln=nc.dram_tensor("scr_oln", [T, DM], BF16, kind="Internal").ap(),
        x1=nc.dram_tensor("scr_x1", [DM, T], F32, kind="Internal").ap(),
        r2=nc.dram_tensor("scr_r2", [DM, T], BF16, kind="Internal").ap(),
        gh=nc.dram_tensor("scr_gh", [4096, T], BF16, kind="Internal").ap(),
        hh=nc.dram_tensor("scr_hh", [4096, T], BF16, kind="Internal").ap(),
    )
    with ExitStack() as es:
        block = es.enter_context(nc.Block())
        c = Ctx(nc, es, block)
        emit_l3attn(c, x_d, y_d, P, scr, "gla", upto=upto)
        check_sems(c)
    return nc


def l3attn_inmaps(inp, xs):
    f = lambda k: np.asarray(inp[k], np.float32)
    base = dict(
        s1_t=_tile_vec(f("L3_attn_scale1"), 8),
        s2_t=_tile_vec(f("L3_attn_scale2"), 8),
        wqT=np.ascontiguousarray(f("L3_attn_wq").T),
        wkT=np.ascontiguousarray(f("L3_attn_wk").T),
        wvT=np.ascontiguousarray(f("L3_attn_wv").T),
        wogT=np.ascontiguousarray(f("L3_attn_wog").T),
        woT=np.ascontiguousarray(f("L3_attn_wo").T),
        wg1T=np.ascontiguousarray(f("L3_attn_wg1").T),
        bg1c=np.ascontiguousarray(f("L3_attn_bg1").reshape(16, 1)),
        bog_t=_tile_vec(f("L3_attn_bog"), 8),
        wg2b=np.ascontiguousarray(f("L3_attn_wg2").T),
        bg2r=np.ascontiguousarray(f("L3_attn_bg2").reshape(1, 1024)),
        wgaT=np.ascontiguousarray(f("L3_attn_w_gate").T),
        wuaT=np.ascontiguousarray(f("L3_attn_w_up").T),
        wdaT=np.ascontiguousarray(f("L3_attn_w_down").T),
    )
    ims = []
    for cid in range(NC8):
        tm = np.zeros((128, 8), np.float32)
        tm[:, 0:cid + 1] = 1.0
        ims.append(dict(x=xs[cid], tmask=tm, imask=np.ascontiguousarray(1.0 - tm), **base))
    return ims


def _tile_vec(v, nch):
    return np.ascontiguousarray(np.asarray(v, dtype=np.float32).reshape(nch, 128).T)


def l0ffn_inmaps(inp, xs):
    b1t = _tile_vec(inp["L0_ffn_b1"], 32)
    b2t = _tile_vec(inp["L0_ffn_b2"], 8)
    return [dict(x=xs[cid], w1=inp["L0_ffn_w1"], w2=inp["L0_ffn_w2"],
                 b1t=b1t, b2t=b2t) for cid in range(NC8)]


def emit_fence(c):
    sn = {k: s.v for k, s in c.sems.items()}
    block = c.block
    for engname in ("sync", "gpsimd", "scalar", "vector", "tensor"):
        deco = getattr(block, engname)

        @deco
        def _(eng, sn=sn):
            for k, v in sn.items():
                eng.wait_ge(c.sems[k].h, v)


def build_full():
    nc = bass.Bass("TRN2", target_bir_lowering=False)
    x_d = nc.dram_tensor("x", [DM, T], F32, kind="ExternalInput").ap()
    y_d = nc.dram_tensor("y", [DM, T], F32, kind="ExternalOutput").ap()
    xb = [nc.dram_tensor(f"xb{i}", [DM, T], F32, kind="Internal").ap() for i in range(11)]

    def decl(prefix, names):
        return {nm: nc.dram_tensor(f"{prefix}{nm}", sh, F32, kind="ExternalInput").ap()
                for nm, sh in names.items()}

    Pf0 = decl("f0_", dict(w1=[4096, 1024], w2=[1024, 4096], b1t=[128, 32], b2t=[128, 8]))
    bw1_scr = nc.dram_tensor("bw1_scr", [4096, 1024], BF16, kind="Internal").ap()
    bw2_scr = nc.dram_tensor("bw2_scr", [1024, 4096], BF16, kind="Internal").ap()

    mamba_shapes = dict(winT=[1024, 4096], wxpT=[2048, 96], wdtT=[64, 2048],
                        woutT=[2048, 1024], convw_t=[128, 16, 4], convb_t=[128, 16],
                        bdt_t=[128, 16], alog_t=[128, 16, 16], d_t=[128, 16],
                        ident=[128, 128], gw_t=[128, 8], zero_state=[128, 16, 16])
    Pa0 = decl("a0_", mamba_shapes)
    scr_mb = dict(
        xc=nc.dram_tensor("mscr_xc", [2048, T], F32, kind="Internal").ap(),
        sz=nc.dram_tensor("mscr_sz", [2048, T], BF16, kind="Internal").ap(),
        dt=nc.dram_tensor("mscr_dt", [2048, T], F32, kind="Internal").ap(),
        bc=nc.dram_tensor("mscr_bc", [32, T], BF16, kind="Internal").ap(),
    )

    Pmoe = decl("moe_", dict(rw=[1024, 16], identf=[128, 128], gwa=[16 * 1024, 1024],
                             uwa=[16 * 1024, 1024], dwa=[16 * 1024, 1024]))
    scr_moe = dict(
        xlnf=nc.dram_tensor("moescr_xlnf", [DM, T], F32, kind="Internal").ap(),
        xln=nc.dram_tensor("moescr_xln", [DM, T], BF16, kind="Internal").ap(),
        m=nc.dram_tensor("moescr_m", [16, T], F32, kind="Internal").ap(),
        acc=nc.dram_tensor("moescr_acc", [DM, T], F32, kind="Internal").ap(),
    )

    Pqa = decl("qa_", dict(qkvT=[1024, 3072], bqkv_t=[128, 24], woT=[1024, 1024],
                           ob_t=[128, 8], cosT=[64, T], sinT=[64, T], pswap=[64, 64]))
    scr_qa = dict(
        xln=nc.dram_tensor("qascr_xln", [DM, T], BF16, kind="Internal").ap(),
        qkv=nc.dram_tensor("qascr_qkv", [3072, T], BF16, kind="Internal").ap(),
        y=nc.dram_tensor("qascr_y", [DM, T], BF16, kind="Internal").ap(),
    )

    Pf2 = decl("f2_", dict(w1eT=[1024, 512], w2eT=[512, 1024], wgT=[1024, 8],
                           bg_row=[1, 8], sel=[8, 4, 128], identf=[128, 128]))

    Pm2 = [decl(f"m2{i}_", mamba_shapes) for i in range(4)]
    fw_d = nc.dram_tensor("m2_fw_t", [128, 8], F32, kind="ExternalInput").ap()
    xa = [nc.dram_tensor(f"xa{i}", [DM, T], F32, kind="Internal").ap() for i in range(4)]
    xln0 = nc.dram_tensor("m2_xln0", [DM, T], F32, kind="Internal").ap()

    Pf3 = decl("f3_", dict(wrT=[1024, 1024], wkT=[1024, 1024], wvT=[1024, 1024],
                           mur_t=[128, 8], muk_t=[128, 8]))

    Pgla = decl("gla_", dict(s1_t=[128, 8], s2_t=[128, 8], wqT=[1024, 1024],
                             wkT=[1024, 1024], wvT=[1024, 1024], wogT=[1024, 1024],
                             woT=[1024, 1024], wg1T=[1024, 16], bg1c=[16, 1],
                             bog_t=[128, 8], wg2b=[16, 1024], bg2r=[1, 1024],
                             wgaT=[1024, 4096], wuaT=[1024, 4096], wdaT=[4096, 1024],
                             tmask=[128, 8], imask=[128, 8]))
    scr_gla = dict(
        z=nc.dram_tensor("gscr_z", [DM, T], F32, kind="Internal").ap(),
        r1=nc.dram_tensor("gscr_r1", [DM, T], BF16, kind="Internal").ap(),
        r1f=nc.dram_tensor("gscr_r1f", [DM, T], F32, kind="Internal").ap(),
        ogs=nc.dram_tensor("gscr_ogs", [DM, T], BF16, kind="Internal").ap(),
        qloc=nc.dram_tensor("gscr_qloc", [T, DM], F32, kind="Internal").ap(),
        gin32=nc.dram_tensor("gscr_gin32", [T, DM], F32, kind="Internal").ap(),
        gout32=nc.dram_tensor("gscr_gout32", [8 * T, DM], F32, kind="Internal").ap(),
        gin16=nc.dram_tensor("gscr_gin16", [2 * T, DM], BF16, kind="Internal").ap(),
        gout16=nc.dram_tensor("gscr_gout16", [8 * 2 * T, DM], BF16, kind="Internal").ap(),
        oln=nc.dram_tensor("gscr_oln", [T, DM], BF16, kind="Internal").ap(),
        x1=nc.dram_tensor("gscr_x1", [DM, T], F32, kind="Internal").ap(),
        r2=nc.dram_tensor("gscr_r2", [DM, T], BF16, kind="Internal").ap(),
        gh=nc.dram_tensor("gscr_gh", [4096, T], BF16, kind="Internal").ap(),
        hh=nc.dram_tensor("gscr_hh", [4096, T], BF16, kind="Internal").ap(),
    )

    with ExitStack() as es:
        block = es.enter_context(nc.Block())
        c = Ctx(nc, es, block)
        with c.scope() as cw:
            emit_l0ffn(cw, x_d, xb[0], Pf0["w1"], Pf0["w2"], Pf0["b1t"], Pf0["b2t"],
                       bw1_scr, bw2_scr)
        emit_fence(c)
        with c.scope() as cw:
            emit_mamba(cw, xb[0], xb[1], Pa0, scr_mb, "ln", "a0")
        emit_fence(c)
        with c.scope() as cw:
            emit_l0ffn(cw, xb[1], xb[2], Pf0["w1"], Pf0["w2"], Pf0["b1t"], Pf0["b2t"],
                       bw1_scr, bw2_scr)
        emit_fence(c)
        with c.scope() as cw:
            emit_l1moe(cw, xb[2], xb[3], Pmoe, scr_moe, "mA")
        emit_fence(c)
        with c.scope() as cw:
            emit_l1attn(cw, xb[3], xb[4], Pqa, scr_qa, "qa")
        emit_fence(c)
        with c.scope() as cw:
            emit_l1moe(cw, xb[4], xb[5], Pmoe, scr_moe, "mB")
        emit_fence(c)
        with c.scope() as cw:
            emit_l2ffn(cw, xb[5], xb[6], Pf2, "fA")
        emit_fence(c)
        with c.scope() as cw:
            emit_ln_full(cw, xb[6], xln0, "oln")
        cur = xln0
        for i in range(4):
            emit_fence(c)
            with c.scope() as cw:
                emit_mamba(cw, cur, xa[i], Pm2[i], scr_mb, "rms", f"m2{i}")
            cur = xa[i]
        emit_fence(c)
        with c.scope() as cw:
            emit_final_rms(cw, cur, xb[7], fw_d, "fin", resid_d=xb[6])
        emit_fence(c)
        with c.scope() as cw:
            emit_l2ffn(cw, xb[7], xb[8], Pf2, "fB")
        emit_fence(c)
        with c.scope() as cw:
            emit_l3ffn(cw, xb[8], xb[9], Pf3, "tA")
        emit_fence(c)
        with c.scope() as cw:
            emit_l3attn(cw, xb[9], xb[10], Pgla, scr_gla, "gla")
        emit_fence(c)
        with c.scope() as cw:
            emit_l3ffn(cw, xb[10], y_d, Pf3, "tB")
        check_sems(c)
    return nc


def full_inmaps(inp, xs):
    def strip(d):
        return {k: v for k, v in d.items() if k != "x"}

    per_core = []
    f0 = [strip(d) for d in l0ffn_inmaps(inp, [None] * NC8)]
    a0 = [strip(d) for d in l0attn_inmaps(inp, [None] * NC8)]
    moe = [strip(d) for d in l1moe_inmaps(inp, [None] * NC8)]
    qa = [strip(d) for d in l1attn_inmaps(inp, [None] * NC8)]
    f2 = [strip(d) for d in l2ffn_inmaps(inp, [None] * NC8)]
    m2 = [strip(d) for d in l2attn_inmaps(inp, [None] * NC8)]
    f3 = [strip(d) for d in l3ffn_inmaps(inp, [None] * NC8)]
    gla = [strip(d) for d in l3attn_inmaps(inp, [None] * NC8)]
    for cid in range(NC8):
        d = dict(x=xs[cid])
        for pfx, src_d in (("f0_", f0[cid]), ("a0_", a0[cid]), ("moe_", moe[cid]),
                           ("qa_", qa[cid]), ("f2_", f2[cid]), ("f3_", f3[cid]),
                           ("gla_", gla[cid])):
            for k, v in src_d.items():
                d[pfx + k] = v
        for k, v in m2[cid].items():
            if k == "fw_t":
                d["m2_fw_t"] = v
            else:
                li, nm = k.split("_", 1)
                d[f"m2{li[1]}_{nm}"] = v
        per_core.append(d)
    return per_core


_BUILD_CACHE = {}


def _get_nc(key, builder):
    if key not in _BUILD_CACHE:
        _BUILD_CACHE[key] = builder()
    return _BUILD_CACHE[key]


def kernel(**inp):
    x = inp["L0_attn_x"]
    xs = [np.ascontiguousarray(np.asarray(x[cid], np.float32).T) for cid in range(NC8)]

    nc = _get_nc("full", build_full)
    res = run_bass_kernel_spmd(nc, full_inmaps(inp, xs), core_ids=list(range(NC8)))
    out = np.stack([np.asarray(res.results[cid]["y"]).T for cid in range(NC8)])
    return np.ascontiguousarray(out.astype(np.float32))


def kernel_unfused(**inp):
    x = inp["L0_attn_x"]
    xs = [np.ascontiguousarray(np.asarray(x[cid], np.float32).T) for cid in range(NC8)]

    steps = [
        ("l0ffn", build_l0ffn, l0ffn_inmaps),
        ("l0attn", lambda: build_mamba("ln"), l0attn_inmaps),
        ("l0ffn", build_l0ffn, l0ffn_inmaps),
        ("l1moe", build_l1moe, l1moe_inmaps),
        ("l1attn", build_l1attn, l1attn_inmaps),
        ("l1moe", build_l1moe, l1moe_inmaps),
        ("l2ffn", build_l2ffn, l2ffn_inmaps),
        ("l2attn", build_l2attn, l2attn_inmaps),
        ("l2ffn", build_l2ffn, l2ffn_inmaps),
        ("l3ffn", build_l3ffn, l3ffn_inmaps),
        ("l3attn", build_l3attn, l3attn_inmaps),
        ("l3ffn", build_l3ffn, l3ffn_inmaps),
    ]
    for key, builder, im_fn in steps:
        nc = _get_nc(key, builder)
        res = run_bass_kernel_spmd(nc, im_fn(inp, xs), core_ids=list(range(NC8)))
        xs = [np.ascontiguousarray(np.asarray(res.results[cid]["y"])) for cid in range(NC8)]
    out = np.stack([xs[cid].T for cid in range(NC8)])
    return np.ascontiguousarray(out.astype(np.float32))


# version 07 (final)
# speedup vs baseline: 1.0055x; 1.0055x over previous
import numpy as np
from contextlib import ExitStack
import concourse.bass as bass
import concourse.mybir as mybir
from concourse.bass_utils import run_bass_kernel_spmd

F32 = mybir.dt.float32
BF16 = mybir.dt.bfloat16
AF = mybir.ActivationFunctionType
ALU = mybir.AluOpType

NC8 = 8
DM = 1024
T = 4096


def ts(i, n=128):
    return slice(i * n, (i + 1) * n)


class Sem:
    def __init__(self, es, nc, name):
        self.h = es.enter_context(nc.semaphore(name))
        self.name = name
        self.v = 0
        self.emitted = 0

    def reserve(self, n=1, step=1):
        out = [self.v + step * (i + 1) for i in range(n)]
        self.v = out[-1]
        return out

    def tag(self, instr, step=1):
        self.emitted += step
        instr.then_inc(self.h, step)


def check_sems(c):
    bad = [(s.name, s.v, s.emitted) for s in c.sems.values() if s.v != s.emitted]
    assert not bad, "sem reserve/tag mismatch (name, planned, emitted): %s" % bad


class Ctx:
    GUID = [0]

    def __init__(self, nc, es, block):
        self.nc, self.es, self.block = nc, es, block
        self.sems = {}
        self._uid = 0

    def sem(self, name):
        if name not in self.sems:
            self._uid += 1
            self.sems[name] = Sem(self.es, self.nc, f"{name}{self._uid}")
        return self.sems[name]

    def sb(self, name, shape, dtype):
        self._uid += 1
        return self.es.enter_context(self.nc.sbuf_tensor(f"{name}_{self._uid}", shape, dtype))

    def scope(self):
        from contextlib import contextmanager

        @contextmanager
        def _scope():
            with ExitStack() as es2:
                c2 = Ctx(self.nc, es2, self.block)
                c2.sems = self.sems
                Ctx.GUID[0] += 1
                c2._uid = Ctx.GUID[0] * 100000
                yield c2

        return _scope()

    def ps(self, name, shape, dtype=F32):
        self._uid += 1
        return self.es.enter_context(self.nc.psum_tensor(f"{name}_{self._uid}", shape, dtype))



def emit_binarize(c, w_dram, M, K, wT_sb, bw_scratch_dram, tag, skip=False):
    block = c.block
    s_ld = c.sem("s_ld"); s_sc = c.sem("s_sc"); s_gp = c.sem("s_gp")
    s_act = c.sem("s_act"); s_dve = c.sem("s_dve")
    if skip:
        sn = {k: s.v for k, s in c.sems.items()}
        tok_tr = s_sc.reserve(K // 128, 16)

        @block.scalar
        def _(scalar):
            for k, v in sn.items():
                scalar.wait_ge(c.sems[k].h, v)
            for kc in range(K // 128):
                s_sc.tag(scalar.dma_start_transpose(wT_sb[:, kc, :], bw_scratch_dram[:, ts(kc)]), 16)

        return tok_tr[-1]
    nh = M // 128
    wrow = c.sb(f"{tag}_wrow", [128, K], F32)
    sg = c.sb(f"{tag}_sg", [128, K], BF16)
    bwt = c.sb(f"{tag}_bw", [128, K], BF16)
    asum = c.sb(f"{tag}_asum", [128, 1], F32)
    dve0, gp0, act0 = s_dve.v, s_gp.v, s_act.v

    for i in range(nh):
        tok_ld = s_ld.reserve(1, 16)[0]

        @block.sync
        def _(sync, i=i, tok_ld=tok_ld):
            if i == 0:
                sync.wait_ge(s_dve.h, dve0)
                sync.wait_ge(s_gp.h, gp0)
                sync.wait_ge(s_act.h, act0)
            else:
                sync.wait_ge(s_gp.h, s_gp.v)
            s_ld.tag(sync.dma_start(out=wrow[:, :], in_=w_dram[ts(i), :]), 16)

        tok_sign = s_act.reserve(1)[0]

        @block.scalar
        def _(scalar, tok_ld=tok_ld):
            scalar.wait_ge(s_ld.h, tok_ld)
            s_act.tag(scalar.sign(sg[:, :], wrow[:, :]))

        tok_red, tok_bw = s_dve.reserve(2)

        @block.vector
        def _(vector, tok_ld=tok_ld, tok_sign=tok_sign, tok_red=tok_red):
            vector.wait_ge(s_ld.h, tok_ld)
            s_dve.tag(vector.tensor_reduce(asum[:, 0:1], wrow[:, :], mybir.AxisListType.X,
                                           ALU.add, apply_absolute_value=True))
            vector.wait_ge(s_dve.h, tok_red)
            vector.wait_ge(s_act.h, tok_sign)
            s_dve.tag(vector.tensor_scalar(bwt[:, :], sg[:, :], asum[:, 0:1], 1.0 / K,
                                           ALU.mult, ALU.mult))

        tok_wst = s_gp.reserve(1, 16)[0]

        @block.gpsimd
        def _(gpsimd, i=i, tok_bw=tok_bw):
            gpsimd.wait_ge(s_dve.h, tok_bw)
            s_gp.tag(gpsimd.dma_start(out=bw_scratch_dram[ts(i), :], in_=bwt[:, :]), 16)

    tok_tr = s_sc.reserve(K // 128, 16)

    @block.scalar
    def _(scalar):
        scalar.wait_ge(s_gp.h, s_gp.v)
        for kc in range(K // 128):
            s_sc.tag(scalar.dma_start_transpose(wT_sb[:, kc, :], bw_scratch_dram[:, ts(kc)]), 16)

    return tok_tr[-1]



class LNRes:
    pass


def emit_ln_chunk(c, x_t, xln_bf, t_bf, nG, Tc, ones_col, ones_row, xsq, ps_sum, ps_ssq,
                  ps_rep2, stat, tok_x, prev, eps_t=None, out_f32=False,
                  extra_dve_wait=None, ssq_start=True, sq_war=None):
    block = c.block
    s_mm = c.sem("s_mm"); s_act = c.sem("s_act"); s_dve = c.sem("s_dve"); s_ld = c.sem("s_ld")
    r = LNRes()
    mean_t = stat[0:1, 0:Tc]
    rstd_t = stat[0:1, Tc:2 * Tc]
    mean2_t = stat[0:1, 2 * Tc:3 * Tc]
    var_t = stat[0:1, 3 * Tc:4 * Tc]
    sd_t = stat[0:1, 2 * Tc:3 * Tc]

    tok_sq = s_act.reserve(nG)
    tok_mm = s_mm.reserve(2)
    r.tok_sum, r.tok_ssq = tok_mm
    r.tok_mean, r.tok_mean2 = s_act.reserve(2)
    r.tok_var = s_dve.reserve(1)[0]
    r.tok_sd = s_act.reserve(1)[0]
    r.tok_rstd = s_dve.reserve(1)[0]
    r.tok_rep = s_mm.reserve(1)[0]
    _, r.tok_xln = s_dve.reserve(2)

    @block.scalar
    def _(scalar):
        scalar.wait_ge(s_ld.h, tok_x)
        if prev is not None:
            scalar.wait_ge(s_mm.h, prev.tok_ssq)
        if sq_war is not None:
            scalar.wait_ge(sq_war[0].h, sq_war[1])
        for g in range(nG):
            s_act.tag(scalar.square(xsq[:, g, :], x_t[:, g, :]))

    @block.tensor
    def _(tensor):
        tensor.wait_ge(s_ld.h, tok_x)
        if prev is not None:
            tensor.wait_ge(s_act.h, prev.tok_mean)
            tensor.wait_ge(s_dve.h, prev.tok_var)
        for g in range(nG):
            ins = tensor.matmul(ps_sum[0:1, :], ones_col[:, 0:1], x_t[:, g, :],
                                start=(g == 0), stop=(g == nG - 1))
        s_mm.tag(ins)
        for g in range(nG):
            tensor.wait_ge(s_act.h, tok_sq[g])
            ins = tensor.matmul(ps_ssq[0:1, :], ones_col[:, 0:1], xsq[:, g, :],
                                start=(g == 0 and ssq_start), stop=(g == nG - 1))
        s_mm.tag(ins)

    @block.scalar
    def _(scalar):
        scalar.wait_ge(s_mm.h, r.tok_ssq)
        s_act.tag(scalar.mul(mean_t, ps_sum[0:1, :], 1.0 / DM))
        scalar.wait_ge(s_act.h, r.tok_mean)
        s_act.tag(scalar.square(mean2_t, mean_t))

    @block.vector
    def _(vector):
        vector.wait_ge(s_act.h, r.tok_mean2)
        s_dve.tag(vector.scalar_tensor_tensor(var_t, ps_ssq[0:1, :], 1.0 / DM, mean2_t,
                                              ALU.mult, ALU.subtract))

    @block.scalar
    def _(scalar):
        scalar.wait_ge(s_dve.h, r.tok_var)
        s_act.tag(scalar.activation(sd_t, var_t, AF.Sqrt, bias=eps_t[0:1, 0:1]))

    @block.vector
    def _(vector):
        vector.wait_ge(s_act.h, r.tok_sd)
        s_dve.tag(vector.reciprocal(rstd_t, sd_t))

    @block.tensor
    def _(tensor):
        tensor.wait_ge(s_dve.h, r.tok_rstd)
        if prev is not None:
            tensor.wait_ge(s_dve.h, prev.tok_xln)
        ins = tensor.matmul(ps_rep2[:, :], ones_row[0:1, :], stat[0:1, 0:2 * Tc],
                            start=True, stop=True)
        s_mm.tag(ins)

    @block.vector
    def _(vector):
        vector.wait_ge(s_mm.h, r.tok_rep)
        if extra_dve_wait is not None:
            vector.wait_ge(s_dve.h, extra_dve_wait)
        mrep_b = ps_rep2[:, 0:Tc].unsqueeze(1).broadcast_to([128, nG, Tc])
        rrep_b = ps_rep2[:, Tc:2 * Tc].unsqueeze(1).broadcast_to([128, nG, Tc])
        s_dve.tag(vector.scalar_tensor_tensor(t_bf[:, :, :], x_t[:, :, :], 1.0, mrep_b,
                                              ALU.mult, ALU.subtract))
        vector.wait_ge(s_dve.h, r.tok_xln - 1)
        s_dve.tag(vector.tensor_tensor(xln_bf[:, :, :], t_bf[:, :, :], rrep_b, ALU.mult))
    return r



def emit_l0ffn(c, x_d, y_d, w1_d, w2_d, b1t_d, b2t_d, bw1_scr, bw2_scr, skip_binarize=False):
    block = c.block
    s_ld = c.sem("s_ld"); s_sc = c.sem("s_sc"); s_act = c.sem("s_act")
    s_dve = c.sem("s_dve"); s_mm = c.sem("s_mm"); s_st = c.sem("s_st")

    Tc = 256
    NCH = T // Tc
    nG = DM // 128

    bw1t = c.sb("bw1t", [128, 8, 4096], BF16)
    bw2t = c.sb("bw2t", [128, 32, 1024], BF16)
    ones_col = c.sb("ones_col", [128, 1], F32)
    ones_row = c.sb("ones_row", [1, 128], F32)
    eps_t = c.sb("eps_t", [1, 1], F32)
    b1t = c.sb("b1t", [128, 32], F32)
    b2t = c.sb("b2t", [128, 8], F32)
    x_t = c.sb("x_t", [128, nG, Tc], F32)
    y_t = c.sb("y_t", [128, nG, Tc], F32)
    xsq = y_t
    t_bf = c.sb("t_bf", [128, nG, Tc], BF16)
    xln_bf = c.sb("xln_bf", [128, nG, Tc], BF16)
    h_bf = c.sb("h_bf", [128, 32, Tc], BF16)
    ffn_t = [c.sb(f"ffn{i}", [128, Tc], F32) for i in range(2)]
    stat = c.sb("stat", [1, 4 * Tc], F32)
    with c.scope() as c1:
        emit_binarize(c1, w1_d, 4096, 1024, bw1t, bw1_scr, "w1", skip=skip_binarize)
    with c.scope() as c2:
        tok_w = emit_binarize(c2, w2_d, 1024, 4096, bw2t, bw2_scr, "w2", skip=skip_binarize)
    ps_sum = c.ps("ps_sum", [1, Tc])
    ps_ssq = c.ps("ps_ssq", [1, Tc])
    ps_rep2 = c.ps("ps_rep2", [128, 2 * Tc])
    ps_h = [c.ps(f"ps_h{i}", [128, Tc]) for i in range(2)]
    ps_o = [c.ps(f"ps_o{i}", [128, Tc]) for i in range(2)]

    tok_misc = s_ld.reserve(2, 16)
    tok_ones = s_dve.reserve(1)[0]

    @block.vector
    def _(vector):
        vector.memset(ones_col[:, :], 1.0)
        vector.memset(eps_t[:, :], 1e-5)
        s_dve.tag(vector.memset(ones_row[:, :], 1.0))

    @block.sync
    def _(sync):
        s_ld.tag(sync.dma_start(out=b1t[:, :], in_=b1t_d), 16)
        s_ld.tag(sync.dma_start(out=b2t[:, :], in_=b2t_d), 16)

    prev_ln = None
    prev_ffn = None
    for ci in range(NCH):
        col = slice(ci * Tc, (ci + 1) * Tc)
        tok_x = s_ld.reserve(1, 16)[0]

        @block.sync
        def _(sync):
            if prev_ffn is not None:
                sync.wait_ge(s_mm.h, prev_ffn["tok_om"][-1])
                sync.wait_ge(s_act.h, prev_ffn["tok_oadd"][-1])
                sync.wait_ge(s_dve.h, prev_ffn["tok_res"][-1])
            else:
                sync.wait_ge(s_sc.h, tok_w)
                sync.wait_ge(s_dve.h, tok_ones)
            s_ld.tag(sync.dma_start(out=x_t[:, :, :],
                                    in_=x_d[:, col].rearrange("(g p) t -> p g t", p=128)), 16)

        ln = emit_ln_chunk(c, x_t, xln_bf, t_bf, nG, Tc, ones_col, ones_row, xsq,
                           ps_sum, ps_ssq, ps_rep2, stat, tok_x, prev_ln, eps_t,
                           sq_war=(s_st, prev_ffn["tok_st"]) if prev_ffn else None)
        prev_ln = ln

        ffn = {}
        ffn["tok_hm"] = s_mm.reserve(32)
        ffn["tok_hevac"] = s_act.reserve(32)
        ffn["tok_om"] = s_mm.reserve(8)
        ffn["tok_oadd"] = s_act.reserve(8)
        ffn["tok_res"] = s_dve.reserve(8)

        @block.tensor
        def _(tensor):
            tensor.wait_ge(s_dve.h, ln.tok_xln)
            for m in range(32):
                par = m % 2
                if m >= 2:
                    tensor.wait_ge(s_act.h, ffn["tok_hevac"][m - 2])
                elif prev_ffn is not None:
                    tensor.wait_ge(s_act.h, prev_ffn["tok_hevac"][30 + m])
                for k in range(nG):
                    ins = tensor.matmul(ps_h[par][:, :], bw1t[:, k, ts(m)], xln_bf[:, k, :],
                                        start=(k == 0), stop=(k == nG - 1))
                s_mm.tag(ins)

        @block.scalar
        def _(scalar):
            for m in range(32):
                par = m % 2
                scalar.wait_ge(s_mm.h, ffn["tok_hm"][m])
                s_act.tag(scalar.activation(h_bf[:, m, :], ps_h[par][:, :],
                                            AF.Gelu, bias=b1t[:, m:m + 1]))

        @block.tensor
        def _(tensor):
            tensor.wait_ge(s_act.h, ffn["tok_hevac"][31])
            for m2 in range(8):
                par = m2 % 2
                if m2 >= 2:
                    tensor.wait_ge(s_act.h, ffn["tok_oadd"][m2 - 2])
                elif prev_ffn is not None:
                    tensor.wait_ge(s_act.h, prev_ffn["tok_oadd"][6 + m2])
                for k2 in range(32):
                    ins = tensor.matmul(ps_o[par][:, :], bw2t[:, k2, ts(m2)], h_bf[:, k2, :],
                                        start=(k2 == 0), stop=(k2 == 31))
                s_mm.tag(ins)

        @block.scalar
        def _(scalar):
            for m2 in range(8):
                par = m2 % 2
                scalar.wait_ge(s_mm.h, ffn["tok_om"][m2])
                s_act.tag(scalar.add(ffn_t[par][:, :], ps_o[par][:, :], b2t[:, m2:m2 + 1]))

        @block.vector
        def _(vector):
            if prev_ffn is not None:
                vector.wait_ge(s_st.h, prev_ffn["tok_st"])
            for m2 in range(8):
                par = m2 % 2
                vector.wait_ge(s_act.h, ffn["tok_oadd"][m2])
                s_dve.tag(vector.scalar_tensor_tensor(
                    y_t[:, m2, :], ffn_t[par][:, :], 0.5, x_t[:, m2, :],
                    ALU.mult, ALU.add))

        ffn["tok_st"] = s_st.reserve(1, 16)[0]

        @block.sync
        def _(sync):
            sync.wait_ge(s_dve.h, ffn["tok_res"][-1])
            s_st.tag(sync.dma_start(out=y_d[:, col].rearrange("(g p) t -> p g t", p=128),
                                    in_=y_t[:, :, :]), 16)
        prev_ffn = ffn

    @block.sync
    def _(sync):
        sync.wait_ge(s_st.h, s_st.v)


def build_l0ffn():
    nc = bass.Bass("TRN2", target_bir_lowering=False)
    x_d = nc.dram_tensor("x", [DM, T], F32, kind="ExternalInput").ap()
    w1_d = nc.dram_tensor("w1", [4096, 1024], F32, kind="ExternalInput").ap()
    w2_d = nc.dram_tensor("w2", [1024, 4096], F32, kind="ExternalInput").ap()
    b1t_d = nc.dram_tensor("b1t", [128, 32], F32, kind="ExternalInput").ap()
    b2t_d = nc.dram_tensor("b2t", [128, 8], F32, kind="ExternalInput").ap()
    y_d = nc.dram_tensor("y", [DM, T], F32, kind="ExternalOutput").ap()
    bw1_scr = nc.dram_tensor("bw1_scr", [4096, 1024], BF16, kind="Internal").ap()
    bw2_scr = nc.dram_tensor("bw2_scr", [1024, 4096], BF16, kind="Internal").ap()
    with ExitStack() as es:
        block = es.enter_context(nc.Block())
        c = Ctx(nc, es, block)
        emit_l0ffn(c, x_d, y_d, w1_d, w2_d, b1t_d, b2t_d, bw1_scr, bw2_scr)
        check_sems(c)
    return nc




def emit_rms_chunk(c, x_t, xln_bf, t_f32, nG, Tc, ones_col, ones_row, xsq, ps_ssq,
                   ps_rep2, stat, tok_x, prev, eps_t, gw_t, sq_war=None, out_war=None):
    block = c.block
    s_mm = c.sem("s_mm"); s_act = c.sem("s_act"); s_dve = c.sem("s_dve"); s_ld = c.sem("s_ld")
    r = LNRes()
    ms_t = stat[0:1, 0:Tc]
    sd_t = stat[0:1, Tc:2 * Tc]
    rstd_t = stat[0:1, 2 * Tc:3 * Tc]

    tok_sq = s_act.reserve(nG)
    r.tok_ssq = s_mm.reserve(1)[0]
    r.tok_mean = s_act.reserve(1)[0]
    r.tok_sd = s_act.reserve(1)[0]
    r.tok_rstd = s_dve.reserve(1)[0]
    r.tok_rep = s_mm.reserve(1)[0]
    toks_xln = s_dve.reserve(1 + nG)
    r.tok_xln = toks_xln[-1]
    r.tok_var = r.tok_rstd

    @block.scalar
    def _(scalar):
        scalar.wait_ge(s_ld.h, tok_x)
        if prev is not None:
            scalar.wait_ge(s_mm.h, prev.tok_ssq)
        if sq_war is not None:
            scalar.wait_ge(sq_war[0].h, sq_war[1])
        for g in range(nG):
            s_act.tag(scalar.square(xsq[:, g, :], x_t[:, g, :]))

    @block.tensor
    def _(tensor):
        if prev is not None:
            tensor.wait_ge(s_act.h, prev.tok_mean)
        for g in range(nG):
            tensor.wait_ge(s_act.h, tok_sq[g])
            ins = tensor.matmul(ps_ssq[0:1, :], ones_col[:, 0:1], xsq[:, g, :],
                                start=(g == 0), stop=(g == nG - 1))
        s_mm.tag(ins)

    @block.scalar
    def _(scalar):
        scalar.wait_ge(s_mm.h, r.tok_ssq)
        s_act.tag(scalar.mul(ms_t, ps_ssq[0:1, :], 1.0 / DM))
        scalar.wait_ge(s_act.h, r.tok_mean)
        s_act.tag(scalar.activation(sd_t, ms_t, AF.Sqrt, bias=eps_t[0:1, 0:1]))

    @block.vector
    def _(vector):
        vector.wait_ge(s_act.h, r.tok_sd)
        s_dve.tag(vector.reciprocal(rstd_t, sd_t))

    @block.tensor
    def _(tensor):
        tensor.wait_ge(s_dve.h, r.tok_rstd)
        if prev is not None:
            tensor.wait_ge(s_dve.h, prev.tok_xln)
        ins = tensor.matmul(ps_rep2[:, 0:Tc], ones_row[0:1, :], rstd_t, start=True, stop=True)
        s_mm.tag(ins)

    @block.vector
    def _(vector):
        vector.wait_ge(s_mm.h, r.tok_rep)
        if out_war is not None:
            vector.wait_ge(out_war[0].h, out_war[1])
        rrep_b = ps_rep2[:, 0:Tc].unsqueeze(1).broadcast_to([128, nG, Tc])
        s_dve.tag(vector.tensor_tensor(t_f32[:, :, :], x_t[:, :, :], rrep_b, ALU.mult))
        vector.wait_ge(s_dve.h, toks_xln[0])
        for g in range(nG):
            s_dve.tag(vector.tensor_scalar(xln_bf[:, g, :], t_f32[:, g, :],
                                           gw_t[:, g:g + 1], 1.0, ALU.mult, ALU.mult))
    return r


def emit_mamba(c, x_d, y_d, P, scr, norm, tag, ydbg=None):
    block = c.block
    s_ld = c.sem("s_ld"); s_gp = c.sem("s_gp")
    s_act = c.sem("s_act"); s_dve = c.sem("s_dve"); s_mm = c.sem("s_mm")
    s_st = c.sem("s_st")
    Tc = 256
    NCH = T // Tc
    nG = 8
    nE = 16

    convw = c.sb(f"{tag}_convw", [128, 16, 4], F32)
    convb = c.sb(f"{tag}_convb", [128, 16], F32)
    bdt = c.sb(f"{tag}_bdt", [128, 16], F32)
    negA = c.sb(f"{tag}_negA", [128, 16, 16], F32)
    aexp = c.sb(f"{tag}_aexp", [128, 16, 16], F32)
    d_t = c.sb(f"{tag}_dt", [128, 16], F32)
    ident = c.sb(f"{tag}_ident", [128, 128], BF16)
    gw_t = c.sb(f"{tag}_gw", [128, 8], F32)
    ones_col = c.sb(f"{tag}_ones_col", [128, 1], F32)
    ones_row = c.sb(f"{tag}_ones_row", [1, 128], F32)
    eps_t = c.sb(f"{tag}_eps", [1, 1], F32)
    hstate = c.sb(f"{tag}_hstate", [128, 16, 16], F32)

    ent0 = dict(ld=s_ld.v, dve=s_dve.v, act=s_act.v, mm=s_mm.v, st=s_st.v, gp=s_gp.v)
    tok_wld = s_gp.reserve(8, 16)[-1]

    @block.gpsimd
    def _(gpsimd):
        gpsimd.wait_ge(s_gp.h, ent0["gp"])
        gpsimd.wait_ge(s_ld.h, ent0["ld"])
        gpsimd.wait_ge(s_dve.h, ent0["dve"])
        gpsimd.wait_ge(s_act.h, ent0["act"])
        gpsimd.wait_ge(s_mm.h, ent0["mm"])
        gpsimd.wait_ge(s_st.h, ent0["st"])
        s_gp.tag(gpsimd.dma_start(out=ident[:, :], in_=P["ident"]), 16)
        s_gp.tag(gpsimd.dma_start(out=convw[:, :, :], in_=P["convw_t"]), 16)
        s_gp.tag(gpsimd.dma_start(out=convb[:, :], in_=P["convb_t"]), 16)
        s_gp.tag(gpsimd.dma_start(out=bdt[:, :], in_=P["bdt_t"]), 16)
        s_gp.tag(gpsimd.dma_start(out=negA[:, :, :], in_=P["alog_t"]), 16)
        s_gp.tag(gpsimd.dma_start(out=d_t[:, :], in_=P["d_t"]), 16)
        s_gp.tag(gpsimd.dma_start(out=gw_t[:, :], in_=P["gw_t"]), 16)
        s_gp.tag(gpsimd.dma_start(out=hstate[:, :, :], in_=P["zero_state"]), 16)

    tok_aexp = s_act.reserve(1)[0]
    tok_anegv = s_dve.reserve(1)[0]
    tok_cst = s_dve.reserve(1)[0]

    @block.scalar
    def _(scalar):
        scalar.wait_ge(s_gp.h, tok_wld)
        s_act.tag(scalar.activation(aexp[:, :, :], negA[:, :, :], AF.Exp))

    @block.vector
    def _(vector):
        vector.wait_ge(s_act.h, tok_aexp)
        s_dve.tag(vector.tensor_scalar(negA[:, :, :], aexp[:, :, :], -1.0, 1.0, ALU.mult, ALU.mult))
        vector.memset(ones_col[:, :], 1.0)
        vector.memset(ones_row[:, :], 1.0)
        s_dve.tag(vector.memset(eps_t[:, :], 1e-5 if norm == "ln" else 1e-6))

    with c.scope() as ca:
        winT = ca.sb("winT", [128, 8, 4096], BF16)
        wxpT = ca.sb("wxpT", [128, 16, 96], F32)
        wdtT = ca.sb("wdtT", [64, 2048], F32)
        ent = dict(ld=s_ld.v, dve=s_dve.v, act=s_act.v, mm=s_mm.v, st=s_st.v)
        tok_wldA = s_gp.reserve(3, 16)[-1]

        @block.gpsimd
        def _(gpsimd):
            gpsimd.wait_ge(s_gp.h, tok_wld)
            gpsimd.wait_ge(s_ld.h, ent["ld"])
            gpsimd.wait_ge(s_st.h, ent["st"])
            gpsimd.wait_ge(s_dve.h, ent["dve"])
            gpsimd.wait_ge(s_act.h, ent["act"])
            gpsimd.wait_ge(s_mm.h, ent["mm"])
            s_gp.tag(gpsimd.dma_start(out=winT[:, :, :], in_=P["winT"].rearrange("(g p) m -> p g m", p=128)), 16)
            s_gp.tag(gpsimd.dma_start(out=wxpT[:, :, :], in_=P["wxpT"].rearrange("(g p) m -> p g m", p=128)), 16)
            s_gp.tag(gpsimd.dma_start(out=wdtT[:, :], in_=P["wdtT"]), 16)

        x_t = ca.sb("x_t", [128, nG, Tc], F32)
        arena = ca.sb("arena", [128, 16, Tc], F32)
        xsq = arena[:, 0:nG, :]
        t_f = arena[:, nG:16, :]
        dt_t = arena[:, 0:16, :]
        xln_bf = ca.sb("xln_bf", [128, nG, Tc], BF16)
        xin_ext = ca.sb("xin_ext", [128, 16, Tc + 3], F32)
        sz_t = ca.sb("sz_t", [128, 16, Tc], BF16)
        cvb = ca.sb("cvb", [128, 16, Tc], F32)
        xc_t = ca.sb("xc_t", [128, 16, Tc], F32)
        xdbl = ca.sb("xdbl", [96, Tc], F32)
        dts = [ca.sb(f"dts{i}", [128, Tc], F32) for i in range(3)]
        stat = ca.sb("stat", [1, 4 * Tc], F32)
        ps_sum = ca.ps("ps_sum", [1, Tc])
        ps_ssq = ca.ps("ps_ssq", [1, Tc])
        ps_rep2 = ca.ps("ps_rep2", [128, 2 * Tc])
        ps_ab = [ca.ps(f"ps_ab{i}", [128, Tc]) for i in range(2)]
        ps_xdbl = ca.ps("ps_xdbl", [96, Tc])

        prev_ln = None
        prev = None
        for ci in range(NCH):
            col = slice(ci * Tc, (ci + 1) * Tc)
            tok_x = s_ld.reserve(1, 16)[0]

            @block.sync
            def _(sync, ci=ci, col=col, tok_x=tok_x, prev=prev):
                if prev is None:
                    sync.wait_ge(s_gp.h, tok_wldA)
                    sync.wait_ge(s_dve.h, tok_cst)
                else:
                    sync.wait_ge(s_mm.h, prev["tok_xz"][-1])
                    sync.wait_ge(s_dve.h, prev["tok_dtdve"][-1])
                    sync.wait_ge(s_act.h, prev["tok_sq_proxy"])
                s_ld.tag(sync.dma_start(out=x_t[:, :, :],
                                        in_=x_d[:, col].rearrange("(g p) t -> p g t", p=128)), 16)

            war = (s_st, prev["tok_stall"]) if prev is not None else None
            if norm == "ln":
                ln = emit_ln_chunk(c, x_t, xln_bf, t_f, nG, Tc, ones_col, ones_row, xsq,
                                   ps_sum, ps_ssq, ps_rep2, stat, tok_x, prev_ln, eps_t,
                                   sq_war=war)
            else:
                ln = emit_rms_chunk(c, x_t, xln_bf, t_f, nG, Tc, ones_col, ones_row, xsq,
                                    ps_ssq, ps_rep2, stat, tok_x, prev_ln, eps_t, gw_t,
                                    sq_war=war)
            prev_ln = ln

            cur = {}
            cur["tok_sq_proxy"] = ln.tok_mean
            cur["tok_xz"] = s_mm.reserve(32)
            cur["tok_evac"] = s_act.reserve(32)
            tok_halo = s_dve.reserve(1)[0]

            @block.vector
            def _(vector, prev=prev):
                if prev is not None:
                    vector.wait_ge(s_dve.h, prev["tok_conv"][-1])
                    s_dve.tag(vector.tensor_copy(xin_ext[:, :, 0:3], xin_ext[:, :, Tc:Tc + 3]))
                else:
                    s_dve.tag(vector.memset(xin_ext[:, :, 0:3], 0.0))

            @block.tensor
            def _(tensor, ln=ln, cur=cur, prev=prev):
                tensor.wait_ge(s_dve.h, ln.tok_xln)
                if prev is None:
                    tensor.wait_ge(s_gp.h, tok_wldA)
                for m in range(32):
                    par = m % 2
                    if m >= 2:
                        tensor.wait_ge(s_act.h, cur["tok_evac"][m - 2])
                    elif prev is not None:
                        tensor.wait_ge(s_act.h, prev["tok_dtevac"][14 + m])
                    for k in range(nG):
                        ins = tensor.matmul(ps_ab[par][:, :], winT[:, k, ts(m)], xln_bf[:, k, :],
                                            start=(k == 0), stop=(k == nG - 1))
                    s_mm.tag(ins)

            @block.scalar
            def _(scalar, cur=cur, tok_halo=tok_halo, prev=prev):
                for m in range(32):
                    par = m % 2
                    scalar.wait_ge(s_mm.h, cur["tok_xz"][m])
                    if m == 0:
                        scalar.wait_ge(s_dve.h, tok_halo)
                        if prev is not None:
                            scalar.wait_ge(s_st.h, prev["tok_stall"])
                    if m < 16:
                        s_act.tag(scalar.copy(xin_ext[:, m, 3:3 + Tc], ps_ab[par][:, :]))
                    else:
                        s_act.tag(scalar.activation(sz_t[:, m - 16, :], ps_ab[par][:, :], AF.Silu))

            cur["tok_conv"] = s_dve.reserve(8)

            @block.vector
            def _(vector, cur=cur):
                tk = cur["tok_conv"]
                tmp = arena[:, 0:16, :]
                vector.wait_ge(s_act.h, cur["tok_evac"][15])
                s_dve.tag(vector.tensor_tensor(
                    cvb[:, :, :], xin_ext[:, :, 0:Tc],
                    convw[:, :, 0:1].broadcast_to([128, 16, Tc]), ALU.mult))
                s_dve.tag(vector.tensor_tensor(
                    tmp, xin_ext[:, :, 1:1 + Tc],
                    convw[:, :, 1:2].broadcast_to([128, 16, Tc]), ALU.mult))
                vector.wait_ge(s_dve.h, tk[1])
                s_dve.tag(vector.tensor_tensor(cvb[:, :, :], cvb[:, :, :], tmp, ALU.add))
                s_dve.tag(vector.tensor_tensor(
                    tmp, xin_ext[:, :, 2:2 + Tc],
                    convw[:, :, 2:3].broadcast_to([128, 16, Tc]), ALU.mult))
                vector.wait_ge(s_dve.h, tk[3])
                s_dve.tag(vector.tensor_tensor(cvb[:, :, :], cvb[:, :, :], tmp, ALU.add))
                s_dve.tag(vector.tensor_tensor(
                    tmp, xin_ext[:, :, 3:3 + Tc],
                    convw[:, :, 3:4].broadcast_to([128, 16, Tc]), ALU.mult))
                vector.wait_ge(s_dve.h, tk[5])
                s_dve.tag(vector.tensor_tensor(cvb[:, :, :], cvb[:, :, :], tmp, ALU.add))
                vector.wait_ge(s_dve.h, tk[6])
                s_dve.tag(vector.tensor_tensor(
                    cvb[:, :, :], cvb[:, :, :],
                    convb[:, :].unsqueeze(2).broadcast_to([128, 16, Tc]), ALU.add))

            tok_cvsilu = s_act.reserve(1)[0]

            @block.scalar
            def _(scalar, cur=cur, prev=prev):
                scalar.wait_ge(s_dve.h, cur["tok_conv"][-1])
                if prev is not None:
                    scalar.wait_ge(s_st.h, prev["tok_stall"])
                s_act.tag(scalar.activation(xc_t[:, :, :], cvb[:, :, :], AF.Silu))

            cur["tok_xdbl_mm"] = s_mm.reserve(1)[0]
            tok_xdbl_cp = s_dve.reserve(1)[0]
            cur["tok_xdbl_cp"] = tok_xdbl_cp

            @block.tensor
            def _(tensor, cur=cur, prev=prev):
                tensor.wait_ge(s_act.h, tok_cvsilu)
                if prev is not None:
                    tensor.wait_ge(s_dve.h, prev["tok_xdbl_cp"])
                for k in range(nE):
                    ins = tensor.matmul(ps_xdbl[:, :], wxpT[:, k, :], xc_t[:, k, :],
                                        start=(k == 0), stop=(k == nE - 1))
                s_mm.tag(ins)

            @block.vector
            def _(vector, cur=cur, prev=prev):
                vector.wait_ge(s_mm.h, cur["tok_xdbl_mm"])
                if prev is not None:
                    vector.wait_ge(s_gp.h, prev["tok_stb"])
                s_dve.tag(vector.tensor_copy(xdbl[:, :], ps_xdbl[:, :]))

            cur["tok_dtm"] = s_mm.reserve(16)
            cur["tok_dtevac"] = s_act.reserve(16)
            cur["tok_dtact"] = s_act.reserve(3)
            cur["tok_dtdve"] = s_dve.reserve(3)

            @block.tensor
            def _(tensor, cur=cur):
                tensor.wait_ge(s_dve.h, cur["tok_xdbl_cp"])
                for m in range(16):
                    par = m % 2
                    if m >= 2:
                        tensor.wait_ge(s_act.h, cur["tok_dtevac"][m - 2])
                    else:
                        tensor.wait_ge(s_act.h, cur["tok_evac"][30 + m])
                    ins = tensor.matmul(ps_ab[par][:, :], wdtT[:, ts(m)], xdbl[0:64, :],
                                        start=True, stop=True)
                    s_mm.tag(ins)

            @block.scalar
            def _(scalar, cur=cur, prev=prev):
                for m in range(16):
                    par = m % 2
                    scalar.wait_ge(s_mm.h, cur["tok_dtm"][m])
                    if m == 0 and prev is not None:
                        scalar.wait_ge(s_st.h, prev["tok_stall"])
                    s_act.tag(scalar.copy(dt_t[:, m, :], ps_ab[par][:, :]))

            @block.vector
            def _(vector, cur=cur):
                vector.wait_ge(s_act.h, cur["tok_dtevac"][-1])
                s_dve.tag(vector.tensor_tensor(
                    dt_t[:, :, :], dt_t[:, :, :],
                    bdt[:, :].unsqueeze(2).broadcast_to([128, 16, Tc]), ALU.add))

            @block.scalar
            def _(scalar, cur=cur):
                scalar.wait_ge(s_dve.h, cur["tok_dtdve"][0])
                s_act.tag(scalar.activation(cvb[:, :, :], dt_t[:, :, :], AF.Abs))
                scalar.wait_ge(s_act.h, cur["tok_dtact"][0])
                s_act.tag(scalar.activation(cvb[:, :, :], cvb[:, :, :], AF.Exp, scale=-1.0))
                scalar.wait_ge(s_act.h, cur["tok_dtact"][1])
                s_act.tag(scalar.activation(cvb[:, :, :], cvb[:, :, :], AF.Ln, bias=1.0))

            @block.vector
            def _(vector, cur=cur):
                rbuf = xin_ext[:, :, 0:Tc]
                vector.wait_ge(s_dve.h, cur["tok_dtdve"][0])
                s_dve.tag(vector.tensor_scalar_max(rbuf, dt_t[:, :, :], 0.0))
                vector.wait_ge(s_dve.h, cur["tok_dtdve"][1])
                vector.wait_ge(s_act.h, cur["tok_dtact"][2])
                s_dve.tag(vector.tensor_tensor(dt_t[:, :, :], rbuf, cvb[:, :, :], ALU.add))

            tok_sta = s_st.reserve(3, 16)
            cur["tok_stall"] = tok_sta[-1]

            @block.sync
            def _(sync, cur=cur, col=col):
                sync.wait_ge(s_act.h, cur["tok_evac"][31])
                s_st.tag(sync.dma_start(out=scr["sz"][:, col].rearrange("(g p) t -> p g t", p=128),
                                        in_=sz_t[:, :, :]), 16)
                sync.wait_ge(s_mm.h, cur["tok_xdbl_mm"])
                s_st.tag(sync.dma_start(out=scr["xc"][:, col].rearrange("(g p) t -> p g t", p=128),
                                        in_=xc_t[:, :, :]), 16)
                sync.wait_ge(s_dve.h, cur["tok_dtdve"][-1])
                s_st.tag(sync.dma_start(out=scr["dt"][:, col].rearrange("(g p) t -> p g t", p=128),
                                        in_=dt_t[:, :, :]), 16)

            tok_stb = s_gp.reserve(1, 16)[0]
            cur["tok_stb"] = tok_stb

            @block.gpsimd
            def _(gpsimd, cur=cur, col=col, prev=prev):
                if prev is not None:
                    gpsimd.wait_ge(s_gp.h, prev["tok_stb"])
                else:
                    gpsimd.wait_ge(s_gp.h, tok_wldA)
                gpsimd.wait_ge(s_dve.h, cur["tok_xdbl_cp"])
                s_gp.tag(gpsimd.dma_start(out=scr["bc"][:, col], in_=xdbl[64:96, :]), 16)

            prev = cur
        tok_a_end = dict(mm=s_mm.v, act=s_act.v, dve=s_dve.v, ld=s_ld.v, gp=s_gp.v,
                         st=s_st.v)

    with c.scope() as cb:
        woutT = cb.sb("woutT", [128, 16, 1024], BF16)
        tok_wldB = s_gp.reserve(1, 16)[0]

        @block.gpsimd
        def _(gpsimd):
            gpsimd.wait_ge(s_gp.h, tok_a_end["gp"])
            gpsimd.wait_ge(s_ld.h, tok_a_end["ld"])
            gpsimd.wait_ge(s_dve.h, tok_a_end["dve"])
            gpsimd.wait_ge(s_act.h, tok_a_end["act"])
            gpsimd.wait_ge(s_mm.h, tok_a_end["mm"])
            gpsimd.wait_ge(s_st.h, tok_a_end["st"])
            s_gp.tag(gpsimd.dma_start(out=woutT[:, :, :], in_=P["woutT"].rearrange("(g p) m -> p g m", p=128)), 16)

        dtc = cb.sb("dtc", [128, 16, Tc], F32)
        y_t = cb.sb("y_t", [128, 8, Tc], F32)
        xo_t = cb.sb("xo_t", [128, 8, Tc], F32)
        xcc = [cb.sb(f"xcc{i}", [128, 16, Tc], F32) for i in range(2)]
        szc = cb.sb("szc", [128, 16, Tc], BF16)
        dtx = cb.sb("dtx", [128, 16, Tc], BF16)
        brep = [cb.sb(f"brep{i}", [128, 16, Tc], BF16) for i in range(2)]
        crep1 = cb.sb("crep1", [128, 16, Tc], BF16)
        crep = [crep1, crep1]
        x_t = cb.sb("x_t", [128, nG, Tc], F32)
        PW = Tc + 1
        dA_t = [cb.sb(f"dA_t{i}", [128, 8, PW], BF16) for i in range(2)]
        dBx_t = [cb.sb(f"dBx_t{i}", [128, 8, PW], BF16) for i in range(2)]
        h_t = [cb.sb(f"h_t{i}", [128, 8, PW], F32) for i in range(2)]
        yg_bf = cb.sb("yg_bf", [128, 16, Tc], BF16)
        prod_bf = [cb.sb(f"prod_bf{i}", [128, 8, Tc], BF16) for i in range(2)]
        ps_y8 = cb.ps("ps_y8", [128, 8, Tc])
        ps_y = [ps_y8[:, i, :] for i in range(8)]
        ps_o = [cb.ps(f"ps_o{i}", [128, Tc]) for i in range(2)]

        tok_colz = c.sem("s_dve").reserve(1)[0]

        @block.vector
        def _(vector):
            vector.wait_ge(s_gp.h, s_gp.v)
            vector.memset(dA_t[0][:, :, 0:1], 0.0)
            s_dve.tag(vector.memset(dA_t[1][:, :, 0:1], 0.0))

        prevb = None
        prevb2 = None
        for ci in range(NCH):
            col = slice(ci * Tc, (ci + 1) * Tc)
            pc = ci % 2
            tok_in = s_ld.reserve(4, 16)[-1]
            tok_g3 = s_ld.reserve(2, 16)[-1]

            @block.sync
            def _(sync, ci=ci, col=col, pc=pc, prevb=prevb, prevb2=prevb2):
                if prevb is None:
                    sync.wait_ge(s_gp.h, tok_a_end["gp"])
                    sync.wait_ge(s_dve.h, tok_a_end["dve"])
                    sync.wait_ge(s_act.h, tok_a_end["act"])
                    sync.wait_ge(s_mm.h, tok_a_end["mm"])
                    sync.wait_ge(s_st.h, tok_a_end["st"])
                else:
                    sync.wait_ge(s_act.h, prevb["tok_exp_last"])
                    sync.wait_ge(s_gpc.h, prevb["tok_dtx"])
                if prevb2 is not None:
                    sync.wait_ge(s_dve.h, prevb2["tok_gate"])
                if prevb is not None:
                    sync.wait_ge(s_mm.h, prevb["tok_ymm_last"])
                s_ld.tag(sync.dma_start(out=dtc[:, :, :], in_=scr["dt"][:, col].rearrange("(g p) t -> p g t", p=128)), 16)
                s_ld.tag(sync.dma_start(out=xcc[pc][:, :, :], in_=scr["xc"][:, col].rearrange("(g p) t -> p g t", p=128)), 16)
                s_ld.tag(sync.dma_start(out=brep[pc][:, :, :], in_=scr["bc"][0:16, col].unsqueeze(0).broadcast_to([128, 16, Tc])), 16)
                s_ld.tag(sync.dma_start(out=crep[pc][:, :, :], in_=scr["bc"][16:32, col].unsqueeze(0).broadcast_to([128, 16, Tc])), 16)
                if prevb is not None:
                    sync.wait_ge(s_dve.h, prevb["tok_gate"])
                    sync.wait_ge(s_dve.h, prevb["tok_res"][-1])
                s_ld.tag(sync.dma_start(out=szc[:, :, :], in_=scr["sz"][:, col].rearrange("(g p) t -> p g t", p=128)), 16)
                s_ld.tag(sync.dma_start(out=x_t[:, :, :], in_=x_d[:, col].rearrange("(g p) t -> p g t", p=128)), 16)

            curb = {}
            s_gpc = c.sem("s_gpc")
            tok_dtx = s_gpc.reserve(1)[0]
            curb["tok_dtx"] = tok_dtx

            @block.gpsimd
            def _(gpsimd, tok_in=tok_in, pc=pc):
                gpsimd.wait_ge(s_ld.h, tok_in)
                s_gpc.tag(gpsimd.tensor_tensor(dtx[:, :, :], dtc[:, :, :], xcc[pc][:, :, :], ALU.mult))

            ev_last = None
            prev_scan = prevb["tok_scan_last"] if prevb is not None else None
            prev_scan2 = prevb["tok_scan_last2"] if prevb is not None else None
            last_ymm = prevb["tok_ymm_last"] if prevb is not None else None
            last_ymm2 = prevb["tok_ymm_last2"] if prevb is not None else None
            pend_prods = prevb["pend_prods"] if prevb is not None else []
            prod_toks = prevb["prod_toks"] if prevb is not None else {}
            pend_ymms = prevb["pend_ymms"] if prevb is not None else []

            def emit_ymm(py):
                @block.tensor
                def _(tensor, py=py):
                    tp = prod_toks[py["s"]]
                    tensor.wait_ge(tp[0].h, tp[1])
                    if py["n"] == 0:
                        if py["half"] == 1:
                            tensor.wait_ge(s_dve.h, py["half_ev0"])
                        elif py["prevb"] is not None:
                            tensor.wait_ge(s_dve.h, py["prevb"]["tok_ev_last"])
                    for b in range(4):
                        ins = tensor.matmul(ps_y8[:, 2 * b:2 * b + 2, :], ident[:, :],
                                            prod_bf[py["pn"]][:, 2 * b:2 * b + 2, :],
                                            start=(py["n"] == 0), stop=(py["n"] == nE - 1))
                    s_mm.tag(ins)

            for half in range(2):
                g0 = half * 8
                half_ev = None
                for n in range(nE):
                    s = ci * 2 * nE + half * nE + n
                    pn = s % 2
                    t_dbx = s_gpc.reserve(1)[0]

                    @block.gpsimd
                    def _(gpsimd, g0=g0, n=n, pn=pn, pc=pc, prev_scan2=prev_scan2):
                        if prev_scan2 is not None:
                            gpsimd.wait_ge(s_dve.h, prev_scan2)
                        s_gpc.tag(gpsimd.tensor_tensor(
                            dBx_t[pn][:, :, 1:Tc + 1], dtx[:, g0:g0 + 8, :],
                            brep[pc][:, n:n + 1, :].broadcast_to([128, 8, Tc]), ALU.mult))

                    while pend_prods and s - pend_prods[0]["s"] >= 2:
                        pp = pend_prods.pop(0)
                        pp["tok"] = (s_gpc, s_gpc.reserve(1)[0])
                        prod_toks[pp["s"]] = pp["tok"]

                        @block.gpsimd
                        def _(gpsimd, pp=pp, pc=pc):
                            gpsimd.wait_ge(s_dve.h, pp["t_scan"])
                            if pp["ymm2"] is not None:
                                gpsimd.wait_ge(s_mm.h, pp["ymm2"])
                            s_gpc.tag(gpsimd.tensor_tensor(
                                prod_bf[pp["pn"]][:, :, :], h_t[pp["pn"]][:, :, 1:Tc + 1],
                                crep[pc][:, pp["n"]:pp["n"] + 1, :].broadcast_to([128, 8, Tc]),
                                ALU.mult))
                    t_exp = s_act.reserve(1)[0]
                    t_colc = s_dve.reserve(1)[0]
                    t_scan = s_dve.reserve(1)
                    t_stc = s_dve.reserve(1)[0]
                    if pn == 0:
                        t_prod = (s_dve, s_dve.reserve(1)[0])
                        prod_toks[s] = t_prod
                    else:
                        t_prod = None

                    if pn == 1:
                        pend_prods.append(dict(s=s, n=n, pn=pn, t_scan=t_scan[-1],
                                               ymm2=last_ymm2))

                    @block.scalar
                    def _(scalar, g0=g0, n=n, pn=pn, half=half, tok_in=tok_in,
                          prev_scan2=prev_scan2):
                        if half == 0 and n == 0:
                            scalar.wait_ge(s_ld.h, tok_in)
                        if prev_scan2 is not None:
                            scalar.wait_ge(s_dve.h, prev_scan2)
                        for j in range(8):
                            ins = scalar.activation(dA_t[pn][:, j, 1:Tc + 1], dtc[:, g0 + j, :],
                                                    AF.Exp, scale=negA[:, g0 + j, n:n + 1])
                        s_act.tag(ins)
                    curb["tok_exp_last"] = t_exp

                    @block.vector
                    def _(vector, g0=g0, n=n, pn=pn, s=s, pc=pc, t_exp=t_exp, t_scan=t_scan,
                          t_dbx=t_dbx, t_colc=t_colc, last_ymm=last_ymm, half=half,
                          prevb=prevb):
                        vector.wait_ge(s_act.h, t_exp)
                        vector.wait_ge(s_gpc.h, t_dbx)
                        pp2 = prod_toks.get(s - 2)
                        if pp2 is not None and pp2[0] is s_gpc:
                            vector.wait_ge(s_gpc.h, pp2[1])
                        s_dve.tag(vector.tensor_copy(dBx_t[pn][:, :, 0:1],
                                                     hstate[:, g0:g0 + 8, n:n + 1]))
                        vector.wait_ge(s_dve.h, t_colc)
                        s_dve.tag(vector.tensor_tensor_scan(
                            h_t[pn].rearrange("p j t -> p (j t)"),
                            dA_t[pn].rearrange("p j t -> p (j t)"),
                            dBx_t[pn].rearrange("p j t -> p (j t)"),
                            0.0, ALU.mult, ALU.add))
                        vector.wait_ge(s_dve.h, t_scan[-1])
                        s_dve.tag(vector.tensor_copy(hstate[:, g0:g0 + 8, n:n + 1],
                                                     h_t[pn][:, :, Tc:Tc + 1]))
                        if pn == 0:
                            if last_ymm is not None:
                                vector.wait_ge(s_mm.h, last_ymm)
                            s_dve.tag(vector.tensor_tensor(
                                prod_bf[pn][:, :, :], h_t[pn][:, :, 1:Tc + 1],
                                crep[pc][:, n:n + 1, :].broadcast_to([128, 8, Tc]), ALU.mult))
                    prev_scan2 = prev_scan
                    prev_scan = t_scan[-1]

                    while pend_ymms and s - pend_ymms[0]["s"] >= 2:
                        emit_ymm(pend_ymms.pop(0))
                    t_ymm = s_mm.reserve(1)[0]
                    pend_ymms.append(dict(n=n, pn=pn, s=s, half=half, half_ev0=ev_last,
                                          prevb=prevb, tok=t_ymm))
                    last_ymm2 = last_ymm
                    last_ymm = t_ymm
                    curb["tok_ymm_last"] = t_ymm

                while pend_prods:
                    pp = pend_prods.pop(0)
                    pp["tok"] = (s_gpc, s_gpc.reserve(1)[0])
                    prod_toks[pp["s"]] = pp["tok"]

                    @block.gpsimd
                    def _(gpsimd, pp=pp, pc=pc):
                        gpsimd.wait_ge(s_dve.h, pp["t_scan"])
                        if pp["ymm2"] is not None:
                            gpsimd.wait_ge(s_mm.h, pp["ymm2"])
                        s_gpc.tag(gpsimd.tensor_tensor(
                            prod_bf[pp["pn"]][:, :, :], h_t[pp["pn"]][:, :, 1:Tc + 1],
                            crep[pc][:, pp["n"]:pp["n"] + 1, :].broadcast_to([128, 8, Tc]),
                            ALU.mult))
                while pend_ymms:
                    emit_ymm(pend_ymms.pop(0))
                t_ev = s_dve.reserve(9)
                ev_last = t_ev[-1]

                @block.vector
                def _(vector, g0=g0, half=half, t_ev=t_ev, tymm=curb["tok_ymm_last"], ci=ci, pc=pc, tok_g3=tok_g3, prevb=prevb):
                    vector.wait_ge(s_mm.h, tymm)
                    vector.wait_ge(s_ld.h, tok_g3)
                    if half == 0 and prevb is not None:
                        vector.wait_ge(s_mm.h, prevb["tok_om"][-1])
                        vector.wait_ge(s_dve.h, prevb["tok_res"][-1])
                    for j in range(8):
                        s_dve.tag(vector.scalar_tensor_tensor(
                            y_t[:, j, :], xcc[pc][:, g0 + j, :], d_t[:, g0 + j:g0 + j + 1],
                            ps_y[j][:, :], ALU.mult, ALU.add))
                    vector.wait_ge(s_dve.h, t_ev[7])
                    s_dve.tag(vector.tensor_tensor(yg_bf[:, g0:g0 + 8, :], y_t[:, :, :],
                                                   szc[:, g0:g0 + 8, :], ALU.mult))
            curb["tok_ev_last"] = ev_last
            curb["tok_scan_last"] = prev_scan
            curb["tok_scan_last2"] = prev_scan2
            curb["tok_ymm_last2"] = last_ymm2
            curb["pend_prods"] = pend_prods
            curb["prod_toks"] = prod_toks
            curb["pend_ymms"] = pend_ymms
            tok_gate = ev_last
            curb["tok_gate"] = tok_gate


            curb["tok_om"] = s_mm.reserve(8)
            curb["tok_res"] = s_dve.reserve(8)
            tok_st = s_st.reserve(1, 16)[0]
            curb["tok_st"] = tok_st

            @block.tensor
            def _(tensor, curb=curb, tok_gate=tok_gate, prevb=prevb):
                tensor.wait_ge(s_dve.h, tok_gate)
                if prevb is None:
                    tensor.wait_ge(s_gp.h, tok_wldB)
                for m in range(8):
                    par = m % 2
                    if m >= 2:
                        tensor.wait_ge(s_dve.h, curb["tok_res"][m - 2])
                    elif prevb is not None:
                        tensor.wait_ge(s_dve.h, prevb["tok_res"][6 + m])
                    for k in range(nE):
                        ins = tensor.matmul(ps_o[par][:, :], woutT[:, k, ts(m)], yg_bf[:, k, :],
                                            start=(k == 0), stop=(k == nE - 1))
                    s_mm.tag(ins)

            @block.vector
            def _(vector, curb=curb, tok_g3=tok_g3, prevb=prevb):
                vector.wait_ge(s_ld.h, tok_g3)
                if prevb is not None:
                    vector.wait_ge(s_st.h, prevb["tok_st"])
                for m in range(8):
                    par = m % 2
                    vector.wait_ge(s_mm.h, curb["tok_om"][m])
                    s_dve.tag(vector.scalar_tensor_tensor(
                        xo_t[:, m, :], ps_o[par][:, :], 1.0, x_t[:, m, :], ALU.mult, ALU.add))

            if ydbg is not None:
                s_gp.reserve(1, 16)

                @block.gpsimd
                def _(gpsimd, col=col, tok_gate=tok_gate):
                    gpsimd.wait_ge(s_dve.h, tok_gate)
                    s_gp.tag(gpsimd.dma_start(out=ydbg[:, col].rearrange("(g p) t -> p g t", p=128),
                                              in_=yg_bf[:, :, :]), 16)

            @block.sync
            def _(sync, curb=curb, col=col):
                sync.wait_ge(s_dve.h, curb["tok_res"][-1])
                s_st.tag(sync.dma_start(out=y_d[:, col].rearrange("(g p) t -> p g t", p=128),
                                        in_=xo_t[:, :, :]), 16)
            prevb2 = prevb
            prevb = curb

    @block.sync
    def _(sync):
        sync.wait_ge(s_st.h, s_st.v)


def mamba_host_params(W_in, conv_w, conv_b, W_xp, W_dt, b_dt, A_log, Dv, W_out, gw=None):
    d = {}
    d["winT"] = np.ascontiguousarray(np.asarray(W_in, np.float32).T)
    d["wxpT"] = np.ascontiguousarray(np.asarray(W_xp, np.float32).T)
    d["wdtT"] = np.ascontiguousarray(np.asarray(W_dt, np.float32).T)
    d["woutT"] = np.ascontiguousarray(np.asarray(W_out, np.float32).T)
    cw = np.asarray(conv_w, np.float32).reshape(2048, 4)
    d["convw_t"] = np.ascontiguousarray(cw.reshape(16, 128, 4).transpose(1, 0, 2))
    d["convb_t"] = _tile_vec(conv_b, 16)
    d["bdt_t"] = _tile_vec(b_dt, 16)
    d["alog_t"] = np.ascontiguousarray(np.asarray(A_log, np.float32).reshape(16, 128, 16).transpose(1, 0, 2))
    d["d_t"] = _tile_vec(Dv, 16)
    d["ident"] = np.eye(128, dtype=np.float32)
    d["zero_state"] = np.zeros((128, 16, 16), dtype=np.float32)
    d["gw_t"] = _tile_vec(gw if gw is not None else np.ones(1024, np.float32), 8)
    return d


def build_mamba(norm, debug=False):
    nc = bass.Bass("TRN2", target_bir_lowering=False)
    x_d = nc.dram_tensor("x", [DM, T], F32, kind="ExternalInput").ap()
    y_d = nc.dram_tensor("y", [DM, T], F32, kind="ExternalOutput").ap()
    names = ["winT", "wxpT", "wdtT", "woutT", "convw_t", "convb_t", "bdt_t",
             "alog_t", "d_t", "ident", "gw_t", "zero_state"]
    shapes = dict(winT=[1024, 4096], wxpT=[2048, 96], wdtT=[64, 2048], woutT=[2048, 1024],
                  convw_t=[128, 16, 4], convb_t=[128, 16], bdt_t=[128, 16],
                  alog_t=[128, 16, 16], d_t=[128, 16], ident=[128, 128], gw_t=[128, 8],
                  zero_state=[128, 16, 16])
    P = {nm: nc.dram_tensor(nm, shapes[nm], F32, kind="ExternalInput").ap() for nm in names}
    kd = "ExternalOutput" if debug else "Internal"
    ydbg = nc.dram_tensor("ydbg", [2048, T], BF16, kind=kd).ap() if debug else None
    scr = dict(
        xc=nc.dram_tensor("scr_xc", [2048, T], F32, kind=kd).ap(),
        sz=nc.dram_tensor("scr_sz", [2048, T], BF16, kind=kd).ap(),
        dt=nc.dram_tensor("scr_dt", [2048, T], F32, kind=kd).ap(),
        bc=nc.dram_tensor("scr_bc", [32, T], BF16, kind=kd).ap(),
    )
    with ExitStack() as es:
        block = es.enter_context(nc.Block())
        c = Ctx(nc, es, block)
        emit_mamba(c, x_d, y_d, P, scr, norm, "mb", ydbg=ydbg)
        check_sems(c)
    return nc


def l0attn_inmaps(inp, xs):
    P = mamba_host_params(inp["L0_attn_W_in"], inp["L0_attn_conv_w"], inp["L0_attn_conv_b"],
                          inp["L0_attn_W_xp"], inp["L0_attn_W_dt"], inp["L0_attn_b_dt"],
                          inp["L0_attn_A_log"], inp["L0_attn_D"], inp["L0_attn_W_out"])
    return [dict(x=xs[cid], **P) for cid in range(NC8)]



def emit_l3ffn(c, x_d, y_d, P, tag):
    block = c.block
    s_ld = c.sem("s_ld"); s_gp = c.sem("s_gp")
    s_act = c.sem("s_act"); s_dve = c.sem("s_dve"); s_mm = c.sem("s_mm")
    s_st = c.sem("s_st")
    Tc = 256
    NCH = T // Tc
    nG = 8

    wrT = c.sb(f"{tag}_wrT", [128, 8, 1024], BF16)
    wkT = c.sb(f"{tag}_wkT", [128, 8, 1024], BF16)
    wvT = c.sb(f"{tag}_wvT", [128, 8, 1024], BF16)
    mur = c.sb(f"{tag}_mur", [128, 8], F32)
    muk = c.sb(f"{tag}_muk", [128, 8], F32)
    ones_col = c.sb(f"{tag}_onc", [128, 1], F32)
    ones_row = c.sb(f"{tag}_onr", [1, 128], F32)
    eps_t = c.sb(f"{tag}_eps", [1, 1], F32)

    tok_wld = s_gp.reserve(5, 16)[-1]

    @block.gpsimd
    def _(gpsimd):
        s_gp.tag(gpsimd.dma_start(out=wrT[:, :, :], in_=P["wrT"].rearrange("(g p) m -> p g m", p=128)), 16)
        s_gp.tag(gpsimd.dma_start(out=wkT[:, :, :], in_=P["wkT"].rearrange("(g p) m -> p g m", p=128)), 16)
        s_gp.tag(gpsimd.dma_start(out=wvT[:, :, :], in_=P["wvT"].rearrange("(g p) m -> p g m", p=128)), 16)
        s_gp.tag(gpsimd.dma_start(out=mur[:, :], in_=P["mur_t"]), 16)
        s_gp.tag(gpsimd.dma_start(out=muk[:, :], in_=P["muk_t"]), 16)

    tok_cst = s_dve.reserve(1)[0]

    @block.vector
    def _(vector):
        vector.memset(ones_col[:, :], 1.0)
        vector.memset(ones_row[:, :], 1.0)
        s_dve.tag(vector.memset(eps_t[:, :], 1e-5))

    x_t = c.sb(f"{tag}_x", [128, nG, Tc], F32)
    xsq = c.sb(f"{tag}_xsq", [128, nG, Tc], F32)
    t_f = c.sb(f"{tag}_tf", [128, nG, Tc], F32)
    xln_ext = c.sb(f"{tag}_xle", [128, nG, Tc + 1], F32)
    xr_bf = c.sb(f"{tag}_xr", [128, nG, Tc], BF16)
    xk_bf = c.sb(f"{tag}_xk", [128, nG, Tc], BF16)
    r_t = c.sb(f"{tag}_r", [128, nG, Tc], F32)
    ka_bf = c.sb(f"{tag}_ka", [128, nG, Tc], BF16)
    vv_t = c.sb(f"{tag}_vv", [128, Tc], F32)
    xo_t = c.sb(f"{tag}_xo", [128, nG, Tc], F32)
    stat = c.sb(f"{tag}_st", [1, 4 * Tc], F32)
    ps_sum = c.ps(f"{tag}_psum", [1, Tc])
    ps_ssq = c.ps(f"{tag}_pssq", [1, Tc])
    ps_rep2 = c.ps(f"{tag}_prep2", [128, 2 * Tc])
    ps_ab = [c.ps(f"{tag}_pab{i}", [128, Tc]) for i in range(2)]

    xln = xln_ext[:, :, 1:Tc + 1]
    xlnp = xln_ext[:, :, 0:Tc]

    prev_ln = None
    prev = None
    for ci in range(NCH):
        col = slice(ci * Tc, (ci + 1) * Tc)
        tok_x = s_ld.reserve(1, 16)[0]

        @block.sync
        def _(sync, col=col, prev=prev):
            if prev is None:
                sync.wait_ge(s_gp.h, tok_wld)
                sync.wait_ge(s_dve.h, tok_cst)
            else:
                sync.wait_ge(s_mm.h, prev["tok_km"][-1])
                sync.wait_ge(s_dve.h, prev["tok_res"][-1])
                sync.wait_ge(s_act.h, prev["tok_ract"][-1])
            s_ld.tag(sync.dma_start(out=x_t[:, :, :],
                                    in_=x_d[:, col].rearrange("(g p) t -> p g t", p=128)), 16)

        tok_halo = s_dve.reserve(1)[0]

        @block.vector
        def _(vector, prev=prev, prev_ln=prev_ln):
            if prev is None:
                s_dve.tag(vector.memset(xln_ext[:, :, 0:1], 0.0))
            else:
                vector.wait_ge(s_dve.h, prev["tok_mix"][-1])
                s_dve.tag(vector.tensor_copy(xln_ext[:, :, 0:1], xln_ext[:, :, Tc:Tc + 1]))

        ln = emit_ln_chunk(c, x_t, xln, t_f, nG, Tc, ones_col, ones_row, xsq,
                           ps_sum, ps_ssq, ps_rep2, stat, tok_x, prev_ln, eps_t,
                           out_f32=True, extra_dve_wait=tok_halo)
        prev_ln = ln

        cur = {}
        cur["tok_mix"] = s_dve.reserve(1 + 16)

        @block.vector
        def _(vector, ln=ln, cur=cur):
            vector.wait_ge(s_dve.h, ln.tok_xln)
            tk = cur["tok_mix"]
            s_dve.tag(vector.tensor_tensor(t_f[:, :, :], xln, xlnp, ALU.subtract))
            vector.wait_ge(s_dve.h, tk[0])
            for g in range(nG):
                s_dve.tag(vector.scalar_tensor_tensor(xr_bf[:, g, :], t_f[:, g, :],
                                                      mur[:, g:g + 1], xlnp[:, g, :],
                                                      ALU.mult, ALU.add))
                s_dve.tag(vector.scalar_tensor_tensor(xk_bf[:, g, :], t_f[:, g, :],
                                                      muk[:, g:g + 1], xlnp[:, g, :],
                                                      ALU.mult, ALU.add))

        cur["tok_rm"] = s_mm.reserve(8)
        cur["tok_ract"] = s_act.reserve(8)
        cur["tok_km"] = s_mm.reserve(8)
        cur["tok_kact"] = s_act.reserve(16)

        @block.tensor
        def _(tensor, cur=cur, prev=prev):
            tensor.wait_ge(s_dve.h, cur["tok_mix"][-1])
            for m in range(nG):
                par = m % 2
                if m >= 2:
                    tensor.wait_ge(s_act.h, cur["tok_ract"][m - 2])
                elif prev is not None:
                    tensor.wait_ge(s_dve.h, prev["tok_res"][6 + m])
                for k in range(nG):
                    ins = tensor.matmul(ps_ab[par][:, :], wrT[:, k, ts(m)], xr_bf[:, k, :],
                                        start=(k == 0), stop=(k == nG - 1))
                s_mm.tag(ins)

        @block.scalar
        def _(scalar, cur=cur):
            for m in range(nG):
                par = m % 2
                scalar.wait_ge(s_mm.h, cur["tok_rm"][m])
                s_act.tag(scalar.activation(r_t[:, m, :], ps_ab[par][:, :], AF.Sigmoid))

        @block.tensor
        def _(tensor, cur=cur):
            for m in range(nG):
                par = m % 2
                if m >= 2:
                    tensor.wait_ge(s_act.h, cur["tok_kact"][2 * (m - 2)])
                else:
                    tensor.wait_ge(s_act.h, cur["tok_ract"][6 + m])
                for k in range(nG):
                    ins = tensor.matmul(ps_ab[par][:, :], wkT[:, k, ts(m)], xk_bf[:, k, :],
                                        start=(k == 0), stop=(k == nG - 1))
                s_mm.tag(ins)

        @block.scalar
        def _(scalar, cur=cur):
            for m in range(nG):
                par = m % 2
                a = cur["tok_kact"][2 * m:2 * m + 2]
                scalar.wait_ge(s_mm.h, cur["tok_km"][m])
                s_act.tag(scalar.activation(vv_t[:, :], ps_ab[par][:, :], AF.Relu))
                scalar.wait_ge(s_act.h, a[0])
                s_act.tag(scalar.square(ka_bf[:, m, :], vv_t[:, :]))

        cur["tok_vm"] = s_mm.reserve(8)
        _vr = s_dve.reserve(16)
        cur["tok_vdve"] = _vr[0::2]
        cur["tok_res"] = _vr[1::2]

        @block.tensor
        def _(tensor, cur=cur, prev=prev):
            tensor.wait_ge(s_act.h, cur["tok_kact"][-1])
            for m in range(nG):
                par = m % 2
                if m >= 2:
                    tensor.wait_ge(s_dve.h, cur["tok_vdve"][m - 2])
                for k in range(nG):
                    ins = tensor.matmul(ps_ab[par][:, :], wvT[:, k, ts(m)], ka_bf[:, k, :],
                                        start=(k == 0), stop=(k == nG - 1))
                s_mm.tag(ins)

        @block.vector
        def _(vector, cur=cur, tok_x=tok_x, prev=prev):
            vector.wait_ge(s_ld.h, tok_x)
            if prev is not None:
                vector.wait_ge(s_st.h, prev["tok_st"])
            for m in range(nG):
                par = m % 2
                vector.wait_ge(s_mm.h, cur["tok_vm"][m])
                s_dve.tag(vector.tensor_tensor(t_f[:, m, :], r_t[:, m, :], ps_ab[par][:, :], ALU.mult))
                vector.wait_ge(s_dve.h, cur["tok_vdve"][m])
                s_dve.tag(vector.scalar_tensor_tensor(xo_t[:, m, :], t_f[:, m, :], 0.5,
                                                      x_t[:, m, :], ALU.mult, ALU.add))

        tok_st = s_st.reserve(1, 16)[0]
        cur["tok_st"] = tok_st

        @block.sync
        def _(sync, cur=cur, col=col):
            sync.wait_ge(s_dve.h, cur["tok_res"][-1])
            s_st.tag(sync.dma_start(out=y_d[:, col].rearrange("(g p) t -> p g t", p=128),
                                    in_=xo_t[:, :, :]), 16)
        prev = cur

    @block.sync
    def _(sync):
        sync.wait_ge(s_st.h, s_st.v)


def build_l3ffn():
    nc = bass.Bass("TRN2", target_bir_lowering=False)
    x_d = nc.dram_tensor("x", [DM, T], F32, kind="ExternalInput").ap()
    y_d = nc.dram_tensor("y", [DM, T], F32, kind="ExternalOutput").ap()
    names = dict(wrT=[1024, 1024], wkT=[1024, 1024], wvT=[1024, 1024],
                 mur_t=[128, 8], muk_t=[128, 8])
    P = {nm: nc.dram_tensor(nm, sh, F32, kind="ExternalInput").ap() for nm, sh in names.items()}
    with ExitStack() as es:
        block = es.enter_context(nc.Block())
        c = Ctx(nc, es, block)
        emit_l3ffn(c, x_d, y_d, P, "l3f")
        check_sems(c)
    return nc


def l3ffn_inmaps(inp, xs):
    P = dict(wrT=np.ascontiguousarray(inp["L3_ffn_W_r"].T),
             wkT=np.ascontiguousarray(inp["L3_ffn_W_k"].T),
             wvT=np.ascontiguousarray(inp["L3_ffn_W_v"].T),
             mur_t=_tile_vec(inp["L3_ffn_mu_r"], 8),
             muk_t=_tile_vec(inp["L3_ffn_mu_k"], 8))
    return [dict(x=xs[cid], **P) for cid in range(NC8)]



def emit_l2ffn(c, x_d, y_d, P, tag, bisect=0):
    block = c.block
    s_ld = c.sem("s_ld"); s_gp = c.sem("s_gp")
    s_act = c.sem("s_act"); s_dve = c.sem("s_dve"); s_mm = c.sem("s_mm")
    s_st = c.sem("s_st")
    Tc = 256
    NCH = T // Tc
    nG = 8
    BIG = 60000.0

    w1T = c.sb(f"{tag}_w1T", [128, 8, 512], BF16)
    w2T = c.sb(f"{tag}_w2T", [128, 4, 1024], BF16)
    wgT = c.sb(f"{tag}_wgT", [128, 8, 8], F32)
    bg_row = c.sb(f"{tag}_bg", [1, 8], F32)
    sel = c.sb(f"{tag}_sel", [8, 4, 128], F32)
    identf = c.sb(f"{tag}_idf", [128, 128], F32)
    ones_col = c.sb(f"{tag}_onc", [128, 1], F32)
    ones_row = c.sb(f"{tag}_onr", [1, 128], F32)
    eps_t = c.sb(f"{tag}_eps", [1, 1], F32)

    tok_wld = s_gp.reserve(6, 16)[-1]

    @block.gpsimd
    def _(gpsimd):
        s_gp.tag(gpsimd.dma_start(out=w1T[:, :, :], in_=P["w1eT"].rearrange("(g p) m -> p g m", p=128)), 16)
        s_gp.tag(gpsimd.dma_start(out=w2T[:, :, :], in_=P["w2eT"].rearrange("(g p) m -> p g m", p=128)), 16)
        s_gp.tag(gpsimd.dma_start(out=wgT[:, :, :], in_=P["wgT"].rearrange("(g p) m -> p g m", p=128)), 16)
        s_gp.tag(gpsimd.dma_start(out=bg_row[:, :], in_=P["bg_row"]), 16)
        s_gp.tag(gpsimd.dma_start(out=sel[:, :, :], in_=P["sel"]), 16)
        s_gp.tag(gpsimd.dma_start(out=identf[:, :], in_=P["identf"]), 16)

    tok_cst = s_dve.reserve(1)[0]

    @block.vector
    def _(vector):
        vector.memset(ones_col[:, :], 1.0)
        vector.memset(ones_row[:, :], 1.0)
        s_dve.tag(vector.memset(eps_t[:, :], 1e-5))

    x_t = c.sb(f"{tag}_x", [128, nG, Tc], F32)
    xsq = c.sb(f"{tag}_xsq", [128, nG, Tc], F32)
    t_f = c.sb(f"{tag}_tf", [128, nG, Tc], F32)
    xln_f = c.sb(f"{tag}_xlf", [128, nG, Tc], F32)
    xln_bf = c.sb(f"{tag}_xlb", [128, nG, Tc], BF16)
    h_bf = c.sb(f"{tag}_h", [128, 4, Tc], BF16)
    ht_bf = c.sb(f"{tag}_ht", [128, 4, Tc], BF16)
    lgt = [c.sb(f"{tag}_lg{i}", [128, 8], F32) for i in range(2)]
    rsc = c.sb(f"{tag}_rsc", [128, 24], F32)
    rs1 = c.sb(f"{tag}_rs1", [128, 6], F32)
    m_tok = [c.sb(f"{tag}_mtk{i}", [128, 8], F32) for i in range(2)]
    m_fm = c.sb(f"{tag}_mfm", [8, 2, 128], F32)
    xo_t = c.sb(f"{tag}_xo", [128, nG, Tc], F32)
    stat = c.sb(f"{tag}_st", [1, 4 * Tc], F32)
    ps_sum = c.ps(f"{tag}_psum", [1, Tc])
    ps_ssq = c.ps(f"{tag}_pssq", [1, Tc])
    ps_rep2 = c.ps(f"{tag}_prep2", [128, 2 * Tc])
    ps_big = c.ps(f"{tag}_pbig", [128, Tc])
    ps_mr = c.ps(f"{tag}_pmrj", [128, Tc])
    m_rep = c.sb(f"{tag}_mrep", [128, 4, Tc], F32)
    ps_ab = [c.ps(f"{tag}_pab{i}", [128, Tc]) for i in range(2)]

    prev_ln = None
    prev = None
    for ci in range(NCH):
        col = slice(ci * Tc, (ci + 1) * Tc)
        tok_x = s_ld.reserve(1, 16)[0]

        @block.sync
        def _(sync, col=col, prev=prev):
            if prev is None:
                sync.wait_ge(s_gp.h, tok_wld)
                sync.wait_ge(s_dve.h, tok_cst)
            else:
                sync.wait_ge(s_mm.h, prev["tok_dm"][-1])
                sync.wait_ge(s_dve.h, prev["tok_res"][-1])
                sync.wait_ge(s_act.h, prev["tok_hact"][-1])
            s_ld.tag(sync.dma_start(out=x_t[:, :, :],
                                    in_=x_d[:, col].rearrange("(g p) t -> p g t", p=128)), 16)

        ln = emit_ln_chunk(c, x_t, xln_f, t_f, nG, Tc, ones_col, ones_row, xsq,
                           ps_sum, ps_ssq, ps_rep2, stat, tok_x, prev_ln, eps_t)
        prev_ln = ln
        cur = {}
        tok_cast = s_act.reserve(1)[0]

        @block.scalar
        def _(scalar, ln=ln):
            scalar.wait_ge(s_dve.h, ln.tok_xln)
            s_act.tag(scalar.copy(xln_bf[:, :, :], xln_f[:, :, :]))

        if bisect != 1:
            cur["tok_lgm"] = [None, None]
            cur["tok_lgev"] = [None, None]
            cur["tok_rt"] = [None] * 18
            cur["tok_rtact"] = [None, None]
            cur["tok_mtr"] = [None, None]
            cur["tok_mev"] = [None, None]

        for sub in range(2):
            if bisect == 1:
                break
            cur["tok_lgm"][sub] = s_mm.reserve(1)[0]
            cur["tok_lgev"][sub] = s_dve.reserve(1)[0]
            if bisect in (0, 3, 4):
                cur["tok_rt"][9 * sub:9 * sub + 9] = s_dve.reserve(9)
                cur["tok_rtact"][sub] = s_act.reserve(1)[0]
            if bisect in (0, 4):
                cur["tok_mtr"][sub] = s_mm.reserve(1)[0]
                cur["tok_mev"][sub] = s_dve.reserve(1)[0]
            tsl = slice(sub * 128, (sub + 1) * 128)

            @block.tensor
            def _(tensor, sub=sub, tsl=tsl, ln=ln, cur=cur, prev=prev):
                tensor.wait_ge(s_dve.h, ln.tok_xln)
                if sub == 1:
                    if bisect in (0, 4):
                        tensor.wait_ge(s_dve.h, cur["tok_mev"][0])
                    else:
                        tensor.wait_ge(s_dve.h, cur["tok_lgev"][0])
                elif prev is not None:
                    if bisect in (0, 4):
                        tensor.wait_ge(s_dve.h, prev["tok_mev"][1])
                    tensor.wait_ge(s_dve.h, prev["tok_lgev"][1])
                for k in range(nG):
                    tensor.matmul(ps_big[:, 0:8], xln_f[:, k, tsl], wgT[:, k, :],
                                  start=(k == 0), stop=False)
                ins = tensor.matmul(ps_big[:, 0:8], ones_row[0:1, :], bg_row[0:1, :],
                                    start=False, stop=True)
                s_mm.tag(ins)

            @block.vector
            def _(vector, sub=sub, cur=cur):
                vector.wait_ge(s_mm.h, cur["tok_lgm"][sub])
                s_dve.tag(vector.tensor_copy(lgt[sub][:, :], ps_big[:, 0:8]))

            if bisect == 2:
                continue

            @block.vector
            def _(vector, sub=sub, cur=cur):
                tk = cur["tok_rt"][9 * sub:9 * sub + 9]
                vector.wait_ge(s_dve.h, cur["tok_lgev"][sub])
                s_dve.tag(vector.tensor_reduce(rs1[:, 0:1], lgt[sub][:, :],
                                               mybir.AxisListType.X, ALU.max))
                vector.wait_ge(s_dve.h, tk[0])
                s_dve.tag(vector.tensor_scalar(rsc[:, 0:8], lgt[sub][:, :],
                                               rs1[:, 0:1], 1.0, ALU.is_ge, ALU.mult))
                vector.wait_ge(s_dve.h, tk[1])
                s_dve.tag(vector.scalar_tensor_tensor(rsc[:, 8:16], rsc[:, 0:8],
                                                      -BIG, lgt[sub][:, :], ALU.mult, ALU.add))
                vector.wait_ge(s_dve.h, tk[2])
                s_dve.tag(vector.tensor_reduce(rs1[:, 1:2], rsc[:, 8:16],
                                               mybir.AxisListType.X, ALU.max))
                vector.wait_ge(s_dve.h, tk[3])
                s_dve.tag(vector.tensor_scalar(rsc[:, 16:24], rsc[:, 8:16],
                                               rs1[:, 1:2], 1.0, ALU.is_ge, ALU.mult))
                vector.wait_ge(s_dve.h, tk[4])
                s_dve.tag(vector.tensor_tensor(rs1[:, 2:3], rs1[:, 0:1],
                                               rs1[:, 1:2], ALU.subtract))

            @block.scalar
            def _(scalar, sub=sub, cur=cur):
                scalar.wait_ge(s_dve.h, cur["tok_rt"][9 * sub + 5])
                s_act.tag(scalar.activation(rs1[:, 3:4], rs1[:, 2:3], AF.Sigmoid))

            @block.vector
            def _(vector, sub=sub, cur=cur):
                tk = cur["tok_rt"][9 * sub:9 * sub + 9]
                vector.wait_ge(s_act.h, cur["tok_rtact"][sub])
                s_dve.tag(vector.tensor_scalar(rs1[:, 4:5], rs1[:, 3:4],
                                               -1.0, 1.0, ALU.mult, ALU.add))
                vector.wait_ge(s_dve.h, tk[6])
                s_dve.tag(vector.tensor_scalar(m_tok[sub][:, :], rsc[:, 0:8],
                                               rs1[:, 3:4], 1.0, ALU.mult, ALU.mult))
                vector.wait_ge(s_dve.h, tk[7])
                s_dve.tag(vector.scalar_tensor_tensor(m_tok[sub][:, :], rsc[:, 16:24],
                                                      rs1[:, 4:5], m_tok[sub][:, :],
                                                      ALU.mult, ALU.add))

            if bisect == 3:
                continue

            @block.tensor
            def _(tensor, sub=sub, cur=cur):
                tensor.wait_ge(s_dve.h, cur["tok_rt"][9 * sub + 8])
                s_mm.tag(tensor.transpose(ps_big[0:8, 128:256], m_tok[sub][:, :], identf[:, :]))

            @block.vector
            def _(vector, sub=sub, cur=cur):
                vector.wait_ge(s_mm.h, cur["tok_mtr"][sub])
                s_dve.tag(vector.tensor_copy(m_fm[:, sub, :], ps_big[0:8, 128:256]))


        if bisect == 0:
            cur["tok_mrj"] = s_mm.reserve(4)
            cur["tok_mrd"] = s_dve.reserve(4)

            @block.tensor
            def _(tensor, cur=cur, prev=prev):
                tensor.wait_ge(s_dve.h, cur["tok_mev"][1])
                for j in range(4):
                    if j > 0:
                        tensor.wait_ge(s_dve.h, cur["tok_mrd"][j - 1])
                    elif prev is not None:
                        tensor.wait_ge(s_dve.h, prev["tok_mrd"][3])
                    ins = tensor.matmul(ps_mr[:, :], sel[:, j, :],
                                        m_fm[:, :, :].rearrange("e s t -> e (s t)"),
                                        start=True, stop=True)
                    s_mm.tag(ins)

            @block.vector
            def _(vector, cur=cur):
                for j in range(4):
                    vector.wait_ge(s_mm.h, cur["tok_mrj"][j])
                    s_dve.tag(vector.tensor_copy(m_rep[:, j, :], ps_mr[:, :]))

        cur["tok_hm"] = s_mm.reserve(4)
        cur["tok_hact"] = s_act.reserve(4)
        cur["tok_htd"] = s_dve.reserve(4)

        @block.tensor
        def _(tensor, cur=cur, prev=prev):
            tensor.wait_ge(s_act.h, cur["tok_cast"] if False else tok_cast)
            for m in range(4):
                par = m % 2
                if m >= 2:
                    tensor.wait_ge(s_act.h, cur["tok_hact"][m - 2])
                elif prev is not None:
                    tensor.wait_ge(s_dve.h, prev["tok_res"][6 + m])
                for k in range(nG):
                    ins = tensor.matmul(ps_ab[par][:, :], w1T[:, k, ts(m)], xln_bf[:, k, :],
                                        start=(k == 0), stop=(k == nG - 1))
                s_mm.tag(ins)

        @block.scalar
        def _(scalar, cur=cur):
            for m in range(4):
                par = m % 2
                scalar.wait_ge(s_mm.h, cur["tok_hm"][m])
                s_act.tag(scalar.activation(h_bf[:, m, :], ps_ab[par][:, :], AF.Gelu))

        @block.vector
        def _(vector, cur=cur):
            for m in range(4):
                vector.wait_ge(s_act.h, cur["tok_hact"][m])
                if bisect != 0:
                    s_dve.tag(vector.tensor_copy(ht_bf[:, m, :], h_bf[:, m, :]))
                    continue
                if m == 0:
                    vector.wait_ge(s_dve.h, cur["tok_mrd"][-1])
                s_dve.tag(vector.tensor_tensor(ht_bf[:, m, :], h_bf[:, m, :],
                                               m_rep[:, m, :], ALU.mult))

        cur["tok_dm"] = s_mm.reserve(8)
        cur["tok_res"] = s_dve.reserve(8)

        @block.tensor
        def _(tensor, cur=cur, prev=prev):
            tensor.wait_ge(s_dve.h, cur["tok_htd"][-1])
            for m in range(nG):
                par = m % 2
                if m >= 2:
                    tensor.wait_ge(s_dve.h, cur["tok_res"][m - 2])
                elif prev is not None:
                    tensor.wait_ge(s_dve.h, prev["tok_res"][6 + m])
                for k in range(4):
                    ins = tensor.matmul(ps_ab[par][:, :], w2T[:, k, ts(m)], ht_bf[:, k, :],
                                        start=(k == 0), stop=(k == 3))
                s_mm.tag(ins)

        @block.vector
        def _(vector, cur=cur, tok_x=tok_x, prev=prev):
            vector.wait_ge(s_ld.h, tok_x)
            if prev is not None:
                vector.wait_ge(s_st.h, prev["tok_st"])
            for m in range(nG):
                par = m % 2
                vector.wait_ge(s_mm.h, cur["tok_dm"][m])
                s_dve.tag(vector.scalar_tensor_tensor(xo_t[:, m, :], ps_ab[par][:, :], 1.0,
                                                      x_t[:, m, :], ALU.mult, ALU.add))

        tok_st = s_st.reserve(1, 16)[0]
        cur["tok_st"] = tok_st

        @block.sync
        def _(sync, cur=cur, col=col):
            sync.wait_ge(s_dve.h, cur["tok_res"][-1])
            s_st.tag(sync.dma_start(out=y_d[:, col].rearrange("(g p) t -> p g t", p=128),
                                    in_=xo_t[:, :, :]), 16)
        prev = cur

    @block.sync
    def _(sync):
        sync.wait_ge(s_st.h, s_st.v)


def build_l2ffn(bisect=0):
    nc = bass.Bass("TRN2", target_bir_lowering=False)
    x_d = nc.dram_tensor("x", [DM, T], F32, kind="ExternalInput").ap()
    y_d = nc.dram_tensor("y", [DM, T], F32, kind="ExternalOutput").ap()
    names = dict(w1eT=[1024, 512], w2eT=[512, 1024], wgT=[1024, 8], bg_row=[1, 8],
                 sel=[8, 4, 128], identf=[128, 128])
    P = {nm: nc.dram_tensor(nm, sh, F32, kind="ExternalInput").ap() for nm, sh in names.items()}
    with ExitStack() as es:
        block = es.enter_context(nc.Block())
        c = Ctx(nc, es, block)
        emit_l2ffn(c, x_d, y_d, P, "l2f", bisect=bisect)
        check_sems(c)
    return nc


def l2ffn_inmaps(inp, xs):
    W1 = np.asarray(inp["L2_ffn_W1"], np.float32)
    W2 = np.asarray(inp["L2_ffn_W2"], np.float32)
    sel = np.zeros((8, 4, 128), np.float32)
    for j in range(4):
        for p in range(128):
            sel[(j * 128 + p) // 64, j, p] = 1.0
    P = dict(w1eT=np.ascontiguousarray(W1.transpose(2, 0, 1).reshape(1024, 512)),
             w2eT=np.ascontiguousarray(W2.transpose(0, 2, 1).reshape(8 * 64, 1024)),
             wgT=np.ascontiguousarray(np.asarray(inp["L2_ffn_Wg"], np.float32).T),
             bg_row=np.ascontiguousarray(np.asarray(inp["L2_ffn_bg"], np.float32).reshape(1, 8)),
             sel=sel, identf=np.eye(128, dtype=np.float32))
    return [dict(x=xs[cid], **P) for cid in range(NC8)]


def emit_final_rms(c, x_d, y_d, fw_t_d, tag, resid_d=None):
    block = c.block
    s_ld = c.sem("s_ld"); s_gp = c.sem("s_gp")
    s_act = c.sem("s_act"); s_dve = c.sem("s_dve"); s_mm = c.sem("s_mm")
    s_st = c.sem("s_st")
    Tc = 256
    NCH = T // Tc
    nG = 8
    with c.scope() as cf:
        gw_t = cf.sb(f"{tag}_gw", [128, 8], F32)
        ones_col = cf.sb(f"{tag}_onc", [128, 1], F32)
        ones_row = cf.sb(f"{tag}_onr", [1, 128], F32)
        eps_t = cf.sb(f"{tag}_eps", [1, 1], F32)
        x_t = cf.sb(f"{tag}_x", [128, nG, Tc], F32)
        xsq = cf.sb(f"{tag}_xsq", [128, nG, Tc], F32)
        t_f = cf.sb(f"{tag}_tf", [128, nG, Tc], F32)
        y_t = cf.sb(f"{tag}_y", [128, nG, Tc], F32)
        rs_t = cf.sb(f"{tag}_rs", [128, nG, Tc], F32) if resid_d is not None else None
        stat = cf.sb(f"{tag}_st", [1, 4 * Tc], F32)
        ps_ssq = cf.ps(f"{tag}_pssq", [1, Tc])
        ps_rep2 = cf.ps(f"{tag}_prep2", [128, 2 * Tc])
        ent = dict(ld=s_ld.v, dve=s_dve.v, act=s_act.v, mm=s_mm.v, st=s_st.v)
        tok_gw = s_gp.reserve(1, 16)[0]

        @block.gpsimd
        def _(gpsimd):
            gpsimd.wait_ge(s_ld.h, ent["ld"])
            gpsimd.wait_ge(s_st.h, ent["st"])
            gpsimd.wait_ge(s_dve.h, ent["dve"])
            gpsimd.wait_ge(s_act.h, ent["act"])
            gpsimd.wait_ge(s_mm.h, ent["mm"])
            s_gp.tag(gpsimd.dma_start(out=gw_t[:, :], in_=fw_t_d), 16)

        tok_cst = s_dve.reserve(1)[0]

        @block.vector
        def _(vector):
            vector.wait_ge(s_gp.h, tok_gw)
            vector.memset(ones_col[:, :], 1.0)
            vector.memset(ones_row[:, :], 1.0)
            s_dve.tag(vector.memset(eps_t[:, :], 1e-6))

        prev_ln = None
        prev_st = None
        for ci in range(NCH):
            col = slice(ci * Tc, (ci + 1) * Tc)
            tok_x = s_ld.reserve(1, 16)[0]

            if resid_d is not None:
                tok_x = s_ld.reserve(1, 16)[0]

            @block.sync
            def _(sync, col=col, prev_ln=prev_ln, tok_x=tok_x):
                if prev_ln is None:
                    sync.wait_ge(s_gp.h, tok_gw)
                    sync.wait_ge(s_dve.h, tok_cst)
                else:
                    sync.wait_ge(s_dve.h, prev_ln.tok_xln)
                    sync.wait_ge(s_act.h, prev_ln.tok_mean)
                s_ld.tag(sync.dma_start(out=x_t[:, :, :],
                                        in_=x_d[:, col].rearrange("(g p) t -> p g t", p=128)), 16)
                if resid_d is not None:
                    s_ld.tag(sync.dma_start(out=rs_t[:, :, :],
                                            in_=resid_d[:, col].rearrange("(g p) t -> p g t", p=128)), 16)

            ln = emit_rms_chunk(c, x_t, y_t, t_f, nG, Tc, ones_col, ones_row, xsq,
                                ps_ssq, ps_rep2, stat, tok_x, prev_ln, eps_t, gw_t,
                                out_war=(s_st, prev_st) if prev_st is not None else None)
            prev_ln = ln
            if resid_d is not None:
                tok_radd = s_dve.reserve(1)[0]

                @block.vector
                def _(vector, ln=ln, tok_x=tok_x):
                    vector.wait_ge(s_dve.h, ln.tok_xln)
                    vector.wait_ge(s_ld.h, tok_x)
                    s_dve.tag(vector.tensor_tensor(y_t[:, :, :], y_t[:, :, :],
                                                   rs_t[:, :, :], ALU.add))
                tok_fin = tok_radd
            else:
                tok_fin = None
            tok_st = s_st.reserve(1, 16)[0]
            prev_st = tok_st

            @block.sync
            def _(sync, col=col, ln=ln, tok_fin=tok_fin):
                sync.wait_ge(s_dve.h, ln.tok_xln if tok_fin is None else tok_fin)
                s_st.tag(sync.dma_start(out=y_d[:, col].rearrange("(g p) t -> p g t", p=128),
                                        in_=y_t[:, :, :]), 16)

    @block.sync
    def _(sync):
        sync.wait_ge(s_st.h, s_st.v)


def emit_ln_full(c, x_d, y_d, tag, out_bf16=False):
    block = c.block
    s_ld = c.sem("s_ld"); s_gp = c.sem("s_gp")
    s_act = c.sem("s_act"); s_dve = c.sem("s_dve"); s_mm = c.sem("s_mm")
    s_st = c.sem("s_st")
    Tc = 256
    NCH = T // Tc
    nG = 8
    with c.scope() as cf:
        ones_col = cf.sb(f"{tag}_onc", [128, 1], F32)
        ones_row = cf.sb(f"{tag}_onr", [1, 128], F32)
        eps_t = cf.sb(f"{tag}_eps", [1, 1], F32)
        x_t = cf.sb(f"{tag}_x", [128, nG, Tc], F32)
        xsq = cf.sb(f"{tag}_xsq", [128, nG, Tc], F32)
        t_f = cf.sb(f"{tag}_tf", [128, nG, Tc], F32)
        y_t = cf.sb(f"{tag}_y", [128, nG, Tc], BF16 if out_bf16 else F32)
        stat = cf.sb(f"{tag}_st", [1, 4 * Tc], F32)
        ps_sum = cf.ps(f"{tag}_psum", [1, Tc])
        ps_ssq = cf.ps(f"{tag}_pssq", [1, Tc])
        ps_rep2 = cf.ps(f"{tag}_prep2", [128, 2 * Tc])
        ent = dict(ld=s_ld.v, dve=s_dve.v, act=s_act.v, mm=s_mm.v, st=s_st.v, gp=s_gp.v)
        tok_cst = s_dve.reserve(1)[0]

        @block.vector
        def _(vector):
            vector.memset(ones_col[:, :], 1.0)
            vector.memset(ones_row[:, :], 1.0)
            s_dve.tag(vector.memset(eps_t[:, :], 1e-5))

        prev_ln = None
        prev_st = None
        for ci in range(NCH):
            col = slice(ci * Tc, (ci + 1) * Tc)
            tok_x = s_ld.reserve(1, 16)[0]

            @block.sync
            def _(sync, col=col, prev_ln=prev_ln):
                if prev_ln is None:
                    sync.wait_ge(s_dve.h, tok_cst)
                    sync.wait_ge(s_gp.h, ent["gp"])
                    sync.wait_ge(s_act.h, ent["act"])
                    sync.wait_ge(s_mm.h, ent["mm"])
                    sync.wait_ge(s_st.h, ent["st"])
                else:
                    sync.wait_ge(s_dve.h, prev_ln.tok_xln)
                    sync.wait_ge(s_act.h, prev_ln.tok_mean)
                s_ld.tag(sync.dma_start(out=x_t[:, :, :],
                                        in_=x_d[:, col].rearrange("(g p) t -> p g t", p=128)), 16)

            ln = emit_ln_chunk(c, x_t, y_t, t_f, nG, Tc, ones_col, ones_row, xsq,
                               ps_sum, ps_ssq, ps_rep2, stat, tok_x, prev_ln, eps_t,
                               sq_war=(s_st, prev_st) if prev_st is not None else None,
                               extra_dve_wait=None)
            prev_ln = ln
            tok_st = s_st.reserve(1, 16)[0]
            prev_st = tok_st

            @block.sync
            def _(sync, col=col, ln=ln):
                sync.wait_ge(s_dve.h, ln.tok_xln)
                s_st.tag(sync.dma_start(out=y_d[:, col].rearrange("(g p) t -> p g t", p=128),
                                        in_=y_t[:, :, :]), 16)

    @block.sync
    def _(sync):
        sync.wait_ge(s_st.h, s_st.v)


def build_l2attn(debug=False):
    nc = bass.Bass("TRN2", target_bir_lowering=False)
    x_d = nc.dram_tensor("x", [DM, T], F32, kind="ExternalInput").ap()
    y_d = nc.dram_tensor("y", [DM, T], F32, kind="ExternalOutput").ap()
    shapes = dict(winT=[1024, 4096], wxpT=[2048, 96], wdtT=[64, 2048], woutT=[2048, 1024],
                  convw_t=[128, 16, 4], convb_t=[128, 16], bdt_t=[128, 16],
                  alog_t=[128, 16, 16], d_t=[128, 16], ident=[128, 128], gw_t=[128, 8],
                  zero_state=[128, 16, 16])
    Ps = []
    for i in range(4):
        Ps.append({nm: nc.dram_tensor(f"L{i}_{nm}", shapes[nm], F32, kind="ExternalInput").ap()
                   for nm in shapes})
    fw_d = nc.dram_tensor("fw_t", [128, 8], F32, kind="ExternalInput").ap()
    xb = [nc.dram_tensor(f"xb{i}", [DM, T], F32,
                         kind="ExternalOutput" if debug else "Internal").ap() for i in range(4)]
    scr = dict(
        xc=nc.dram_tensor("scr_xc", [2048, T], F32, kind="Internal").ap(),
        sz=nc.dram_tensor("scr_sz", [2048, T], BF16, kind="Internal").ap(),
        dt=nc.dram_tensor("scr_dt", [2048, T], F32, kind="Internal").ap(),
        bc=nc.dram_tensor("scr_bc", [32, T], BF16, kind="Internal").ap(),
    )
    xln0 = nc.dram_tensor("xln0", [DM, T], F32, kind="Internal").ap()
    with ExitStack() as es:
        block = es.enter_context(nc.Block())
        c = Ctx(nc, es, block)
        emit_ln_full(c, x_d, xln0, "oln")
        cur = xln0
        for i in range(4):
            emit_mamba(c, cur, xb[i], Ps[i], scr, "rms", f"mb{i}")
            cur = xb[i]
        emit_final_rms(c, cur, y_d, fw_d, "fin", resid_d=x_d)
        check_sems(c)
    return nc


def l2attn_inmaps(inp, xs):
    base = {}
    for i in range(4):
        P = mamba_host_params(inp["L2_attn_in_w"][i], inp["L2_attn_conv_w"][i],
                              inp["L2_attn_conv_b"][i], inp["L2_attn_xproj_w"][i],
                              inp["L2_attn_dt_w"][i], inp["L2_attn_dt_b"][i],
                              inp["L2_attn_A_log"][i], inp["L2_attn_Dp"][i],
                              inp["L2_attn_out_w"][i], gw=inp["L2_attn_norm_w"][i])
        for nm, v in P.items():
            base[f"L{i}_{nm}"] = v
    base["fw_t"] = _tile_vec(inp["L2_attn_final_w"], 8)
    return [dict(x=xs[cid], **base) for cid in range(NC8)]



def emit_l1attn(c, x_d, y_d, P, scr, tag):
    block = c.block
    s_ld = c.sem("s_ld"); s_gp = c.sem("s_gp")
    s_act = c.sem("s_act"); s_dve = c.sem("s_dve"); s_mm = c.sem("s_mm")
    s_st = c.sem("s_st")
    nG = 8

    emit_ln_full(c, x_d, scr["xln"], f"{tag}_ln", out_bf16=True)

    with c.scope() as c1:
        Tc = 512
        qkvT = c1.sb(f"{tag}_qkvT", [128, 8, 3072], BF16)
        bqkv = c1.sb(f"{tag}_bq", [128, 24], F32)
        xln_t = c1.sb(f"{tag}_xln", [128, 8, Tc], BF16)
        ev = [c1.sb(f"{tag}_ev{i}", [128, Tc], BF16) for i in range(2)]
        ps_q = [c1.ps(f"{tag}_psq{i}", [128, Tc]) for i in range(2)]
        ent = dict(ld=s_ld.v, dve=s_dve.v, act=s_act.v, mm=s_mm.v, st=s_st.v, gp=s_gp.v)
        tok_w = s_gp.reserve(2, 16)[-1]

        @block.gpsimd
        def _(gpsimd):
            gpsimd.wait_ge(s_gp.h, ent["gp"])
            gpsimd.wait_ge(s_ld.h, ent["ld"])
            gpsimd.wait_ge(s_st.h, ent["st"])
            gpsimd.wait_ge(s_dve.h, ent["dve"])
            gpsimd.wait_ge(s_act.h, ent["act"])
            gpsimd.wait_ge(s_mm.h, ent["mm"])
            s_gp.tag(gpsimd.dma_start(out=qkvT[:, :, :], in_=P["qkvT"].rearrange("(g p) m -> p g m", p=128)), 16)
            s_gp.tag(gpsimd.dma_start(out=bqkv[:, :], in_=P["bqkv_t"]), 16)

        prev = None
        for ci in range(T // Tc):
            col = slice(ci * Tc, (ci + 1) * Tc)
            tok_x = s_ld.reserve(1, 16)[0]

            @block.sync
            def _(sync, col=col, prev=prev):
                if prev is None:
                    sync.wait_ge(s_gp.h, tok_w)
                else:
                    sync.wait_ge(s_mm.h, prev["mm"][-1])
                s_ld.tag(sync.dma_start(out=xln_t[:, :, :],
                                        in_=scr["xln"][:, col].rearrange("(g p) t -> p g t", p=128)), 16)

            cur = dict(mm=s_mm.reserve(24), act=s_act.reserve(24), st=s_st.reserve(24, 16))

            @block.tensor
            def _(tensor, cur=cur, tok_x=tok_x, prev=prev):
                tensor.wait_ge(s_ld.h, tok_x)
                for m in range(24):
                    par = m % 2
                    if m >= 2:
                        tensor.wait_ge(s_act.h, cur["act"][m - 2])
                    elif prev is not None:
                        tensor.wait_ge(s_act.h, prev["act"][22 + m])
                    for k in range(nG):
                        ins = tensor.matmul(ps_q[par][:, :], qkvT[:, k, ts(m)], xln_t[:, k, :],
                                            start=(k == 0), stop=(k == nG - 1))
                    s_mm.tag(ins)

            @block.scalar
            def _(scalar, cur=cur, prev=prev):
                for m in range(24):
                    par = m % 2
                    scalar.wait_ge(s_mm.h, cur["mm"][m])
                    if prev is not None and m < 2:
                        scalar.wait_ge(s_st.h, prev["st"][22 + m])
                    elif m >= 2:
                        scalar.wait_ge(s_st.h, cur["st"][m - 2] - 16)
                    s_act.tag(scalar.add(ev[par][:, :], ps_q[par][:, :], bqkv[:, m:m + 1]))

            @block.sync
            def _(sync, cur=cur, col=col):
                for m in range(24):
                    sync.wait_ge(s_act.h, cur["act"][m])
                    s_st.tag(sync.dma_start(out=scr["qkv"][ts(m), col], in_=ev[m % 2][:, :]), 16)
            prev = cur

    with c.scope() as c2:
        KH = 64
        nKC = T // 128
        nP = T // 512
        kq_f = [c2.sb(f"{tag}_kqf{i}", [64, T], BF16) for i in range(2)]
        kraw = [c2.sb(f"{tag}_kraw{i}", [64, T], BF16) for i in range(2)]
        vtok = [c2.sb(f"{tag}_vtok{i}", [128, nKC, 80], BF16) for i in range(2)]
        q_bf = c2.sb(f"{tag}_qbf", [64, 512], BF16)
        qraw = c2.sb(f"{tag}_qraw", [64, 512], BF16)
        cosT = c2.sb(f"{tag}_cos", [64, T], F32)
        sinT = c2.sb(f"{tag}_sin", [64, T], F32)
        pswap = c2.sb(f"{tag}_pswp", [64, 64], BF16)
        ra = c2.sb(f"{tag}_ra", [64, 512], F32)
        rb = c2.sb(f"{tag}_rb", [64, 512], F32)
        sqf = c2.sb(f"{tag}_sqf", [64, T], F32)
        nrm = c2.sb(f"{tag}_nrm", [1, T], F32)
        sc1 = c2.sb(f"{tag}_sc1", [1, 8], F32)
        nb_t = c2.sb(f"{tag}_nb", [128, 1], F32)
        pt_bf = [c2.sb(f"{tag}_pt{i}", [128, 512], BF16) for i in range(2)]
        y_f = c2.sb(f"{tag}_yf", [64, 512], F32)
        y_bf = c2.sb(f"{tag}_ybf", [64, 512], BF16)
        ones_col = c2.sb(f"{tag}_onc", [128, 1], F32)
        ones_row = c2.sb(f"{tag}_onr", [1, 128], F32)
        ps_s = [c2.ps(f"{tag}_pss{i}", [128, 512]) for i in range(2)]
        ps_sw = c2.ps(f"{tag}_pssw", [64, 512])
        ps_av = c2.ps(f"{tag}_psav", [72, 512])
        ps_rc = c2.ps(f"{tag}_psrc", [64, 512])
        ps_nr = c2.ps(f"{tag}_psnr", [1, 512])

        ent = dict(ld=s_ld.v, dve=s_dve.v, act=s_act.v, mm=s_mm.v, st=s_st.v, gp=s_gp.v)
        tok_tbl = s_gp.reserve(3, 16)[-1]

        @block.gpsimd
        def _(gpsimd):
            gpsimd.wait_ge(s_gp.h, ent["gp"])
            gpsimd.wait_ge(s_ld.h, ent["ld"])
            gpsimd.wait_ge(s_st.h, ent["st"])
            gpsimd.wait_ge(s_dve.h, ent["dve"])
            gpsimd.wait_ge(s_act.h, ent["act"])
            gpsimd.wait_ge(s_mm.h, ent["mm"])
            s_gp.tag(gpsimd.dma_start(out=cosT[:, :], in_=P["cosT"]), 16)
            s_gp.tag(gpsimd.dma_start(out=sinT[:, :], in_=P["sinT"]), 16)
            s_gp.tag(gpsimd.dma_start(out=pswap[:, :], in_=P["pswap"]), 16)

        tok_cst = s_dve.reserve(1)[0]

        @block.vector
        def _(vector):
            vector.wait_ge(s_gp.h, tok_tbl)
            vector.memset(ones_col[:, :], 1.0)
            vector.memset(vtok[0][:, :, :], 0.0)
            vector.memset(vtok[1][:, :, :], 0.0)
            s_dve.tag(vector.memset(ones_row[:, :], 1.0))

        def emit_rope(eng_tokens, dst, srcap, t0c, Tn):
            np_ = Tn // 512
            t_mm = s_mm.reserve(np_)
            t_dv = s_dve.reserve(3 * np_)

            @block.tensor
            def _(tensor):
                for p in range(np_):
                    for w in (eng_tokens if p == 0 else []):
                        tensor.wait_ge(w[0].h, w[1])
                    if p > 0:
                        tensor.wait_ge(s_dve.h, t_dv[3 * (p - 1) + 1])
                    s_mm.tag(tensor.matmul(ps_sw[:, :], pswap[:, :], srcap[:, ts(p, 512)],
                                           start=True, stop=True))

            @block.vector
            def _(vector):
                for p in range(np_):
                    for w in (eng_tokens if p == 0 else []):
                        vector.wait_ge(w[0].h, w[1])
                    csl = slice(t0c + p * 512, t0c + (p + 1) * 512)
                    s_dve.tag(vector.tensor_tensor(ra[:, :], srcap[:, ts(p, 512)],
                                                   cosT[:, csl], ALU.mult))
                    vector.wait_ge(s_mm.h, t_mm[p])
                    s_dve.tag(vector.tensor_tensor(rb[:, :], ps_sw[:, :],
                                                   sinT[:, csl], ALU.mult))
                    vector.wait_ge(s_dve.h, t_dv[3 * p + 1])
                    s_dve.tag(vector.tensor_tensor(dst[:, ts(p, 512)], ra[:, :],
                                                   rb[:, :], ALU.add))
            return t_dv[-1]

        prev_h = None
        prev2_h = None
        for h in range(16):
            ph = h % 2
            krow = slice(1024 + 64 * h, 1024 + 64 * h + 64)
            qrow = slice(64 * h, 64 * h + 64)
            vrow = slice(2048 + 64 * h, 2048 + 64 * h + 64)
            tok_kld = s_ld.reserve(1, 16)[0]
            tok_vld = s_ld.reserve(nKC, 16)[-1]

            @block.sync
            def _(sync, h=h, krow=krow, vrow=vrow, ph=ph, prev2_h=prev2_h):
                if prev2_h is None:
                    sync.wait_ge(s_dve.h, tok_cst)
                else:
                    sync.wait_ge(s_mm.h, prev2_h["avmm"])
                    sync.wait_ge(s_dve.h, prev2_h["last_dve"])
                s_ld.tag(sync.dma_start(out=kraw[ph][:, :], in_=scr["qkv"][krow, :]), 16)
                for tt in range(nKC):
                    s_ld.tag(sync.dma_start_transpose(vtok[ph][:, tt, 0:64],
                                                      scr["qkv"][vrow, ts(tt)]), 16)

            tok_krope = emit_rope([(s_ld, tok_kld)], kq_f[ph], kraw[ph][:, :], 0, T)

            tok_onev = s_dve.reserve(1)[0]

            @block.vector
            def _(vector, tok_vld=tok_vld):
                vector.wait_ge(s_ld.h, tok_vld)
                s_dve.tag(vector.memset(vtok[ph][:, :, 64:65], 1.0))

            t_sq = s_act.reserve(1)[0]
            t_nr = s_mm.reserve(nP)
            t_ne = s_dve.reserve(nP)
            t_km = s_dve.reserve(1)[0]
            t_ksq = s_act.reserve(1)[0]

            @block.scalar
            def _(scalar, tok_krope=tok_krope, ph=ph):
                scalar.wait_ge(s_dve.h, tok_krope)
                s_act.tag(scalar.square(sqf[:, :], kq_f[ph][:, :]))

            @block.tensor
            def _(tensor, t_sq=t_sq, t_ne=t_ne, prev_h=prev_h):
                tensor.wait_ge(s_act.h, t_sq)
                if prev_h is not None:
                    tensor.wait_ge(s_act.h, prev_h["ksq"])
                for piece in range(nP):
                    ins = tensor.matmul(ps_nr[0:1, :], ones_col[0:64, 0:1],
                                        sqf[:, ts(piece, 512)], start=True, stop=True)
                    s_mm.tag(ins)
                    tensor.wait_ge(s_dve.h, t_ne[piece])

            @block.vector
            def _(vector, t_nr=t_nr, t_ne=t_ne):
                for piece in range(nP):
                    vector.wait_ge(s_mm.h, t_nr[piece])
                    s_dve.tag(vector.tensor_copy(nrm[0:1, ts(piece, 512)], ps_nr[0:1, :]))

            @block.vector
            def _(vector, t_ne=t_ne, t_km=t_km):
                vector.wait_ge(s_dve.h, t_ne[-1])
                s_dve.tag(vector.tensor_reduce(sc1[0:1, 1:2], nrm[0:1, :],
                                               mybir.AxisListType.X, ALU.max))

            @block.scalar
            def _(scalar, t_km=t_km):
                scalar.wait_ge(s_dve.h, t_km)
                s_act.tag(scalar.activation(sc1[0:1, 3:4], sc1[0:1, 1:2], AF.Sqrt))

            cur_h = dict(ksq=t_ksq)
            last_dve = t_km
            avmm_last = None
            for qc in range(nP):
                qsl = slice(qc * 512, (qc + 1) * 512)
                tok_qld = s_ld.reserve(1, 16)[0]

                @block.sync
                def _(sync, h=h, qrow=qrow, qsl=qsl, last_dve=last_dve):
                    sync.wait_ge(s_dve.h, last_dve)
                    s_ld.tag(sync.dma_start(out=qraw[:, :], in_=scr["qkv"][qrow, qsl]), 16)

                tok_qrope = emit_rope([(s_ld, tok_qld)], q_bf, qraw[:, :], qc * 512, 512)

                tq = dict(sq=s_act.reserve(1)[0], nr=s_mm.reserve(1)[0], ne=s_dve.reserve(1)[0],
                          qm=s_dve.reserve(1)[0], qs=s_act.reserve(1)[0], mu=s_dve.reserve(1)[0],
                          ng=s_dve.reserve(1)[0], bmm2=s_mm.reserve(1)[0], bcp=s_dve.reserve(1)[0])

                @block.scalar
                def _(scalar, tok_qrope=tok_qrope, tq=tq):
                    scalar.wait_ge(s_dve.h, tok_qrope)
                    s_act.tag(scalar.square(sqf[:, 0:512], q_bf[:, :]))

                @block.tensor
                def _(tensor, tq=tq):
                    tensor.wait_ge(s_act.h, tq["sq"])
                    s_mm.tag(tensor.matmul(ps_nr[0:1, :], ones_col[0:64, 0:1], sqf[:, 0:512],
                                           start=True, stop=True))

                @block.vector
                def _(vector, tq=tq):
                    vector.wait_ge(s_mm.h, tq["nr"])
                    s_dve.tag(vector.tensor_copy(nrm[0:1, 0:512], ps_nr[0:1, :]))
                    vector.wait_ge(s_dve.h, tq["ne"])
                    s_dve.tag(vector.tensor_reduce(sc1[0:1, 0:1], nrm[0:1, 0:512],
                                                   mybir.AxisListType.X, ALU.max))

                @block.scalar
                def _(scalar, tq=tq):
                    scalar.wait_ge(s_dve.h, tq["qm"])
                    s_act.tag(scalar.activation(sc1[0:1, 2:3], sc1[0:1, 0:1], AF.Sqrt))

                @block.vector
                def _(vector, tq=tq, t_ksq=t_ksq):
                    vector.wait_ge(s_act.h, tq["qs"])
                    vector.wait_ge(s_act.h, t_ksq)
                    s_dve.tag(vector.tensor_tensor(sc1[0:1, 4:5], sc1[0:1, 2:3],
                                                   sc1[0:1, 3:4], ALU.mult))
                    vector.wait_ge(s_dve.h, tq["mu"])
                    s_dve.tag(vector.tensor_scalar(sc1[0:1, 5:6], sc1[0:1, 4:5],
                                                   -0.125, 1.0, ALU.mult, ALU.mult))

                @block.tensor
                def _(tensor, tq=tq):
                    tensor.wait_ge(s_dve.h, tq["ng"])
                    s_mm.tag(tensor.matmul(ps_s[0][:, 0:1], ones_row[0:1, :], sc1[0:1, 5:6],
                                           start=True, stop=True))

                @block.vector
                def _(vector, tq=tq):
                    vector.wait_ge(s_mm.h, tq["bmm2"])
                    s_dve.tag(vector.tensor_copy(nb_t[:, :], ps_s[0][:, 0:1]))

                _mmtoks = s_mm.reserve(2 * nKC)
                toks_sc = [_mmtoks[0]] + _mmtoks[1::2][:nKC - 1]
                toks_av = _mmtoks[2::2] + [_mmtoks[-1]]
                toks_ex = s_act.reserve(nKC)

                @block.tensor
                def _(tensor, tq=tq, toks_sc=toks_sc, toks_ex=toks_ex, toks_av=toks_av,
                      tok_qrope=tok_qrope, avmm_last=avmm_last):
                    tensor.wait_ge(s_dve.h, tq["bcp"])
                    for kc in range(nKC + 1):
                        if kc < nKC:
                            par = kc % 2
                            if kc >= 2:
                                tensor.wait_ge(s_act.h, toks_ex[kc - 2])
                            ins = tensor.matmul(ps_s[par][:, :], kq_f[ph][:, ts(kc)], q_bf[:, :],
                                                start=True, stop=True)
                            s_mm.tag(ins)
                        if kc >= 1:
                            j = kc - 1
                            tensor.wait_ge(s_act.h, toks_ex[j])
                            ins = tensor.matmul(ps_av[:, :], vtok[ph][:, j, 0:72], pt_bf[j % 2][:, :],
                                                start=(j == 0), stop=(j == nKC - 1))
                            s_mm.tag(ins)

                @block.scalar
                def _(scalar, toks_sc=toks_sc, toks_ex=toks_ex, toks_av=toks_av):
                    for kc in range(nKC):
                        par = kc % 2
                        scalar.wait_ge(s_mm.h, toks_sc[kc])
                        if kc >= 2:
                            scalar.wait_ge(s_mm.h, toks_av[kc - 2])
                        s_act.tag(scalar.activation(pt_bf[par][:, :], ps_s[par][:, :], AF.Exp,
                                                    bias=nb_t[:, 0:1], scale=0.125))

                tq["rcp"] = s_dve.reserve(1)[0]
                tq["yev"] = s_act.reserve(1)[0]
                tq["rbc"] = s_mm.reserve(1)[0]
                tq["ysc"] = s_dve.reserve(1)[0]
                tok_yst = s_st.reserve(1, 16)[0]

                @block.vector
                def _(vector, tq=tq, toks_av=toks_av):
                    vector.wait_ge(s_mm.h, toks_av[-1])
                    s_dve.tag(vector.reciprocal(nrm[0:1, 0:512], ps_av[64:65, :]))

                @block.scalar
                def _(scalar, tq=tq, toks_av=toks_av):
                    scalar.wait_ge(s_mm.h, toks_av[-1])
                    s_act.tag(scalar.copy(y_f[:, :], ps_av[0:64, :]))

                @block.tensor
                def _(tensor, tq=tq):
                    tensor.wait_ge(s_dve.h, tq["rcp"])
                    s_mm.tag(tensor.matmul(ps_rc[:, :], ones_row[0:1, 0:64], nrm[0:1, 0:512],
                                           start=True, stop=True))

                @block.vector
                def _(vector, tq=tq):
                    vector.wait_ge(s_mm.h, tq["rbc"])
                    vector.wait_ge(s_act.h, tq["yev"])
                    s_dve.tag(vector.tensor_tensor(y_bf[:, :], y_f[:, :], ps_rc[0:64, :], ALU.mult))

                @block.sync
                def _(sync, h=h, qsl=qsl, tq=tq):
                    sync.wait_ge(s_dve.h, tq["ysc"])
                    s_st.tag(sync.dma_start(out=scr["y"][ts(h, 64), qsl], in_=y_bf[:, :]), 16)
                last_dve = tq["ysc"]
                avmm_last = toks_av[-1]
            cur_h["avmm"] = avmm_last
            cur_h["last_dve"] = last_dve
            prev2_h = prev_h
            prev_h = cur_h

    with c.scope() as c3:
        Tc = 512
        woT = c3.sb(f"{tag}_woT", [128, 8, 1024], BF16)
        ob = c3.sb(f"{tag}_ob", [128, 8], F32)
        yin = c3.sb(f"{tag}_yin", [128, 8, Tc], BF16)
        x_t = c3.sb(f"{tag}_x3", [128, 8, Tc], F32)
        of = [c3.sb(f"{tag}_of{i}", [128, Tc], F32) for i in range(2)]
        xo_t = c3.sb(f"{tag}_xo3", [128, 8, Tc], F32)
        ps_o = [c3.ps(f"{tag}_pso{i}", [128, Tc]) for i in range(2)]
        ent = dict(ld=s_ld.v, dve=s_dve.v, act=s_act.v, mm=s_mm.v, st=s_st.v, gp=s_gp.v)
        tok_w = s_gp.reserve(2, 16)[-1]

        @block.gpsimd
        def _(gpsimd):
            gpsimd.wait_ge(s_gp.h, ent["gp"])
            gpsimd.wait_ge(s_ld.h, ent["ld"])
            gpsimd.wait_ge(s_st.h, ent["st"])
            gpsimd.wait_ge(s_dve.h, ent["dve"])
            gpsimd.wait_ge(s_act.h, ent["act"])
            gpsimd.wait_ge(s_mm.h, ent["mm"])
            s_gp.tag(gpsimd.dma_start(out=woT[:, :, :], in_=P["woT"].rearrange("(g p) m -> p g m", p=128)), 16)
            s_gp.tag(gpsimd.dma_start(out=ob[:, :], in_=P["ob_t"]), 16)

        prev = None
        for ci in range(T // Tc):
            col = slice(ci * Tc, (ci + 1) * Tc)
            tok_in = s_ld.reserve(2, 16)[-1]

            @block.sync
            def _(sync, col=col, prev=prev):
                if prev is None:
                    sync.wait_ge(s_gp.h, tok_w)
                else:
                    sync.wait_ge(s_dve.h, prev["res"][-1])
                    sync.wait_ge(s_st.h, prev["st"])
                s_ld.tag(sync.dma_start(out=yin[:, :, :],
                                        in_=scr["y"][:, col].rearrange("(g p) t -> p g t", p=128)), 16)
                s_ld.tag(sync.dma_start(out=x_t[:, :, :],
                                        in_=x_d[:, col].rearrange("(g p) t -> p g t", p=128)), 16)

            cur = dict(mm=s_mm.reserve(8), act=s_act.reserve(8), res=s_dve.reserve(8),
                       st=s_st.reserve(1, 16)[0])

            @block.tensor
            def _(tensor, cur=cur, tok_in=tok_in, prev=prev):
                tensor.wait_ge(s_ld.h, tok_in)
                for m in range(8):
                    par = m % 2
                    if m >= 2:
                        tensor.wait_ge(s_act.h, cur["act"][m - 2])
                    elif prev is not None:
                        tensor.wait_ge(s_act.h, prev["act"][6 + m])
                    for k in range(nG):
                        ins = tensor.matmul(ps_o[par][:, :], woT[:, k, ts(m)], yin[:, k, :],
                                            start=(k == 0), stop=(k == nG - 1))
                    s_mm.tag(ins)

            @block.scalar
            def _(scalar, cur=cur, prev=prev):
                for m in range(8):
                    par = m % 2
                    scalar.wait_ge(s_mm.h, cur["mm"][m])
                    if m >= 2:
                        scalar.wait_ge(s_dve.h, cur["res"][m - 2])
                    elif prev is not None:
                        scalar.wait_ge(s_dve.h, prev["res"][6 + m])
                    s_act.tag(scalar.add(of[par][:, :], ps_o[par][:, :], ob[:, m:m + 1]))

            @block.vector
            def _(vector, cur=cur, tok_in=tok_in):
                vector.wait_ge(s_ld.h, tok_in)
                for m in range(8):
                    par = m % 2
                    vector.wait_ge(s_act.h, cur["act"][m])
                    s_dve.tag(vector.scalar_tensor_tensor(xo_t[:, m, :], of[par][:, :], 1.0,
                                                          x_t[:, m, :], ALU.mult, ALU.add))

            @block.sync
            def _(sync, cur=cur, col=col):
                sync.wait_ge(s_dve.h, cur["res"][-1])
                s_st.tag(sync.dma_start(out=y_d[:, col].rearrange("(g p) t -> p g t", p=128),
                                        in_=xo_t[:, :, :]), 16)
            prev = cur

    @block.sync
    def _(sync):
        sync.wait_ge(s_st.h, s_st.v)


def build_l1attn(debug=False):
    nc = bass.Bass("TRN2", target_bir_lowering=False)
    x_d = nc.dram_tensor("x", [DM, T], F32, kind="ExternalInput").ap()
    y_d = nc.dram_tensor("y", [DM, T], F32, kind="ExternalOutput").ap()
    names = dict(qkvT=[1024, 3072], bqkv_t=[128, 24], woT=[1024, 1024], ob_t=[128, 8],
                 cosT=[64, T], sinT=[64, T], pswap=[64, 64])
    P = {nm: nc.dram_tensor(nm, sh, F32, kind="ExternalInput").ap() for nm, sh in names.items()}
    kd = "ExternalOutput" if debug else "Internal"
    scr = dict(
        xln=nc.dram_tensor("scr_xln", [DM, T], BF16, kind=kd).ap(),
        qkv=nc.dram_tensor("scr_qkv", [3072, T], BF16, kind=kd).ap(),
        y=nc.dram_tensor("scr_ya", [DM, T], BF16, kind=kd).ap(),
    )
    with ExitStack() as es:
        block = es.enter_context(nc.Block())
        c = Ctx(nc, es, block)
        emit_l1attn(c, x_d, y_d, P, scr, "l1a")
        check_sems(c)
    return nc


def l1attn_inmaps(inp, xs):
    qkv_w = np.asarray(inp["L1_attn_qkv_w"], np.float32)
    qkv_b = np.asarray(inp["L1_attn_qkv_b"], np.float32)
    perm = np.arange(3072)
    for h in range(16):
        base = 64 * h
        pq = np.concatenate([np.arange(base, base + 64, 2), np.arange(base + 1, base + 64, 2)])
        perm[base:base + 64] = pq
        perm[1024 + base:1024 + base + 64] = 1024 + pq
    qkv_wp = qkv_w[perm]
    qkv_bp = qkv_b[perm]
    inv = 1.0 / (10000.0 ** (np.arange(0, 64, 2, dtype=np.float32) / np.float32(64.0)))
    th = np.arange(T, dtype=np.float32)[:, None] * inv[None, :]
    cos1 = np.cos(th).T.astype(np.float32)
    sin1 = np.sin(th).T.astype(np.float32)
    cosT = np.ascontiguousarray(np.concatenate([cos1, cos1], 0))
    sinT = np.ascontiguousarray(np.concatenate([-sin1, sin1], 0))
    pswap = np.zeros((64, 64), np.float32)
    for i in range(32):
        pswap[32 + i, i] = 1.0
        pswap[i, 32 + i] = 1.0
    P = dict(qkvT=np.ascontiguousarray(qkv_wp.T), bqkv_t=_tile_vec(qkv_bp, 24),
             woT=np.ascontiguousarray(np.asarray(inp["L1_attn_out_w"], np.float32).T),
             ob_t=_tile_vec(inp["L1_attn_out_b"], 8), cosT=cosT, sinT=sinT, pswap=pswap)
    return [dict(x=xs[cid], **P) for cid in range(NC8)]




def emit_l1moe(c, x_d, y_d, P, scr, tag):
    block = c.block
    s_ld = c.sem("s_ld"); s_gp = c.sem("s_gp")
    s_act = c.sem("s_act"); s_dve = c.sem("s_dve"); s_mm = c.sem("s_mm")
    s_st = c.sem("s_st")
    nG = 8
    BIG = 60000.0

    emit_ln_full(c, x_d, scr["xlnf"], f"{tag}ln")

    with c.scope() as c1:
        Tc = 256
        NCH = T // Tc
        rw = c1.sb(f"{tag}_rw", [128, 8, 16], F32)
        identf = c1.sb(f"{tag}_idf", [128, 128], F32)
        x_t = c1.sb(f"{tag}_x1", [128, nG, Tc], F32)
        xbf_t = c1.sb(f"{tag}_xb1", [128, nG, Tc], BF16)
        lgt = c1.sb(f"{tag}_lgt", [128, 16], F32)
        prb = c1.sb(f"{tag}_prb", [128, 16], F32)
        rsc = c1.sb(f"{tag}_rsc", [128, 64], F32)
        rs1 = c1.sb(f"{tag}_rs1", [128, 6], F32)
        m_tok = c1.sb(f"{tag}_mtk", [128, 16], F32)
        m_fm = c1.sb(f"{tag}_mfm", [16, 2, 128], F32)
        ps_big = c1.ps(f"{tag}_pbig", [128, 256])
        ent = dict(ld=s_ld.v, dve=s_dve.v, act=s_act.v, mm=s_mm.v, st=s_st.v)
        tok_w = s_gp.reserve(2, 16)[-1]

        @block.gpsimd
        def _(gpsimd):
            for s, v in ((s_ld, ent["ld"]), (s_st, ent["st"]), (s_dve, ent["dve"]),
                         (s_act, ent["act"]), (s_mm, ent["mm"])):
                gpsimd.wait_ge(s.h, v)
            s_gp.tag(gpsimd.dma_start(out=rw[:, :, :], in_=P["rw"].rearrange("(g p) m -> p g m", p=128)), 16)
            s_gp.tag(gpsimd.dma_start(out=identf[:, :], in_=P["identf"]), 16)

        prev = None
        for ci in range(NCH):
            col = slice(ci * Tc, (ci + 1) * Tc)
            tok_x = s_ld.reserve(1, 16)[0]

            @block.sync
            def _(sync, col=col, prev=prev):
                if prev is None:
                    sync.wait_ge(s_gp.h, tok_w)
                else:
                    sync.wait_ge(s_act.h, prev["cast"])
                    sync.wait_ge(s_mm.h, prev["lgm"][1])
                s_ld.tag(sync.dma_start(out=x_t[:, :, :],
                                        in_=scr["xlnf"][:, col].rearrange("(g p) t -> p g t", p=128)), 16)

            cur = dict(lgm=[None, None], lgev=[None, None], rt=[None] * 24,
                       rtact=[None, None], mtr=[None, None], mev=[None, None])
            cur["cast"] = s_act.reserve(1)[0]

            @block.scalar
            def _(scalar, tok_x=tok_x, prev=prev):
                scalar.wait_ge(s_ld.h, tok_x)
                if prev is not None:
                    scalar.wait_ge(s_st.h, prev["xst"])
                s_act.tag(scalar.copy(xbf_t[:, :, :], x_t[:, :, :]))

            cur["xst"] = s_st.reserve(1, 16)[0]

            @block.sync
            def _(sync, cur=cur, col=col):
                sync.wait_ge(s_act.h, cur["cast"])
                s_st.tag(sync.dma_start(out=scr["xln"][:, col].rearrange("(g p) t -> p g t", p=128),
                                        in_=xbf_t[:, :, :]), 16)

            for sub in range(2):
                tsl = slice(sub * 128, (sub + 1) * 128)
                cur["lgm"][sub] = s_mm.reserve(1)[0]
                cur["lgev"][sub] = s_dve.reserve(1)[0]
                cur["rt"][12 * sub:12 * sub + 12] = s_dve.reserve(12)
                cur["rtact"][sub] = s_act.reserve(1)[0]
                cur["mtr"][sub] = s_mm.reserve(1)[0]
                cur["mev"][sub] = s_dve.reserve(1)[0]

                @block.tensor
                def _(tensor, sub=sub, tsl=tsl, cur=cur, prev=prev, tok_x=tok_x):
                    tensor.wait_ge(s_ld.h, tok_x)
                    if sub == 1:
                        tensor.wait_ge(s_dve.h, cur["mev"][0])
                    elif prev is not None:
                        tensor.wait_ge(s_dve.h, prev["mev"][1])
                    for k in range(nG):
                        ins = tensor.matmul(ps_big[:, 0:16], x_t[:, k, tsl], rw[:, k, :],
                                            start=(k == 0), stop=(k == nG - 1))
                    s_mm.tag(ins)

                @block.vector
                def _(vector, sub=sub, cur=cur):
                    vector.wait_ge(s_mm.h, cur["lgm"][sub])
                    s_dve.tag(vector.tensor_copy(lgt[:, :], ps_big[:, 0:16]))

                @block.vector
                def _(vector, sub=sub, cur=cur):
                    tk = cur["rt"][12 * sub:12 * sub + 12]
                    vector.wait_ge(s_dve.h, cur["lgev"][sub])
                    s_dve.tag(vector.tensor_reduce(rs1[:, 0:1], lgt[:, :], mybir.AxisListType.X, ALU.max))
                    vector.wait_ge(s_dve.h, tk[0])
                    s_dve.tag(vector.tensor_scalar(rs1[:, 1:2], rs1[:, 0:1], -1.0, 1.0, ALU.mult, ALU.mult))

                @block.scalar
                def _(scalar, sub=sub, cur=cur):
                    scalar.wait_ge(s_dve.h, cur["rt"][12 * sub + 1])
                    s_act.tag(scalar.activation(prb[:, :], lgt[:, :], AF.Exp, bias=rs1[:, 1:2]))

                @block.vector
                def _(vector, sub=sub, cur=cur):
                    tk = cur["rt"][12 * sub:12 * sub + 12]
                    vector.wait_ge(s_act.h, cur["rtact"][sub])
                    s_dve.tag(vector.tensor_reduce(rs1[:, 2:3], prb[:, :], mybir.AxisListType.X, ALU.add))
                    vector.wait_ge(s_dve.h, tk[2])
                    s_dve.tag(vector.reciprocal(rs1[:, 3:4], rs1[:, 2:3]))
                    vector.wait_ge(s_dve.h, tk[3])
                    s_dve.tag(vector.tensor_scalar(prb[:, :], prb[:, :], rs1[:, 3:4], 1.0, ALU.mult, ALU.mult))
                    vector.wait_ge(s_dve.h, tk[4])
                    s_dve.tag(vector.tensor_reduce(rs1[:, 4:5], prb[:, :], mybir.AxisListType.X, ALU.max))
                    vector.wait_ge(s_dve.h, tk[5])
                    s_dve.tag(vector.tensor_scalar(rsc[:, 0:16], prb[:, :], rs1[:, 4:5], 1.0, ALU.is_ge, ALU.mult))
                    vector.wait_ge(s_dve.h, tk[6])
                    s_dve.tag(vector.scalar_tensor_tensor(rsc[:, 16:32], rsc[:, 0:16], -BIG, prb[:, :], ALU.mult, ALU.add))
                    vector.wait_ge(s_dve.h, tk[7])
                    s_dve.tag(vector.tensor_reduce(rs1[:, 5:6], rsc[:, 16:32], mybir.AxisListType.X, ALU.max))
                    vector.wait_ge(s_dve.h, tk[8])
                    s_dve.tag(vector.tensor_scalar(rsc[:, 32:48], rsc[:, 16:32], rs1[:, 5:6], 1.0, ALU.is_ge, ALU.mult))
                    vector.wait_ge(s_dve.h, tk[9])
                    s_dve.tag(vector.tensor_tensor(rsc[:, 48:64], rsc[:, 0:16], rsc[:, 32:48], ALU.add))
                    vector.wait_ge(s_dve.h, tk[10])
                    s_dve.tag(vector.tensor_tensor(m_tok[:, :], prb[:, :], rsc[:, 48:64], ALU.mult))

                @block.tensor
                def _(tensor, sub=sub, cur=cur, prev=prev):
                    tensor.wait_ge(s_dve.h, cur["rt"][12 * sub + 11])
                    if sub == 0 and prev is not None:
                        tensor.wait_ge(s_st.h, prev["mst"])
                    s_mm.tag(tensor.transpose(ps_big[0:16, 128:256], m_tok[:, :], identf[:, :]))

                @block.vector
                def _(vector, sub=sub, cur=cur):
                    vector.wait_ge(s_mm.h, cur["mtr"][sub])
                    s_dve.tag(vector.tensor_copy(m_fm[:, sub, :], ps_big[0:16, 128:256]))

            cur["mst"] = s_st.reserve(1, 16)[0]

            @block.sync
            def _(sync, cur=cur, col=col):
                sync.wait_ge(s_dve.h, cur["mev"][1])
                s_st.tag(sync.dma_start(out=scr["m"][:, col],
                                        in_=m_fm[:, :, :].rearrange("e s t -> e (s t)")), 16)
            prev = cur

    with c.scope() as c2:
        Tc = 512
        NCH = T // Tc
        NE = 16
        gw = [c2.sb(f"{tag}_gw{i}", [128, 8, 1024], BF16) for i in range(2)]
        uw = [c2.sb(f"{tag}_uw{i}", [128, 8, 1024], BF16) for i in range(2)]
        dw = [c2.sb(f"{tag}_dw{i}", [128, 8, 1024], BF16) for i in range(2)]
        xln_t = [c2.sb(f"{tag}_xl2{i}", [128, nG, Tc], BF16) for i in range(2)]
        mrow = [c2.sb(f"{tag}_mr{i}", [1, Tc], F32) for i in range(2)]
        ones_row = c2.sb(f"{tag}_onr", [1, 128], F32)
        sg_bf = [c2.sb(f"{tag}_sg{i}", [128, Tc], BF16) for i in range(2)]
        h1 = [c2.sb(f"{tag}_h1{i}", [128, Tc], F32) for i in range(2)]
        h_bf = c2.sb(f"{tag}_h", [128, 8, Tc], BF16)
        o_t = c2.sb(f"{tag}_o", [128, 8, Tc], F32)
        ps_a = [c2.ps(f"{tag}_pa{i}", [128, Tc]) for i in range(2)]
        ps_b = [c2.ps(f"{tag}_pb{i}", [128, Tc]) for i in range(2)]
        ps_m = c2.ps(f"{tag}_pm", [128, Tc])
        ent = dict(ld=s_ld.v, dve=s_dve.v, act=s_act.v, mm=s_mm.v, st=s_st.v, gp=s_gp.v)
        tok_one = s_dve.reserve(1)[0]

        @block.vector
        def _(vector):
            for s, v in ((s_st, ent["st"]), (s_act, ent["act"]),
                         (s_mm, ent["mm"]), (s_gp, ent["gp"])):
                vector.wait_ge(s.h, v)
            s_dve.tag(vector.memset(ones_row[:, :], 1.0))

        tok_wl = [None] * NE
        down_last = [None] * NE
        acc_tok = [[None] * NCH for _ in range(NE)]
        prev_dve_last = tok_one
        prev_acc = None
        prev_dn_last = None
        gu_last = {}
        tok_xc = {}

        def emit_wload(e):
            tok_wl[e] = s_gp.reserve(3, 16)[-1]
            pe = e % 2

            @block.gpsimd
            def _(gpsimd, e=e, pe=pe):
                if e < 2:
                    for s, v in ((s_ld, ent["ld"]), (s_st, ent["st"]), (s_dve, ent["dve"]),
                                 (s_act, ent["act"]), (s_mm, ent["mm"])):
                        gpsimd.wait_ge(s.h, v)
                else:
                    gpsimd.wait_ge(s_mm.h, down_last[e - 2])
                s_gp.tag(gpsimd.dma_start(out=gw[pe][:, :, :],
                                          in_=P["gwa"][ts(e, 1024), :].rearrange("(g p) m -> p g m", p=128)), 16)
                s_gp.tag(gpsimd.dma_start(out=uw[pe][:, :, :],
                                          in_=P["uwa"][ts(e, 1024), :].rearrange("(g p) m -> p g m", p=128)), 16)
                s_gp.tag(gpsimd.dma_start(out=dw[pe][:, :, :],
                                          in_=P["dwa"][ts(e, 1024), :].rearrange("(g p) m -> p g m", p=128)), 16)

        emit_wload(0)
        emit_wload(1)

        gidx = 0
        for e in range(NE):
            pe = e % 2
            for ci in range(NCH):
                col = slice(ci * Tc, (ci + 1) * Tc)
                pxc = gidx % 2
                tok_xc[(e, ci)] = s_ld.reserve(2, 16)[-1]

                @block.sync
                def _(sync, e=e, ci=ci, col=col, pxc=pxc, gidx=gidx):
                    if gidx < 2:
                        for s, v in ((s_st, ent["st"]), (s_dve, ent["dve"]), (s_act, ent["act"]),
                                     (s_mm, ent["mm"]), (s_gp, ent["gp"])):
                            sync.wait_ge(s.h, v)
                    else:
                        g2 = (e * NCH + ci - 2)
                        sync.wait_ge(s_mm.h, gu_last[(g2 // NCH, g2 % NCH)])
                    s_ld.tag(sync.dma_start(out=xln_t[pxc][:, :, :],
                                            in_=scr["xln"][:, col].rearrange("(g p) t -> p g t", p=128)), 16)
                    s_ld.tag(sync.dma_start(out=mrow[pxc][:, :], in_=scr["m"][e:e + 1, col]), 16)

                t_mrm = s_mm.reserve(1)[0]
                t_gu = s_mm.reserve(16)
                t_sil = s_act.reserve(8)
                t_h = s_dve.reserve(16)
                t_dn = s_mm.reserve(8)
                t_oev = s_dve.reserve(8)
                gu_last[(e, ci)] = t_gu[-1]
                down_last[e] = t_dn[-1]

                @block.tensor
                def _(tensor, e=e, pe=pe, ci=ci, pxc=pxc, t_gu=t_gu, t_sil=t_sil,
                      t_h=t_h, t_dn=t_dn, pdl=prev_dve_last):
                    tensor.wait_ge(s_dve.h, pdl)
                    tensor.wait_ge(s_ld.h, tok_xc[(e, ci)])
                    ins = tensor.matmul(ps_m[:, :], ones_row[0:1, :], mrow[pxc][0:1, :],
                                        start=True, stop=True)
                    s_mm.tag(ins)
                    if ci == 0:
                        tensor.wait_ge(s_gp.h, tok_wl[e])
                    for m in range(8):
                        par = m % 2
                        if m >= 2:
                            tensor.wait_ge(s_act.h, t_sil[m - 2])
                        for k in range(nG):
                            ins = tensor.matmul(ps_a[par][:, :], gw[pe][:, k, ts(m)],
                                                xln_t[pxc][:, k, :], start=(k == 0), stop=(k == nG - 1))
                        s_mm.tag(ins)
                        if m >= 2:
                            tensor.wait_ge(s_dve.h, t_h[2 * (m - 2)])
                        for k in range(nG):
                            ins = tensor.matmul(ps_b[par][:, :], uw[pe][:, k, ts(m)],
                                                xln_t[pxc][:, k, :], start=(k == 0), stop=(k == nG - 1))
                        s_mm.tag(ins)
                    tensor.wait_ge(s_dve.h, t_h[15])
                    for mo in range(8):
                        par = mo % 2
                        if mo >= 2:
                            tensor.wait_ge(s_dve.h, t_oev[mo - 2])
                        for k in range(nG):
                            ins = tensor.matmul(ps_a[par][:, :], dw[pe][:, k, ts(mo)],
                                                h_bf[:, k, :], start=(k == 0), stop=(k == nG - 1))
                        s_mm.tag(ins)

                @block.scalar
                def _(scalar, t_gu=t_gu, t_sil=t_sil, t_h=t_h, pdl=prev_dve_last):
                    scalar.wait_ge(s_dve.h, pdl)
                    for m in range(8):
                        par = m % 2
                        scalar.wait_ge(s_mm.h, t_gu[2 * m])
                        if m >= 2:
                            scalar.wait_ge(s_dve.h, t_h[2 * (m - 2)])
                        s_act.tag(scalar.activation(sg_bf[par][:, :], ps_a[par][:, :], AF.Silu))

                @block.vector
                def _(vector, t_gu=t_gu, t_sil=t_sil, t_h=t_h, t_dn=t_dn, t_oev=t_oev,
                      t_mrm=t_mrm, pdn=prev_dn_last, pacc=prev_acc):
                    for m in range(8):
                        par = m % 2
                        vector.wait_ge(s_mm.h, t_gu[2 * m + 1])
                        vector.wait_ge(s_act.h, t_sil[m])
                        if m == 0 and pdn is not None:
                            vector.wait_ge(s_mm.h, pdn)
                        s_dve.tag(vector.tensor_tensor(h1[par][:, :], sg_bf[par][:, :],
                                                       ps_b[par][:, :], ALU.mult))
                        vector.wait_ge(s_dve.h, t_h[2 * m])
                        if m == 0:
                            vector.wait_ge(s_mm.h, t_mrm)
                        s_dve.tag(vector.tensor_tensor(h_bf[:, m, :], h1[par][:, :],
                                                       ps_m[:, :], ALU.mult))
                    for mo in range(8):
                        par = mo % 2
                        vector.wait_ge(s_mm.h, t_dn[mo])
                        if mo == 0 and pacc is not None:
                            vector.wait_ge(s_gp.h, pacc)
                        s_dve.tag(vector.tensor_copy(o_t[:, mo, :], ps_a[par][:, :]))

                t_acc = s_gp.reserve(1, 16)[0]
                acc_tok[e][ci] = t_acc

                @block.gpsimd
                def _(gpsimd, e=e, ci=ci, col=col, t_oev=t_oev):
                    gpsimd.wait_ge(s_dve.h, t_oev[-1])
                    if e > 0:
                        gpsimd.wait_ge(s_gp.h, acc_tok[e - 1][ci])
                        s_gp.tag(gpsimd.dma_start(out=scr["acc"][:, col].rearrange("(g p) t -> p g t", p=128),
                                                  in_=o_t[:, :, :], accum_op=ALU.add), 16)
                    else:
                        s_gp.tag(gpsimd.dma_start(out=scr["acc"][:, col].rearrange("(g p) t -> p g t", p=128),
                                                  in_=o_t[:, :, :]), 16)

                prev_dve_last = t_oev[-1]
                prev_acc = t_acc
                prev_dn_last = t_dn[-1]
                gidx += 1
            if e + 2 < NE:
                emit_wload(e + 2)

    with c.scope() as c3:
        Tc = 512
        NCH = T // Tc
        a_t = c3.sb(f"{tag}_a3", [128, nG, Tc], F32)
        x_t = c3.sb(f"{tag}_x3", [128, nG, Tc], F32)
        xo_t = c3.sb(f"{tag}_xo3", [128, nG, Tc], F32)
        ent = dict(ld=s_ld.v, dve=s_dve.v, act=s_act.v, mm=s_mm.v, st=s_st.v, gp=s_gp.v)
        prev_res = None
        prev_st = None
        for ci in range(NCH):
            col = slice(ci * Tc, (ci + 1) * Tc)
            tok_x = s_ld.reserve(2, 16)[-1]

            @block.sync
            def _(sync, col=col, prev_res=prev_res):
                if prev_res is None:
                    for s, v in ((s_st, ent["st"]), (s_dve, ent["dve"]), (s_act, ent["act"]),
                                 (s_mm, ent["mm"]), (s_gp, ent["gp"])):
                        sync.wait_ge(s.h, v)
                else:
                    sync.wait_ge(s_dve.h, prev_res)
                s_ld.tag(sync.dma_start(out=a_t[:, :, :],
                                        in_=scr["acc"][:, col].rearrange("(g p) t -> p g t", p=128)), 16)
                s_ld.tag(sync.dma_start(out=x_t[:, :, :],
                                        in_=x_d[:, col].rearrange("(g p) t -> p g t", p=128)), 16)

            tok_res = s_dve.reserve(1)[0]

            @block.vector
            def _(vector, tok_x=tok_x, prev_st=prev_st):
                vector.wait_ge(s_ld.h, tok_x)
                if prev_st is not None:
                    vector.wait_ge(s_st.h, prev_st)
                s_dve.tag(vector.scalar_tensor_tensor(xo_t[:, :, :], a_t[:, :, :], 0.5,
                                                      x_t[:, :, :], ALU.mult, ALU.add))

            tok_st = s_st.reserve(1, 16)[0]

            @block.sync
            def _(sync, col=col, tok_res=tok_res):
                sync.wait_ge(s_dve.h, tok_res)
                s_st.tag(sync.dma_start(out=y_d[:, col].rearrange("(g p) t -> p g t", p=128),
                                        in_=xo_t[:, :, :]), 16)
            prev_res = tok_res
            prev_st = tok_st

    @block.sync
    def _(sync):
        sync.wait_ge(s_st.h, s_st.v)


def build_l1moe():
    nc = bass.Bass("TRN2", target_bir_lowering=False)
    x_d = nc.dram_tensor("x", [DM, T], F32, kind="ExternalInput").ap()
    y_d = nc.dram_tensor("y", [DM, T], F32, kind="ExternalOutput").ap()
    names = dict(rw=[1024, 16], identf=[128, 128], gwa=[16 * 1024, 1024],
                 uwa=[16 * 1024, 1024], dwa=[16 * 1024, 1024])
    P = {nm: nc.dram_tensor(nm, sh, F32, kind="ExternalInput").ap() for nm, sh in names.items()}
    scr = dict(
        xlnf=nc.dram_tensor("scr_xlnf", [DM, T], F32, kind="Internal").ap(),
        xln=nc.dram_tensor("scr_xln", [DM, T], BF16, kind="Internal").ap(),
        m=nc.dram_tensor("scr_m", [16, T], F32, kind="Internal").ap(),
        acc=nc.dram_tensor("scr_acc", [DM, T], F32, kind="Internal").ap(),
    )
    with ExitStack() as es:
        block = es.enter_context(nc.Block())
        c = Ctx(nc, es, block)
        emit_l1moe(c, x_d, y_d, P, scr, "moe")
        check_sems(c)
    return nc


def l1moe_inmaps(inp, xs):
    P = dict(rw=np.ascontiguousarray(np.asarray(inp["L1_ffn_router_w"], np.float32)),
             identf=np.eye(128, dtype=np.float32),
             gwa=np.ascontiguousarray(np.asarray(inp["L1_ffn_gate_w"], np.float32).reshape(16384, 1024)),
             uwa=np.ascontiguousarray(np.asarray(inp["L1_ffn_up_w"], np.float32).reshape(16384, 1024)),
             dwa=np.ascontiguousarray(np.asarray(inp["L1_ffn_down_w"], np.float32).reshape(16384, 1024)))
    return [dict(x=xs[cid], **P) for cid in range(NC8)]




def _snap(c):
    return {k: s.v for k, s in c.sems.items()}


def _joinall(eng, c, sn):
    for k, v in sn.items():
        eng.wait_ge(c.sems[k].h, v)


def emit_l3rms(c, x_d, y_d, st_d, tag, y32_d=None):
    block = c.block
    s_ld = c.sem("s_ld"); s_gp = c.sem("s_gp")
    s_act = c.sem("s_act"); s_dve = c.sem("s_dve"); s_mm = c.sem("s_mm")
    s_st = c.sem("s_st")
    Tc = 256
    NCH = T // Tc
    nG = 8
    with c.scope() as cf:
        sc_t = cf.sb(f"{tag}_sc", [128, 8], F32)
        ones_col = cf.sb(f"{tag}_onc", [128, 1], F32)
        ones_row = cf.sb(f"{tag}_onr", [1, 128], F32)
        x_t = cf.sb(f"{tag}_x", [128, nG, Tc], F32)
        xsq = cf.sb(f"{tag}_xsq", [128, nG, Tc], F32)
        t_f = cf.sb(f"{tag}_tf", [128, nG, Tc], F32)
        y_t = cf.sb(f"{tag}_y", [128, nG, Tc], BF16)
        y32_t = cf.sb(f"{tag}_y32", [128, nG, Tc], F32) if y32_d is not None else None
        stat = cf.sb(f"{tag}_st", [1, 4 * Tc], F32)
        ps_ssq = cf.ps(f"{tag}_pssq", [1, Tc])
        ps_rep = cf.ps(f"{tag}_prep", [128, Tc])
        sn = _snap(c)
        tok_gw = s_gp.reserve(1, 16)[0]

        @block.gpsimd
        def _(gpsimd):
            _joinall(gpsimd, c, sn)
            s_gp.tag(gpsimd.dma_start(out=sc_t[:, :], in_=st_d), 16)

        tok_cst = s_dve.reserve(1)[0]

        @block.vector
        def _(vector):
            vector.wait_ge(s_gp.h, tok_gw)
            vector.memset(ones_col[:, :], 1.0)
            s_dve.tag(vector.memset(ones_row[:, :], 1.0))

        prev = None
        for ci in range(NCH):
            col = slice(ci * Tc, (ci + 1) * Tc)
            tok_x = s_ld.reserve(1, 16)[0]

            @block.sync
            def _(sync, col=col, prev=prev):
                if prev is None:
                    sync.wait_ge(s_dve.h, tok_cst)
                else:
                    sync.wait_ge(s_dve.h, prev["xln"][-1])
                    sync.wait_ge(s_act.h, prev["sq"][-1])
                s_ld.tag(sync.dma_start(out=x_t[:, :, :],
                                        in_=x_d[:, col].rearrange("(g p) t -> p g t", p=128)), 16)

            cur = {}
            cur["sq"] = s_act.reserve(nG)
            cur["ssq"] = s_mm.reserve(1)[0]
            cur["ms"] = s_act.reserve(2)
            cur["rc"] = s_dve.reserve(2)
            cur["rep"] = s_mm.reserve(1)[0]
            cur["xln"] = s_dve.reserve(1 + nG + (nG if y32_d is not None else 0))

            @block.scalar
            def _(scalar, cur=cur, prev=prev, tok_x=tok_x):
                scalar.wait_ge(s_ld.h, tok_x)
                if prev is not None:
                    scalar.wait_ge(s_mm.h, prev["ssq"])
                    scalar.wait_ge(s_st.h, prev["stt"])
                for g in range(nG):
                    s_act.tag(scalar.square(xsq[:, g, :], x_t[:, g, :]))

            @block.tensor
            def _(tensor, cur=cur, prev=prev):
                if prev is not None:
                    tensor.wait_ge(s_act.h, prev["ms"][0])
                for g in range(nG):
                    tensor.wait_ge(s_act.h, cur["sq"][g])
                    ins = tensor.matmul(ps_ssq[0:1, :], ones_col[:, 0:1], xsq[:, g, :],
                                        start=(g == 0), stop=(g == nG - 1))
                s_mm.tag(ins)

            @block.scalar
            def _(scalar, cur=cur):
                scalar.wait_ge(s_mm.h, cur["ssq"])
                s_act.tag(scalar.mul(stat[0:1, 0:Tc], ps_ssq[0:1, :], 1.0 / DM))
                scalar.wait_ge(s_act.h, cur["ms"][0])
                s_act.tag(scalar.activation(stat[0:1, Tc:2 * Tc], stat[0:1, 0:Tc], AF.Sqrt))

            @block.vector
            def _(vector, cur=cur):
                vector.wait_ge(s_act.h, cur["ms"][1])
                s_dve.tag(vector.tensor_scalar(stat[0:1, 2 * Tc:3 * Tc], stat[0:1, Tc:2 * Tc],
                                               1e-6, 1.0, ALU.add, ALU.mult))
                vector.wait_ge(s_dve.h, cur["rc"][0])
                s_dve.tag(vector.reciprocal(stat[0:1, 3 * Tc:4 * Tc], stat[0:1, 2 * Tc:3 * Tc]))

            @block.tensor
            def _(tensor, cur=cur, prev=prev):
                tensor.wait_ge(s_dve.h, cur["rc"][1])
                if prev is not None:
                    tensor.wait_ge(s_dve.h, prev["xln"][-1])
                ins = tensor.matmul(ps_rep[:, :], ones_row[0:1, :], stat[0:1, 3 * Tc:4 * Tc],
                                    start=True, stop=True)
                s_mm.tag(ins)

            @block.vector
            def _(vector, cur=cur, prev=prev):
                vector.wait_ge(s_mm.h, cur["rep"])
                if prev is not None:
                    vector.wait_ge(s_st.h, prev["stt"])
                rrep_b = ps_rep[:, :].unsqueeze(1).broadcast_to([128, nG, Tc])
                s_dve.tag(vector.tensor_tensor(t_f[:, :, :], x_t[:, :, :], rrep_b, ALU.mult))
                vector.wait_ge(s_dve.h, cur["xln"][0])
                for g in range(nG):
                    s_dve.tag(vector.tensor_scalar(y_t[:, g, :], t_f[:, g, :],
                                                   sc_t[:, g:g + 1], 1.0, ALU.mult, ALU.mult))
                if y32_d is not None:
                    for g in range(nG):
                        s_dve.tag(vector.tensor_scalar(y32_t[:, g, :], t_f[:, g, :],
                                                       sc_t[:, g:g + 1], 1.0, ALU.mult, ALU.mult))

            cur["stt"] = s_st.reserve(1 if y32_d is None else 2, 16)[-1]

            @block.sync
            def _(sync, col=col, cur=cur):
                sync.wait_ge(s_dve.h, cur["xln"][-1])
                s_st.tag(sync.dma_start(out=y_d[:, col].rearrange("(g p) t -> p g t", p=128),
                                        in_=y_t[:, :, :]), 16)
                if y32_d is not None:
                    s_st.tag(sync.dma_start(out=y32_d[:, col].rearrange("(g p) t -> p g t", p=128),
                                            in_=y32_t[:, :, :]), 16)
            prev = cur

    @block.sync
    def _(sync):
        sync.wait_ge(s_st.h, s_st.v)


def emit_l3attn(c, x_d, y_d, P, scr, tag, upto=10):
    def _bail():
        @c.block.sync
        def _(sync):
            sync.wait_ge(c.sem("s_st").h, c.sem("s_st").v)

    block = c.block
    s_ld = c.sem("s_ld"); s_gp = c.sem("s_gp")
    s_act = c.sem("s_act"); s_dve = c.sem("s_dve"); s_mm = c.sem("s_mm")
    s_st = c.sem("s_st")
    nG = 8

    g1x = c.sb(f"{tag}_g1x", [16, T], BF16)

    emit_ln_full(c, x_d, scr["z"], f"{tag}ln")
    emit_l3rms(c, scr["z"], scr["r1"], P["s1_t"], f"{tag}r1", y32_d=scr["r1f"])
    if upto <= 1:
        return _bail()

    with c.scope() as c2:
        Tc = 512
        NCH = T // Tc
        wogT = c2.sb(f"{tag}_wog", [128, 8, 1024], BF16)
        wg1T = c2.sb(f"{tag}_wg1", [128, 8, 16], BF16)
        bog_t = c2.sb(f"{tag}_bog", [128, 8], F32)
        bg1c = c2.sb(f"{tag}_bg1", [16, 1], F32)
        r1_t = [c2.sb(f"{tag}_r1{i}", [128, nG, Tc], BF16) for i in range(2)]
        og_bf = c2.sb(f"{tag}_ogb", [128, nG, Tc], BF16)
        ps_ab = [c2.ps(f"{tag}_pab{i}", [128, Tc]) for i in range(2)]
        ps_g1 = c2.ps(f"{tag}_pg1", [16, Tc])
        sn = _snap(c)
        tok_w = s_gp.reserve(4, 16)[-1]

        @block.gpsimd
        def _(gpsimd):
            _joinall(gpsimd, c, sn)
            s_gp.tag(gpsimd.dma_start(out=wogT[:, :, :],
                                      in_=P["wogT"].rearrange("(g p) m -> p g m", p=128)), 16)
            s_gp.tag(gpsimd.dma_start(out=wg1T[:, :, :],
                                      in_=P["wg1T"].rearrange("(g p) m -> p g m", p=128)), 16)
            s_gp.tag(gpsimd.dma_start(out=bog_t[:, :], in_=P["bog_t"]), 16)
            s_gp.tag(gpsimd.dma_start(out=bg1c[:, :], in_=P["bg1c"]), 16)


        prev = None
        for ci in range(NCH):
            col = slice(ci * Tc, (ci + 1) * Tc)
            pxc = ci % 2
            tok_x = s_ld.reserve(1, 16)[0]

            @block.sync
            def _(sync, col=col, pxc=pxc, prev=prev, ci=ci):
                if prev is None:
                    sync.wait_ge(s_gp.h, tok_w)
                elif ci >= 2:
                    sync.wait_ge(s_mm.h, prev["pv"])
                s_ld.tag(sync.dma_start(out=r1_t[pxc][:, :, :],
                                        in_=scr["r1"][:, col].rearrange("(g p) t -> p g t", p=128)), 16)

            cur = {}
            cur["g1m"] = s_mm.reserve(1)[0]
            cur["ogm"] = s_mm.reserve(8)
            cur["pv"] = cur["ogm"][-1]
            cur["g1e"] = s_act.reserve(1)[0]
            cur["oge"] = s_act.reserve(8)

            @block.tensor
            def _(tensor, cur=cur, prev=prev, pxc=pxc, tok_x=tok_x):
                tensor.wait_ge(s_ld.h, tok_x)
                if prev is not None:
                    tensor.wait_ge(s_act.h, prev["g1e"])
                for k in range(nG):
                    ins = tensor.matmul(ps_g1[:, :], wg1T[:, k, :], r1_t[pxc][:, k, :],
                                        start=(k == 0), stop=(k == nG - 1))
                s_mm.tag(ins)
                for m in range(nG):
                    par = m % 2
                    if m >= 2:
                        tensor.wait_ge(s_act.h, cur["oge"][m - 2])
                    elif prev is not None:
                        tensor.wait_ge(s_act.h, prev["oge"][6 + m])
                    for k in range(nG):
                        ins = tensor.matmul(ps_ab[par][:, :], wogT[:, k, ts(m)],
                                            r1_t[pxc][:, k, :], start=(k == 0), stop=(k == nG - 1))
                    s_mm.tag(ins)

            @block.scalar
            def _(scalar, cur=cur, prev=prev, col=col):
                scalar.wait_ge(s_mm.h, cur["g1m"])
                s_act.tag(scalar.activation(g1x[0:16, col], ps_g1[:, :], AF.Silu,
                                            bias=bg1c[:, 0:1]))
                for m in range(nG):
                    par = m % 2
                    scalar.wait_ge(s_mm.h, cur["ogm"][m])
                    if prev is not None and m == 0:
                        scalar.wait_ge(s_st.h, prev["stt"])
                    s_act.tag(scalar.activation(og_bf[:, m, :], ps_ab[par][:, :], AF.Silu,
                                                bias=bog_t[:, m:m + 1]))

            cur["stt"] = s_st.reserve(1, 16)[0]

            @block.sync
            def _(sync, cur=cur, col=col):
                sync.wait_ge(s_act.h, cur["oge"][-1])
                s_st.tag(sync.dma_start(out=scr["ogs"][:, col].rearrange("(g p) t -> p g t", p=128),
                                        in_=og_bf[:, :, :]), 16)
            prev = cur

    if upto <= 2:
        return _bail()
    with c.scope() as c3:
        NT = T // 128
        wvT = c3.sb(f"{tag}_wv", [128, 8, 1024], BF16)
        wg2b = c3.sb(f"{tag}_wg2", [16, 1024], BF16)
        bg2r = c3.sb(f"{tag}_bg2", [1, 1024], BF16)
        ones_bf = c3.sb(f"{tag}_onb", [1, 128], BF16)
        r1t = [c3.sb(f"{tag}_r1t{i}", [128, nG, 128], BF16) for i in range(2)]
        obuf16 = c3.sb(f"{tag}_ob16", [128, 2, 1024], BF16)
        ps_ab = [c3.ps(f"{tag}_pab3{i}", [128, 512]) for i in range(2)]
        sn = _snap(c)
        tok_w = s_gp.reserve(3, 16)[-1]
        tok_onb = None

        @block.gpsimd
        def _(gpsimd):
            _joinall(gpsimd, c, sn)
            s_gp.tag(gpsimd.dma_start(out=wvT[:, :, :],
                                      in_=P["wvT"].rearrange("(g p) m -> p g m", p=128)), 16)
            s_gp.tag(gpsimd.dma_start(out=wg2b[:, :], in_=P["wg2b"]), 16)
            s_gp.tag(gpsimd.dma_start(out=bg2r[:, :], in_=P["bg2r"]), 16)

        tok_onb = s_dve.reserve(1)[0]

        @block.vector
        def _(vector):
            _joinall(vector, c, sn)
            s_dve.tag(vector.memset(ones_bf[:, :], 1.0))

        prev = None
        for i in range(NT):
            pxc = i % 2
            tsl = slice(i * 128, (i + 1) * 128)
            tok_x = s_ld.reserve(1, 16)[-1]

            @block.sync
            def _(sync, tsl=tsl, pxc=pxc, prev=prev, i=i):
                if prev is None:
                    sync.wait_ge(s_gp.h, tok_w)
                elif i >= 2:
                    sync.wait_ge(s_mm.h, prev["mm2"][-1])
                s_ld.tag(sync.dma_start(out=r1t[pxc][:, :, :],
                                        in_=scr["r1"][:, tsl].rearrange("(g p) t -> p g t", p=128)), 16)

            cur = {}
            cur["mm"] = s_mm.reserve(4)
            cur["mm2"] = cur["mm"]
            cur["ev"] = s_dve.reserve(2)
            cur["aev"] = s_act.reserve(2)

            @block.tensor
            def _(tensor, cur=cur, prev=prev, pxc=pxc, tok_x=tok_x, tsl=tsl):
                tensor.wait_ge(s_ld.h, tok_x)
                mi = 0
                for wt, which in ((wvT, 2),):
                    src_t = r1t[pxc]
                    for hf in range(2):
                        par = mi % 2
                        if prev is not None:
                            tensor.wait_ge(s_act.h, prev["aev"][mi]) if prev.get("aev") else None
                        for k in range(nG):
                            ins = tensor.matmul(ps_ab[par][:, :], src_t[:, k, :],
                                                wt[:, k, ts(hf, 512)],
                                                start=(k == 0), stop=(k == nG - 1))
                        s_mm.tag(ins)
                        mi += 1
                tensor.wait_ge(s_dve.h, tok_onb)
                for hf in range(2):
                    par = mi % 2
                    tensor.wait_ge(s_dve.h, cur["ev"][mi - 2])
                    tensor.matmul(ps_ab[par][:, :], g1x[:, tsl], wg2b[:, ts(hf, 512)],
                                  start=True, stop=False)
                    ins = tensor.matmul(ps_ab[par][:, :], ones_bf[0:1, :],
                                        bg2r[0:1, ts(hf, 512)], start=False, stop=True)
                    s_mm.tag(ins)
                    mi += 1

            @block.vector
            def _(vector, cur=cur, prev=prev):
                for j in range(2):
                    par = j % 2
                    vector.wait_ge(s_mm.h, cur["mm"][j])
                    if prev is not None:
                        vector.wait_ge(s_st.h, prev["stt"][-1])
                    s_dve.tag(vector.tensor_copy(obuf16[:, 0, ts(j, 512)], ps_ab[par][:, :]))

            @block.scalar
            def _(scalar, cur=cur):
                for hf in range(2):
                    par = hf
                    scalar.wait_ge(s_mm.h, cur["mm"][2 + hf])
                    s_act.tag(scalar.activation(obuf16[:, 1, ts(hf, 512)], ps_ab[par][:, :],
                                                AF.Sigmoid, scale=1.0 / 16.0))

            cur["stt"] = s_st.reserve(2, 16)

            @block.sync
            def _(sync, cur=cur, i=i):
                sync.wait_ge(s_dve.h, cur["ev"][-1])
                sync.wait_ge(s_act.h, cur["aev"][-1])
                for pl in range(2):
                    s_st.tag(sync.dma_start(out=scr["gin16"][pl * T + i * 128:pl * T + (i + 1) * 128, :],
                                            in_=obuf16[:, pl, :]), 16)
            prev = cur

    s_cc = c.sem("s_cc")
    tok_cc = s_cc.reserve(1, 1)[0]

    @block.gpsimd
    def _(gpsimd):
        gpsimd.wait_ge(s_st.h, s_st.v)
        ins = gpsimd.collective_compute("AllGather", ALU.bypass,
                                        replica_groups=[list(range(8))],
                                        ins=[scr["gin16"]], outs=[scr["gout16"]])
        s_cc.tag(ins, 1)

    with c.scope() as ck:
        NT = T // 128
        wkT = ck.sb(f"{tag}_wkk", [128, 8, 1024], F32)
        r1ft = [ck.sb(f"{tag}_r1fk{i}", [128, 8, 128], F32) for i in range(2)]
        kobuf = ck.sb(f"{tag}_kob", [128, 1024], F32)
        ps_ab = [ck.ps(f"{tag}_pabk{i}", [128, 512]) for i in range(2)]
        sn = _snap(c)
        tok_w = s_gp.reserve(1, 16)[0]

        @block.gpsimd
        def _(gpsimd):
            _joinall(gpsimd, c, sn)
            s_gp.tag(gpsimd.dma_start(out=wkT[:, :, :],
                                      in_=P["wkT"].rearrange("(g p) m -> p g m", p=128)), 16)

        prev = None
        for i in range(NT):
            pxc = i % 2
            tsl = slice(i * 128, (i + 1) * 128)
            tok_x = s_ld.reserve(1, 16)[0]

            @block.sync
            def _(sync, tsl=tsl, pxc=pxc, prev=prev, i=i):
                if prev is None:
                    sync.wait_ge(s_gp.h, tok_w)
                elif i >= 2:
                    sync.wait_ge(s_mm.h, prev["mm"][-1])
                s_ld.tag(sync.dma_start(out=r1ft[pxc][:, :, :],
                                        in_=scr["r1f"][:, tsl].rearrange("(g p) t -> p g t", p=128)), 16)

            cur = {}
            cur["mm"] = s_mm.reserve(2)
            cur["ev"] = s_dve.reserve(2)

            @block.tensor
            def _(tensor, cur=cur, prev=prev, pxc=pxc, tok_x=tok_x):
                tensor.wait_ge(s_ld.h, tok_x)
                for hf in range(2):
                    par = hf
                    if prev is not None:
                        tensor.wait_ge(s_dve.h, prev["ev"][hf])
                    for k in range(nG):
                        ins = tensor.matmul(ps_ab[par][:, :], r1ft[pxc][:, k, :],
                                            wkT[:, k, ts(hf, 512)],
                                            start=(k == 0), stop=(k == nG - 1))
                    s_mm.tag(ins)

            @block.vector
            def _(vector, cur=cur, prev=prev):
                for hf in range(2):
                    vector.wait_ge(s_mm.h, cur["mm"][hf])
                    if prev is not None:
                        vector.wait_ge(s_st.h, prev["stt"])
                    s_dve.tag(vector.tensor_copy(kobuf[:, ts(hf, 512)], ps_ab[hf][:, :]))

            cur["stt"] = s_st.reserve(1, 16)[0]

            @block.sync
            def _(sync, cur=cur, i=i):
                sync.wait_ge(s_dve.h, cur["ev"][-1])
                s_st.tag(sync.dma_start(out=scr["gin32"][i * 128:(i + 1) * 128, :],
                                        in_=kobuf[:, :]), 16)
            prev = cur

    tok_cc1 = s_cc.reserve(1, 1)[0]

    @block.gpsimd
    def _(gpsimd):
        gpsimd.wait_ge(s_st.h, s_st.v)
        ins = gpsimd.collective_compute("AllGather", ALU.bypass,
                                        replica_groups=[list(range(8))],
                                        ins=[scr["gin32"]], outs=[scr["gout32"]])
        s_cc.tag(ins, 1)

    with c.scope() as cq:
        NT = T // 128
        wqT = cq.sb(f"{tag}_wqq", [128, 8, 1024], F32)
        r1ft = [cq.sb(f"{tag}_r1fq{i}", [128, 8, 128], F32) for i in range(2)]
        qobuf = cq.sb(f"{tag}_qob", [128, 1024], F32)
        ps_ab = [cq.ps(f"{tag}_pabq{i}", [128, 512]) for i in range(2)]
        sn = _snap(c)
        tok_w = s_gp.reserve(1, 16)[0]

        @block.gpsimd
        def _(gpsimd):
            _joinall(gpsimd, c, sn)
            s_gp.tag(gpsimd.dma_start(out=wqT[:, :, :],
                                      in_=P["wqT"].rearrange("(g p) m -> p g m", p=128)), 16)

        prev = None
        for i in range(NT):
            pxc = i % 2
            tsl = slice(i * 128, (i + 1) * 128)
            tok_x = s_ld.reserve(1, 16)[0]

            @block.sync
            def _(sync, tsl=tsl, pxc=pxc, prev=prev, i=i):
                if prev is None:
                    sync.wait_ge(s_gp.h, tok_w)
                elif i >= 2:
                    sync.wait_ge(s_mm.h, prev["mm"][-1])
                s_ld.tag(sync.dma_start(out=r1ft[pxc][:, :, :],
                                        in_=scr["r1f"][:, tsl].rearrange("(g p) t -> p g t", p=128)), 16)

            cur = {}
            cur["mm"] = s_mm.reserve(2)
            cur["ev"] = s_dve.reserve(2)

            @block.tensor
            def _(tensor, cur=cur, prev=prev, pxc=pxc, tok_x=tok_x):
                tensor.wait_ge(s_ld.h, tok_x)
                for hf in range(2):
                    par = hf
                    if prev is not None:
                        tensor.wait_ge(s_dve.h, prev["ev"][hf])
                    for k in range(nG):
                        ins = tensor.matmul(ps_ab[par][:, :], r1ft[pxc][:, k, :],
                                            wqT[:, k, ts(hf, 512)],
                                            start=(k == 0), stop=(k == nG - 1))
                    s_mm.tag(ins)

            @block.vector
            def _(vector, cur=cur, prev=prev):
                for hf in range(2):
                    vector.wait_ge(s_mm.h, cur["mm"][hf])
                    if prev is not None:
                        vector.wait_ge(s_st.h, prev["stt"])
                    s_dve.tag(vector.tensor_copy(qobuf[:, ts(hf, 512)], ps_ab[hf][:, :]))

            cur["stt"] = s_st.reserve(1, 16)[0]

            @block.sync
            def _(sync, cur=cur, i=i):
                sync.wait_ge(s_dve.h, cur["ev"][-1])
                s_st.tag(sync.dma_start(out=scr["qloc"][i * 128:(i + 1) * 128, :],
                                        in_=qobuf[:, :]), 16)
            prev = cur

    if upto <= 3:
        return _bail()
    if upto <= 4:
        return _bail()
    with c.scope() as c5:
        NT = T // 128
        tmask = c5.sb(f"{tag}_tm", [128, 8], F32)
        imask = c5.sb(f"{tag}_im", [128, 8], F32)
        eps128 = c5.sb(f"{tag}_eps", [128, 1], F32)
        q8 = [c5.sb(f"{tag}_q{i}", [128, 1024], F32) for i in range(2)]
        k8 = [c5.sb(f"{tag}_k{i}", [128, 8, 1024], F32) for i in range(2)]
        v8 = [c5.sb(f"{tag}_v{i}", [128, 8, 1024], BF16) for i in range(2)]
        a8 = [c5.sb(f"{tag}_a{i}", [128, 8, 1024], BF16) for i in range(2)]
        qdec = c5.sb(f"{tag}_qd", [128, 1024], F32)
        atld = c5.sb(f"{tag}_at", [128, 1024], F32)
        tmp = c5.sb(f"{tag}_tp", [128, 1024], F32)
        tmp2 = c5.sb(f"{tag}_t2", [128, 1024], F32)
        y_t = c5.sb(f"{tag}_y", [128, 1024], F32)
        d_t = c5.sb(f"{tag}_d", [128, 1024], F32)
        oln_bf = [c5.sb(f"{tag}_ol{i}", [128, 1024], BF16) for i in range(2)]
        stt_s = c5.sb(f"{tag}_ss", [128, 16, 4], F32)
        sd_s = c5.sb(f"{tag}_sd", [128, 16], F32)
        sn = _snap(c)
        tok_msk = s_gp.reserve(2, 16)[-1]

        @block.gpsimd
        def _(gpsimd):
            _joinall(gpsimd, c, sn)
            s_gp.tag(gpsimd.dma_start(out=tmask[:, :], in_=P["tmask"]), 16)
            s_gp.tag(gpsimd.dma_start(out=imask[:, :], in_=P["imask"]), 16)

        tok_eps = s_dve.reserve(1)[0]

        @block.vector
        def _(vector):
            _joinall(vector, c, sn)
            s_dve.tag(vector.memset(eps128[:, :], 1e-5))

        prev = None
        for i in range(NT):
            pxc = i % 2
            tok_ld = s_ld.reserve(25, 16)[-1]

            @block.sync
            def _(sync, i=i, pxc=pxc, prev=prev):
                if prev is None:
                    sync.wait_ge(s_gp.h, s_gp.v)
                    sync.wait_ge(s_cc.h, tok_cc)
                elif i >= 2:
                    sync.wait_ge(s_dve.h, prev["free"])
                s_ld.tag(sync.dma_start(out=q8[pxc][:, :],
                                        in_=scr["qloc"][i * 128:(i + 1) * 128, :]), 16)
                for tau in range(8):
                    b32 = tau * T
                    b16 = tau * 2 * T
                    s_ld.tag(sync.dma_start(out=k8[pxc][:, tau, :],
                                            in_=scr["gout32"][b32 + i * 128:b32 + (i + 1) * 128, :]), 16)
                    s_ld.tag(sync.dma_start(out=v8[pxc][:, tau, :],
                                            in_=scr["gout16"][b16 + i * 128:b16 + (i + 1) * 128, :]), 16)
                    s_ld.tag(sync.dma_start(out=a8[pxc][:, tau, :],
                                            in_=scr["gout16"][b16 + T + i * 128:b16 + T + (i + 1) * 128, :]), 16)

            NOPS = 1 + 7 * 7 + 5 + 8
            tks = s_dve.reserve(NOPS)
            tok_sqr = s_act.reserve(1)[0]
            cur = {"free": None}

            @block.vector
            def _(vector, i=i, pxc=pxc, prev=prev, tks=tks, tok_ld=tok_ld, cur=cur):
                ji = [0]

                def step(w_extra=()):
                    j = ji[0]
                    if j > 0:
                        vector.wait_ge(s_dve.h, tks[j - 1])
                    for sem, v in w_extra:
                        vector.wait_ge(sem.h, v)
                    ji[0] += 1
                    return tks[j]

                w0 = [(s_ld, tok_ld), (s_dve, tok_eps)]
                if prev is not None:
                    w0.append((s_st, prev["stt"]))
                step(w0)
                vector.tensor_copy(qdec[:, :], q8[pxc][:, :]).then_inc(s_dve.h, 1)
                s_dve.emitted += 1
                for tau in range(7, -1, -1):
                    step()
                    vector.tensor_tensor(tmp[:, :], qdec[:, :], k8[pxc][:, tau, :],
                                         ALU.mult).then_inc(s_dve.h, 1)
                    s_dve.emitted += 1
                    step()
                    vector.tensor_reduce(stt_s[:, :, 0:1], tmp[:, :].rearrange("p (h k) -> p h k", k=64),
                                         mybir.AxisListType.X, ALU.add).then_inc(s_dve.h, 1)
                    s_dve.emitted += 1
                    step()
                    vector.tensor_scalar(stt_s[:, :, 1:2], stt_s[:, :, 0:1], tmask[:, tau:tau + 1],
                                         1.0, ALU.mult, ALU.mult).then_inc(s_dve.h, 1)
                    s_dve.emitted += 1
                    step()
                    pbc = stt_s[:, :, 1:2].broadcast_to([128, 16, 64])
                    dst = y_t if tau == 7 else tmp2
                    vector.tensor_tensor(dst[:, :].rearrange("p (h k) -> p h k", k=64),
                                         v8[pxc][:, tau, :].rearrange("p (h k) -> p h k", k=64),
                                         pbc, ALU.mult).then_inc(s_dve.h, 1)
                    s_dve.emitted += 1
                    if tau != 7:
                        step()
                        vector.tensor_tensor(y_t[:, :], y_t[:, :], tmp2[:, :],
                                             ALU.add).then_inc(s_dve.h, 1)
                        s_dve.emitted += 1
                    else:
                        step()
                        vector.tensor_copy(tmp2[:, :], y_t[:, :]).then_inc(s_dve.h, 1)
                        s_dve.emitted += 1
                    if tau > 0:
                        step()
                        vector.tensor_scalar(atld[:, :], a8[pxc][:, tau, :], tmask[:, tau:tau + 1],
                                             imask[:, tau:tau + 1], ALU.mult, ALU.add).then_inc(s_dve.h, 1)
                        s_dve.emitted += 1
                        step()
                        vector.tensor_tensor(qdec[:, :], qdec[:, :], atld[:, :],
                                             ALU.mult).then_inc(s_dve.h, 1)
                        s_dve.emitted += 1
                step()
                vector.tensor_reduce(stt_s[:, :, 2:3], y_t[:, :].rearrange("p (h k) -> p h k", k=64),
                                     mybir.AxisListType.X, ALU.add).then_inc(s_dve.h, 1)
                s_dve.emitted += 1
                step()
                vector.tensor_scalar(stt_s[:, :, 2:3], stt_s[:, :, 2:3], 1.0 / 64.0, 1.0,
                                     ALU.mult, ALU.mult).then_inc(s_dve.h, 1)
                s_dve.emitted += 1
                step()
                mbc = stt_s[:, :, 2:3].broadcast_to([128, 16, 64])
                vector.tensor_tensor(d_t[:, :].rearrange("p (h k) -> p h k", k=64),
                                     y_t[:, :].rearrange("p (h k) -> p h k", k=64),
                                     mbc, ALU.subtract).then_inc(s_dve.h, 1)
                s_dve.emitted += 1
                step()
                vector.tensor_tensor(tmp[:, :], d_t[:, :], d_t[:, :], ALU.mult).then_inc(s_dve.h, 1)
                s_dve.emitted += 1
                step()
                vector.tensor_reduce(stt_s[:, :, 3:4], tmp[:, :].rearrange("p (h k) -> p h k", k=64),
                                     mybir.AxisListType.X, ALU.add).then_inc(s_dve.h, 1)
                s_dve.emitted += 1
                step()
                vector.tensor_scalar(stt_s[:, :, 3:4], stt_s[:, :, 3:4], 1.0 / 64.0, 1.0,
                                     ALU.mult, ALU.mult).then_inc(s_dve.h, 1)
                s_dve.emitted += 1
                step([(s_act, tok_sqr)])
                vector.reciprocal(stt_s[:, :, 2:3], stt_s[:, :, 3:4]).then_inc(s_dve.h, 1)
                s_dve.emitted += 1
                step()
                rbc = stt_s[:, :, 2:3].broadcast_to([128, 16, 64])
                vector.tensor_tensor(oln_bf[pxc][:, :].rearrange("p (h k) -> p h k", k=64),
                                     d_t[:, :].rearrange("p (h k) -> p h k", k=64),
                                     rbc, ALU.mult).then_inc(s_dve.h, 1)
                s_dve.emitted += 1
                cur["free"] = tks[ji[0] - 1]

            @block.scalar
            def _(scalar, tks=tks, tok_sqr=tok_sqr):
                scalar.wait_ge(s_dve.h, tks[NOPS - 3])
                s_act.tag(scalar.activation(stt_s[:, :, 3:4], stt_s[:, :, 3:4], AF.Sqrt,
                                            bias=eps128[:, 0:1]))

            cur["stt"] = s_st.reserve(1, 16)[0]

            @block.sync
            def _(sync, i=i, pxc=pxc, cur=cur, tks=tks):
                sync.wait_ge(s_dve.h, tks[-1])
                s_st.tag(sync.dma_start(out=scr["oln"][i * 128:(i + 1) * 128, :],
                                        in_=oln_bf[pxc][:, :]), 16)
            prev = cur

    if upto <= 5:
        return _bail()
    with c.scope() as c6:
        Tc = 512
        NCH = T // Tc
        woT = c6.sb(f"{tag}_wo", [128, 8, 1024], BF16)
        o_fm = c6.sb(f"{tag}_ofm", [128, nG, Tc], BF16)
        ogs_t = c6.sb(f"{tag}_ogt", [128, nG, Tc], BF16)
        z_t = c6.sb(f"{tag}_zt", [128, nG, Tc], F32)
        og_bf = c6.sb(f"{tag}_ogg", [128, nG, Tc], BF16)
        x1_t = c6.sb(f"{tag}_x1", [128, nG, Tc], F32)
        ps_ab = [c6.ps(f"{tag}_pab6{i}", [128, Tc]) for i in range(2)]
        sn = _snap(c)
        tok_w = s_gp.reserve(1, 16)[0]

        @block.gpsimd
        def _(gpsimd):
            _joinall(gpsimd, c, sn)
            s_gp.tag(gpsimd.dma_start(out=woT[:, :, :],
                                      in_=P["woT"].rearrange("(g p) m -> p g m", p=128)), 16)

        prev = None
        for ci in range(NCH):
            col = slice(ci * Tc, (ci + 1) * Tc)
            t0 = ci * Tc
            tok_tr = s_act.reserve(32, 16)

            @block.scalar
            def _(scalar, ci=ci, t0=t0, prev=prev):
                if prev is None:
                    scalar.wait_ge(s_gp.h, tok_w)
                else:
                    scalar.wait_ge(s_dve.h, prev["gg"])
                for g in range(nG):
                    for j in range(4):
                        s_act.tag(scalar.dma_start_transpose(
                            o_fm[:, g, ts(j)],
                            scr["oln"][t0 + j * 128:t0 + (j + 1) * 128, ts(g)]), 16)

            tok_x = s_ld.reserve(2, 16)[-1]

            @block.sync
            def _(sync, col=col, prev=prev):
                if prev is not None:
                    sync.wait_ge(s_dve.h, prev["gg"])
                    sync.wait_ge(s_dve.h, prev["res"][-1])
                s_ld.tag(sync.dma_start(out=ogs_t[:, :, :],
                                        in_=scr["ogs"][:, col].rearrange("(g p) t -> p g t", p=128)), 16)
                s_ld.tag(sync.dma_start(out=z_t[:, :, :],
                                        in_=scr["z"][:, col].rearrange("(g p) t -> p g t", p=128)), 16)

            cur = {}
            cur["gg"] = s_dve.reserve(1)[0]

            @block.vector
            def _(vector, cur=cur, prev=prev, tok_x=tok_x, tok_tr=tok_tr):
                vector.wait_ge(s_act.h, tok_tr[-1])
                vector.wait_ge(s_ld.h, tok_x)
                if prev is not None:
                    vector.wait_ge(s_mm.h, prev["mm"][-1])
                s_dve.tag(vector.tensor_tensor(og_bf[:, :, :], o_fm[:, :, :],
                                               ogs_t[:, :, :], ALU.mult))

            cur["mm"] = s_mm.reserve(8)
            cur["res"] = s_dve.reserve(8)

            @block.tensor
            def _(tensor, cur=cur, prev=prev):
                tensor.wait_ge(s_dve.h, cur["gg"])
                for m in range(nG):
                    par = m % 2
                    if m >= 2:
                        tensor.wait_ge(s_dve.h, cur["res"][m - 2])
                    elif prev is not None:
                        tensor.wait_ge(s_dve.h, prev["res"][6 + m])
                    for k in range(nG):
                        ins = tensor.matmul(ps_ab[par][:, :], woT[:, k, ts(m)],
                                            og_bf[:, k, :], start=(k == 0), stop=(k == nG - 1))
                    s_mm.tag(ins)

            @block.vector
            def _(vector, cur=cur, prev=prev):
                for m in range(nG):
                    par = m % 2
                    vector.wait_ge(s_mm.h, cur["mm"][m])
                    if m == 0 and prev is not None:
                        vector.wait_ge(s_st.h, prev["stt"])
                    s_dve.tag(vector.scalar_tensor_tensor(x1_t[:, m, :], ps_ab[par][:, :], 1.0,
                                                          z_t[:, m, :], ALU.mult, ALU.add))

            cur["stt"] = s_st.reserve(1, 16)[0]

            @block.sync
            def _(sync, col=col, cur=cur):
                sync.wait_ge(s_dve.h, cur["res"][-1])
                s_st.tag(sync.dma_start(out=scr["x1"][:, col].rearrange("(g p) t -> p g t", p=128),
                                        in_=x1_t[:, :, :]), 16)
            prev = cur

    if upto <= 6:
        return _bail()
    emit_l3rms(c, scr["x1"], scr["r2"], P["s2_t"], f"{tag}r2")
    if upto <= 7:
        return _bail()

    with c.scope() as c8:
        Tc = 512
        NCH = T // Tc
        wg_h = [c8.sb(f"{tag}_wgh{i}", [128, 8, 2048], BF16) for i in range(2)]
        wu_h = [c8.sb(f"{tag}_wuh{i}", [128, 8, 2048], BF16) for i in range(2)]
        r2_t = [c8.sb(f"{tag}_r28{i}", [128, nG, Tc], BF16) for i in range(2)]
        sg_bf = c8.sb(f"{tag}_sg8", [128, 16, Tc], BF16)
        hh_bf = c8.sb(f"{tag}_hh8", [128, 16, Tc], BF16)
        ps_ab = [c8.ps(f"{tag}_pab8{i}", [128, Tc]) for i in range(2)]
        sn = _snap(c)
        tok_w = [None, None]
        for hh in range(2):
            tok_w[hh] = s_gp.reserve(2, 16)[-1]

            @block.gpsimd
            def _(gpsimd, hh=hh):
                if hh == 0:
                    _joinall(gpsimd, c, sn)
                s_gp.tag(gpsimd.dma_start(
                    out=wg_h[hh][:, :, :],
                    in_=P["wgaT"][:, hh * 2048:(hh + 1) * 2048].rearrange("(g p) m -> p g m", p=128)), 16)
                s_gp.tag(gpsimd.dma_start(
                    out=wu_h[hh][:, :, :],
                    in_=P["wuaT"][:, hh * 2048:(hh + 1) * 2048].rearrange("(g p) m -> p g m", p=128)), 16)

        prev = None
        for hh in range(2):
            for ci in range(NCH):
                gidx = hh * NCH + ci
                pxc = gidx % 2
                col = slice(ci * Tc, (ci + 1) * Tc)
                tok_x = s_ld.reserve(1, 16)[0]

                @block.sync
                def _(sync, col=col, pxc=pxc, prev=prev, gidx=gidx, hh=hh):
                    if prev is None:
                        sync.wait_ge(s_gp.h, tok_w[0])
                    elif gidx >= 2:
                        sync.wait_ge(s_mm.h, prev["pv"])
                    s_ld.tag(sync.dma_start(out=r2_t[pxc][:, :, :],
                                            in_=scr["r2"][:, col].rearrange("(g p) t -> p g t", p=128)), 16)

                cur = {}
                cur["mm"] = s_mm.reserve(32)
                cur["pv"] = cur["mm"][-1]
                cur["sil"] = s_act.reserve(16)
                cur["ev"] = s_dve.reserve(16)

                @block.tensor
                def _(tensor, cur=cur, prev=prev, pxc=pxc, hh=hh, ci=ci, tok_x=tok_x):
                    tensor.wait_ge(s_ld.h, tok_x)
                    if ci == 0:
                        tensor.wait_ge(s_gp.h, tok_w[hh])
                    for m in range(16):
                        par = m % 2
                        if m >= 2:
                            tensor.wait_ge(s_dve.h, cur["ev"][m - 2])
                        elif prev is not None:
                            tensor.wait_ge(s_dve.h, prev["ev"][14 + m])
                        for k in range(nG):
                            ins = tensor.matmul(ps_ab[par][:, :], wg_h[hh][:, k, ts(m)],
                                                r2_t[pxc][:, k, :], start=(k == 0), stop=(k == nG - 1))
                        s_mm.tag(ins)
                        tensor.wait_ge(s_act.h, cur["sil"][m])
                        for k in range(nG):
                            ins = tensor.matmul(ps_ab[par][:, :], wu_h[hh][:, k, ts(m)],
                                                r2_t[pxc][:, k, :], start=(k == 0), stop=(k == nG - 1))
                        s_mm.tag(ins)

                @block.scalar
                def _(scalar, cur=cur, prev=prev):
                    for m in range(16):
                        par = m % 2
                        scalar.wait_ge(s_mm.h, cur["mm"][2 * m])
                        if m < 2 and prev is not None:
                            scalar.wait_ge(s_dve.h, prev["ev"][14 + m])
                        elif m >= 2:
                            scalar.wait_ge(s_dve.h, cur["ev"][m - 2])
                        s_act.tag(scalar.activation(sg_bf[:, m, :], ps_ab[par][:, :], AF.Silu))

                @block.vector
                def _(vector, cur=cur, prev=prev):
                    for m in range(16):
                        par = m % 2
                        vector.wait_ge(s_mm.h, cur["mm"][2 * m + 1])
                        if m == 0 and prev is not None:
                            vector.wait_ge(s_st.h, prev["stt"])
                        s_dve.tag(vector.tensor_tensor(hh_bf[:, m, :], sg_bf[:, m, :],
                                                       ps_ab[par][:, :], ALU.mult))

                cur["stt"] = s_st.reserve(1, 16)[0]

                @block.sync
                def _(sync, col=col, cur=cur, hh=hh):
                    sync.wait_ge(s_dve.h, cur["ev"][-1])
                    s_st.tag(sync.dma_start(
                        out=scr["hh"][hh * 2048:(hh + 1) * 2048, col].rearrange("(g p) t -> p g t", p=128),
                        in_=hh_bf[:, :, :]), 16)
                prev = cur

    with c.scope() as ca:
        Tc = 512
        NCH = T // Tc
        wda = ca.sb(f"{tag}_wda", [128, 32, 1024], BF16)
        hh_t = [ca.sb(f"{tag}_hht{i}", [128, 32, Tc], BF16) for i in range(2)]
        x_t = ca.sb(f"{tag}_xt", [128, nG, Tc], F32)
        x1_t = ca.sb(f"{tag}_x1c", [128, nG, Tc], F32)
        t1 = ca.sb(f"{tag}_t1", [128, Tc], F32)
        xo_t = ca.sb(f"{tag}_xoc", [128, nG, Tc], F32)
        ps_ab = [ca.ps(f"{tag}_pabA{i}", [128, Tc]) for i in range(2)]
        sn = _snap(c)
        tok_w = s_gp.reserve(1, 16)[0]

        @block.gpsimd
        def _(gpsimd):
            _joinall(gpsimd, c, sn)
            s_gp.tag(gpsimd.dma_start(out=wda[:, :, :],
                                      in_=P["wdaT"].rearrange("(g p) m -> p g m", p=128)), 16)

        prev = None
        for ci in range(NCH):
            col = slice(ci * Tc, (ci + 1) * Tc)
            pxc = ci % 2
            tok_x = s_ld.reserve(3, 16)[-1]

            @block.sync
            def _(sync, col=col, pxc=pxc, prev=prev, ci=ci):
                if prev is None:
                    sync.wait_ge(s_gp.h, tok_w)
                else:
                    if ci >= 2:
                        sync.wait_ge(s_mm.h, prev["pv"])
                    sync.wait_ge(s_dve.h, prev["res"][-1])
                s_ld.tag(sync.dma_start(out=hh_t[pxc][:, :, :],
                                        in_=scr["hh"][:, col].rearrange("(g p) t -> p g t", p=128)), 16)
                s_ld.tag(sync.dma_start(out=x_t[:, :, :],
                                        in_=x_d[:, col].rearrange("(g p) t -> p g t", p=128)), 16)
                s_ld.tag(sync.dma_start(out=x1_t[:, :, :],
                                        in_=scr["x1"][:, col].rearrange("(g p) t -> p g t", p=128)), 16)

            cur = {}
            cur["mm"] = s_mm.reserve(8)
            cur["pv"] = cur["mm"][-1]
            cur["res"] = s_dve.reserve(16)

            @block.tensor
            def _(tensor, cur=cur, prev=prev, pxc=pxc, tok_x=tok_x):
                tensor.wait_ge(s_ld.h, tok_x)
                for m in range(nG):
                    par = m % 2
                    if m >= 2:
                        tensor.wait_ge(s_dve.h, cur["res"][2 * (m - 2) + 1])
                    elif prev is not None:
                        tensor.wait_ge(s_dve.h, prev["res"][2 * (6 + m) + 1])
                    for k in range(32):
                        ins = tensor.matmul(ps_ab[par][:, :], wda[:, k, ts(m)],
                                            hh_t[pxc][:, k, :], start=(k == 0), stop=(k == 31))
                    s_mm.tag(ins)

            @block.vector
            def _(vector, cur=cur, prev=prev, tok_x=tok_x):
                for m in range(nG):
                    par = m % 2
                    vector.wait_ge(s_mm.h, cur["mm"][m])
                    if m == 0 and prev is not None:
                        vector.wait_ge(s_st.h, prev["stt"])
                    s_dve.tag(vector.scalar_tensor_tensor(t1[:, :], ps_ab[par][:, :], 1.0,
                                                          x1_t[:, m, :], ALU.mult, ALU.add))
                    vector.wait_ge(s_dve.h, cur["res"][2 * m])
                    s_dve.tag(vector.tensor_tensor(xo_t[:, m, :], t1[:, :], x_t[:, m, :],
                                                   ALU.add))

            cur["stt"] = s_st.reserve(1, 16)[0]

            @block.sync
            def _(sync, col=col, cur=cur):
                sync.wait_ge(s_dve.h, cur["res"][-1])
                s_st.tag(sync.dma_start(out=y_d[:, col].rearrange("(g p) t -> p g t", p=128),
                                        in_=xo_t[:, :, :]), 16)
            prev = cur

    @block.sync
    def _(sync):
        sync.wait_ge(s_st.h, s_st.v)


def build_l3attn(core_id=0, upto=10):
    nc = bass.Bass("TRN2", target_bir_lowering=False)
    x_d = nc.dram_tensor("x", [DM, T], F32, kind="ExternalInput").ap()
    y_d = nc.dram_tensor("y", [DM, T], F32, kind="ExternalOutput").ap()
    names = dict(s1_t=[128, 8], s2_t=[128, 8], wqT=[1024, 1024], wkT=[1024, 1024],
                 wvT=[1024, 1024], wogT=[1024, 1024], woT=[1024, 1024],
                 wg1T=[1024, 16], bg1c=[16, 1], bog_t=[128, 8], wg2b=[16, 1024],
                 bg2r=[1, 1024],
                 wgaT=[1024, 4096], wuaT=[1024, 4096], wdaT=[4096, 1024],
                 tmask=[128, 8], imask=[128, 8])
    P = {nm: nc.dram_tensor(nm, sh, F32, kind="ExternalInput").ap() for nm, sh in names.items()}
    scr = dict(
        z=nc.dram_tensor("scr_z", [DM, T], F32, kind="Internal").ap(),
        r1=nc.dram_tensor("scr_r1", [DM, T], BF16, kind="Internal").ap(),
        r1f=nc.dram_tensor("scr_r1f", [DM, T], F32, kind="Internal").ap(),
        ogs=nc.dram_tensor("scr_ogs", [DM, T], BF16, kind="Internal").ap(),
        qloc=nc.dram_tensor("scr_qloc", [T, DM], F32, kind="Internal").ap(),
        gin32=nc.dram_tensor("scr_gin32", [T, DM], F32, kind="Internal").ap(),
        gout32=nc.dram_tensor("scr_gout32", [8 * T, DM], F32, kind="Internal").ap(),
        gin16=nc.dram_tensor("scr_gin16", [2 * T, DM], BF16, kind="Internal").ap(),
        gout16=nc.dram_tensor("scr_gout16", [8 * 2 * T, DM], BF16, kind="Internal").ap(),
        oln=nc.dram_tensor("scr_oln", [T, DM], BF16, kind="Internal").ap(),
        x1=nc.dram_tensor("scr_x1", [DM, T], F32, kind="Internal").ap(),
        r2=nc.dram_tensor("scr_r2", [DM, T], BF16, kind="Internal").ap(),
        gh=nc.dram_tensor("scr_gh", [4096, T], BF16, kind="Internal").ap(),
        hh=nc.dram_tensor("scr_hh", [4096, T], BF16, kind="Internal").ap(),
    )
    with ExitStack() as es:
        block = es.enter_context(nc.Block())
        c = Ctx(nc, es, block)
        emit_l3attn(c, x_d, y_d, P, scr, "gla", upto=upto)
        check_sems(c)
    return nc


def l3attn_inmaps(inp, xs):
    f = lambda k: np.asarray(inp[k], np.float32)
    base = dict(
        s1_t=_tile_vec(f("L3_attn_scale1"), 8),
        s2_t=_tile_vec(f("L3_attn_scale2"), 8),
        wqT=np.ascontiguousarray(f("L3_attn_wq").T),
        wkT=np.ascontiguousarray(f("L3_attn_wk").T),
        wvT=np.ascontiguousarray(f("L3_attn_wv").T),
        wogT=np.ascontiguousarray(f("L3_attn_wog").T),
        woT=np.ascontiguousarray(f("L3_attn_wo").T),
        wg1T=np.ascontiguousarray(f("L3_attn_wg1").T),
        bg1c=np.ascontiguousarray(f("L3_attn_bg1").reshape(16, 1)),
        bog_t=_tile_vec(f("L3_attn_bog"), 8),
        wg2b=np.ascontiguousarray(f("L3_attn_wg2").T),
        bg2r=np.ascontiguousarray(f("L3_attn_bg2").reshape(1, 1024)),
        wgaT=np.ascontiguousarray(f("L3_attn_w_gate").T),
        wuaT=np.ascontiguousarray(f("L3_attn_w_up").T),
        wdaT=np.ascontiguousarray(f("L3_attn_w_down").T),
    )
    ims = []
    for cid in range(NC8):
        tm = np.zeros((128, 8), np.float32)
        tm[:, 0:cid + 1] = 1.0
        ims.append(dict(x=xs[cid], tmask=tm, imask=np.ascontiguousarray(1.0 - tm), **base))
    return ims


def _tile_vec(v, nch):
    return np.ascontiguousarray(np.asarray(v, dtype=np.float32).reshape(nch, 128).T)


def l0ffn_inmaps(inp, xs):
    b1t = _tile_vec(inp["L0_ffn_b1"], 32)
    b2t = _tile_vec(inp["L0_ffn_b2"], 8)
    return [dict(x=xs[cid], w1=inp["L0_ffn_w1"], w2=inp["L0_ffn_w2"],
                 b1t=b1t, b2t=b2t) for cid in range(NC8)]


def emit_fence(c):
    sn = {k: s.v for k, s in c.sems.items()}
    block = c.block
    for engname in ("sync", "gpsimd", "scalar", "vector", "tensor"):
        deco = getattr(block, engname)

        @deco
        def _(eng, sn=sn):
            for k, v in sn.items():
                eng.wait_ge(c.sems[k].h, v)


def build_full():
    nc = bass.Bass("TRN2", target_bir_lowering=False)
    x_d = nc.dram_tensor("x", [DM, T], F32, kind="ExternalInput").ap()
    y_d = nc.dram_tensor("y", [DM, T], F32, kind="ExternalOutput").ap()
    xb = [nc.dram_tensor(f"xb{i}", [DM, T], F32, kind="Internal").ap() for i in range(11)]

    def decl(prefix, names):
        return {nm: nc.dram_tensor(f"{prefix}{nm}", sh, F32, kind="ExternalInput").ap()
                for nm, sh in names.items()}

    Pf0 = decl("f0_", dict(w1=[4096, 1024], w2=[1024, 4096], b1t=[128, 32], b2t=[128, 8]))
    bw1_scr = nc.dram_tensor("bw1_scr", [4096, 1024], BF16, kind="Internal").ap()
    bw2_scr = nc.dram_tensor("bw2_scr", [1024, 4096], BF16, kind="Internal").ap()

    mamba_shapes = dict(winT=[1024, 4096], wxpT=[2048, 96], wdtT=[64, 2048],
                        woutT=[2048, 1024], convw_t=[128, 16, 4], convb_t=[128, 16],
                        bdt_t=[128, 16], alog_t=[128, 16, 16], d_t=[128, 16],
                        ident=[128, 128], gw_t=[128, 8], zero_state=[128, 16, 16])
    Pa0 = decl("a0_", mamba_shapes)
    scr_mb = dict(
        xc=nc.dram_tensor("mscr_xc", [2048, T], F32, kind="Internal").ap(),
        sz=nc.dram_tensor("mscr_sz", [2048, T], BF16, kind="Internal").ap(),
        dt=nc.dram_tensor("mscr_dt", [2048, T], F32, kind="Internal").ap(),
        bc=nc.dram_tensor("mscr_bc", [32, T], BF16, kind="Internal").ap(),
    )

    Pmoe = decl("moe_", dict(rw=[1024, 16], identf=[128, 128], gwa=[16 * 1024, 1024],
                             uwa=[16 * 1024, 1024], dwa=[16 * 1024, 1024]))
    scr_moe = dict(
        xlnf=nc.dram_tensor("moescr_xlnf", [DM, T], F32, kind="Internal").ap(),
        xln=nc.dram_tensor("moescr_xln", [DM, T], BF16, kind="Internal").ap(),
        m=nc.dram_tensor("moescr_m", [16, T], F32, kind="Internal").ap(),
        acc=nc.dram_tensor("moescr_acc", [DM, T], F32, kind="Internal").ap(),
    )

    Pqa = decl("qa_", dict(qkvT=[1024, 3072], bqkv_t=[128, 24], woT=[1024, 1024],
                           ob_t=[128, 8], cosT=[64, T], sinT=[64, T], pswap=[64, 64]))
    scr_qa = dict(
        xln=nc.dram_tensor("qascr_xln", [DM, T], BF16, kind="Internal").ap(),
        qkv=nc.dram_tensor("qascr_qkv", [3072, T], BF16, kind="Internal").ap(),
        y=nc.dram_tensor("qascr_y", [DM, T], BF16, kind="Internal").ap(),
    )

    Pf2 = decl("f2_", dict(w1eT=[1024, 512], w2eT=[512, 1024], wgT=[1024, 8],
                           bg_row=[1, 8], sel=[8, 4, 128], identf=[128, 128]))

    Pm2 = [decl(f"m2{i}_", mamba_shapes) for i in range(4)]
    fw_d = nc.dram_tensor("m2_fw_t", [128, 8], F32, kind="ExternalInput").ap()
    xa = [nc.dram_tensor(f"xa{i}", [DM, T], F32, kind="Internal").ap() for i in range(4)]
    xln0 = nc.dram_tensor("m2_xln0", [DM, T], F32, kind="Internal").ap()

    Pf3 = decl("f3_", dict(wrT=[1024, 1024], wkT=[1024, 1024], wvT=[1024, 1024],
                           mur_t=[128, 8], muk_t=[128, 8]))

    Pgla = decl("gla_", dict(s1_t=[128, 8], s2_t=[128, 8], wqT=[1024, 1024],
                             wkT=[1024, 1024], wvT=[1024, 1024], wogT=[1024, 1024],
                             woT=[1024, 1024], wg1T=[1024, 16], bg1c=[16, 1],
                             bog_t=[128, 8], wg2b=[16, 1024], bg2r=[1, 1024],
                             wgaT=[1024, 4096], wuaT=[1024, 4096], wdaT=[4096, 1024],
                             tmask=[128, 8], imask=[128, 8]))
    scr_gla = dict(
        z=nc.dram_tensor("gscr_z", [DM, T], F32, kind="Internal").ap(),
        r1=nc.dram_tensor("gscr_r1", [DM, T], BF16, kind="Internal").ap(),
        r1f=nc.dram_tensor("gscr_r1f", [DM, T], F32, kind="Internal").ap(),
        ogs=nc.dram_tensor("gscr_ogs", [DM, T], BF16, kind="Internal").ap(),
        qloc=nc.dram_tensor("gscr_qloc", [T, DM], F32, kind="Internal").ap(),
        gin32=nc.dram_tensor("gscr_gin32", [T, DM], F32, kind="Internal").ap(),
        gout32=nc.dram_tensor("gscr_gout32", [8 * T, DM], F32, kind="Internal").ap(),
        gin16=nc.dram_tensor("gscr_gin16", [2 * T, DM], BF16, kind="Internal").ap(),
        gout16=nc.dram_tensor("gscr_gout16", [8 * 2 * T, DM], BF16, kind="Internal").ap(),
        oln=nc.dram_tensor("gscr_oln", [T, DM], BF16, kind="Internal").ap(),
        x1=nc.dram_tensor("gscr_x1", [DM, T], F32, kind="Internal").ap(),
        r2=nc.dram_tensor("gscr_r2", [DM, T], BF16, kind="Internal").ap(),
        gh=nc.dram_tensor("gscr_gh", [4096, T], BF16, kind="Internal").ap(),
        hh=nc.dram_tensor("gscr_hh", [4096, T], BF16, kind="Internal").ap(),
    )

    with ExitStack() as es:
        block = es.enter_context(nc.Block())
        c = Ctx(nc, es, block)
        with c.scope() as cw:
            emit_l0ffn(cw, x_d, xb[0], Pf0["w1"], Pf0["w2"], Pf0["b1t"], Pf0["b2t"],
                       bw1_scr, bw2_scr)
        emit_fence(c)
        with c.scope() as cw:
            emit_mamba(cw, xb[0], xb[1], Pa0, scr_mb, "ln", "a0")
        emit_fence(c)
        with c.scope() as cw:
            emit_l0ffn(cw, xb[1], xb[2], Pf0["w1"], Pf0["w2"], Pf0["b1t"], Pf0["b2t"],
                       bw1_scr, bw2_scr, skip_binarize=True)
        emit_fence(c)
        with c.scope() as cw:
            emit_l1moe(cw, xb[2], xb[3], Pmoe, scr_moe, "mA")
        emit_fence(c)
        with c.scope() as cw:
            emit_l1attn(cw, xb[3], xb[4], Pqa, scr_qa, "qa")
        emit_fence(c)
        with c.scope() as cw:
            emit_l1moe(cw, xb[4], xb[5], Pmoe, scr_moe, "mB")
        emit_fence(c)
        with c.scope() as cw:
            emit_l2ffn(cw, xb[5], xb[6], Pf2, "fA")
        emit_fence(c)
        with c.scope() as cw:
            emit_ln_full(cw, xb[6], xln0, "oln")
        cur = xln0
        for i in range(4):
            emit_fence(c)
            with c.scope() as cw:
                emit_mamba(cw, cur, xa[i], Pm2[i], scr_mb, "rms", f"m2{i}")
            cur = xa[i]
        emit_fence(c)
        with c.scope() as cw:
            emit_final_rms(cw, cur, xb[7], fw_d, "fin", resid_d=xb[6])
        emit_fence(c)
        with c.scope() as cw:
            emit_l2ffn(cw, xb[7], xb[8], Pf2, "fB")
        emit_fence(c)
        with c.scope() as cw:
            emit_l3ffn(cw, xb[8], xb[9], Pf3, "tA")
        emit_fence(c)
        with c.scope() as cw:
            emit_l3attn(cw, xb[9], xb[10], Pgla, scr_gla, "gla")
        emit_fence(c)
        with c.scope() as cw:
            emit_l3ffn(cw, xb[10], y_d, Pf3, "tB")
        check_sems(c)
    return nc


def full_inmaps(inp, xs):
    def strip(d):
        return {k: v for k, v in d.items() if k != "x"}

    per_core = []
    f0 = [strip(d) for d in l0ffn_inmaps(inp, [None] * NC8)]
    a0 = [strip(d) for d in l0attn_inmaps(inp, [None] * NC8)]
    moe = [strip(d) for d in l1moe_inmaps(inp, [None] * NC8)]
    qa = [strip(d) for d in l1attn_inmaps(inp, [None] * NC8)]
    f2 = [strip(d) for d in l2ffn_inmaps(inp, [None] * NC8)]
    m2 = [strip(d) for d in l2attn_inmaps(inp, [None] * NC8)]
    f3 = [strip(d) for d in l3ffn_inmaps(inp, [None] * NC8)]
    gla = [strip(d) for d in l3attn_inmaps(inp, [None] * NC8)]
    for cid in range(NC8):
        d = dict(x=xs[cid])
        for pfx, src_d in (("f0_", f0[cid]), ("a0_", a0[cid]), ("moe_", moe[cid]),
                           ("qa_", qa[cid]), ("f2_", f2[cid]), ("f3_", f3[cid]),
                           ("gla_", gla[cid])):
            for k, v in src_d.items():
                d[pfx + k] = v
        for k, v in m2[cid].items():
            if k == "fw_t":
                d["m2_fw_t"] = v
            else:
                li, nm = k.split("_", 1)
                d[f"m2{li[1]}_{nm}"] = v
        per_core.append(d)
    return per_core


_BUILD_CACHE = {}


def _get_nc(key, builder):
    if key not in _BUILD_CACHE:
        _BUILD_CACHE[key] = builder()
    return _BUILD_CACHE[key]


def kernel(**inp):
    x = inp["L0_attn_x"]
    xs = [np.ascontiguousarray(np.asarray(x[cid], np.float32).T) for cid in range(NC8)]

    nc = _get_nc("full", build_full)
    res = run_bass_kernel_spmd(nc, full_inmaps(inp, xs), core_ids=list(range(NC8)))
    out = np.stack([np.asarray(res.results[cid]["y"]).T for cid in range(NC8)])
    return np.ascontiguousarray(out.astype(np.float32))


def kernel_unfused(**inp):
    x = inp["L0_attn_x"]
    xs = [np.ascontiguousarray(np.asarray(x[cid], np.float32).T) for cid in range(NC8)]

    steps = [
        ("l0ffn", build_l0ffn, l0ffn_inmaps),
        ("l0attn", lambda: build_mamba("ln"), l0attn_inmaps),
        ("l0ffn", build_l0ffn, l0ffn_inmaps),
        ("l1moe", build_l1moe, l1moe_inmaps),
        ("l1attn", build_l1attn, l1attn_inmaps),
        ("l1moe", build_l1moe, l1moe_inmaps),
        ("l2ffn", build_l2ffn, l2ffn_inmaps),
        ("l2attn", build_l2attn, l2attn_inmaps),
        ("l2ffn", build_l2ffn, l2ffn_inmaps),
        ("l3ffn", build_l3ffn, l3ffn_inmaps),
        ("l3attn", build_l3attn, l3attn_inmaps),
        ("l3ffn", build_l3ffn, l3ffn_inmaps),
    ]
    for key, builder, im_fn in steps:
        nc = _get_nc(key, builder)
        res = run_bass_kernel_spmd(nc, im_fn(inp, xs), core_ids=list(range(NC8)))
        xs = [np.ascontiguousarray(np.asarray(res.results[cid]["y"])) for cid in range(NC8)]
    out = np.stack([xs[cid].T for cid in range(NC8)])
    return np.ascontiguousarray(out.astype(np.float32))
